# version 02
# baseline (speedup 1.0000x reference)
from contextlib import ExitStack

import numpy as np
import ml_dtypes

import concourse.bass as bass
import concourse.mybir as mybir
from concourse.bass_utils import run_bass_kernel_spmd

F32 = mybir.dt.float32
BF16 = mybir.dt.bfloat16
I32 = mybir.dt.int32
AF = mybir.ActivationFunctionType
ALU = mybir.AluOpType
AX = mybir.AxisListType
IOffs = bass.IndirectOffsetOnAxis

D = 1024
T = 2048
NT = 16384
NTILE = 128
OTILE = 16
HEADS = 16
DH = 64
NE = 8
DFF = 4096
CAP = 5120
CLOC = 896
NSLOT = NE * CLOC
SCOLS = NSLOT // 128
BIG = 1.0e9
SBIG = 1.0e4

ENGINES = ("sync", "gpsimd", "tensor", "vector", "scalar")


class Sem:
    def __init__(self, stack, nc, name):
        self.h = stack.enter_context(nc.semaphore(name))
        self.n = 0

    def bump(self, k):
        self.n += k
        return self.n


class Prog:

    def __init__(self, nc, stack):
        self.nc = nc
        self.q = {e: [] for e in ENGINES}
        self.sems = {name: Sem(stack, nc, name) for name in
                     ("LD", "LG", "T", "V", "A", "ST")}

    def wait(self, eng, sem, val=None):
        s = self.sems[sem]
        v = s.n if val is None else val
        assert v <= s.n, f"wait {sem}>={v} but only {s.n} emitted"
        if v > 0:
            self.q[eng].append(lambda E, h=s.h, v=v: E.wait_ge(h, v))

    def dma(self, eng, sem, out, in_, **kw):
        s = self.sems[sem]
        s.bump(16)
        self.q[eng].append(
            lambda E, o=out, i=in_, h=s.h, kw=kw:
            E.dma_start(out=o, in_=i, **kw).then_inc(h, 16))
        return s.n

    def dmat(self, eng, sem, out, in_):
        s = self.sems[sem]
        s.bump(16)
        self.q[eng].append(
            lambda E, o=out, i=in_, h=s.h:
            E.dma_start_transpose(o, i).then_inc(h, 16))
        return s.n

    def _bound_reg(self, E, bounds):
        if not hasattr(self, "_bregs"):
            self._bregs = {}
        if bounds not in self._bregs:
            r = E.alloc_register(f"bnd{bounds}")
            E.reg_mov(r, bounds)
            self._bregs[bounds] = r
        return self._bregs[bounds]

    def igather(self, eng, sem, out, in_, idx, bounds):
        s = self.sems[sem]
        s.bump(16)

        def f(E, o=out, i=in_, x=idx, b=bounds, h=s.h):
            return E.indirect_dma_start(
                out=o, out_offset=None, in_=i,
                in_offset=IOffs(ap=x, axis=0),
                bounds_check=self._bound_reg(E, b), oob_is_err=False
            ).then_inc(h, 16)
        self.q[eng].append(f)
        return s.n

    def iscatter(self, eng, sem, out, in_, idx, bounds):
        s = self.sems[sem]
        s.bump(16)

        def f(E, o=out, i=in_, x=idx, b=bounds, h=s.h):
            return E.indirect_dma_start(
                out=o, out_offset=IOffs(ap=x, axis=0), in_=i, in_offset=None,
                bounds_check=self._bound_reg(E, b), oob_is_err=False
            ).then_inc(h, 16)
        self.q[eng].append(f)
        return s.n

    def op(self, eng, fn):
        self.q[eng].append(fn)
        if eng == "vector":
            self.q[eng].append(lambda E: E.drain())

    def mark(self, eng, sem, k=1):
        s = self.sems[sem]
        v = s.bump(k)
        if eng in ("vector", "scalar"):
            self.q[eng].append(lambda E: E.drain())
        self.q[eng].append(lambda E, h=s.h, k=k: E.sem_inc(h, k))
        return v

    def op_inc(self, eng, sem, fn, k=1):
        s = self.sems[sem]
        v = s.bump(k)
        self.q[eng].append(lambda E, f=fn, h=s.h, k=k: f(E).then_inc(h, k))
        return v


def ts(i, n):
    return slice(i * n, (i + 1) * n)


_DT_SIZE = {F32: 4, BF16: 2, I32: 4}


class Arena:

    def __init__(self, nc):
        self.nc = nc
        self.stack = [nc.sbuf_base]
        self.peak = nc.sbuf_base
        self.n = 0

    def push(self):
        self.stack.append(self.stack[-1])

    def pop(self):
        self.stack.pop()

    def take(self, name, shape, dtype):
        off = (self.stack[-1] + 63) // 64 * 64
        pp = 1
        for d in shape[1:]:
            pp *= d
        pp *= _DT_SIZE[dtype]
        self.n += 1
        t = self.nc.alloc_sbuf_tensor_at(f"{name}_{self.n}", shape, dtype,
                                         offset=off)
        self.stack[-1] = off + pp
        self.peak = max(self.peak, self.stack[-1])
        assert self.stack[-1] <= self.nc.sbuf_top, \
            f"SBUF overflow at {name}: {self.stack[-1]} > {self.nc.sbuf_top}"
        return t


def build(dbg=(), stop_after=None):
    nc = bass.Bass("TRN2", target_bir_lowering=False, debug=False)
    dbg = set(dbg)

    x_hbm = nc.dram_tensor("x", [NT, D], F32, kind="ExternalInput").ap()
    wT = {k: nc.dram_tensor(f"w{k}T", [D, D], F32, kind="ExternalInput").ap()
          for k in "qkvo"}
    bias = {k: nc.dram_tensor(f"b{k}", [1, D], F32, kind="ExternalInput").ap()
            for k in "qkvo"}
    wg = nc.dram_tensor("wg", [D, NE], F32, kind="ExternalInput").ap()
    cfc = nc.dram_tensor("cfc", [NE, D, DFF], F32, kind="ExternalInput").ap()
    cpj = nc.dram_tensor("cpj", [NE, DFF, D], F32, kind="ExternalInput").ap()
    tri_hbm = nc.dram_tensor("tri_rot", [128, 128], BF16, kind="ExternalInput").ap()
    one_hbm = nc.dram_tensor("ones_bf", [128, 128], BF16, kind="ExternalInput").ap()
    idb_hbm = nc.dram_tensor("id_bf", [128, 128], BF16, kind="ExternalInput").ap()
    idf_hbm = nc.dram_tensor("id_f", [128, 128], F32, kind="ExternalInput").ap()
    caus_hbm = nc.dram_tensor("caus_bf", [128, 128], BF16, kind="ExternalInput").ap()
    orow_hbm = nc.dram_tensor("onesrow_bf", [1, 2048], BF16, kind="ExternalInput").ap()
    orowf_hbm = nc.dram_tensor("onesrow_f", [1, 128], F32, kind="ExternalInput").ap()
    ocol_hbm = nc.dram_tensor("onescol_f", [128, 1], F32, kind="ExternalInput").ap()
    tok_hbm = nc.dram_tensor("tokid_f", [128, OTILE], F32, kind="ExternalInput").ap()

    y_out = nc.dram_tensor("y", [T, D], F32, kind="ExternalOutput").ap()

    xn_pad = nc.dram_tensor("xn_pad", [NT + 1, D], BF16).ap()
    exp_x = nc.dram_tensor("exp_x", [NSLOT, D], BF16).ap()
    exp_out = nc.dram_tensor("exp_out", [NSLOT + 1, D], BF16).ap()
    slot_tab = nc.dram_tensor("slot_tab", [NSLOT, 1], F32).ap()
    y_mid = nc.dram_tensor("y_mid", [T, D], BF16).ap()

    dbg_out = {}

    def dbgt(name, shape, dtype=F32):
        dbg_out[name] = nc.dram_tensor("dbg_" + name, shape, dtype,
                                       kind="ExternalOutput").ap()
        return dbg_out[name]

    with ExitStack() as top:
        e = top.enter_context
        p = Prog(nc, top)
        S = p.sems
        A = Arena(nc)

        c_tri = A.take("c_tri", [128, 128], BF16)
        c_one = A.take("c_one", [128, 128], BF16)
        c_idb = A.take("c_idb", [128, 128], BF16)
        c_idf = A.take("c_idf", [128, 128], F32)
        c_caus = A.take("c_caus", [128, 128], BF16)
        c_or = A.take("c_or", [1, 2048], BF16)
        c_orf = A.take("c_orf", [1, 128], F32)
        c_oc = A.take("c_oc", [128, 1], F32)
        c_tok = A.take("c_tok", [128, OTILE], F32)
        c_eps = A.take("c_eps", [128, 1], F32)
        zrow = A.take("zrow", [1, D], BF16)
        b_sb = {k: A.take(f"b_{k}", [1, D], BF16) for k in "qkvo"}
        wg_f = A.take("wg_f", [128, NE * 8], F32)
        wg_b = A.take("wg_b", [128, NE * 8], BF16)
        g_all = A.take("g_all", [128, NTILE * NE], F32)
        idx1 = A.take("idx1", [128, NTILE], F32)
        idx2 = A.take("idx2", [128, NTILE], F32)
        p1 = A.take("p1", [128, NTILE], F32)
        p2 = A.take("p2", [128, NTILE], F32)
        g0i = A.take("g0i", [128, OTILE], I32)
        g1i = A.take("g1i", [128, OTILE], I32)
        tabi4 = [A.take(f"tabi{i}", [128, 14], I32) for i in range(4)]
        mhat = A.take("mhat", [128, 1], F32)
        nrm = A.take("nrm", [1, 8], F32)

        ps_a = e(nc.psum_tensor("ps_a", [128, 512], F32))
        ps_b = e(nc.psum_tensor("ps_b", [128, 512], F32))
        ps_s = e(nc.psum_tensor("ps_s", [128, 2048], F32))
        ps_t = e(nc.psum_tensor("ps_t", [128, 128], BF16))
        ps_o = e(nc.psum_tensor("ps_o", [128, 512], F32))

        block = e(nc.Block())

        def finish_early():
            yz = A.take("yz", [128, D], F32)
            p.op("vector", lambda E: E.memset(yz[:, :], 0.0))
            vz = p.mark("vector", "V")
            p.wait("sync", "V", vz)
            p.wait("sync", "ST", S["ST"].n)
            for t in range(OTILE):
                p.dma("sync", "ST", y_out[ts(t, 128), :], yz[:, :])
            p.wait("sync", "ST", S["ST"].n)
            for eng, meth in (("sync", block.sync), ("gpsimd", block.gpsimd),
                              ("tensor", block.tensor), ("vector", block.vector),
                              ("scalar", block.scalar)):
                def runner(E, ops=p.q[eng]):
                    for f in ops:
                        f(E)
                meth(runner)

        for dst, src in ((c_tri, tri_hbm), (c_one, one_hbm), (c_idb, idb_hbm),
                         (c_idf, idf_hbm), (c_caus, caus_hbm), (c_or, orow_hbm),
                         (c_orf, orowf_hbm), (c_oc, ocol_hbm), (c_tok, tok_hbm)):
            p.dma("sync", "LD", dst[:, :], src)
        p.dma("sync", "LD", wg_f[:, :].rearrange("p (dc e) -> p dc e", e=NE),
              wg.rearrange("(dc p) e -> p dc e", p=128))
        for k in "qkvo":
            p.dma("gpsimd", "LG", b_sb[k][:, :], bias[k])
        ld_const = S["LD"].n
        lg_bias = S["LG"].n
        p.op("vector", lambda E: E.memset(c_eps[:, :], 1e-5))
        p.op("vector", lambda E: E.memset(zrow[:, :], 0.0))
        p.wait("vector", "LD", ld_const)
        p.op("vector", lambda E: E.tensor_copy(wg_b[:, :], wg_f[:, :]))
        v_const = p.mark("vector", "V")
        p.wait("sync", "V", v_const)
        p.dma("sync", "LD", xn_pad[NT:NT + 1, :], zrow[:, :])
        p.dma("sync", "LD", exp_out[NSLOT:NSLOT + 1, :], zrow[:, :])

        A.push()
        qt_sb = [A.take(f"qt{i}", [128, T], BF16) for i in range(8)]
        kt_sb = [A.take(f"kt{i}", [128, T], BF16) for i in range(8)]
        v_sb = [A.take(f"v{i}", [128, D], BF16) for i in range(OTILE)]

        A.push()
        NXB = 3
        xb = [A.take(f"xb{i}", [128, D], F32) for i in range(NXB)]
        xnb = [A.take(f"xnb{i}", [128, D], BF16) for i in range(NXB)]
        sq1 = A.take("sq1", [128, D], F32)
        colr = A.take("colr", [128, 8 * NXB], F32)

        xnst_t = [0] * NTILE
        vfree = [0] * NXB
        afree = [0] * NXB
        for i in range(NTILE):
            b = i % NXB
            if vfree[b]:
                p.wait("sync", "V", vfree[b])
                p.wait("sync", "A", afree[b])
            ldx = p.dma("sync", "LD", xb[b][:, :], x_hbm[ts(i, 128), :])
            cS = colr[:, b * 8 + 0:b * 8 + 1]
            cM = colr[:, b * 8 + 1:b * 8 + 2]
            cV = colr[:, b * 8 + 2:b * 8 + 3]
            cSD = colr[:, b * 8 + 3:b * 8 + 4]
            cR = colr[:, b * 8 + 4:b * 8 + 5]
            cMS = colr[:, b * 8 + 5:b * 8 + 6]
            p.wait("scalar", "LD", ldx)
            p.op("scalar", lambda E, b=b, cS=cS:
                 E.activation(sq1[:, :], xb[b][:, :], AF.Square, accum_out=cS))
            a_sq = p.mark("scalar", "A")
            p.wait("vector", "LD", ldx)
            p.op("vector", lambda E, b=b, cM=cM: E.reduce_sum(cM, xb[b][:, :], axis=AX.X))
            p.op("vector", lambda E, cM=cM: E.tensor_scalar_mul(cM, cM, 1.0 / D))
            p.wait("vector", "A", a_sq)
            p.op("vector", lambda E, cM=cM, cMS=cMS: E.tensor_mul(cMS, cM, cM))
            p.op("vector", lambda E, cS=cS, cV=cV:
                 E.tensor_scalar(cV, cS, 1.0 / D, None, op0=ALU.mult))
            p.op("vector", lambda E, cV=cV, cMS=cMS: E.tensor_sub(cV, cV, cMS))
            v_var = p.mark("vector", "V")
            p.wait("scalar", "V", v_var)
            p.op("scalar", lambda E, cV=cV, cSD=cSD:
                 E.activation(cSD, cV, AF.Sqrt, bias=c_eps[:, :]))
            a_sd = p.mark("scalar", "A")
            afree[b] = a_sd
            p.wait("vector", "A", a_sd)
            p.op("vector", lambda E, cSD=cSD, cR=cR: E.reciprocal(cR, cSD))
            p.op("vector", lambda E, b=b, cM=cM, cR=cR:
                 E.tensor_scalar(xnb[b][:, :], xb[b][:, :], cM, cR,
                                 op0=ALU.subtract, op1=ALU.mult))
            vfree[b] = p.mark("vector", "V")
            p.wait("sync", "V", vfree[b])
            xnst_t[i] = p.dma("sync", "LD", xn_pad[ts(i, 128), :], xnb[b][:, :])
        st_all_xn = S["LD"].n
        st_own_xn = xnst_t[OTILE - 1]
        v_p1 = S["V"].n
        a_p1 = S["A"].n

        if "xn" in dbg:
            od = dbgt("xn", [NT, D])
            rb = A.take("xn_rb", [128, D], BF16)
            rd = A.take("xn_rd", [128, D], F32)
            for i in range(NTILE):
                p.wait("gpsimd", "LD", st_all_xn)
                lg = p.dma("gpsimd", "LG", rb[:, :], xn_pad[ts(i, 128), :])
                p.wait("vector", "LG", lg)
                p.op("vector", lambda E: E.tensor_copy(rd[:, :], rb[:, :]))
                vv = p.mark("vector", "V")
                p.wait("gpsimd", "V", vv)
                p.dma("gpsimd", "ST", od[ts(i, 128), :], rd[:, :])

        if stop_after == "p1":
            finish_early()
            return nc, dbg_out
        A.pop()

        A.push()
        xnt = A.take("xnt", [128, 8 * 2048], BF16)
        wbuf = A.take("wbuf", [128, 8 * 1024], BF16)

        p.wait("sync", "V", v_p1)
        p.wait("sync", "A", a_p1)
        ldT = [0] * 8
        for dc in range(8):
            ldT[0] = p.dmat("sync", "LD", xnt[:, ts(dc, 2048)],
                            xn_pad[0:2048, ts(dc, 128)])

        lg_w = {}
        lg_w["q"] = p.dma("gpsimd", "LG",
                          wbuf[:, :].rearrange("p (dc i) -> p dc i", i=D),
                          wT["q"].rearrange("(dc p) i -> p dc i", p=128))

        def proj_qk(dst_tiles, wkey, bias_row, t_sem_name):
            p.wait("tensor", "LG", lg_w[wkey])
            p.wait("tensor", "LD", ldT[0])
            p.wait("tensor", "LG", lg_bias)
            last = 0
            evs = []
            for ic in range(8):
                for tc in range(4):
                    bank = ps_a if (ic * 4 + tc) % 2 == 0 else ps_b
                    if len(evs) >= 2:
                        p.wait("tensor", "V", evs[-2])

                    def mm(E, ic=ic, tc=tc, bank=bank, w=wkey, br=bias_row):
                        E.matmul(bank[:, :], br[0:1, ts(ic, 128)],
                                 c_or[0:1, ts(tc, 512)], start=True, stop=False)
                        for dc in range(8):
                            last = E.matmul(
                                bank[:, :],
                                wbuf[:, dc * 1024 + ic * 128: dc * 1024 + (ic + 1) * 128],
                                xnt[:, dc * 2048 + tc * 512: dc * 2048 + tc * 512 + 512],
                                start=False, stop=(dc == 7))
                        return last
                    t_mm = p.op_inc("tensor", "T", mm)
                    p.wait("vector", "T", t_mm)
                    p.op("vector", lambda E, ic=ic, tc=tc, bank=bank, dt=dst_tiles:
                         E.tensor_copy(dt[ic][:, ts(tc, 512)], bank[:, :]))
                    evs.append(p.mark("vector", "V"))
            return evs[-1]

        v_q = proj_qk(qt_sb, "q", b_sb["q"], "T")
        t_q_done = S["T"].n
        p.wait("gpsimd", "T", t_q_done)
        lg_w["k"] = p.dma("gpsimd", "LG",
                          wbuf[:, :].rearrange("p (dc i) -> p dc i", i=D),
                          wT["k"].rearrange("(dc p) i -> p dc i", p=128))
        v_k = proj_qk(kt_sb, "k", b_sb["k"], "T")
        t_k_done = S["T"].n
        p.wait("gpsimd", "T", t_k_done)
        lg_w["v"] = p.dma("gpsimd", "LG",
                          wbuf[:, :].rearrange("p (dc i) -> p dc i", i=D),
                          wT["v"].rearrange("(dc p) i -> p dc i", p=128))

        p.wait("tensor", "LG", lg_w["v"])
        v_vevs = []
        for tt in range(OTILE):
            for ic2 in range(2):
                bank = ps_a if (tt * 2 + ic2) % 2 == 0 else ps_b
                if len(v_vevs) >= 2:
                    p.wait("tensor", "V", v_vevs[-2])

                def mmv(E, tt=tt, ic2=ic2, bank=bank):
                    E.matmul(bank[:, :], c_or[0:1, ts(tt, 128)],
                             b_sb["v"][0:1, ts(ic2, 512)], start=True, stop=False)
                    for dc in range(8):
                        last = E.matmul(
                            bank[:, :],
                            xnt[:, dc * 2048 + tt * 128: dc * 2048 + (tt + 1) * 128],
                            wbuf[:, dc * 1024 + ic2 * 512: dc * 1024 + ic2 * 512 + 512],
                            start=False, stop=(dc == 7))
                    return last
                t_mm = p.op_inc("tensor", "T", mmv)
                p.wait("vector", "T", t_mm)
                p.op("vector", lambda E, tt=tt, ic2=ic2, bank=bank:
                     E.tensor_copy(v_sb[tt][:, ts(ic2, 512)], bank[:, :]))
                v_vevs.append(p.mark("vector", "V"))
        v_qkv_done = v_vevs[-1]
        t_qkv_done = S["T"].n
        if "qkv" in dbg:
            oq = dbgt("qt", [1024, 2048], BF16)
            ok_ = dbgt("kt", [1024, 2048], BF16)
            ov = dbgt("v", [2048, 1024], BF16)
            p.wait("scalar", "V", v_qkv_done)
            for i in range(8):
                p.dma("scalar", "ST", oq[ts(i, 128), :], qt_sb[i][:, :])
                p.dma("scalar", "ST", ok_[ts(i, 128), :], kt_sb[i][:, :])
            for i in range(OTILE):
                p.dma("scalar", "ST", ov[ts(i, 128), :], v_sb[i][:, :])
        if stop_after == "qkv":
            finish_early()
            return nc, dbg_out

        g_evs = []
        for ch in range(8):
            if ch > 0:
                p.wait("sync", "T", S["T"].n)
                for dc in range(8):
                    ldT[ch] = p.dmat("sync", "LD", xnt[:, ts(dc, 2048)],
                                     xn_pad[ch * 2048:(ch + 1) * 2048, ts(dc, 128)])
                p.wait("sync", "LD", st_all_xn)
            p.wait("tensor", "LD", ldT[ch])
            for tt in range(16):
                gi = ch * 16 + tt

                def mmg(E, tt=tt):
                    for dc in range(8):
                        last = E.matmul(
                            ps_o[:, 0:8],
                            xnt[:, dc * 2048 + tt * 128: dc * 2048 + (tt + 1) * 128],
                            wg_b[:, dc * 8:(dc + 1) * 8],
                            start=(dc == 0), stop=(dc == 7))
                    return last
                if g_evs:
                    p.wait("tensor", "V", g_evs[-1])
                t_g = p.op_inc("tensor", "T", mmg)
                p.wait("vector", "T", t_g)
                p.op("vector", lambda E, gi=gi:
                     E.tensor_copy(g_all[:, gi * 8:(gi + 1) * 8], ps_o[:, 0:8]))
                g_evs.append(p.mark("vector", "V"))
        v_g_done = g_evs[-1]
        t_g_done = S["T"].n

        if "g" in dbg:
            od = dbgt("g", [128, NTILE * NE])
            p.wait("scalar", "V", v_g_done)
            p.dma("scalar", "ST", od, g_all[:, :])

        if stop_after == "g":
            finish_early()
            return nc, dbg_out

        A.push()
        r_v1 = A.take("r_v1", [128, NTILE], F32)
        r_v2 = A.take("r_v2", [128, NTILE], F32)
        r_eq = A.take("r_eq", [128, NTILE], F32)
        r_t = A.take("r_t", [128, NTILE], F32)
        r_big = A.take("r_big", [128, NTILE], F32)
        g2 = A.take("g2", [128, NTILE * NE], F32)
        zf = A.take("zf", [128, NTILE], F32)
        s12 = A.take("s12", [128, NTILE], F32)

        g3 = g_all[:, :].rearrange("p (t e) -> p t e", e=NE)
        g23 = g2[:, :].rearrange("p (t e) -> p t e", e=NE)
        p.op("vector", lambda E: E.memset(r_big[:, :], SBIG))
        p.op("vector", lambda E: E.memset(zf[:, :], 0.0))
        p.op("vector", lambda E: E.reduce_max(r_v1[:, :], g3, axis=AX.X))
        p.op("vector", lambda E: E.memset(idx1[:, :], SBIG))
        for ee in range(NE):
            p.op("vector", lambda E, ee=ee:
                 E.tensor_tensor(r_eq[:, :], g3[:, :, ee], r_v1[:, :], op=ALU.is_equal))
            p.op("vector", lambda E, ee=ee: E.scalar_tensor_tensor(
                r_t[:, :], r_eq[:, :], -(SBIG - ee), r_big[:, :],
                op0=ALU.mult, op1=ALU.add))
            p.op("vector", lambda E: E.tensor_tensor(idx1[:, :], idx1[:, :],
                                                     r_t[:, :], op=ALU.min))
        for ee in range(NE):
            p.op("vector", lambda E, ee=ee: E.tensor_single_scalar(
                r_eq[:, :], idx1[:, :], float(ee), op=ALU.is_equal))
            p.op("vector", lambda E, ee=ee: E.scalar_tensor_tensor(
                g23[:, :, ee], r_eq[:, :], -1e30, g3[:, :, ee],
                op0=ALU.mult, op1=ALU.add))
        p.op("vector", lambda E: E.reduce_max(r_v2[:, :], g23, axis=AX.X))
        p.op("vector", lambda E: E.memset(idx2[:, :], SBIG))
        for ee in range(NE):
            p.op("vector", lambda E, ee=ee:
                 E.tensor_tensor(r_eq[:, :], g23[:, :, ee], r_v2[:, :], op=ALU.is_equal))
            p.op("vector", lambda E, ee=ee: E.scalar_tensor_tensor(
                r_t[:, :], r_eq[:, :], -(SBIG - ee), r_big[:, :],
                op0=ALU.mult, op1=ALU.add))
            p.op("vector", lambda E: E.tensor_tensor(idx2[:, :], idx2[:, :],
                                                     r_t[:, :], op=ALU.min))
        p.op("vector", lambda E: E.tensor_sub(s12[:, :], r_v1[:, :], r_v2[:, :]))
        v_rt = p.mark("vector", "V")
        p.wait("scalar", "V", v_rt)
        p.op("scalar", lambda E: E.activation(p1[:, :], s12[:, :], AF.Sigmoid))
        p.op("scalar", lambda E: E.activation(p2[:, :], s12[:, :], AF.Sigmoid,
                                              scale=-1.0))
        a_probs = p.mark("scalar", "A")

        if "route" in dbg:
            for nm, buf in (("idx1", idx1), ("idx2", idx2), ("p1", p1), ("p2", p2)):
                od = dbgt(nm, [128, NTILE])
                p.wait("scalar", "A", a_probs)
                p.wait("scalar", "V", v_rt)
                p.dma("scalar", "ST", od, buf[:, :])

        idxT = [A.take(f"idxT{k}", [128, 128], F32) for k in range(2)]
        m_ke = A.take("m_ke", [128, 128], F32)
        incl = A.take("incl", [128, 128], F32)
        plane = A.take("plane", [128, 128], F32)
        rsel = [A.take(f"rsel{k}", [128, 128], F32) for k in range(2)]
        rtt = [A.take(f"rtt{k}", [128, 128], F32) for k in range(2)]
        totb = A.take("totb", [128, 2 * NE], BF16)
        carr = A.take("carr", [128, 2 * NE], F32)

        p.wait("tensor", "V", v_rt)
        for k, src in enumerate((idx1, idx2)):
            t_tr = p.op_inc("tensor", "T", lambda E, s=src:
                            E.transpose(ps_a[:, 0:128], s[:, :], c_idf[:, :]))
            p.wait("vector", "T", t_tr)
            p.op("vector", lambda E, k=k: E.tensor_copy(idxT[k][:, :], ps_a[:, 0:128]))
            v_m = p.mark("vector", "V")
            p.wait("tensor", "V", v_m)
        v_idxT = p.mark("vector", "V")

        for k in range(2):
            for ee in range(NE):
                p.op("vector", lambda E, k=k, ee=ee: E.tensor_single_scalar(
                    m_ke[:, :], idxT[k][:, :], float(ee), op=ALU.is_equal))
                p.op("vector", lambda E: E.tensor_tensor_scan(
                    incl[:, :], m_ke[:, :], zf[:, 0:128], 0.0,
                    op0=ALU.add, op1=ALU.add))
                p.op("vector", lambda E, k=k, ee=ee: E.tensor_copy(
                    totb[:, k * NE + ee:k * NE + ee + 1], incl[:, 127:128]))
        v_tot = p.mark("vector", "V")
        p.wait("tensor", "V", v_tot)

        def mm_car(E):
            E.matmul(ps_o[:, 0:8], c_tri[:, :], totb[:, 0:NE], start=True, stop=True)
            return E.matmul(ps_o[:, 8:16], c_tri[:, :], totb[:, NE:2 * NE],
                            start=True, stop=False)

        def mm_car2(E):
            return E.matmul(ps_o[:, 8:16], c_one[:, :], totb[:, 0:NE],
                            start=False, stop=True)
        t_car = p.op_inc("tensor", "T", lambda E: (mm_car(E), mm_car2(E))[1])
        p.wait("vector", "T", t_car)
        p.op("vector", lambda E: E.tensor_copy(carr[:, 0:NE], ps_o[:, 0:8]))
        p.op("vector", lambda E: E.tensor_copy(carr[:, NE:2 * NE], ps_o[:, 8:16]))
        for k in range(2):
            p.op("vector", lambda E, k=k: E.memset(rsel[k][:, :], 0.0))
            for ee in range(NE):
                p.op("vector", lambda E, k=k, ee=ee: E.tensor_single_scalar(
                    m_ke[:, :], idxT[k][:, :], float(ee), op=ALU.is_equal))
                p.op("vector", lambda E: E.tensor_tensor_scan(
                    incl[:, :], m_ke[:, :], zf[:, 0:128], 0.0,
                    op0=ALU.add, op1=ALU.add))
                p.op("vector", lambda E, k=k, ee=ee: E.scalar_tensor_tensor(
                    plane[:, :], incl[:, :], carr[:, k * NE + ee:k * NE + ee + 1],
                    m_ke[:, :], op0=ALU.add, op1=ALU.subtract))
                p.op("vector", lambda E: E.tensor_mul(plane[:, :], plane[:, :],
                                                      m_ke[:, :]))
                p.op("vector", lambda E, k=k: E.tensor_add(rsel[k][:, :],
                                                           rsel[k][:, :], plane[:, :]))
        v_rsel = p.mark("vector", "V")
        p.wait("tensor", "V", v_rsel)
        for k in range(2):
            t_tr = p.op_inc("tensor", "T", lambda E, k=k:
                            E.transpose(ps_a[:, 0:128], rsel[k][:, :], c_idf[:, :]))
            p.wait("vector", "T", t_tr)
            p.op("vector", lambda E, k=k: E.tensor_copy(rtt[k][:, :], ps_a[:, 0:128]))
            v_m = p.mark("vector", "V")
            p.wait("tensor", "V", v_m)
        v_rtt = p.mark("vector", "V")

        if "rank" in dbg:
            for k in range(2):
                od = dbgt(f"rtt{k}", [128, 128])
                p.wait("scalar", "V", v_rtt)
                p.dma("scalar", "ST", od, rtt[k][:, :])
            p.dma("scalar", "ST", dbgt("rsel0", [128, 128]), rsel[0][:, :])
            p.dma("scalar", "ST", dbgt("idxT0", [128, 128]), idxT[0][:, :])
            p.dma("scalar", "ST", dbgt("carr", [128, 16]), carr[:, :])

        if stop_after == "rank":
            finish_early()
            return nc, dbg_out

        oloc = [A.take(f"oloc{k}", [16, 128], F32) for k in range(2)]
        lincl = A.take("lincl", [16, 128], F32)
        lm = A.take("lm", [16, 128], F32)
        lpl = A.take("lpl", [16, 128], F32)
        lrsel = [A.take(f"lrsel{k}", [16, 128], F32) for k in range(2)]
        lrtt = [A.take(f"lrtt{k}", [128, OTILE], F32) for k in range(2)]
        ltotb = A.take("ltotb", [16, 2 * NE], BF16)
        lcarr = A.take("lcarr", [16, 2 * NE], F32)

        for k, src in enumerate((idx1, idx2)):
            t_tr = p.op_inc("tensor", "T", lambda E, s=src:
                            E.transpose(ps_a[0:16, 0:128], s[:, 0:OTILE],
                                        c_idf[:, :]))
            p.wait("vector", "T", t_tr)
            p.op("vector", lambda E, k=k: E.tensor_copy(oloc[k][:, :], ps_a[0:16, 0:128]))
            v_m = p.mark("vector", "V")
            p.wait("tensor", "V", v_m)
        v_oloc = p.mark("vector", "V")
        for k in range(2):
            for ee in range(NE):
                p.op("vector", lambda E, k=k, ee=ee: E.tensor_single_scalar(
                    lm[:, :], oloc[k][:, :], float(ee), op=ALU.is_equal))
                p.op("vector", lambda E: E.tensor_tensor_scan(
                    lincl[:, :], lm[:, :], zf[0:16, 0:128], 0.0,
                    op0=ALU.add, op1=ALU.add))
                p.op("vector", lambda E, k=k, ee=ee: E.tensor_copy(
                    ltotb[:, k * NE + ee:k * NE + ee + 1], lincl[:, 127:128]))
        v_ltot = p.mark("vector", "V")
        p.wait("tensor", "V", v_ltot)

        def mm_lcar(E):
            E.matmul(ps_o[0:16, 0:8], c_tri[0:16, 0:16], ltotb[:, 0:NE],
                     start=True, stop=True)
            E.matmul(ps_o[0:16, 8:16], c_tri[0:16, 0:16], ltotb[:, NE:2 * NE],
                     start=True, stop=False)
            return E.matmul(ps_o[0:16, 8:16], c_one[0:16, 0:16], ltotb[:, 0:NE],
                            start=False, stop=True)
        t_lcar = p.op_inc("tensor", "T", mm_lcar)
        p.wait("vector", "T", t_lcar)
        p.op("vector", lambda E: E.tensor_copy(lcarr[:, 0:NE], ps_o[0:16, 0:8]))
        p.op("vector", lambda E: E.tensor_copy(lcarr[:, NE:2 * NE], ps_o[0:16, 8:16]))
        for k in range(2):
            p.op("vector", lambda E, k=k: E.memset(lrsel[k][:, :], 0.0))
            for ee in range(NE):
                p.op("vector", lambda E, k=k, ee=ee: E.tensor_single_scalar(
                    lm[:, :], oloc[k][:, :], float(ee), op=ALU.is_equal))
                p.op("vector", lambda E: E.tensor_tensor_scan(
                    lincl[:, :], lm[:, :], zf[0:16, 0:128], 0.0,
                    op0=ALU.add, op1=ALU.add))
                p.op("vector", lambda E, k=k, ee=ee: E.scalar_tensor_tensor(
                    lpl[:, :], lincl[:, :], lcarr[:, k * NE + ee:k * NE + ee + 1],
                    lm[:, :], op0=ALU.add, op1=ALU.subtract))
                p.op("vector", lambda E: E.tensor_mul(lpl[:, :], lpl[:, :], lm[:, :]))
                p.op("vector", lambda E, k=k: E.tensor_add(lrsel[k][:, :],
                                                           lrsel[k][:, :], lpl[:, :]))
        v_lrsel = p.mark("vector", "V")
        p.wait("tensor", "V", v_lrsel)
        for k in range(2):
            t_tr = p.op_inc("tensor", "T", lambda E, k=k:
                            E.transpose(ps_a[:, 0:OTILE], lrsel[k][:, :],
                                        c_idf[0:16, 0:16]))
            p.wait("vector", "T", t_tr)
            p.op("vector", lambda E, k=k: E.tensor_copy(lrtt[k][:, :],
                                                        ps_a[:, 0:OTILE]))
            v_m = p.mark("vector", "V")
            p.wait("tensor", "V", v_m)
        v_lrtt = p.mark("vector", "V")

        dropf = A.take("dropf", [128, OTILE], F32)
        oflf = A.take("oflf", [128, OTILE], F32)
        basei = A.take("basei", [128, OTILE], F32)
        sidxf = A.take("sidxf", [128, OTILE], F32)
        sidxi = [A.take(f"sidxi{k}", [128, OTILE], I32) for k in range(2)]
        svals = [A.take(f"svals{k}", [128, OTILE], F32) for k in range(2)]
        tinv = A.take("tinv", [128, OTILE], F32)
        gtmp = A.take("gtmp", [128, OTILE], F32)
        tabf = A.take("tabf", [128, SCOLS], F32)
        tini = A.take("tini", [128, SCOLS], F32)

        p.op("vector", lambda E: E.tensor_scalar(
            tinv[:, :], c_tok[:, :], -1.0, float(NT), op0=ALU.mult, op1=ALU.add))
        p.op("vector", lambda E: E.memset(tini[:, :], float(NT)))
        for k in range(2):
            own = (idx1 if k == 0 else idx2)[:, 0:OTILE]
            p.op("vector", lambda E, k=k: E.tensor_single_scalar(
                dropf[:, :], rtt[k][:, 0:OTILE], float(CAP), op=ALU.is_ge))
            p.op("vector", lambda E, k=k: E.tensor_single_scalar(
                oflf[:, :], lrtt[k][:, :], float(CLOC), op=ALU.is_ge))
            p.op("vector", lambda E, k=k, own=own: E.scalar_tensor_tensor(
                basei[:, :], own, float(CLOC), lrtt[k][:, :],
                op0=ALU.mult, op1=ALU.add))
            p.op("vector", lambda E: E.scalar_tensor_tensor(
                sidxf[:, :], oflf[:, :], BIG, basei[:, :],
                op0=ALU.mult, op1=ALU.add))
            p.op("vector", lambda E, k=k: E.tensor_copy(sidxi[k][:, :], sidxf[:, :]))
            p.op("vector", lambda E: E.tensor_mul(gtmp[:, :], dropf[:, :], tinv[:, :]))
            p.op("vector", lambda E, k=k: E.tensor_add(svals[k][:, :],
                                                       gtmp[:, :], c_tok[:, :]))
            p.op("vector", lambda E: E.tensor_scalar(
                gtmp[:, :], basei[:, :], -1.0, float(NSLOT), op0=ALU.mult, op1=ALU.add))
            p.op("vector", lambda E: E.tensor_mul(gtmp[:, :], gtmp[:, :], oflf[:, :]))
            p.op("vector", lambda E: E.tensor_add(gtmp[:, :], gtmp[:, :], basei[:, :]))
            gi_sb = g0i if k == 0 else g1i
            p.op("vector", lambda E, gi_sb=gi_sb: E.tensor_copy(gi_sb[:, :], gtmp[:, :]))
        v_scati = p.mark("vector", "V")

        if stop_after == "scatidx":
            finish_early()
            return nc, dbg_out

        p.wait("gpsimd", "V", v_scati)
        lg_tini = p.dma("gpsimd", "LG",
                        slot_tab.rearrange("(p f) one -> p (f one)", p=128),
                        tini[:, :])
        p.wait("gpsimd", "LG", lg_tini)
        for k in range(2):
            for t in range(OTILE):
                p.iscatter("gpsimd", "LG", slot_tab[:, :],
                           svals[k][:, t:t + 1], sidxi[k][:, t:t + 1], NSLOT - 1)
        lg_scat = S["LG"].n
        p.wait("gpsimd", "LG", lg_scat)
        lg_tab = p.dma("gpsimd", "LG", tabf[:, :],
                       slot_tab.rearrange("(p f) one -> p (f one)", p=128))
        p.wait("vector", "LG", lg_tab)
        for i4 in range(4):
            p.op("vector", lambda E, i4=i4: E.tensor_copy(
                tabi4[i4][:, :], tabf[:, i4 * 14:(i4 + 1) * 14]))
        v_tabi = p.mark("vector", "V")

        if "tab" in dbg:
            od = dbgt("tab", [128, SCOLS])
            p.wait("scalar", "V", v_tabi)
            p.dma("scalar", "ST", od, tabf[:, :])
            od2 = dbgt("lrtt0", [128, OTILE])
            p.dma("scalar", "ST", od2, lrtt[0][:, :])
            od3 = dbgt("g0", [128, OTILE], I32)
            p.dma("scalar", "ST", od3, g0i[:, :])
            od4 = dbgt("g1", [128, OTILE], I32)
            p.dma("scalar", "ST", od4, g1i[:, :])

        if stop_after == "route":
            finish_early()
            return nc, dbg_out
        A.pop()
        A.pop()

        gxb = [A.take(f"gxb{i}", [128, D], BF16) for i in range(4)]
        disp_st = [0] * SCOLS
        p.wait("gpsimd", "V", v_tabi)
        p.wait("gpsimd", "LD", st_all_xn)
        for f in range(SCOLS):
            b = f % 4
            if f >= 4:
                p.wait("gpsimd", "LG", disp_st[f - 4])
            lg_g = p.igather("gpsimd", "LG", gxb[b][:, :], xn_pad[:, :],
                             tabi4[f // 14][:, f % 14:f % 14 + 1], NT)
            p.wait("gpsimd", "LG", lg_g)
            disp_st[f] = p.dma(
                "gpsimd", "LG",
                exp_x.rearrange("(pp f) d -> pp f d", f=SCOLS)[:, f, :],
                gxb[b][:, :])
        lg_disp = S["LG"].n


        A.push()
        sq2 = A.take("sq2", [128, 2048], F32)
        pbuf = A.take("pbuf", [128, 2048], BF16)
        ptb = A.take("ptb", [128, 2048], BF16)
        ybuf = [A.take(f"ybuf{i}", [128, D], BF16) for i in range(2)]
        lcol = A.take("lcol", [128, 10], F32)

        p.op("vector", lambda E: E.memset(nrm[:, :], 0.0))
        v_nz = p.mark("vector", "V")
        a_norm_sq = []
        for which, tiles_, slot in (("q", qt_sb, 0), ("k", kt_sb, 1)):
            for it in range(8):
                p.wait("scalar", "V", v_qkv_done if which == "k" else v_q)
                p.wait("scalar", "T", t_g_done)
                p.op("scalar", lambda E, tiles_=tiles_, it=it:
                     E.activation(sq2[:, :], tiles_[it][:, :], AF.Square))
                a_s = p.mark("scalar", "A")
                for tc in range(4):
                    p.wait("tensor", "A", a_s)
                    p.wait("tensor", "V", v_nz)
                    t_n = p.op_inc("tensor", "T", lambda E, tc=tc:
                                   E.matmul(ps_b[0:1, 0:512], c_oc[:, :],
                                            sq2[:, ts(tc, 512)], start=True, stop=True))
                    p.wait("vector", "T", t_n)
                    p.op("vector", lambda E: E.reduce_max(nrm[0:1, 2:3],
                                                          ps_b[0:1, 0:512], axis=AX.X))
                    p.op("vector", lambda E, slot=slot: E.tensor_tensor(
                        nrm[0:1, slot:slot + 1], nrm[0:1, slot:slot + 1],
                        nrm[0:1, 2:3], op=ALU.max))
                    v_nz = p.mark("vector", "V")
        p.op("vector", lambda E: E.tensor_mul(nrm[0:1, 3:4], nrm[0:1, 0:1],
                                              nrm[0:1, 1:2]))
        v_nn = p.mark("vector", "V")
        p.wait("scalar", "V", v_nn)
        p.op("scalar", lambda E: E.activation(nrm[0:1, 4:5], nrm[0:1, 3:4], AF.Sqrt))
        a_nn = p.mark("scalar", "A")
        p.wait("vector", "A", a_nn)
        p.op("vector", lambda E: E.tensor_scalar_mul(nrm[0:1, 5:6],
                                                     nrm[0:1, 4:5], -0.125))
        v_mh = p.mark("vector", "V")
        p.wait("tensor", "V", v_mh)
        t_mh = p.op_inc("tensor", "T", lambda E:
                        E.matmul(ps_b[:, 0:1], c_orf[0:1, :], nrm[0:1, 5:6],
                                 start=True, stop=True))
        p.wait("vector", "T", t_mh)
        p.op("vector", lambda E: E.tensor_copy(mhat[:, :], ps_b[:, 0:1]))
        v_mhat = p.mark("vector", "V")
        if "attn" in dbg:
            p.wait("scalar", "V", v_mhat)
            p.dma("scalar", "ST", dbgt("mhat", [128, 1]), mhat[:, :])
            p.dma("scalar", "ST", dbgt("nrm", [1, 8]), nrm[:, :])
        p.wait("scalar", "V", v_mhat)

        ymid_st = [0] * OTILE
        a_exp_prev = 0
        a_o_prev = 0
        t_pv_prev = 0
        v_pm_hist = []
        for qt in range(OTILE):
            L = (qt + 1) * 128
            nkb = L // 128
            yb = ybuf[qt % 2]
            if qt >= 2:
                p.wait("scalar", "LD", ymid_st[qt - 2])
            for h in range(HEADS):
                til, base = h // 2, (h % 2) * 64
                qsl = qt_sb[til][base:base + 64, ts(qt, 128)]
                lc = lcol[:, h % 4:h % 4 + 1]

                p.wait("tensor", "A", a_exp_prev)

                def mm_s(E, qsl=qsl, til=til, base=base, L=L):
                    for kb4 in range((L + 511) // 512):
                        w = min(512, L - kb4 * 512)
                        last = E.matmul(ps_s[:, kb4 * 512:kb4 * 512 + w],
                                        qsl, kt_sb[til][base:base + 64,
                                                        kb4 * 512:kb4 * 512 + w],
                                        start=True, stop=True)
                    return last
                t_s = p.op_inc("tensor", "T", mm_s)
                p.wait("scalar", "T", max(t_s, t_pv_prev))
                if len(v_pm_hist) >= 2:
                    p.wait("scalar", "V", v_pm_hist[-2])
                if L > 128:
                    p.op("scalar", lambda E, L=L, lc=lc: E.activation(
                        pbuf[:, 0:L - 128], ps_s[:, 0:L - 128], AF.Exp,
                        bias=mhat[:, :], scale=0.125, accum_out=lc))
                p.op("scalar", lambda E, L=L: E.activation(
                    pbuf[:, L - 128:L], ps_s[:, L - 128:L], AF.Exp,
                    bias=mhat[:, :], scale=0.125))
                a_exp = p.mark("scalar", "A")
                p.wait("vector", "A", a_exp)
                p.op("vector", lambda E, L=L: E.tensor_mul(
                    pbuf[:, L - 128:L], pbuf[:, L - 128:L], c_caus[:, :]))
                p.op("vector", lambda E, L=L: E.reduce_sum(
                    lcol[:, 8:9], pbuf[:, L - 128:L], axis=AX.X))
                if L > 128:
                    p.op("vector", lambda E, lc=lc: E.tensor_add(lc, lc, lcol[:, 8:9]))
                else:
                    p.op("vector", lambda E, lc=lc: E.tensor_copy(lc, lcol[:, 8:9]))
                rl = lcol[:, 4 + h % 4:5 + h % 4]
                p.op("vector", lambda E, lc=lc, rl=rl: E.reciprocal(rl, lc))
                v_pm = p.mark("vector", "V")
                v_pm_hist.append(v_pm)
                a_exp_prev = a_exp

                for kb in range(nkb):
                    p.wait("tensor", "V", v_pm)
                    if kb == 0:
                        p.wait("tensor", "A", a_o_prev)

                    def mm_t(E, kb=kb):
                        return E.transpose(ps_t[:, :], pbuf[:, ts(kb, 128)],
                                           c_idb[:, :])
                    t_tr = p.op_inc("tensor", "T", mm_t)
                    p.wait("vector", "T", t_tr)
                    p.op("vector", lambda E, kb=kb:
                         E.tensor_copy(ptb[:, ts(kb, 128)], ps_t[:, :]))
                    v_pt = p.mark("vector", "V")
                    p.wait("tensor", "V", v_pt)

                    def mm_pv(E, kb=kb, h=h, nkb=nkb):
                        return E.matmul(ps_o[:, 0:64], ptb[:, ts(kb, 128)],
                                        v_sb[kb][:, ts(h, 64)],
                                        start=(kb == 0), stop=(kb == nkb - 1))
                    t_pv = p.op_inc("tensor", "T", mm_pv)
                t_pv_prev = t_pv
                p.wait("scalar", "T", t_pv)
                p.wait("scalar", "V", v_pm)
                p.op("scalar", lambda E, yb=yb, h=h, rl=rl: E.activation(
                    yb[:, ts(h, 64)], ps_o[:, 0:64], AF.Copy,
                    scale=rl))
                a_o = p.mark("scalar", "A")
                a_o_prev = a_o
            p.wait("scalar", "A", a_o)
            ymid_st[qt] = p.dma("scalar", "LD", y_mid[ts(qt, 128), :], yb[:, :])
        ld_ymid = S["LD"].n
        t_attn_done = S["T"].n
        a_attn_done = S["A"].n
        v_attn_done = S["V"].n
        A.pop()
        A.pop()

        if "attn" in dbg:
            od = dbgt("attnY", [T, D])
            rb2 = A.take("at_rb", [128, D], BF16)
            rd2 = A.take("at_rd", [128, D], F32)
            for i in range(OTILE):
                p.wait("gpsimd", "LD", ld_ymid)
                lg = p.dma("gpsimd", "LG", rb2[:, :], y_mid[ts(i, 128), :])
                p.wait("vector", "LG", lg)
                p.op("vector", lambda E: E.tensor_copy(rd2[:, :], rb2[:, :]))
                vv = p.mark("vector", "V")
                p.wait("gpsimd", "V", vv)
                p.dma("gpsimd", "ST", od[ts(i, 128), :], rd2[:, :])

        if stop_after == "attn":
            finish_early()
            return nc, dbg_out

        attn_sb = A.take("attn_sb", [128, OTILE * D], BF16)

        A.push()
        ytb = A.take("ytb", [128, 8 * 2048], BF16)
        wo_sb = A.take("wo_sb", [128, 8 * 1024], BF16)
        p.wait("gpsimd", "T", t_attn_done)
        p.wait("gpsimd", "A", a_attn_done)
        p.wait("gpsimd", "V", v_attn_done)
        lg_wo = p.dma("gpsimd", "LG",
                      wo_sb[:, :].rearrange("p (dc i) -> p dc i", i=D),
                      wT["o"].rearrange("(dc p) i -> p dc i", p=128))
        p.wait("sync", "LD", ld_ymid)
        p.wait("sync", "T", t_attn_done)
        ldyT = 0
        for ic in range(8):
            ldyT = p.dmat("sync", "LD", ytb[:, ts(ic, 2048)],
                          y_mid[0:2048, ts(ic, 128)])
        p.wait("tensor", "LD", ldyT)
        p.wait("tensor", "LG", lg_wo)
        ev2 = []
        for t in range(OTILE):
            for jc in range(2):
                bank = ps_a if (t * 2 + jc) % 2 == 0 else ps_b
                if len(ev2) >= 2:
                    p.wait("tensor", "V", ev2[-2])

                def mm_yo(E, t=t, jc=jc, bank=bank):
                    E.matmul(bank[:, :], c_or[0:1, ts(t, 128)],
                             b_sb["o"][0:1, ts(jc, 512)], start=True, stop=False)
                    for ic in range(8):
                        last = E.matmul(
                            bank[:, :],
                            ytb[:, ic * 2048 + t * 128: ic * 2048 + (t + 1) * 128],
                            wo_sb[:, ic * 1024 + jc * 512: ic * 1024 + jc * 512 + 512],
                            start=False, stop=(ic == 7))
                    return last
                t_mm = p.op_inc("tensor", "T", mm_yo)
                p.wait("vector", "T", t_mm)
                p.op("vector", lambda E, t=t, jc=jc, bank=bank: E.tensor_copy(
                    attn_sb[:, t * D + jc * 512: t * D + jc * 512 + 512],
                    bank[:, :]))
                ev2.append(p.mark("vector", "V"))
        v_yproj = ev2[-1]
        t_yproj = S["T"].n
        if "yproj" in dbg:
            od = dbgt("attnout", [128, OTILE * D], BF16)
            p.wait("scalar", "V", v_yproj)
            p.dma("scalar", "ST", od, attn_sb[:, :])
            oy = dbgt("ymid", [T, D], BF16)
            for i in range(OTILE):
                p.dma("scalar", "ST", oy[ts(i, 128), :], y_mid[ts(i, 128), :])
        A.pop()

        A.push()
        ht = A.take("ht", [128, 32 * CLOC], BF16)
        xtb = A.take("xtb", [128, 8 * CLOC], BF16)
        slab = [A.take(f"slab{i}", [128, 8 * 128], BF16) for i in range(2)]
        pchk = [A.take(f"pchk{i}", [128, 512], BF16) for i in range(2)]
        eo = [A.take(f"eo{i}", [128, 512], BF16) for i in range(2)]

        slab_ld = {}
        pchk_consumed_T = [0, 0]
        slab_consumed_T = [0, 0]
        exp_st = []
        t_h_done_prev = 0
        t_out_done_prev = 0
        p.wait("tensor", "V", v_yproj)
        p.wait("sync", "T", t_yproj)
        p.wait("gpsimd", "T", t_yproj)
        p.wait("gpsimd", "LG", lg_disp)
        a_h_prev = [0, 0]
        for ee in range(NE):
            p.wait("sync", "T", t_h_done_prev)
            ldxT = 0
            for dc in range(8):
                ldxT = p.dmat("sync", "LD", xtb[:, ts(dc, CLOC)],
                              exp_x[ts(ee, CLOC), ts(dc, 128)])
            cfc_v = cfc[ee].rearrange("(dc p) (fc f) -> fc p dc f", p=128, f=128)
            for fc in range(32):
                sb = slab[fc % 2]
                if slab_consumed_T[fc % 2]:
                    p.wait("gpsimd", "T", slab_consumed_T[fc % 2])
                lg_s = p.dma("gpsimd", "LG",
                             sb[:, :].rearrange("p (dc f) -> p dc f", f=128),
                             cfc_v[fc])
                p.wait("tensor", "LG", lg_s)
                if fc == 0:
                    p.wait("tensor", "LD", ldxT)
                for sc in range(2):
                    w = 512 if sc == 0 else CLOC - 512
                    bank = ps_a if sc == 0 else ps_b
                    if a_h_prev[sc]:
                        p.wait("tensor", "A", a_h_prev[sc])

                    def mm_h(E, sb=sb, sc=sc, w=w, bank=bank):
                        for dc in range(8):
                            last = E.matmul(
                                bank[:, 0:w], sb[:, ts(dc, 128)],
                                xtb[:, dc * CLOC + sc * 512: dc * CLOC + sc * 512 + w],
                                start=(dc == 0), stop=(dc == 7))
                        return last
                    t_h = p.op_inc("tensor", "T", mm_h)
                    p.wait("scalar", "T", t_h)
                    if fc == 0 and sc == 0:
                        p.wait("scalar", "T", t_out_done_prev)
                    p.op("scalar", lambda E, fc=fc, sc=sc, w=w, bank=bank:
                         E.activation(ht[:, fc * CLOC + sc * 512:
                                         fc * CLOC + sc * 512 + w],
                                      bank[:, 0:w], AF.Gelu))
                    a_h = p.mark("scalar", "A")
                    a_h_prev[sc] = a_h
                slab_consumed_T[fc % 2] = S["T"].n
            t_h_done_prev = S["T"].n
            a_ht_done = S["A"].n
            banks7 = [ps_s[:, ts(i, 512)] for i in range(4)] + \
                     [ps_a[:, :], ps_b[:, :], ps_o[:, :]]
            for dcp in range(2):
                ev_out = []
                for fc in range(32):
                    cb = pchk[fc % 2]
                    if pchk_consumed_T[fc % 2]:
                        p.wait("gpsimd", "T", pchk_consumed_T[fc % 2])
                    lg_c = p.dma("gpsimd", "LG", cb[:, :],
                                 cpj[ee][ts(fc, 128), ts(dcp, 512)])
                    p.wait("tensor", "LG", lg_c)
                    if fc == 0:
                        p.wait("tensor", "A", a_ht_done)
                        p.wait("tensor", "V")
                        if dcp == 0:
                            p.wait("tensor", "A", a_h_prev[0])
                            p.wait("tensor", "A", a_h_prev[1])

                    def mm_o(E, cb=cb, fc=fc):
                        for st7 in range(7):
                            last = E.matmul(
                                banks7[st7],
                                ht[:, fc * CLOC + st7 * 128: fc * CLOC + (st7 + 1) * 128],
                                cb[:, :], start=(fc == 0), stop=(fc == 31))
                        return last
                    t_o = p.op_inc("tensor", "T", mm_o)
                    pchk_consumed_T[fc % 2] = S["T"].n
                for st7 in range(7):
                    b2 = eo[st7 % 2]
                    if st7 >= 2:
                        p.wait("vector", "LD", exp_st[-2])
                    p.wait("vector", "T", t_o)
                    p.op("vector", lambda E, st7=st7, b2=b2:
                         E.tensor_copy(b2[:, :], banks7[st7]))
                    v_eo = p.mark("vector", "V")
                    p.wait("sync", "V", v_eo)
                    exp_st.append(p.dma(
                        "sync", "LD",
                        exp_out[ee * CLOC + st7 * 128: ee * CLOC + (st7 + 1) * 128,
                                ts(dcp, 512)],
                        b2[:, :]))
            t_out_done_prev = S["T"].n
        ld_expout = S["LD"].n
        if "ffn" in dbg:
            od = dbgt("expout", [NSLOT + 1, D], BF16)
            ox = dbgt("expx", [NSLOT, D], BF16)
            p.wait("gpsimd", "LD", ld_expout)
            rb3 = A.take("ffn_rb", [128, D], BF16)
            for i in range(NSLOT // 128):
                lg = p.dma("gpsimd", "LG", rb3[:, :], exp_x[ts(i, 128), :])
                p.wait("gpsimd", "LG", lg)
                p.dma("gpsimd", "ST", ox[ts(i, 128), :], rb3[:, :])
                lg = p.dma("gpsimd", "LG", rb3[:, :], exp_out[ts(i, 128), :])
                p.wait("gpsimd", "LG", lg)
                p.dma("gpsimd", "ST", od[ts(i, 128), :], rb3[:, :])
        A.pop()

        A.push()
        cx = [A.take(f"cx{i}", [128, D], F32) for i in range(2)]
        mg0 = [A.take(f"mg0{i}", [128, D], BF16) for i in range(2)]
        mg1 = [A.take(f"mg1{i}", [128, D], BF16) for i in range(2)]
        ac1 = A.take("ac1", [128, D], F32)
        ac2 = A.take("ac2", [128, D], F32)
        ac3 = [A.take(f"ac3{i}", [128, D], F32) for i in range(2)]

        p.wait("gpsimd", "LD", ld_expout)
        p.wait("vector", "T")
        st_y = []
        v_comb = [0, 0]
        for t in range(OTILE):
            b = t % 2
            if t >= 2:
                p.wait("sync", "ST", st_y[t - 2])
                p.wait("sync", "V", v_comb[b])
                p.wait("gpsimd", "V", v_comb[b])
            ld_cx = p.dma("sync", "LD", cx[b][:, :], x_hbm[ts(t, 128), :])
            lg0 = p.igather("gpsimd", "LG", mg0[b][:, :], exp_out[:, :],
                            g0i[:, t:t + 1], NSLOT)
            lg1 = p.igather("gpsimd", "LG", mg1[b][:, :], exp_out[:, :],
                            g1i[:, t:t + 1], NSLOT)
            p.wait("vector", "LD", ld_cx)
            p.wait("vector", "LG", lg1)
            p.wait("vector", "V", v_yproj)
            p.op("vector", lambda E, t=t, b=b: E.tensor_add(
                ac1[:, :], cx[b][:, :], attn_sb[:, ts(t, D)]))
            p.op("vector", lambda E, t=t, b=b: E.scalar_tensor_tensor(
                ac2[:, :], mg0[b][:, :], p1[:, t:t + 1], ac1[:, :],
                op0=ALU.mult, op1=ALU.add))
            p.op("vector", lambda E, t=t, b=b: E.scalar_tensor_tensor(
                ac3[b][:, :], mg1[b][:, :], p2[:, t:t + 1], ac2[:, :],
                op0=ALU.mult, op1=ALU.add))
            v_comb[b] = p.mark("vector", "V")
            p.wait("sync", "V", v_comb[b])
            st_y.append(p.dma("sync", "ST", y_out[ts(t, 128), :], ac3[b][:, :]))
        p.wait("sync", "ST", S["ST"].n)
        A.pop()

        for eng, meth in (("sync", block.sync), ("gpsimd", block.gpsimd),
                          ("tensor", block.tensor), ("vector", block.vector),
                          ("scalar", block.scalar)):
            def runner(E, ops=p.q[eng]):
                for f in ops:
                    f(E)
            meth(runner)

    return nc, dbg_out



def make_consts(core):
    b = np.arange(128)
    true_b = (b + core * OTILE) % 128
    tri_rot = (true_b[:, None] < true_b[None, :]).astype(np.float32)
    return dict(
        tri_rot=tri_rot.astype(ml_dtypes.bfloat16),
        ones_bf=np.ones((128, 128), ml_dtypes.bfloat16),
        id_bf=np.eye(128, dtype=np.float32).astype(ml_dtypes.bfloat16),
        id_f=np.eye(128, dtype=np.float32),
        caus_bf=np.tril(np.ones((128, 128), np.float32)).astype(ml_dtypes.bfloat16),
        onesrow_bf=np.ones((1, 2048), ml_dtypes.bfloat16),
        onesrow_f=np.ones((1, 128), np.float32),
        onescol_f=np.ones((128, 1), np.float32),
        tokid_f=(np.arange(OTILE)[None, :] * 128
                 + np.arange(128)[:, None]).astype(np.float32),
    )


def make_inputs(L0_attn_x, L0_attn_Wq, L0_attn_bq, L0_attn_Wk, L0_attn_bk,
                L0_attn_Wv, L0_attn_bv, L0_attn_Wo, L0_attn_bo,
                L0_ffn_w_g, L0_ffn_c_fc, L0_ffn_c_proj):
    x = np.ascontiguousarray(np.asarray(L0_attn_x, np.float32).reshape(NT, D))
    shared = dict(
        wqT=np.ascontiguousarray(np.asarray(L0_attn_Wq, np.float32).T),
        wkT=np.ascontiguousarray(np.asarray(L0_attn_Wk, np.float32).T),
        wvT=np.ascontiguousarray(np.asarray(L0_attn_Wv, np.float32).T),
        woT=np.ascontiguousarray(np.asarray(L0_attn_Wo, np.float32).T),
        bq=np.asarray(L0_attn_bq, np.float32).reshape(1, D),
        bk=np.asarray(L0_attn_bk, np.float32).reshape(1, D),
        bv=np.asarray(L0_attn_bv, np.float32).reshape(1, D),
        bo=np.asarray(L0_attn_bo, np.float32).reshape(1, D),
        wg=np.asarray(L0_ffn_w_g, np.float32),
        cfc=np.asarray(L0_ffn_c_fc, np.float32),
        cpj=np.asarray(L0_ffn_c_proj, np.float32),
    )
    in_maps = []
    for c in range(8):
        m = dict(shared)
        m["x"] = np.ascontiguousarray(np.roll(x, -c * T, axis=0))
        m.update(make_consts(c))
        in_maps.append(m)
    return in_maps


def kernel(**inputs):
    in_maps = make_inputs(**inputs)
    nc, _ = build()
    res = run_bass_kernel_spmd(nc, in_maps, core_ids=list(range(8)))
    y = np.concatenate([res.results[c]["y"] for c in range(8)], axis=0)
    return y.reshape(8, T, D)


# version 06
# speedup vs baseline: 1.0709x; 1.0709x over previous
from contextlib import ExitStack

import numpy as np
import ml_dtypes

import concourse.bass as bass
import concourse.mybir as mybir
from concourse.bass_utils import run_bass_kernel_spmd

F32 = mybir.dt.float32
BF16 = mybir.dt.bfloat16
I32 = mybir.dt.int32
AF = mybir.ActivationFunctionType
ALU = mybir.AluOpType
AX = mybir.AxisListType
IOffs = bass.IndirectOffsetOnAxis

D = 1024
T = 2048
NT = 16384
NTILE = 128
OTILE = 16
HEADS = 16
DH = 64
NE = 8
DFF = 4096
CAP = 5120
CLOC = 896
NSLOT = NE * CLOC
SCOLS = NSLOT // 128
BIG = 1.0e9
SBIG = 1.0e4

ENGINES = ("sync", "gpsimd", "tensor", "vector", "scalar")


class Sem:
    def __init__(self, stack, nc, name):
        self.h = stack.enter_context(nc.semaphore(name))
        self.n = 0

    def bump(self, k):
        self.n += k
        return self.n


class Prog:

    def __init__(self, nc, stack):
        self.nc = nc
        self.q = {e: [] for e in ENGINES}
        self.sems = {name: Sem(stack, nc, name) for name in
                     ("LD", "LG", "T", "V", "A", "ST")}

    def wait(self, eng, sem, val=None):
        s = self.sems[sem]
        v = s.n if val is None else val
        assert v <= s.n, f"wait {sem}>={v} but only {s.n} emitted"
        if v > 0:
            self.q[eng].append(lambda E, h=s.h, v=v: E.wait_ge(h, v))

    def dma(self, eng, sem, out, in_, **kw):
        s = self.sems[sem]
        s.bump(16)
        self.q[eng].append(
            lambda E, o=out, i=in_, h=s.h, kw=kw:
            E.dma_start(out=o, in_=i, **kw).then_inc(h, 16))
        return s.n

    def dmat(self, eng, sem, out, in_):
        s = self.sems[sem]
        s.bump(16)
        self.q[eng].append(
            lambda E, o=out, i=in_, h=s.h:
            E.dma_start_transpose(o, i).then_inc(h, 16))
        return s.n

    def _bound_reg(self, E, bounds):
        if not hasattr(self, "_bregs"):
            self._bregs = {}
        if bounds not in self._bregs:
            r = E.alloc_register(f"bnd{bounds}")
            E.reg_mov(r, bounds)
            self._bregs[bounds] = r
        return self._bregs[bounds]

    def igather(self, eng, sem, out, in_, idx, bounds):
        s = self.sems[sem]
        s.bump(16)

        def f(E, o=out, i=in_, x=idx, b=bounds, h=s.h):
            return E.indirect_dma_start(
                out=o, out_offset=None, in_=i,
                in_offset=IOffs(ap=x, axis=0),
                bounds_check=self._bound_reg(E, b), oob_is_err=False
            ).then_inc(h, 16)
        self.q[eng].append(f)
        return s.n

    def iscatter(self, eng, sem, out, in_, idx, bounds):
        s = self.sems[sem]
        s.bump(16)

        def f(E, o=out, i=in_, x=idx, b=bounds, h=s.h):
            return E.indirect_dma_start(
                out=o, out_offset=IOffs(ap=x, axis=0), in_=i, in_offset=None,
                bounds_check=self._bound_reg(E, b), oob_is_err=False
            ).then_inc(h, 16)
        self.q[eng].append(f)
        return s.n

    def op(self, eng, fn):
        self.q[eng].append(fn)
        if eng == "vector":
            self.q[eng].append(lambda E: E.drain())

    def mark(self, eng, sem, k=1):
        s = self.sems[sem]
        v = s.bump(k)
        if eng in ("vector", "scalar"):
            self.q[eng].append(lambda E: E.drain())
        self.q[eng].append(lambda E, h=s.h, k=k: E.sem_inc(h, k))
        return v

    def op_inc(self, eng, sem, fn, k=1):
        s = self.sems[sem]
        v = s.bump(k)
        self.q[eng].append(lambda E, f=fn, h=s.h, k=k: f(E).then_inc(h, k))
        return v


def ts(i, n):
    return slice(i * n, (i + 1) * n)


_DT_SIZE = {F32: 4, BF16: 2, I32: 4}


class Arena:

    def __init__(self, nc):
        self.nc = nc
        self.stack = [nc.sbuf_base]
        self.peak = nc.sbuf_base
        self.n = 0

    def push(self):
        self.stack.append(self.stack[-1])

    def pop(self):
        self.stack.pop()

    def take(self, name, shape, dtype):
        off = (self.stack[-1] + 63) // 64 * 64
        pp = 1
        for d in shape[1:]:
            pp *= d
        pp *= _DT_SIZE[dtype]
        self.n += 1
        t = self.nc.alloc_sbuf_tensor_at(f"{name}_{self.n}", shape, dtype,
                                         offset=off)
        self.stack[-1] = off + pp
        self.peak = max(self.peak, self.stack[-1])
        assert self.stack[-1] <= self.nc.sbuf_top, \
            f"SBUF overflow at {name}: {self.stack[-1]} > {self.nc.sbuf_top}"
        return t


def build(dbg=(), stop_after=None):
    nc = bass.Bass("TRN2", target_bir_lowering=False, debug=False)
    dbg = set(dbg)

    x_hbm = nc.dram_tensor("x", [NT, D], F32, kind="ExternalInput").ap()
    wT = {k: nc.dram_tensor(f"w{k}T", [D, D], F32, kind="ExternalInput").ap()
          for k in "qkvo"}
    bias = {k: nc.dram_tensor(f"b{k}", [1, D], F32, kind="ExternalInput").ap()
            for k in "qkvo"}
    wg = nc.dram_tensor("wg", [D, NE], F32, kind="ExternalInput").ap()
    cfc = nc.dram_tensor("cfc", [NE, D, DFF], F32, kind="ExternalInput").ap()
    cpj = nc.dram_tensor("cpj", [NE, DFF, D], F32, kind="ExternalInput").ap()
    tri_hbm = nc.dram_tensor("tri_rot", [128, 128], BF16, kind="ExternalInput").ap()
    one_hbm = nc.dram_tensor("ones_bf", [128, 128], BF16, kind="ExternalInput").ap()
    idb_hbm = nc.dram_tensor("id_bf", [128, 128], BF16, kind="ExternalInput").ap()
    idf_hbm = nc.dram_tensor("id_f", [128, 128], F32, kind="ExternalInput").ap()
    caus_hbm = nc.dram_tensor("caus_bf", [128, 128], BF16, kind="ExternalInput").ap()
    orow_hbm = nc.dram_tensor("onesrow_bf", [1, 2048], BF16, kind="ExternalInput").ap()
    orowf_hbm = nc.dram_tensor("onesrow_f", [1, 128], F32, kind="ExternalInput").ap()
    ocol_hbm = nc.dram_tensor("onescol_f", [128, 1], F32, kind="ExternalInput").ap()
    tok_hbm = nc.dram_tensor("tokid_f", [128, OTILE], F32, kind="ExternalInput").ap()

    y_out = nc.dram_tensor("y", [T, D], F32, kind="ExternalOutput").ap()

    xn_pad = nc.dram_tensor("xn_pad", [NT + 1, D], BF16).ap()
    exp_x = nc.dram_tensor("exp_x", [NSLOT, D], BF16).ap()
    exp_out = nc.dram_tensor("exp_out", [NSLOT + 1, D], BF16).ap()
    slot_tab = nc.dram_tensor("slot_tab", [NSLOT, 1], F32).ap()
    y_mid = nc.dram_tensor("y_mid", [T, D], BF16).ap()

    dbg_out = {}

    def dbgt(name, shape, dtype=F32):
        dbg_out[name] = nc.dram_tensor("dbg_" + name, shape, dtype,
                                       kind="ExternalOutput").ap()
        return dbg_out[name]

    with ExitStack() as top:
        e = top.enter_context
        p = Prog(nc, top)
        S = p.sems
        A = Arena(nc)

        c_tri = A.take("c_tri", [128, 128], BF16)
        c_one = A.take("c_one", [128, 128], BF16)
        c_idb = A.take("c_idb", [128, 128], BF16)
        c_idf = A.take("c_idf", [128, 128], F32)
        c_caus = A.take("c_caus", [128, 128], BF16)
        c_or = A.take("c_or", [1, 2048], BF16)
        c_orf = A.take("c_orf", [1, 128], F32)
        c_oc = A.take("c_oc", [128, 1], F32)
        c_tok = A.take("c_tok", [128, OTILE], F32)
        c_eps = A.take("c_eps", [128, 1], F32)
        zrow = A.take("zrow", [1, D], BF16)
        b_sb = {k: A.take(f"b_{k}", [1, D], BF16) for k in "qkvo"}
        wg_f = A.take("wg_f", [128, NE * 8], F32)
        wg_b = A.take("wg_b", [128, NE * 8], BF16)
        g_all = A.take("g_all", [128, NTILE * NE], F32)
        idx1 = A.take("idx1", [128, NTILE], F32)
        idx2 = A.take("idx2", [128, NTILE], F32)
        p1 = A.take("p1", [128, NTILE], F32)
        p2 = A.take("p2", [128, NTILE], F32)
        g0i = A.take("g0i", [128, OTILE], I32)
        g1i = A.take("g1i", [128, OTILE], I32)
        tabi4 = [A.take(f"tabi{i}", [128, 14], I32) for i in range(4)]
        mhat = A.take("mhat", [128, 1], F32)
        nrm = A.take("nrm", [1, 8], F32)

        ps_a = e(nc.psum_tensor("ps_a", [128, 512], F32))
        ps_b = e(nc.psum_tensor("ps_b", [128, 512], F32))
        ps_s = e(nc.psum_tensor("ps_s", [128, 2048], F32))
        ps_t = e(nc.psum_tensor("ps_t", [128, 128], BF16))
        ps_o = e(nc.psum_tensor("ps_o", [128, 512], F32))

        block = e(nc.Block())

        def finish_early():
            yz = A.take("yz", [128, D], F32)
            p.op("vector", lambda E: E.memset(yz[:, :], 0.0))
            vz = p.mark("vector", "V")
            p.wait("sync", "V", vz)
            p.wait("sync", "ST", S["ST"].n)
            for t in range(OTILE):
                p.dma("sync", "ST", y_out[ts(t, 128), :], yz[:, :])
            p.wait("sync", "ST", S["ST"].n)
            for eng, meth in (("sync", block.sync), ("gpsimd", block.gpsimd),
                              ("tensor", block.tensor), ("vector", block.vector),
                              ("scalar", block.scalar)):
                def runner(E, ops=p.q[eng]):
                    for f in ops:
                        f(E)
                meth(runner)

        for dst, src in ((c_tri, tri_hbm), (c_one, one_hbm), (c_idb, idb_hbm),
                         (c_idf, idf_hbm), (c_caus, caus_hbm), (c_or, orow_hbm),
                         (c_orf, orowf_hbm), (c_oc, ocol_hbm), (c_tok, tok_hbm)):
            p.dma("sync", "LD", dst[:, :], src)
        p.dma("sync", "LD", wg_f[:, :].rearrange("p (dc e) -> p dc e", e=NE),
              wg.rearrange("(dc p) e -> p dc e", p=128))
        for k in "qkvo":
            p.dma("gpsimd", "LG", b_sb[k][:, :], bias[k])
        ld_const = S["LD"].n
        lg_bias = S["LG"].n
        p.op("vector", lambda E: E.memset(c_eps[:, :], 1e-5))
        p.op("vector", lambda E: E.memset(zrow[:, :], 0.0))
        p.wait("vector", "LD", ld_const)
        p.op("vector", lambda E: E.tensor_copy(wg_b[:, :], wg_f[:, :]))
        v_const = p.mark("vector", "V")
        p.wait("sync", "V", v_const)
        p.dma("sync", "LD", xn_pad[NT:NT + 1, :], zrow[:, :])
        p.dma("sync", "LD", exp_out[NSLOT:NSLOT + 1, :], zrow[:, :])

        A.push()
        qt_sb = [A.take(f"qt{i}", [128, T], BF16) for i in range(8)]
        kt_sb = [A.take(f"kt{i}", [128, T], BF16) for i in range(8)]
        v_sb = [A.take(f"v{i}", [128, D], BF16) for i in range(OTILE)]
        xnt = A.take("xnt", [128, 8 * 2048], BF16)
        wbuf = A.take("wbuf", [128, 8 * 1024], BF16)
        gxb = [A.take(f"gxb{i}", [128, D], BF16) for i in range(4)]
        sidxi = [A.take(f"sidxi{k}", [128, OTILE], I32) for k in range(2)]
        svals = [A.take(f"svals{k}", [128, OTILE], F32) for k in range(2)]
        tabf = A.take("tabf", [128, SCOLS], F32)
        tini = A.take("tini", [128, SCOLS], F32)

        A.push()
        NXB = 3
        xb = [A.take(f"xb{i}", [128, D], F32) for i in range(NXB)]
        xnb = [A.take(f"xnb{i}", [128, D], BF16) for i in range(NXB)]
        sq1 = A.take("sq1", [128, D], F32)
        colr = A.take("colr", [128, 8 * NXB], F32)

        xnst_t = [0] * NTILE
        vfree = [0] * NXB
        afree = [0] * NXB
        for i in range(NTILE):
            b = i % NXB
            if vfree[b]:
                p.wait("sync", "V", vfree[b])
                p.wait("sync", "A", afree[b])
            ldx = p.dma("sync", "LD", xb[b][:, :], x_hbm[ts(i, 128), :])
            cS = colr[:, b * 8 + 0:b * 8 + 1]
            cM = colr[:, b * 8 + 1:b * 8 + 2]
            cV = colr[:, b * 8 + 2:b * 8 + 3]
            cSD = colr[:, b * 8 + 3:b * 8 + 4]
            cR = colr[:, b * 8 + 4:b * 8 + 5]
            cMS = colr[:, b * 8 + 5:b * 8 + 6]
            p.wait("scalar", "LD", ldx)
            p.op("scalar", lambda E, b=b, cS=cS:
                 E.activation(sq1[:, :], xb[b][:, :], AF.Square, accum_out=cS))
            a_sq = p.mark("scalar", "A")
            p.wait("vector", "LD", ldx)
            p.op("vector", lambda E, b=b, cM=cM: E.reduce_sum(cM, xb[b][:, :], axis=AX.X))
            p.op("vector", lambda E, cM=cM: E.tensor_scalar_mul(cM, cM, 1.0 / D))
            p.wait("vector", "A", a_sq)
            p.op("vector", lambda E, cM=cM, cMS=cMS: E.tensor_mul(cMS, cM, cM))
            p.op("vector", lambda E, cS=cS, cV=cV:
                 E.tensor_scalar(cV, cS, 1.0 / D, None, op0=ALU.mult))
            p.op("vector", lambda E, cV=cV, cMS=cMS: E.tensor_sub(cV, cV, cMS))
            v_var = p.mark("vector", "V")
            p.wait("scalar", "V", v_var)
            p.op("scalar", lambda E, cV=cV, cSD=cSD:
                 E.activation(cSD, cV, AF.Sqrt, bias=c_eps[:, :]))
            a_sd = p.mark("scalar", "A")
            afree[b] = a_sd
            p.wait("vector", "A", a_sd)
            p.op("vector", lambda E, cSD=cSD, cR=cR: E.reciprocal(cR, cSD))
            p.op("vector", lambda E, b=b, cM=cM, cR=cR:
                 E.tensor_scalar(xnb[b][:, :], xb[b][:, :], cM, cR,
                                 op0=ALU.subtract, op1=ALU.mult))
            vfree[b] = p.mark("vector", "V")
            p.wait("sync", "V", vfree[b])
            xnst_t[i] = p.dma("sync", "LD", xn_pad[ts(i, 128), :], xnb[b][:, :])
        st_all_xn = S["LD"].n
        st_own_xn = xnst_t[OTILE - 1]
        v_p1 = S["V"].n
        a_p1 = S["A"].n

        if "xn" in dbg:
            od = dbgt("xn", [NT, D])
            rb = A.take("xn_rb", [128, D], BF16)
            rd = A.take("xn_rd", [128, D], F32)
            for i in range(NTILE):
                p.wait("gpsimd", "LD", st_all_xn)
                lg = p.dma("gpsimd", "LG", rb[:, :], xn_pad[ts(i, 128), :])
                p.wait("vector", "LG", lg)
                p.op("vector", lambda E: E.tensor_copy(rd[:, :], rb[:, :]))
                vv = p.mark("vector", "V")
                p.wait("gpsimd", "V", vv)
                p.dma("gpsimd", "ST", od[ts(i, 128), :], rd[:, :])

        if stop_after == "p1":
            finish_early()
            return nc, dbg_out
        A.pop()

        A.push()

        p.wait("sync", "LD", st_own_xn)
        ldT = [0] * 8
        for dc in range(8):
            ldT[0] = p.dmat("sync", "LD", xnt[:, ts(dc, 2048)],
                            xn_pad[0:2048, ts(dc, 128)])

        lg_w = {}
        lg_w["q"] = p.dma("gpsimd", "LG",
                          wbuf[:, :].rearrange("p (dc i) -> p dc i", i=D),
                          wT["q"].rearrange("(dc p) i -> p dc i", p=128))

        def proj_qk(dst_tiles, wkey, bias_row, t_sem_name):
            p.wait("tensor", "LG", lg_w[wkey])
            p.wait("tensor", "LD", ldT[0])
            p.wait("tensor", "LG", lg_bias)
            last = 0
            evs = []
            for ic in range(8):
                for tc in range(4):
                    bank = ps_a if (ic * 4 + tc) % 2 == 0 else ps_b
                    if len(evs) >= 2:
                        p.wait("tensor", "V", evs[-2])

                    def mm(E, ic=ic, tc=tc, bank=bank, w=wkey, br=bias_row):
                        E.matmul(bank[:, :], br[0:1, ts(ic, 128)],
                                 c_or[0:1, ts(tc, 512)], start=True, stop=False)
                        for dc in range(8):
                            last = E.matmul(
                                bank[:, :],
                                wbuf[:, dc * 1024 + ic * 128: dc * 1024 + (ic + 1) * 128],
                                xnt[:, dc * 2048 + tc * 512: dc * 2048 + tc * 512 + 512],
                                start=False, stop=(dc == 7))
                        return last
                    t_mm = p.op_inc("tensor", "T", mm)
                    p.wait("vector", "T", t_mm)
                    p.op("vector", lambda E, ic=ic, tc=tc, bank=bank, dt=dst_tiles:
                         E.tensor_copy(dt[ic][:, ts(tc, 512)], bank[:, :]))
                    evs.append(p.mark("vector", "V"))
            return evs[-1]

        v_q = proj_qk(qt_sb, "q", b_sb["q"], "T")
        t_q_done = S["T"].n
        p.wait("gpsimd", "T", t_q_done)
        lg_w["k"] = p.dma("gpsimd", "LG",
                          wbuf[:, :].rearrange("p (dc i) -> p dc i", i=D),
                          wT["k"].rearrange("(dc p) i -> p dc i", p=128))
        v_k = proj_qk(kt_sb, "k", b_sb["k"], "T")
        t_k_done = S["T"].n
        p.wait("gpsimd", "T", t_k_done)
        lg_w["v"] = p.dma("gpsimd", "LG",
                          wbuf[:, :].rearrange("p (dc i) -> p dc i", i=D),
                          wT["v"].rearrange("(dc p) i -> p dc i", p=128))

        p.wait("tensor", "LG", lg_w["v"])
        v_vevs = []
        for tt in range(OTILE):
            for ic2 in range(2):
                bank = ps_a if (tt * 2 + ic2) % 2 == 0 else ps_b
                if len(v_vevs) >= 2:
                    p.wait("tensor", "V", v_vevs[-2])

                def mmv(E, tt=tt, ic2=ic2, bank=bank):
                    E.matmul(bank[:, :], c_or[0:1, ts(tt, 128)],
                             b_sb["v"][0:1, ts(ic2, 512)], start=True, stop=False)
                    for dc in range(8):
                        last = E.matmul(
                            bank[:, :],
                            xnt[:, dc * 2048 + tt * 128: dc * 2048 + (tt + 1) * 128],
                            wbuf[:, dc * 1024 + ic2 * 512: dc * 1024 + ic2 * 512 + 512],
                            start=False, stop=(dc == 7))
                    return last
                t_mm = p.op_inc("tensor", "T", mmv)
                p.wait("vector", "T", t_mm)
                p.op("vector", lambda E, tt=tt, ic2=ic2, bank=bank:
                     E.tensor_copy(v_sb[tt][:, ts(ic2, 512)], bank[:, :]))
                v_vevs.append(p.mark("vector", "V"))
        v_qkv_done = v_vevs[-1]
        t_qkv_done = S["T"].n
        if "qkv" in dbg:
            oq = dbgt("qt", [1024, 2048], BF16)
            ok_ = dbgt("kt", [1024, 2048], BF16)
            ov = dbgt("v", [2048, 1024], BF16)
            p.wait("scalar", "V", v_qkv_done)
            for i in range(8):
                p.dma("scalar", "ST", oq[ts(i, 128), :], qt_sb[i][:, :])
                p.dma("scalar", "ST", ok_[ts(i, 128), :], kt_sb[i][:, :])
            for i in range(OTILE):
                p.dma("scalar", "ST", ov[ts(i, 128), :], v_sb[i][:, :])
        if stop_after == "qkv":
            finish_early()
            return nc, dbg_out

        g_evs = []
        for ch in range(8):
            if ch > 0:
                p.wait("sync", "T", S["T"].n)
                for dc in range(8):
                    ldT[ch] = p.dmat("sync", "LD", xnt[:, ts(dc, 2048)],
                                     xn_pad[ch * 2048:(ch + 1) * 2048, ts(dc, 128)])
                p.wait("sync", "LD", st_all_xn)
            p.wait("tensor", "LD", ldT[ch])
            for tt in range(16):
                gi = ch * 16 + tt

                def mmg(E, tt=tt):
                    for dc in range(8):
                        last = E.matmul(
                            ps_o[:, 0:8],
                            xnt[:, dc * 2048 + tt * 128: dc * 2048 + (tt + 1) * 128],
                            wg_b[:, dc * 8:(dc + 1) * 8],
                            start=(dc == 0), stop=(dc == 7))
                    return last
                if g_evs:
                    p.wait("tensor", "V", g_evs[-1])
                t_g = p.op_inc("tensor", "T", mmg)
                p.wait("vector", "T", t_g)
                p.op("vector", lambda E, gi=gi:
                     E.tensor_copy(g_all[:, gi * 8:(gi + 1) * 8], ps_o[:, 0:8]))
                g_evs.append(p.mark("vector", "V"))
        v_g_done = g_evs[-1]
        t_g_done = S["T"].n

        if "g" in dbg:
            od = dbgt("g", [128, NTILE * NE])
            p.wait("scalar", "V", v_g_done)
            p.dma("scalar", "ST", od, g_all[:, :])

        if stop_after == "g":
            finish_early()
            return nc, dbg_out

        A.push()
        r_v1 = A.take("r_v1", [128, NTILE], F32)
        r_v2 = A.take("r_v2", [128, NTILE], F32)
        r_eq = A.take("r_eq", [128, NTILE], F32)
        r_t = A.take("r_t", [128, NTILE], F32)
        r_big = A.take("r_big", [128, NTILE], F32)
        g2 = A.take("g2", [128, NTILE * NE], F32)
        zf = A.take("zf", [128, NTILE], F32)
        s12 = A.take("s12", [128, NTILE], F32)

        g3 = g_all[:, :].rearrange("p (t e) -> p t e", e=NE)
        g23 = g2[:, :].rearrange("p (t e) -> p t e", e=NE)
        p.op("vector", lambda E: E.memset(r_big[:, :], SBIG))
        p.op("vector", lambda E: E.memset(zf[:, :], 0.0))
        p.op("vector", lambda E: E.reduce_max(r_v1[:, :], g3, axis=AX.X))
        p.op("vector", lambda E: E.memset(idx1[:, :], SBIG))
        for ee in range(NE):
            p.op("vector", lambda E, ee=ee:
                 E.tensor_tensor(r_eq[:, :], g3[:, :, ee], r_v1[:, :], op=ALU.is_equal))
            p.op("vector", lambda E, ee=ee: E.scalar_tensor_tensor(
                r_t[:, :], r_eq[:, :], -(SBIG - ee), r_big[:, :],
                op0=ALU.mult, op1=ALU.add))
            p.op("vector", lambda E: E.tensor_tensor(idx1[:, :], idx1[:, :],
                                                     r_t[:, :], op=ALU.min))
        for ee in range(NE):
            p.op("vector", lambda E, ee=ee: E.tensor_single_scalar(
                r_eq[:, :], idx1[:, :], float(ee), op=ALU.is_equal))
            p.op("vector", lambda E, ee=ee: E.scalar_tensor_tensor(
                g23[:, :, ee], r_eq[:, :], -1e30, g3[:, :, ee],
                op0=ALU.mult, op1=ALU.add))
        p.op("vector", lambda E: E.reduce_max(r_v2[:, :], g23, axis=AX.X))
        p.op("vector", lambda E: E.memset(idx2[:, :], SBIG))
        for ee in range(NE):
            p.op("vector", lambda E, ee=ee:
                 E.tensor_tensor(r_eq[:, :], g23[:, :, ee], r_v2[:, :], op=ALU.is_equal))
            p.op("vector", lambda E, ee=ee: E.scalar_tensor_tensor(
                r_t[:, :], r_eq[:, :], -(SBIG - ee), r_big[:, :],
                op0=ALU.mult, op1=ALU.add))
            p.op("vector", lambda E: E.tensor_tensor(idx2[:, :], idx2[:, :],
                                                     r_t[:, :], op=ALU.min))
        p.op("vector", lambda E: E.tensor_sub(s12[:, :], r_v1[:, :], r_v2[:, :]))
        v_rt = p.mark("vector", "V")
        p.wait("scalar", "V", v_rt)
        p.op("scalar", lambda E: E.activation(p1[:, :], s12[:, :], AF.Sigmoid))
        p.op("scalar", lambda E: E.activation(p2[:, :], s12[:, :], AF.Sigmoid,
                                              scale=-1.0))
        a_probs = p.mark("scalar", "A")

        if "route" in dbg:
            for nm, buf in (("idx1", idx1), ("idx2", idx2), ("p1", p1), ("p2", p2)):
                od = dbgt(nm, [128, NTILE])
                p.wait("scalar", "A", a_probs)
                p.wait("scalar", "V", v_rt)
                p.dma("scalar", "ST", od, buf[:, :])

        idxT = [A.take(f"idxT{k}", [128, 128], F32) for k in range(2)]
        m_ke = A.take("m_ke", [128, 128], F32)
        incl = A.take("incl", [128, 128], F32)
        plane = A.take("plane", [128, 128], F32)
        rsel = [A.take(f"rsel{k}", [128, 128], F32) for k in range(2)]
        rtt = [A.take(f"rtt{k}", [128, 128], F32) for k in range(2)]
        totb = A.take("totb", [128, 2 * NE], BF16)
        carr = A.take("carr", [128, 2 * NE], F32)

        p.wait("tensor", "V", v_rt)
        for k, src in enumerate((idx1, idx2)):
            t_tr = p.op_inc("tensor", "T", lambda E, s=src:
                            E.transpose(ps_a[:, 0:128], s[:, :], c_idf[:, :]))
            p.wait("vector", "T", t_tr)
            p.op("vector", lambda E, k=k: E.tensor_copy(idxT[k][:, :], ps_a[:, 0:128]))
            v_m = p.mark("vector", "V")
            p.wait("tensor", "V", v_m)
        v_idxT = p.mark("vector", "V")

        for k in range(2):
            for ee in range(NE):
                p.op("vector", lambda E, k=k, ee=ee: E.tensor_single_scalar(
                    m_ke[:, :], idxT[k][:, :], float(ee), op=ALU.is_equal))
                p.op("vector", lambda E: E.tensor_tensor_scan(
                    incl[:, :], m_ke[:, :], zf[:, 0:128], 0.0,
                    op0=ALU.add, op1=ALU.add))
                p.op("vector", lambda E, k=k, ee=ee: E.tensor_copy(
                    totb[:, k * NE + ee:k * NE + ee + 1], incl[:, 127:128]))
        v_tot = p.mark("vector", "V")
        p.wait("tensor", "V", v_tot)

        def mm_car(E):
            E.matmul(ps_o[:, 0:8], c_tri[:, :], totb[:, 0:NE], start=True, stop=True)
            return E.matmul(ps_o[:, 8:16], c_tri[:, :], totb[:, NE:2 * NE],
                            start=True, stop=False)

        def mm_car2(E):
            return E.matmul(ps_o[:, 8:16], c_one[:, :], totb[:, 0:NE],
                            start=False, stop=True)
        t_car = p.op_inc("tensor", "T", lambda E: (mm_car(E), mm_car2(E))[1])
        p.wait("vector", "T", t_car)
        p.op("vector", lambda E: E.tensor_copy(carr[:, 0:NE], ps_o[:, 0:8]))
        p.op("vector", lambda E: E.tensor_copy(carr[:, NE:2 * NE], ps_o[:, 8:16]))
        for k in range(2):
            p.op("vector", lambda E, k=k: E.memset(rsel[k][:, :], 0.0))
            for ee in range(NE):
                p.op("vector", lambda E, k=k, ee=ee: E.tensor_single_scalar(
                    m_ke[:, :], idxT[k][:, :], float(ee), op=ALU.is_equal))
                p.op("vector", lambda E: E.tensor_tensor_scan(
                    incl[:, :], m_ke[:, :], zf[:, 0:128], 0.0,
                    op0=ALU.add, op1=ALU.add))
                p.op("vector", lambda E, k=k, ee=ee: E.scalar_tensor_tensor(
                    plane[:, :], incl[:, :], carr[:, k * NE + ee:k * NE + ee + 1],
                    m_ke[:, :], op0=ALU.add, op1=ALU.subtract))
                p.op("vector", lambda E: E.tensor_mul(plane[:, :], plane[:, :],
                                                      m_ke[:, :]))
                p.op("vector", lambda E, k=k: E.tensor_add(rsel[k][:, :],
                                                           rsel[k][:, :], plane[:, :]))
        v_rsel = p.mark("vector", "V")
        p.wait("tensor", "V", v_rsel)
        for k in range(2):
            t_tr = p.op_inc("tensor", "T", lambda E, k=k:
                            E.transpose(ps_a[:, 0:128], rsel[k][:, :], c_idf[:, :]))
            p.wait("vector", "T", t_tr)
            p.op("vector", lambda E, k=k: E.tensor_copy(rtt[k][:, :], ps_a[:, 0:128]))
            v_m = p.mark("vector", "V")
            p.wait("tensor", "V", v_m)
        v_rtt = p.mark("vector", "V")

        if "rank" in dbg:
            for k in range(2):
                od = dbgt(f"rtt{k}", [128, 128])
                p.wait("scalar", "V", v_rtt)
                p.dma("scalar", "ST", od, rtt[k][:, :])
            p.dma("scalar", "ST", dbgt("rsel0", [128, 128]), rsel[0][:, :])
            p.dma("scalar", "ST", dbgt("idxT0", [128, 128]), idxT[0][:, :])
            p.dma("scalar", "ST", dbgt("carr", [128, 16]), carr[:, :])

        if stop_after == "rank":
            finish_early()
            return nc, dbg_out

        oloc = [A.take(f"oloc{k}", [16, 128], F32) for k in range(2)]
        lincl = A.take("lincl", [16, 128], F32)
        lm = A.take("lm", [16, 128], F32)
        lpl = A.take("lpl", [16, 128], F32)
        lrsel = [A.take(f"lrsel{k}", [16, 128], F32) for k in range(2)]
        lrtt = [A.take(f"lrtt{k}", [128, OTILE], F32) for k in range(2)]
        ltotb = A.take("ltotb", [16, 2 * NE], BF16)
        lcarr = A.take("lcarr", [16, 2 * NE], F32)

        for k, src in enumerate((idx1, idx2)):
            t_tr = p.op_inc("tensor", "T", lambda E, s=src:
                            E.transpose(ps_a[0:16, 0:128], s[:, 0:OTILE],
                                        c_idf[:, :]))
            p.wait("vector", "T", t_tr)
            p.op("vector", lambda E, k=k: E.tensor_copy(oloc[k][:, :], ps_a[0:16, 0:128]))
            v_m = p.mark("vector", "V")
            p.wait("tensor", "V", v_m)
        v_oloc = p.mark("vector", "V")
        for k in range(2):
            for ee in range(NE):
                p.op("vector", lambda E, k=k, ee=ee: E.tensor_single_scalar(
                    lm[:, :], oloc[k][:, :], float(ee), op=ALU.is_equal))
                p.op("vector", lambda E: E.tensor_tensor_scan(
                    lincl[:, :], lm[:, :], zf[0:16, 0:128], 0.0,
                    op0=ALU.add, op1=ALU.add))
                p.op("vector", lambda E, k=k, ee=ee: E.tensor_copy(
                    ltotb[:, k * NE + ee:k * NE + ee + 1], lincl[:, 127:128]))
        v_ltot = p.mark("vector", "V")
        p.wait("tensor", "V", v_ltot)

        def mm_lcar(E):
            E.matmul(ps_o[0:16, 0:8], c_tri[0:16, 0:16], ltotb[:, 0:NE],
                     start=True, stop=True)
            E.matmul(ps_o[0:16, 8:16], c_tri[0:16, 0:16], ltotb[:, NE:2 * NE],
                     start=True, stop=False)
            return E.matmul(ps_o[0:16, 8:16], c_one[0:16, 0:16], ltotb[:, 0:NE],
                            start=False, stop=True)
        t_lcar = p.op_inc("tensor", "T", mm_lcar)
        p.wait("vector", "T", t_lcar)
        p.op("vector", lambda E: E.tensor_copy(lcarr[:, 0:NE], ps_o[0:16, 0:8]))
        p.op("vector", lambda E: E.tensor_copy(lcarr[:, NE:2 * NE], ps_o[0:16, 8:16]))
        for k in range(2):
            p.op("vector", lambda E, k=k: E.memset(lrsel[k][:, :], 0.0))
            for ee in range(NE):
                p.op("vector", lambda E, k=k, ee=ee: E.tensor_single_scalar(
                    lm[:, :], oloc[k][:, :], float(ee), op=ALU.is_equal))
                p.op("vector", lambda E: E.tensor_tensor_scan(
                    lincl[:, :], lm[:, :], zf[0:16, 0:128], 0.0,
                    op0=ALU.add, op1=ALU.add))
                p.op("vector", lambda E, k=k, ee=ee: E.scalar_tensor_tensor(
                    lpl[:, :], lincl[:, :], lcarr[:, k * NE + ee:k * NE + ee + 1],
                    lm[:, :], op0=ALU.add, op1=ALU.subtract))
                p.op("vector", lambda E: E.tensor_mul(lpl[:, :], lpl[:, :], lm[:, :]))
                p.op("vector", lambda E, k=k: E.tensor_add(lrsel[k][:, :],
                                                           lrsel[k][:, :], lpl[:, :]))
        v_lrsel = p.mark("vector", "V")
        p.wait("tensor", "V", v_lrsel)
        for k in range(2):
            t_tr = p.op_inc("tensor", "T", lambda E, k=k:
                            E.transpose(ps_a[:, 0:OTILE], lrsel[k][:, :],
                                        c_idf[0:16, 0:16]))
            p.wait("vector", "T", t_tr)
            p.op("vector", lambda E, k=k: E.tensor_copy(lrtt[k][:, :],
                                                        ps_a[:, 0:OTILE]))
            v_m = p.mark("vector", "V")
            p.wait("tensor", "V", v_m)
        v_lrtt = p.mark("vector", "V")

        dropf = A.take("dropf", [128, OTILE], F32)
        oflf = A.take("oflf", [128, OTILE], F32)
        basei = A.take("basei", [128, OTILE], F32)
        sidxf = A.take("sidxf", [128, OTILE], F32)
        tinv = A.take("tinv", [128, OTILE], F32)
        gtmp = A.take("gtmp", [128, OTILE], F32)

        p.op("vector", lambda E: E.tensor_scalar(
            tinv[:, :], c_tok[:, :], -1.0, float(NT), op0=ALU.mult, op1=ALU.add))
        p.op("vector", lambda E: E.memset(tini[:, :], float(NT)))
        for k in range(2):
            own = (idx1 if k == 0 else idx2)[:, 0:OTILE]
            p.op("vector", lambda E, k=k: E.tensor_single_scalar(
                dropf[:, :], rtt[k][:, 0:OTILE], float(CAP), op=ALU.is_ge))
            p.op("vector", lambda E, k=k: E.tensor_single_scalar(
                oflf[:, :], lrtt[k][:, :], float(CLOC), op=ALU.is_ge))
            p.op("vector", lambda E, k=k, own=own: E.scalar_tensor_tensor(
                basei[:, :], own, float(CLOC), lrtt[k][:, :],
                op0=ALU.mult, op1=ALU.add))
            p.op("vector", lambda E: E.scalar_tensor_tensor(
                sidxf[:, :], oflf[:, :], BIG, basei[:, :],
                op0=ALU.mult, op1=ALU.add))
            p.op("vector", lambda E, k=k: E.tensor_copy(sidxi[k][:, :], sidxf[:, :]))
            p.op("vector", lambda E: E.tensor_mul(gtmp[:, :], dropf[:, :], tinv[:, :]))
            p.op("vector", lambda E, k=k: E.tensor_add(svals[k][:, :],
                                                       gtmp[:, :], c_tok[:, :]))
            p.op("vector", lambda E: E.tensor_scalar(
                gtmp[:, :], basei[:, :], -1.0, float(NSLOT), op0=ALU.mult, op1=ALU.add))
            p.op("vector", lambda E: E.tensor_mul(gtmp[:, :], gtmp[:, :], oflf[:, :]))
            p.op("vector", lambda E: E.tensor_add(gtmp[:, :], gtmp[:, :], basei[:, :]))
            gi_sb = g0i if k == 0 else g1i
            p.op("vector", lambda E, gi_sb=gi_sb: E.tensor_copy(gi_sb[:, :], gtmp[:, :]))
        v_scati = p.mark("vector", "V")

        if stop_after == "scatidx":
            finish_early()
            return nc, dbg_out

        p.wait("gpsimd", "V", v_scati)
        lg_tini = p.dma("gpsimd", "LG",
                        slot_tab.rearrange("(p f) one -> p (f one)", p=128),
                        tini[:, :])
        p.wait("gpsimd", "LG", lg_tini)
        for k in range(2):
            for t in range(OTILE):
                p.iscatter("gpsimd", "LG", slot_tab[:, :],
                           svals[k][:, t:t + 1], sidxi[k][:, t:t + 1], NSLOT - 1)
        lg_scat = S["LG"].n
        p.wait("gpsimd", "LG", lg_scat)
        lg_tab = p.dma("gpsimd", "LG", tabf[:, :],
                       slot_tab.rearrange("(p f) one -> p (f one)", p=128))
        p.wait("vector", "LG", lg_tab)
        for i4 in range(4):
            p.op("vector", lambda E, i4=i4: E.tensor_copy(
                tabi4[i4][:, :], tabf[:, i4 * 14:(i4 + 1) * 14]))
        v_tabi = p.mark("vector", "V")

        if "tab" in dbg:
            od = dbgt("tab", [128, SCOLS])
            p.wait("scalar", "V", v_tabi)
            p.dma("scalar", "ST", od, tabf[:, :])
            od2 = dbgt("lrtt0", [128, OTILE])
            p.dma("scalar", "ST", od2, lrtt[0][:, :])
            od3 = dbgt("g0", [128, OTILE], I32)
            p.dma("scalar", "ST", od3, g0i[:, :])
            od4 = dbgt("g1", [128, OTILE], I32)
            p.dma("scalar", "ST", od4, g1i[:, :])

        if stop_after == "route":
            finish_early()
            return nc, dbg_out
        A.pop()
        A.pop()

        disp_st = [0] * SCOLS
        p.wait("gpsimd", "V", v_tabi)
        p.wait("gpsimd", "LD", st_all_xn)
        for f in range(SCOLS):
            b = f % 4
            if f >= 4:
                p.wait("gpsimd", "LG", disp_st[f - 4])
            lg_g = p.igather("gpsimd", "LG", gxb[b][:, :], xn_pad[:, :],
                             tabi4[f // 14][:, f % 14:f % 14 + 1], NT)
            p.wait("gpsimd", "LG", lg_g)
            disp_st[f] = p.dma(
                "gpsimd", "LG",
                exp_x.rearrange("(pp f) d -> pp f d", f=SCOLS)[:, f, :],
                gxb[b][:, :])
        lg_disp = S["LG"].n


        A.push()
        sq2 = A.take("sq2", [128, 2048], F32)
        pbuf = A.take("pbuf", [128, 2048], BF16)
        ptb = A.take("ptb", [128, 2048], BF16)
        ybuf = [A.take(f"ybuf{i}", [128, D], BF16) for i in range(2)]
        lcol = A.take("lcol", [128, 10], F32)

        p.op("vector", lambda E: E.memset(nrm[:, :], 0.0))
        v_nz = p.mark("vector", "V")
        a_norm_sq = []
        for which, tiles_, slot in (("q", qt_sb, 0), ("k", kt_sb, 1)):
            for it in range(8):
                p.wait("scalar", "V", v_qkv_done if which == "k" else v_q)
                p.wait("scalar", "T", t_g_done)
                p.wait("scalar", "V", v_scati)
                p.op("scalar", lambda E, tiles_=tiles_, it=it:
                     E.activation(sq2[:, :], tiles_[it][:, :], AF.Square))
                a_s = p.mark("scalar", "A")
                for tc in range(4):
                    p.wait("tensor", "A", a_s)
                    p.wait("tensor", "V", v_nz)
                    t_n = p.op_inc("tensor", "T", lambda E, tc=tc:
                                   E.matmul(ps_b[0:1, 0:512], c_oc[:, :],
                                            sq2[:, ts(tc, 512)], start=True, stop=True))
                    p.wait("vector", "T", t_n)
                    p.op("vector", lambda E: E.reduce_max(nrm[0:1, 2:3],
                                                          ps_b[0:1, 0:512], axis=AX.X))
                    p.op("vector", lambda E, slot=slot: E.tensor_tensor(
                        nrm[0:1, slot:slot + 1], nrm[0:1, slot:slot + 1],
                        nrm[0:1, 2:3], op=ALU.max))
                    v_nz = p.mark("vector", "V")
        p.op("vector", lambda E: E.tensor_mul(nrm[0:1, 3:4], nrm[0:1, 0:1],
                                              nrm[0:1, 1:2]))
        v_nn = p.mark("vector", "V")
        p.wait("scalar", "V", v_nn)
        p.op("scalar", lambda E: E.activation(nrm[0:1, 4:5], nrm[0:1, 3:4], AF.Sqrt))
        a_nn = p.mark("scalar", "A")
        p.wait("vector", "A", a_nn)
        p.op("vector", lambda E: E.tensor_scalar_mul(nrm[0:1, 5:6],
                                                     nrm[0:1, 4:5], -0.125))
        v_mh = p.mark("vector", "V")
        p.wait("tensor", "V", v_mh)
        t_mh = p.op_inc("tensor", "T", lambda E:
                        E.matmul(ps_b[:, 0:1], c_orf[0:1, :], nrm[0:1, 5:6],
                                 start=True, stop=True))
        p.wait("vector", "T", t_mh)
        p.op("vector", lambda E: E.tensor_copy(mhat[:, :], ps_b[:, 0:1]))
        v_mhat = p.mark("vector", "V")
        if "attn" in dbg:
            p.wait("scalar", "V", v_mhat)
            p.dma("scalar", "ST", dbgt("mhat", [128, 1]), mhat[:, :])
            p.dma("scalar", "ST", dbgt("nrm", [1, 8]), nrm[:, :])
        p.wait("scalar", "V", v_mhat)

        ymid_st = [0] * OTILE
        a_exp_prev = 0
        a_o_prev = 0
        t_pv_prev = 0
        v_pm_hist = []
        for qt in range(OTILE):
            L = (qt + 1) * 128
            nkb = L // 128
            yb = ybuf[qt % 2]
            if qt >= 2:
                p.wait("scalar", "LD", ymid_st[qt - 2])
            for h in range(HEADS):
                til, base = h // 2, (h % 2) * 64
                qsl = qt_sb[til][base:base + 64, ts(qt, 128)]
                lc = lcol[:, h % 4:h % 4 + 1]

                p.wait("tensor", "A", a_exp_prev)

                def mm_s(E, qsl=qsl, til=til, base=base, L=L):
                    for kb4 in range((L + 511) // 512):
                        w = min(512, L - kb4 * 512)
                        last = E.matmul(ps_s[:, kb4 * 512:kb4 * 512 + w],
                                        qsl, kt_sb[til][base:base + 64,
                                                        kb4 * 512:kb4 * 512 + w],
                                        start=True, stop=True)
                    return last
                t_s = p.op_inc("tensor", "T", mm_s)
                p.wait("scalar", "T", max(t_s, t_pv_prev))
                if len(v_pm_hist) >= 2:
                    p.wait("scalar", "V", v_pm_hist[-2])
                if L > 128:
                    p.op("scalar", lambda E, L=L, lc=lc: E.activation(
                        pbuf[:, 0:L - 128], ps_s[:, 0:L - 128], AF.Exp,
                        bias=mhat[:, :], scale=0.125, accum_out=lc))
                p.op("scalar", lambda E, L=L: E.activation(
                    pbuf[:, L - 128:L], ps_s[:, L - 128:L], AF.Exp,
                    bias=mhat[:, :], scale=0.125))
                a_exp = p.mark("scalar", "A")
                p.wait("vector", "A", a_exp)
                p.op("vector", lambda E, L=L: E.tensor_mul(
                    pbuf[:, L - 128:L], pbuf[:, L - 128:L], c_caus[:, :]))
                p.op("vector", lambda E, L=L: E.reduce_sum(
                    lcol[:, 8:9], pbuf[:, L - 128:L], axis=AX.X))
                if L > 128:
                    p.op("vector", lambda E, lc=lc: E.tensor_add(lc, lc, lcol[:, 8:9]))
                else:
                    p.op("vector", lambda E, lc=lc: E.tensor_copy(lc, lcol[:, 8:9]))
                rl = lcol[:, 4 + h % 4:5 + h % 4]
                p.op("vector", lambda E, lc=lc, rl=rl: E.reciprocal(rl, lc))
                v_pm = p.mark("vector", "V")
                v_pm_hist.append(v_pm)
                a_exp_prev = a_exp

                for kb in range(nkb):
                    p.wait("tensor", "V", v_pm)
                    if kb == 0:
                        p.wait("tensor", "A", a_o_prev)

                    def mm_t(E, kb=kb):
                        return E.transpose(ps_t[:, :], pbuf[:, ts(kb, 128)],
                                           c_idb[:, :])
                    t_tr = p.op_inc("tensor", "T", mm_t)
                    p.wait("vector", "T", t_tr)
                    p.op("vector", lambda E, kb=kb:
                         E.tensor_copy(ptb[:, ts(kb, 128)], ps_t[:, :]))
                    v_pt = p.mark("vector", "V")
                    p.wait("tensor", "V", v_pt)

                    def mm_pv(E, kb=kb, h=h, nkb=nkb):
                        return E.matmul(ps_o[:, 0:64], ptb[:, ts(kb, 128)],
                                        v_sb[kb][:, ts(h, 64)],
                                        start=(kb == 0), stop=(kb == nkb - 1))
                    t_pv = p.op_inc("tensor", "T", mm_pv)
                t_pv_prev = t_pv
                p.wait("scalar", "T", t_pv)
                p.wait("scalar", "V", v_pm)
                p.op("scalar", lambda E, yb=yb, h=h, rl=rl: E.activation(
                    yb[:, ts(h, 64)], ps_o[:, 0:64], AF.Copy,
                    scale=rl))
                a_o = p.mark("scalar", "A")
                a_o_prev = a_o
            p.wait("scalar", "A", a_o)
            ymid_st[qt] = p.dma("scalar", "LD", y_mid[ts(qt, 128), :], yb[:, :])
        ld_ymid = S["LD"].n
        t_attn_done = S["T"].n
        a_attn_done = S["A"].n
        v_attn_done = S["V"].n
        A.pop()
        A.pop()

        if "attn" in dbg:
            od = dbgt("attnY", [T, D])
            rb2 = A.take("at_rb", [128, D], BF16)
            rd2 = A.take("at_rd", [128, D], F32)
            for i in range(OTILE):
                p.wait("gpsimd", "LD", ld_ymid)
                lg = p.dma("gpsimd", "LG", rb2[:, :], y_mid[ts(i, 128), :])
                p.wait("vector", "LG", lg)
                p.op("vector", lambda E: E.tensor_copy(rd2[:, :], rb2[:, :]))
                vv = p.mark("vector", "V")
                p.wait("gpsimd", "V", vv)
                p.dma("gpsimd", "ST", od[ts(i, 128), :], rd2[:, :])

        if stop_after == "attn":
            finish_early()
            return nc, dbg_out

        attn_sb = A.take("attn_sb", [128, OTILE * D], BF16)

        A.push()
        ytb = A.take("ytb", [128, 8 * 2048], BF16)
        wo_sb = A.take("wo_sb", [128, 8 * 1024], BF16)
        p.wait("gpsimd", "T", t_attn_done)
        p.wait("gpsimd", "A", a_attn_done)
        p.wait("gpsimd", "V", v_attn_done)
        lg_wo = p.dma("gpsimd", "LG",
                      wo_sb[:, :].rearrange("p (dc i) -> p dc i", i=D),
                      wT["o"].rearrange("(dc p) i -> p dc i", p=128))
        p.wait("sync", "LD", ld_ymid)
        p.wait("sync", "T", t_attn_done)
        p.wait("sync", "LG", lg_disp)
        ldyT = 0
        for ic in range(8):
            ldyT = p.dmat("sync", "LD", ytb[:, ts(ic, 2048)],
                          y_mid[0:2048, ts(ic, 128)])
        p.wait("tensor", "LD", ldyT)
        p.wait("tensor", "LG", lg_wo)
        ev2 = []
        for t in range(OTILE):
            for jc in range(2):
                bank = ps_a if (t * 2 + jc) % 2 == 0 else ps_b
                if len(ev2) >= 2:
                    p.wait("tensor", "V", ev2[-2])

                def mm_yo(E, t=t, jc=jc, bank=bank):
                    E.matmul(bank[:, :], c_or[0:1, ts(t, 128)],
                             b_sb["o"][0:1, ts(jc, 512)], start=True, stop=False)
                    for ic in range(8):
                        last = E.matmul(
                            bank[:, :],
                            ytb[:, ic * 2048 + t * 128: ic * 2048 + (t + 1) * 128],
                            wo_sb[:, ic * 1024 + jc * 512: ic * 1024 + jc * 512 + 512],
                            start=False, stop=(ic == 7))
                    return last
                t_mm = p.op_inc("tensor", "T", mm_yo)
                p.wait("vector", "T", t_mm)
                p.op("vector", lambda E, t=t, jc=jc, bank=bank: E.tensor_copy(
                    attn_sb[:, t * D + jc * 512: t * D + jc * 512 + 512],
                    bank[:, :]))
                ev2.append(p.mark("vector", "V"))
        v_yproj = ev2[-1]
        t_yproj = S["T"].n
        if "yproj" in dbg:
            od = dbgt("attnout", [128, OTILE * D], BF16)
            p.wait("scalar", "V", v_yproj)
            p.dma("scalar", "ST", od, attn_sb[:, :])
            oy = dbgt("ymid", [T, D], BF16)
            for i in range(OTILE):
                p.dma("scalar", "ST", oy[ts(i, 128), :], y_mid[ts(i, 128), :])
        A.pop()

        A.push()
        ht = A.take("ht", [128, 32 * CLOC], BF16)
        xtb = A.take("xtb", [128, 8 * CLOC], BF16)
        slab = [A.take(f"slab{i}", [128, 8 * 128], BF16) for i in range(2)]
        pchk = [A.take(f"pchk{i}", [128, 512], BF16) for i in range(2)]
        eo = [A.take(f"eo{i}", [128, 512], BF16) for i in range(2)]

        slab_ld = {}
        pchk_consumed_T = [0, 0]
        slab_consumed_T = [0, 0]
        exp_st = []
        t_h_done_prev = 0
        t_out_done_prev = 0
        p.wait("tensor", "V", v_yproj)
        p.wait("sync", "T", t_yproj)
        p.wait("gpsimd", "T", t_yproj)
        p.wait("gpsimd", "LG", lg_disp)
        a_h_prev = [0, 0]
        for ee in range(NE):
            p.wait("sync", "T", t_h_done_prev)
            ldxT = 0
            for dc in range(8):
                ldxT = p.dmat("sync", "LD", xtb[:, ts(dc, CLOC)],
                              exp_x[ts(ee, CLOC), ts(dc, 128)])
            cfc_v = cfc[ee].rearrange("(dc p) (fc f) -> fc p dc f", p=128, f=128)
            for fc in range(32):
                sb = slab[fc % 2]
                if slab_consumed_T[fc % 2]:
                    p.wait("gpsimd", "T", slab_consumed_T[fc % 2])
                lg_s = p.dma("gpsimd", "LG",
                             sb[:, :].rearrange("p (dc f) -> p dc f", f=128),
                             cfc_v[fc])
                p.wait("tensor", "LG", lg_s)
                if fc == 0:
                    p.wait("tensor", "LD", ldxT)
                for sc in range(2):
                    w = 512 if sc == 0 else CLOC - 512
                    bank = ps_a if sc == 0 else ps_b
                    if a_h_prev[sc]:
                        p.wait("tensor", "A", a_h_prev[sc])

                    def mm_h(E, sb=sb, sc=sc, w=w, bank=bank):
                        for dc in range(8):
                            last = E.matmul(
                                bank[:, 0:w], sb[:, ts(dc, 128)],
                                xtb[:, dc * CLOC + sc * 512: dc * CLOC + sc * 512 + w],
                                start=(dc == 0), stop=(dc == 7))
                        return last
                    t_h = p.op_inc("tensor", "T", mm_h)
                    p.wait("scalar", "T", t_h)
                    if fc == 0 and sc == 0:
                        p.wait("scalar", "T", t_out_done_prev)
                    p.op("scalar", lambda E, fc=fc, sc=sc, w=w, bank=bank:
                         E.activation(ht[:, fc * CLOC + sc * 512:
                                         fc * CLOC + sc * 512 + w],
                                      bank[:, 0:w], AF.Gelu))
                    a_h = p.mark("scalar", "A")
                    a_h_prev[sc] = a_h
                slab_consumed_T[fc % 2] = S["T"].n
            t_h_done_prev = S["T"].n
            a_ht_done = S["A"].n
            banks7 = [ps_s[:, ts(i, 512)] for i in range(4)] + \
                     [ps_a[:, :], ps_b[:, :], ps_o[:, :]]
            for dcp in range(2):
                ev_out = []
                for fc in range(32):
                    cb = pchk[fc % 2]
                    if pchk_consumed_T[fc % 2]:
                        p.wait("gpsimd", "T", pchk_consumed_T[fc % 2])
                    lg_c = p.dma("gpsimd", "LG", cb[:, :],
                                 cpj[ee][ts(fc, 128), ts(dcp, 512)])
                    p.wait("tensor", "LG", lg_c)
                    if fc == 0:
                        p.wait("tensor", "A", a_ht_done)
                        p.wait("tensor", "V")
                        if dcp == 0:
                            p.wait("tensor", "A", a_h_prev[0])
                            p.wait("tensor", "A", a_h_prev[1])

                    def mm_o(E, cb=cb, fc=fc):
                        for st7 in range(7):
                            last = E.matmul(
                                banks7[st7],
                                ht[:, fc * CLOC + st7 * 128: fc * CLOC + (st7 + 1) * 128],
                                cb[:, :], start=(fc == 0), stop=(fc == 31))
                        return last
                    t_o = p.op_inc("tensor", "T", mm_o)
                    pchk_consumed_T[fc % 2] = S["T"].n
                for st7 in range(7):
                    b2 = eo[st7 % 2]
                    if st7 >= 2:
                        p.wait("vector", "LD", exp_st[-2])
                    p.wait("vector", "T", t_o)
                    p.op("vector", lambda E, st7=st7, b2=b2:
                         E.tensor_copy(b2[:, :], banks7[st7]))
                    v_eo = p.mark("vector", "V")
                    p.wait("sync", "V", v_eo)
                    exp_st.append(p.dma(
                        "sync", "LD",
                        exp_out[ee * CLOC + st7 * 128: ee * CLOC + (st7 + 1) * 128,
                                ts(dcp, 512)],
                        b2[:, :]))
            t_out_done_prev = S["T"].n
        ld_expout = S["LD"].n
        if "ffn" in dbg:
            od = dbgt("expout", [NSLOT + 1, D], BF16)
            ox = dbgt("expx", [NSLOT, D], BF16)
            p.wait("gpsimd", "LD", ld_expout)
            rb3 = A.take("ffn_rb", [128, D], BF16)
            for i in range(NSLOT // 128):
                lg = p.dma("gpsimd", "LG", rb3[:, :], exp_x[ts(i, 128), :])
                p.wait("gpsimd", "LG", lg)
                p.dma("gpsimd", "ST", ox[ts(i, 128), :], rb3[:, :])
                lg = p.dma("gpsimd", "LG", rb3[:, :], exp_out[ts(i, 128), :])
                p.wait("gpsimd", "LG", lg)
                p.dma("gpsimd", "ST", od[ts(i, 128), :], rb3[:, :])
        A.pop()

        A.push()
        cx = [A.take(f"cx{i}", [128, D], F32) for i in range(2)]
        mg0 = [A.take(f"mg0{i}", [128, D], BF16) for i in range(2)]
        mg1 = [A.take(f"mg1{i}", [128, D], BF16) for i in range(2)]
        ac1 = A.take("ac1", [128, D], F32)
        ac2 = A.take("ac2", [128, D], F32)
        ac3 = [A.take(f"ac3{i}", [128, D], F32) for i in range(2)]

        p.wait("gpsimd", "LD", ld_expout)
        p.wait("vector", "T")
        st_y = []
        v_comb = [0, 0]
        for t in range(OTILE):
            b = t % 2
            if t >= 2:
                p.wait("sync", "ST", st_y[t - 2])
                p.wait("sync", "V", v_comb[b])
                p.wait("gpsimd", "V", v_comb[b])
            ld_cx = p.dma("sync", "LD", cx[b][:, :], x_hbm[ts(t, 128), :])
            lg0 = p.igather("gpsimd", "LG", mg0[b][:, :], exp_out[:, :],
                            g0i[:, t:t + 1], NSLOT)
            lg1 = p.igather("gpsimd", "LG", mg1[b][:, :], exp_out[:, :],
                            g1i[:, t:t + 1], NSLOT)
            p.wait("vector", "LD", ld_cx)
            p.wait("vector", "LG", lg1)
            p.wait("vector", "V", v_yproj)
            p.op("vector", lambda E, t=t, b=b: E.tensor_add(
                ac1[:, :], cx[b][:, :], attn_sb[:, ts(t, D)]))
            p.op("vector", lambda E, t=t, b=b: E.scalar_tensor_tensor(
                ac2[:, :], mg0[b][:, :], p1[:, t:t + 1], ac1[:, :],
                op0=ALU.mult, op1=ALU.add))
            p.op("vector", lambda E, t=t, b=b: E.scalar_tensor_tensor(
                ac3[b][:, :], mg1[b][:, :], p2[:, t:t + 1], ac2[:, :],
                op0=ALU.mult, op1=ALU.add))
            v_comb[b] = p.mark("vector", "V")
            p.wait("sync", "V", v_comb[b])
            st_y.append(p.dma("sync", "ST", y_out[ts(t, 128), :], ac3[b][:, :]))
        p.wait("sync", "ST", S["ST"].n)
        A.pop()

        for eng, meth in (("sync", block.sync), ("gpsimd", block.gpsimd),
                          ("tensor", block.tensor), ("vector", block.vector),
                          ("scalar", block.scalar)):
            def runner(E, ops=p.q[eng]):
                for f in ops:
                    f(E)
            meth(runner)

    return nc, dbg_out



def make_consts(core):
    b = np.arange(128)
    true_b = (b + core * OTILE) % 128
    tri_rot = (true_b[:, None] < true_b[None, :]).astype(np.float32)
    return dict(
        tri_rot=tri_rot.astype(ml_dtypes.bfloat16),
        ones_bf=np.ones((128, 128), ml_dtypes.bfloat16),
        id_bf=np.eye(128, dtype=np.float32).astype(ml_dtypes.bfloat16),
        id_f=np.eye(128, dtype=np.float32),
        caus_bf=np.tril(np.ones((128, 128), np.float32)).astype(ml_dtypes.bfloat16),
        onesrow_bf=np.ones((1, 2048), ml_dtypes.bfloat16),
        onesrow_f=np.ones((1, 128), np.float32),
        onescol_f=np.ones((128, 1), np.float32),
        tokid_f=(np.arange(OTILE)[None, :] * 128
                 + np.arange(128)[:, None]).astype(np.float32),
    )


def make_inputs(L0_attn_x, L0_attn_Wq, L0_attn_bq, L0_attn_Wk, L0_attn_bk,
                L0_attn_Wv, L0_attn_bv, L0_attn_Wo, L0_attn_bo,
                L0_ffn_w_g, L0_ffn_c_fc, L0_ffn_c_proj):
    x = np.ascontiguousarray(np.asarray(L0_attn_x, np.float32).reshape(NT, D))
    shared = dict(
        wqT=np.ascontiguousarray(np.asarray(L0_attn_Wq, np.float32).T),
        wkT=np.ascontiguousarray(np.asarray(L0_attn_Wk, np.float32).T),
        wvT=np.ascontiguousarray(np.asarray(L0_attn_Wv, np.float32).T),
        woT=np.ascontiguousarray(np.asarray(L0_attn_Wo, np.float32).T),
        bq=np.asarray(L0_attn_bq, np.float32).reshape(1, D),
        bk=np.asarray(L0_attn_bk, np.float32).reshape(1, D),
        bv=np.asarray(L0_attn_bv, np.float32).reshape(1, D),
        bo=np.asarray(L0_attn_bo, np.float32).reshape(1, D),
        wg=np.asarray(L0_ffn_w_g, np.float32),
        cfc=np.asarray(L0_ffn_c_fc, np.float32),
        cpj=np.asarray(L0_ffn_c_proj, np.float32),
    )
    in_maps = []
    for c in range(8):
        m = dict(shared)
        m["x"] = np.ascontiguousarray(np.roll(x, -c * T, axis=0))
        m.update(make_consts(c))
        in_maps.append(m)
    return in_maps


def kernel(**inputs):
    in_maps = make_inputs(**inputs)
    nc, _ = build()
    res = run_bass_kernel_spmd(nc, in_maps, core_ids=list(range(8)))
    y = np.concatenate([res.results[c]["y"] for c in range(8)], axis=0)
    return y.reshape(8, T, D)


# version 09
# speedup vs baseline: 1.3067x; 1.2202x over previous
from contextlib import ExitStack

import numpy as np
import ml_dtypes

import concourse.bass as bass
import concourse.mybir as mybir
from concourse.bass_utils import run_bass_kernel_spmd

F32 = mybir.dt.float32
BF16 = mybir.dt.bfloat16
I32 = mybir.dt.int32
AF = mybir.ActivationFunctionType
ALU = mybir.AluOpType
AX = mybir.AxisListType
IOffs = bass.IndirectOffsetOnAxis

D = 1024
T = 2048
NT = 16384
NTILE = 128
OTILE = 16
HEADS = 16
DH = 64
NE = 8
DFF = 4096
CAP = 5120
CLOC = 896
NSLOT = NE * CLOC
SCOLS = NSLOT // 128
BIG = 1.0e9
SBIG = 1.0e4

ENGINES = ("sync", "gpsimd", "tensor", "vector", "scalar")


class Sem:
    def __init__(self, stack, nc, name):
        self.h = stack.enter_context(nc.semaphore(name))
        self.n = 0

    def bump(self, k):
        self.n += k
        return self.n


class Prog:

    def __init__(self, nc, stack):
        self.nc = nc
        self.q = {e: [] for e in ENGINES}
        self.sems = {name: Sem(stack, nc, name) for name in
                     ("LD", "LG", "T", "V", "A", "ST")}

    def wait(self, eng, sem, val=None):
        s = self.sems[sem]
        v = s.n if val is None else val
        assert v <= s.n, f"wait {sem}>={v} but only {s.n} emitted"
        if v > 0:
            self.q[eng].append(lambda E, h=s.h, v=v: E.wait_ge(h, v))

    def dma(self, eng, sem, out, in_, **kw):
        s = self.sems[sem]
        s.bump(16)
        self.q[eng].append(
            lambda E, o=out, i=in_, h=s.h, kw=kw:
            E.dma_start(out=o, in_=i, **kw).then_inc(h, 16))
        return s.n

    def dmat(self, eng, sem, out, in_):
        s = self.sems[sem]
        s.bump(16)
        self.q[eng].append(
            lambda E, o=out, i=in_, h=s.h:
            E.dma_start_transpose(o, i).then_inc(h, 16))
        return s.n

    def _bound_reg(self, E, bounds):
        if not hasattr(self, "_bregs"):
            self._bregs = {}
        if bounds not in self._bregs:
            r = E.alloc_register(f"bnd{bounds}")
            E.reg_mov(r, bounds)
            self._bregs[bounds] = r
        return self._bregs[bounds]

    def igather(self, eng, sem, out, in_, idx, bounds):
        s = self.sems[sem]
        s.bump(16)

        def f(E, o=out, i=in_, x=idx, b=bounds, h=s.h):
            return E.indirect_dma_start(
                out=o, out_offset=None, in_=i,
                in_offset=IOffs(ap=x, axis=0),
                bounds_check=self._bound_reg(E, b), oob_is_err=False
            ).then_inc(h, 16)
        self.q[eng].append(f)
        return s.n

    def iscatter(self, eng, sem, out, in_, idx, bounds):
        s = self.sems[sem]
        s.bump(16)

        def f(E, o=out, i=in_, x=idx, b=bounds, h=s.h):
            return E.indirect_dma_start(
                out=o, out_offset=IOffs(ap=x, axis=0), in_=i, in_offset=None,
                bounds_check=self._bound_reg(E, b), oob_is_err=False
            ).then_inc(h, 16)
        self.q[eng].append(f)
        return s.n

    def op(self, eng, fn):
        self.q[eng].append(fn)
        if eng == "vector":
            self.q[eng].append(lambda E: E.drain())

    def mark(self, eng, sem, k=1):
        s = self.sems[sem]
        v = s.bump(k)
        if eng in ("vector", "scalar"):
            self.q[eng].append(lambda E: E.drain())
        self.q[eng].append(lambda E, h=s.h, k=k: E.sem_inc(h, k))
        return v

    def op_inc(self, eng, sem, fn, k=1):
        s = self.sems[sem]
        v = s.bump(k)
        self.q[eng].append(lambda E, f=fn, h=s.h, k=k: f(E).then_inc(h, k))
        return v


def ts(i, n):
    return slice(i * n, (i + 1) * n)


_DT_SIZE = {F32: 4, BF16: 2, I32: 4}


class Arena:

    def __init__(self, nc):
        self.nc = nc
        self.stack = [nc.sbuf_base]
        self.peak = nc.sbuf_base
        self.n = 0

    def push(self):
        self.stack.append(self.stack[-1])

    def pop(self):
        self.stack.pop()

    def take(self, name, shape, dtype):
        off = (self.stack[-1] + 63) // 64 * 64
        pp = 1
        for d in shape[1:]:
            pp *= d
        pp *= _DT_SIZE[dtype]
        self.n += 1
        t = self.nc.alloc_sbuf_tensor_at(f"{name}_{self.n}", shape, dtype,
                                         offset=off)
        self.stack[-1] = off + pp
        self.peak = max(self.peak, self.stack[-1])
        assert self.stack[-1] <= self.nc.sbuf_top, \
            f"SBUF overflow at {name}: {self.stack[-1]} > {self.nc.sbuf_top}"
        return t


def build(dbg=(), stop_after=None):
    nc = bass.Bass("TRN2", target_bir_lowering=False, debug=False)
    dbg = set(dbg)

    x_hbm = nc.dram_tensor("x", [NT, D], F32, kind="ExternalInput").ap()
    wT = {k: nc.dram_tensor(f"w{k}T", [D, D], F32, kind="ExternalInput").ap()
          for k in "qkvo"}
    bias = {k: nc.dram_tensor(f"b{k}", [1, D], F32, kind="ExternalInput").ap()
            for k in "qkvo"}
    wg = nc.dram_tensor("wg", [D, NE], F32, kind="ExternalInput").ap()
    cfc = nc.dram_tensor("cfc", [NE, 32, 128, D], F32, kind="ExternalInput").ap()
    cpj = nc.dram_tensor("cpj", [NE, DFF, D], F32, kind="ExternalInput").ap()
    tri_hbm = nc.dram_tensor("tri_rot", [128, 128], BF16, kind="ExternalInput").ap()
    one_hbm = nc.dram_tensor("ones_bf", [128, 128], BF16, kind="ExternalInput").ap()
    idb_hbm = nc.dram_tensor("id_bf", [128, 128], BF16, kind="ExternalInput").ap()
    idf_hbm = nc.dram_tensor("id_f", [128, 128], F32, kind="ExternalInput").ap()
    caus_hbm = nc.dram_tensor("caus_bf", [128, 128], BF16, kind="ExternalInput").ap()
    orow_hbm = nc.dram_tensor("onesrow_bf", [1, 2048], BF16, kind="ExternalInput").ap()
    orowf_hbm = nc.dram_tensor("onesrow_f", [1, 128], F32, kind="ExternalInput").ap()
    ocol_hbm = nc.dram_tensor("onescol_f", [128, 1], F32, kind="ExternalInput").ap()
    tok_hbm = nc.dram_tensor("tokid_f", [128, OTILE], F32, kind="ExternalInput").ap()

    y_out = nc.dram_tensor("y", [T, D], F32, kind="ExternalOutput").ap()

    xn_pad = nc.dram_tensor("xn_pad", [T + 1, D], BF16).ap()
    exp_x = nc.dram_tensor("exp_x", [NSLOT, D], BF16).ap()
    exp_out = nc.dram_tensor("exp_out", [NSLOT + 1, D], BF16).ap()
    slot_tab = nc.dram_tensor("slot_tab", [NSLOT, 1], F32).ap()
    y_mid = nc.dram_tensor("y_mid", [T, D], BF16).ap()

    dbg_out = {}

    def dbgt(name, shape, dtype=F32):
        dbg_out[name] = nc.dram_tensor("dbg_" + name, shape, dtype,
                                       kind="ExternalOutput").ap()
        return dbg_out[name]

    with ExitStack() as top:
        e = top.enter_context
        p = Prog(nc, top)
        S = p.sems
        A = Arena(nc)

        c_tri = A.take("c_tri", [128, 128], BF16)
        c_one = A.take("c_one", [128, 128], BF16)
        c_idb = A.take("c_idb", [128, 128], BF16)
        c_idf = A.take("c_idf", [128, 128], F32)
        c_caus = A.take("c_caus", [128, 128], BF16)
        c_or = A.take("c_or", [1, 2048], BF16)
        c_orf = A.take("c_orf", [1, 128], F32)
        c_oc = A.take("c_oc", [128, 1], F32)
        c_tok = A.take("c_tok", [128, OTILE], F32)
        c_eps = A.take("c_eps", [128, 1], F32)
        zrow = A.take("zrow", [1, D], BF16)
        b_sb = {k: A.take(f"b_{k}", [1, D], BF16) for k in "qkvo"}
        wg_f = A.take("wg_f", [128, NE * 8], F32)
        wg_b = A.take("wg_b", [128, NE * 8], BF16)
        g_all = A.take("g_all", [128, NTILE * NE], F32)
        idx1 = A.take("idx1", [128, NTILE], F32)
        idx2 = A.take("idx2", [128, NTILE], F32)
        p1 = A.take("p1", [128, NTILE], F32)
        p2 = A.take("p2", [128, NTILE], F32)
        g0i = A.take("g0i", [128, OTILE], I32)
        g1i = A.take("g1i", [128, OTILE], I32)
        tabi4 = [A.take(f"tabi{i}", [128, 14], I32) for i in range(4)]
        mhat = A.take("mhat", [128, 1], F32)
        nrm = A.take("nrm", [1, 8], F32)

        ps_a = e(nc.psum_tensor("ps_a", [128, 512], F32))
        ps_b = e(nc.psum_tensor("ps_b", [128, 512], F32))
        ps_s = e(nc.psum_tensor("ps_s", [128, 2048], F32))
        ps_t = e(nc.psum_tensor("ps_t", [128, 512], BF16))
        ps_o = e(nc.psum_tensor("ps_o", [128, 512], F32))

        block = e(nc.Block())

        def finish_early():
            yz = g_all
            p.op("vector", lambda E: E.memset(yz[:, :], 0.0))
            vz = p.mark("vector", "V")
            p.wait("sync", "V", vz)
            p.wait("sync", "ST", S["ST"].n)
            for t in range(OTILE):
                p.dma("sync", "ST", y_out[ts(t, 128), :], yz[:, :])
            p.wait("sync", "ST", S["ST"].n)
            for eng, meth in (("sync", block.sync), ("gpsimd", block.gpsimd),
                              ("tensor", block.tensor), ("vector", block.vector),
                              ("scalar", block.scalar)):
                def runner(E, ops=p.q[eng]):
                    for f in ops:
                        f(E)
                meth(runner)

        for dst, src in ((c_tri, tri_hbm), (c_one, one_hbm), (c_idb, idb_hbm),
                         (c_idf, idf_hbm), (c_caus, caus_hbm), (c_or, orow_hbm),
                         (c_orf, orowf_hbm), (c_oc, ocol_hbm), (c_tok, tok_hbm)):
            p.dma("sync", "LD", dst[:, :], src)
        p.dma("sync", "LD", wg_f[:, :].rearrange("p (dc e) -> p dc e", e=NE),
              wg.rearrange("(dc p) e -> p dc e", p=128))
        for k in "qkvo":
            p.dma("gpsimd", "LG", b_sb[k][:, :], bias[k])
        ld_const = S["LD"].n
        lg_bias = S["LG"].n
        p.op("vector", lambda E: E.memset(c_eps[:, :], 1e-5))
        p.op("vector", lambda E: E.memset(zrow[:, :], 0.0))
        p.wait("vector", "LD", ld_const)
        p.op("vector", lambda E: E.tensor_copy(wg_b[:, :], wg_f[:, :]))
        v_const = p.mark("vector", "V")
        p.wait("sync", "V", v_const)
        p.dma("sync", "LD", xn_pad[T:T + 1, :], zrow[:, :])
        p.dma("sync", "LD", exp_out[NSLOT:NSLOT + 1, :], zrow[:, :])

        A.push()
        qt_sb = [A.take(f"qt{i}", [128, T], BF16) for i in range(8)]
        kt_sb = [A.take(f"kt{i}", [128, T], BF16) for i in range(8)]
        v_sb = [A.take(f"v{i}", [128, D], BF16) for i in range(OTILE)]
        xnt = A.take("xnt", [128, 8 * 2048], BF16)
        wbuf = A.take("wbuf", [128, 8 * 1024], BF16)
        gxb = [A.take(f"gxb{i}", [128, D], BF16) for i in range(2)]
        sidxi = [A.take(f"sidxi{k}", [128, OTILE], I32) for k in range(2)]
        svals = [A.take(f"svals{k}", [128, OTILE], F32) for k in range(2)]
        tabf = A.take("tabf", [128, SCOLS], F32)
        tini = A.take("tini", [128, SCOLS], F32)

        lg_w = {}
        lg_w["q"] = p.dma("gpsimd", "LG",
                          wbuf[:, :].rearrange("p (dc i) -> p dc i", i=D),
                          wT["q"].rearrange("(dc p) i -> p dc i", p=128))
        A.push()
        NXB = 4
        xb = [A.take(f"xb{i}", [128, D], F32) for i in range(NXB)]
        xnb = [A.take(f"xnb{i}", [128, D], BF16) for i in range(NXB)]
        sq1 = A.take("sq1", [128, D], F32)
        colr = A.take("colr", [128, 8 * NXB], F32)
        xTb = [A.take(f"xTb{i}", [128, D], BF16) for i in range(2)]

        ps_t3 = ps_t[:, :].rearrange("p (dc t) -> p dc t", t=128)
        ps_o3 = ps_o[:, :].rearrange("p (dc t) -> p dc t", t=128)
        st_own = [0] * OTILE
        v_lastread = [0] * NXB
        a_lastread = [0] * NXB
        t_xnb_read = [0] * NXB
        g_xnb_read = [0] * NXB
        a_evA_prev = 0
        v_evB_prev = 0
        v_g_prev = 0
        v_chunk0 = 0
        for i in range(NTILE):
            b = i % NXB
            own = i < OTILE
            if v_lastread[b]:
                p.wait("sync", "V", v_lastread[b])
                p.wait("sync", "A", a_lastread[b])
            ldx = p.dma("sync", "LD", xb[b][:, :], x_hbm[ts(i, 128), :])
            cS = colr[:, b * 8 + 0:b * 8 + 1]
            cM = colr[:, b * 8 + 1:b * 8 + 2]
            cMS = colr[:, b * 8 + 2:b * 8 + 3]
            cV = colr[:, b * 8 + 3:b * 8 + 4]
            cR = colr[:, b * 8 + 4:b * 8 + 5]
            cSum = colr[:, b * 8 + 5:b * 8 + 6]
            p.wait("scalar", "LD", ldx)
            p.op("scalar", lambda E, b=b, cS=cS:
                 E.activation(sq1[:, :], xb[b][:, :], AF.Square, accum_out=cS))
            a_sq = p.mark("scalar", "A")
            p.wait("vector", "LD", ldx)
            p.op("vector", lambda E, b=b, cSum=cSum:
                 E.reduce_sum(cSum, xb[b][:, :], axis=AX.X))
            p.op("vector", lambda E, cSum=cSum, cM=cM:
                 E.tensor_scalar_mul(cM, cSum, 1.0 / D))
            p.op("vector", lambda E, cM=cM, cMS=cMS: E.tensor_mul(cMS, cM, cM))
            p.wait("vector", "A", a_sq)
            p.op("vector", lambda E, cS=cS, cMS=cMS, cV=cV:
                 E.scalar_tensor_tensor(cV, cS, 1.0 / D, cMS,
                                        op0=ALU.mult, op1=ALU.subtract))
            v_var = p.mark("vector", "V")
            p.wait("scalar", "V", v_var)
            p.op("scalar", lambda E, cV=cV, cMS=cMS:
                 E.activation(cMS, cV, AF.Sqrt, bias=c_eps[:, :]))
            a_sd = p.mark("scalar", "A")
            a_lastread[b] = a_sd
            p.wait("vector", "A", a_sd)
            p.op("vector", lambda E, cMS=cMS, cR=cR: E.reciprocal(cR, cMS))
            if t_xnb_read[b]:
                p.wait("vector", "T", t_xnb_read[b])
                p.wait("vector", "LG", g_xnb_read[b])
            p.op("vector", lambda E, b=b, cM=cM, cR=cR:
                 E.tensor_scalar(xnb[b][:, :], xb[b][:, :], cM, cR,
                                 op0=ALU.subtract, op1=ALU.mult))
            v_xnb = p.mark("vector", "V")
            v_lastread[b] = v_xnb
            if own:
                p.wait("gpsimd", "V", v_xnb)
                st_own[i] = p.dma("gpsimd", "LG", xn_pad[ts(i, 128), :],
                                  xnb[b][:, :])
                g_xnb_read[b] = st_own[i]
            if own:
                dst = xnt[:, :].rearrange("p (dc t) -> p dc t", t=2048)
                dstA = dst[:, 0:4, i * 128:(i + 1) * 128]
                dstB = dst[:, 4:8, i * 128:(i + 1) * 128]
            else:
                xv = xTb[i % 2]
                dstA = xv[:, 0:512].rearrange("p (dc t) -> p dc t", t=128)
                dstB = xv[:, 512:1024].rearrange("p (dc t) -> p dc t", t=128)
            p.wait("tensor", "V", v_xnb)
            if i == 0:
                p.wait("tensor", "LD", ld_const)
            if a_evA_prev:
                p.wait("tensor", "A", a_evA_prev)

            def trg(E, b, lo, dstp):
                for k2 in range(4):
                    if lo == 0:
                        last = E.transpose(
                            dstp[:, k2 * 128:(k2 + 1) * 128],
                            xnb[b][:, (lo + k2) * 128:(lo + k2 + 1) * 128],
                            c_idb[:, :])
                    else:
                        last = E.matmul(
                            dstp[:, k2 * 128:(k2 + 1) * 128],
                            xnb[b][:, (lo + k2) * 128:(lo + k2 + 1) * 128],
                            c_idb[:, :], start=True, stop=True)
                return last
            t_trA = p.op_inc("tensor", "T", lambda E, b=b: trg(E, b, 0, ps_t))
            if v_evB_prev:
                p.wait("tensor", "V", v_evB_prev)
            t_trB = p.op_inc("tensor", "T", lambda E, b=b: trg(E, b, 4, ps_o))
            p.wait("scalar", "T", t_trA)
            p.op("scalar", lambda E, dstA=dstA:
                 E.activation(dstA, ps_t3[:, 0:4, :], AF.Copy))
            a_evA = p.mark("scalar", "A")
            p.wait("vector", "T", t_trB)
            p.op("vector", lambda E, dstB=dstB:
                 E.tensor_copy(dstB, ps_o3[:, 0:4, :]))
            v_evB = p.mark("vector", "V")
            a_evA_prev, v_evB_prev = a_evA, v_evB
            if own and i == OTILE - 1:
                v_chunk0 = v_evB
            p.wait("tensor", "A", a_evA)
            p.wait("tensor", "V", v_evB)
            if i == 0:
                p.wait("tensor", "V", v_const)
            if v_g_prev:
                p.wait("tensor", "V", v_g_prev)
            if own:
                src, soff, sstr = xnt[:, :], i * 128, 2048
            else:
                src, soff, sstr = xTb[i % 2][:, :], 0, 128

            def mmg(E, src=src, soff=soff, sstr=sstr):
                for dc in range(8):
                    last = E.matmul(
                        ps_a[:, 0:8],
                        src[:, dc * sstr + soff: dc * sstr + soff + 128],
                        wg_b[:, dc * 8:(dc + 1) * 8],
                        start=(dc == 0), stop=(dc == 7))
                return last
            t_g = p.op_inc("tensor", "T", mmg)
            t_xnb_read[b] = t_g
            p.wait("vector", "T", t_g)
            p.op("vector", lambda E, i=i:
                 E.tensor_copy(g_all[:, i * 8:(i + 1) * 8], ps_a[:, 0:8]))
            v_g_prev = p.mark("vector", "V")
        st_own_last = st_own[OTILE - 1]
        v_g_done = v_g_prev
        t_g_done = S["T"].n
        v_p1 = S["V"].n
        a_p1 = S["A"].n

        if stop_after == "p1":
            finish_early()
            return nc, dbg_out
        A.pop()

        A.push()

        def proj_qk(dst_tiles, wkey, bias_row, t_sem_name):
            p.wait("tensor", "LG", lg_w[wkey])
            p.wait("tensor", "V", v_g_done)
            p.wait("tensor", "A", a_evA_prev)
            p.wait("tensor", "LG", lg_bias)
            last = 0
            evs = []
            for ic in range(8):
                for tc in range(4):
                    bank = ps_a if (ic * 4 + tc) % 2 == 0 else ps_b
                    if len(evs) >= 2:
                        p.wait("tensor", "V", evs[-2])

                    def mm(E, ic=ic, tc=tc, bank=bank, w=wkey, br=bias_row):
                        E.matmul(bank[:, :], br[0:1, ts(ic, 128)],
                                 c_or[0:1, ts(tc, 512)], start=True, stop=False)
                        for dc in range(8):
                            last = E.matmul(
                                bank[:, :],
                                wbuf[:, dc * 1024 + ic * 128: dc * 1024 + (ic + 1) * 128],
                                xnt[:, dc * 2048 + tc * 512: dc * 2048 + tc * 512 + 512],
                                start=False, stop=(dc == 7))
                        return last
                    t_mm = p.op_inc("tensor", "T", mm)
                    p.wait("vector", "T", t_mm)
                    p.op("vector", lambda E, ic=ic, tc=tc, bank=bank, dt=dst_tiles:
                         E.tensor_copy(dt[ic][:, ts(tc, 512)], bank[:, :]))
                    evs.append(p.mark("vector", "V"))
            return evs[-1]

        v_q = proj_qk(qt_sb, "q", b_sb["q"], "T")
        t_q_done = S["T"].n
        p.wait("gpsimd", "T", t_q_done)
        lg_w["k"] = p.dma("gpsimd", "LG",
                          wbuf[:, :].rearrange("p (dc i) -> p dc i", i=D),
                          wT["k"].rearrange("(dc p) i -> p dc i", p=128))
        v_k = proj_qk(kt_sb, "k", b_sb["k"], "T")
        t_k_done = S["T"].n
        p.wait("gpsimd", "T", t_k_done)
        lg_w["v"] = p.dma("gpsimd", "LG",
                          wbuf[:, :].rearrange("p (dc i) -> p dc i", i=D),
                          wT["v"].rearrange("(dc p) i -> p dc i", p=128))

        p.wait("tensor", "LG", lg_w["v"])
        v_vevs = []
        for tt in range(OTILE):
            for ic2 in range(2):
                bank = ps_a if (tt * 2 + ic2) % 2 == 0 else ps_b
                if len(v_vevs) >= 2:
                    p.wait("tensor", "V", v_vevs[-2])

                def mmv(E, tt=tt, ic2=ic2, bank=bank):
                    E.matmul(bank[:, :], c_or[0:1, ts(tt, 128)],
                             b_sb["v"][0:1, ts(ic2, 512)], start=True, stop=False)
                    for dc in range(8):
                        last = E.matmul(
                            bank[:, :],
                            xnt[:, dc * 2048 + tt * 128: dc * 2048 + (tt + 1) * 128],
                            wbuf[:, dc * 1024 + ic2 * 512: dc * 1024 + ic2 * 512 + 512],
                            start=False, stop=(dc == 7))
                    return last
                t_mm = p.op_inc("tensor", "T", mmv)
                p.wait("vector", "T", t_mm)
                p.op("vector", lambda E, tt=tt, ic2=ic2, bank=bank:
                     E.tensor_copy(v_sb[tt][:, ts(ic2, 512)], bank[:, :]))
                v_vevs.append(p.mark("vector", "V"))
        v_qkv_done = v_vevs[-1]
        t_qkv_done = S["T"].n
        if "qkv" in dbg:
            oq = dbgt("qt", [1024, 2048], BF16)
            ok_ = dbgt("kt", [1024, 2048], BF16)
            ov = dbgt("v", [2048, 1024], BF16)
            p.wait("scalar", "V", v_qkv_done)
            for i in range(8):
                p.dma("scalar", "ST", oq[ts(i, 128), :], qt_sb[i][:, :])
                p.dma("scalar", "ST", ok_[ts(i, 128), :], kt_sb[i][:, :])
            for i in range(OTILE):
                p.dma("scalar", "ST", ov[ts(i, 128), :], v_sb[i][:, :])
        if stop_after == "qkv":
            finish_early()
            return nc, dbg_out

        if "g" in dbg:
            od = dbgt("g", [128, NTILE * NE])
            p.wait("scalar", "V", v_g_done)
            p.dma("scalar", "ST", od, g_all[:, :])

        A.push()
        r_v1 = A.take("r_v1", [128, NTILE], F32)
        r_v2 = A.take("r_v2", [128, NTILE], F32)
        r_eq = A.take("r_eq", [128, NTILE], F32)
        r_t = A.take("r_t", [128, NTILE], F32)
        r_big = A.take("r_big", [128, NTILE], F32)
        g2 = A.take("g2", [128, NTILE * NE], F32)
        zf = A.take("zf", [128, NTILE], F32)
        s12 = A.take("s12", [128, NTILE], F32)

        g3 = g_all[:, :].rearrange("p (t e) -> p t e", e=NE)
        g23 = g2[:, :].rearrange("p (t e) -> p t e", e=NE)
        p.op("vector", lambda E: E.memset(r_big[:, :], SBIG))
        p.op("vector", lambda E: E.memset(zf[:, :], 0.0))
        p.op("vector", lambda E: E.reduce_max(r_v1[:, :], g3, axis=AX.X))
        p.op("vector", lambda E: E.memset(idx1[:, :], SBIG))
        for ee in range(NE):
            p.op("vector", lambda E, ee=ee:
                 E.tensor_tensor(r_eq[:, :], g3[:, :, ee], r_v1[:, :], op=ALU.is_equal))
            p.op("vector", lambda E, ee=ee: E.scalar_tensor_tensor(
                r_t[:, :], r_eq[:, :], -(SBIG - ee), r_big[:, :],
                op0=ALU.mult, op1=ALU.add))
            p.op("vector", lambda E: E.tensor_tensor(idx1[:, :], idx1[:, :],
                                                     r_t[:, :], op=ALU.min))
        for ee in range(NE):
            p.op("vector", lambda E, ee=ee: E.tensor_single_scalar(
                r_eq[:, :], idx1[:, :], float(ee), op=ALU.is_equal))
            p.op("vector", lambda E, ee=ee: E.scalar_tensor_tensor(
                g23[:, :, ee], r_eq[:, :], -1e30, g3[:, :, ee],
                op0=ALU.mult, op1=ALU.add))
        p.op("vector", lambda E: E.reduce_max(r_v2[:, :], g23, axis=AX.X))
        p.op("vector", lambda E: E.memset(idx2[:, :], SBIG))
        for ee in range(NE):
            p.op("vector", lambda E, ee=ee:
                 E.tensor_tensor(r_eq[:, :], g23[:, :, ee], r_v2[:, :], op=ALU.is_equal))
            p.op("vector", lambda E, ee=ee: E.scalar_tensor_tensor(
                r_t[:, :], r_eq[:, :], -(SBIG - ee), r_big[:, :],
                op0=ALU.mult, op1=ALU.add))
            p.op("vector", lambda E: E.tensor_tensor(idx2[:, :], idx2[:, :],
                                                     r_t[:, :], op=ALU.min))
        p.op("vector", lambda E: E.tensor_sub(s12[:, :], r_v1[:, :], r_v2[:, :]))
        v_rt = p.mark("vector", "V")
        p.wait("scalar", "V", v_rt)
        p.op("scalar", lambda E: E.activation(p1[:, :], s12[:, :], AF.Sigmoid))
        p.op("scalar", lambda E: E.activation(p2[:, :], s12[:, :], AF.Sigmoid,
                                              scale=-1.0))
        a_probs = p.mark("scalar", "A")

        if "route" in dbg:
            for nm, buf in (("idx1", idx1), ("idx2", idx2), ("p1", p1), ("p2", p2)):
                od = dbgt(nm, [128, NTILE])
                p.wait("scalar", "A", a_probs)
                p.wait("scalar", "V", v_rt)
                p.dma("scalar", "ST", od, buf[:, :])

        idxT = [A.take(f"idxT{k}", [128, 128], F32) for k in range(2)]
        m_ke = A.take("m_ke", [128, 128], F32)
        incl = A.take("incl", [128, 128], F32)
        plane = A.take("plane", [128, 128], F32)
        rsel = [A.take(f"rsel{k}", [128, 128], F32) for k in range(2)]
        rtt = [A.take(f"rtt{k}", [128, 128], F32) for k in range(2)]
        totb = A.take("totb", [128, 2 * NE], BF16)
        carr = A.take("carr", [128, 2 * NE], F32)

        p.wait("tensor", "V", v_rt)
        for k, src in enumerate((idx1, idx2)):
            t_tr = p.op_inc("tensor", "T", lambda E, s=src:
                            E.transpose(ps_a[:, 0:128], s[:, :], c_idf[:, :]))
            p.wait("vector", "T", t_tr)
            p.op("vector", lambda E, k=k: E.tensor_copy(idxT[k][:, :], ps_a[:, 0:128]))
            v_m = p.mark("vector", "V")
            p.wait("tensor", "V", v_m)
        v_idxT = p.mark("vector", "V")

        for k in range(2):
            for ee in range(NE):
                p.op("vector", lambda E, k=k, ee=ee: E.tensor_single_scalar(
                    m_ke[:, :], idxT[k][:, :], float(ee), op=ALU.is_equal))
                p.op("vector", lambda E: E.tensor_tensor_scan(
                    incl[:, :], m_ke[:, :], zf[:, 0:128], 0.0,
                    op0=ALU.add, op1=ALU.add))
                p.op("vector", lambda E, k=k, ee=ee: E.tensor_copy(
                    totb[:, k * NE + ee:k * NE + ee + 1], incl[:, 127:128]))
        v_tot = p.mark("vector", "V")
        p.wait("tensor", "V", v_tot)

        def mm_car(E):
            E.matmul(ps_o[:, 0:8], c_tri[:, :], totb[:, 0:NE], start=True, stop=True)
            return E.matmul(ps_o[:, 8:16], c_tri[:, :], totb[:, NE:2 * NE],
                            start=True, stop=False)

        def mm_car2(E):
            return E.matmul(ps_o[:, 8:16], c_one[:, :], totb[:, 0:NE],
                            start=False, stop=True)
        t_car = p.op_inc("tensor", "T", lambda E: (mm_car(E), mm_car2(E))[1])
        p.wait("vector", "T", t_car)
        p.op("vector", lambda E: E.tensor_copy(carr[:, 0:NE], ps_o[:, 0:8]))
        p.op("vector", lambda E: E.tensor_copy(carr[:, NE:2 * NE], ps_o[:, 8:16]))
        for k in range(2):
            p.op("vector", lambda E, k=k: E.memset(rsel[k][:, :], 0.0))
            for ee in range(NE):
                p.op("vector", lambda E, k=k, ee=ee: E.tensor_single_scalar(
                    m_ke[:, :], idxT[k][:, :], float(ee), op=ALU.is_equal))
                p.op("vector", lambda E: E.tensor_tensor_scan(
                    incl[:, :], m_ke[:, :], zf[:, 0:128], 0.0,
                    op0=ALU.add, op1=ALU.add))
                p.op("vector", lambda E, k=k, ee=ee: E.scalar_tensor_tensor(
                    plane[:, :], incl[:, :], carr[:, k * NE + ee:k * NE + ee + 1],
                    m_ke[:, :], op0=ALU.add, op1=ALU.subtract))
                p.op("vector", lambda E: E.tensor_mul(plane[:, :], plane[:, :],
                                                      m_ke[:, :]))
                p.op("vector", lambda E, k=k: E.tensor_add(rsel[k][:, :],
                                                           rsel[k][:, :], plane[:, :]))
        v_rsel = p.mark("vector", "V")
        p.wait("tensor", "V", v_rsel)
        for k in range(2):
            t_tr = p.op_inc("tensor", "T", lambda E, k=k:
                            E.transpose(ps_a[:, 0:128], rsel[k][:, :], c_idf[:, :]))
            p.wait("vector", "T", t_tr)
            p.op("vector", lambda E, k=k: E.tensor_copy(rtt[k][:, :], ps_a[:, 0:128]))
            v_m = p.mark("vector", "V")
            p.wait("tensor", "V", v_m)
        v_rtt = p.mark("vector", "V")

        if "rank" in dbg:
            for k in range(2):
                od = dbgt(f"rtt{k}", [128, 128])
                p.wait("scalar", "V", v_rtt)
                p.dma("scalar", "ST", od, rtt[k][:, :])
            p.dma("scalar", "ST", dbgt("rsel0", [128, 128]), rsel[0][:, :])
            p.dma("scalar", "ST", dbgt("idxT0", [128, 128]), idxT[0][:, :])
            p.dma("scalar", "ST", dbgt("carr", [128, 16]), carr[:, :])

        if stop_after == "rank":
            finish_early()
            return nc, dbg_out

        oloc = [A.take(f"oloc{k}", [16, 128], F32) for k in range(2)]
        lincl = A.take("lincl", [16, 128], F32)
        lm = A.take("lm", [16, 128], F32)
        lpl = A.take("lpl", [16, 128], F32)
        lrsel = [A.take(f"lrsel{k}", [16, 128], F32) for k in range(2)]
        lrtt = [A.take(f"lrtt{k}", [128, OTILE], F32) for k in range(2)]
        ltotb = A.take("ltotb", [16, 2 * NE], BF16)
        lcarr = A.take("lcarr", [16, 2 * NE], F32)

        for k, src in enumerate((idx1, idx2)):
            t_tr = p.op_inc("tensor", "T", lambda E, s=src:
                            E.transpose(ps_a[0:16, 0:128], s[:, 0:OTILE],
                                        c_idf[:, :]))
            p.wait("vector", "T", t_tr)
            p.op("vector", lambda E, k=k: E.tensor_copy(oloc[k][:, :], ps_a[0:16, 0:128]))
            v_m = p.mark("vector", "V")
            p.wait("tensor", "V", v_m)
        v_oloc = p.mark("vector", "V")
        for k in range(2):
            for ee in range(NE):
                p.op("vector", lambda E, k=k, ee=ee: E.tensor_single_scalar(
                    lm[:, :], oloc[k][:, :], float(ee), op=ALU.is_equal))
                p.op("vector", lambda E: E.tensor_tensor_scan(
                    lincl[:, :], lm[:, :], zf[0:16, 0:128], 0.0,
                    op0=ALU.add, op1=ALU.add))
                p.op("vector", lambda E, k=k, ee=ee: E.tensor_copy(
                    ltotb[:, k * NE + ee:k * NE + ee + 1], lincl[:, 127:128]))
        v_ltot = p.mark("vector", "V")
        p.wait("tensor", "V", v_ltot)

        def mm_lcar(E):
            E.matmul(ps_o[0:16, 0:8], c_tri[0:16, 0:16], ltotb[:, 0:NE],
                     start=True, stop=True)
            E.matmul(ps_o[0:16, 8:16], c_tri[0:16, 0:16], ltotb[:, NE:2 * NE],
                     start=True, stop=False)
            return E.matmul(ps_o[0:16, 8:16], c_one[0:16, 0:16], ltotb[:, 0:NE],
                            start=False, stop=True)
        t_lcar = p.op_inc("tensor", "T", mm_lcar)
        p.wait("vector", "T", t_lcar)
        p.op("vector", lambda E: E.tensor_copy(lcarr[:, 0:NE], ps_o[0:16, 0:8]))
        p.op("vector", lambda E: E.tensor_copy(lcarr[:, NE:2 * NE], ps_o[0:16, 8:16]))
        for k in range(2):
            p.op("vector", lambda E, k=k: E.memset(lrsel[k][:, :], 0.0))
            for ee in range(NE):
                p.op("vector", lambda E, k=k, ee=ee: E.tensor_single_scalar(
                    lm[:, :], oloc[k][:, :], float(ee), op=ALU.is_equal))
                p.op("vector", lambda E: E.tensor_tensor_scan(
                    lincl[:, :], lm[:, :], zf[0:16, 0:128], 0.0,
                    op0=ALU.add, op1=ALU.add))
                p.op("vector", lambda E, k=k, ee=ee: E.scalar_tensor_tensor(
                    lpl[:, :], lincl[:, :], lcarr[:, k * NE + ee:k * NE + ee + 1],
                    lm[:, :], op0=ALU.add, op1=ALU.subtract))
                p.op("vector", lambda E: E.tensor_mul(lpl[:, :], lpl[:, :], lm[:, :]))
                p.op("vector", lambda E, k=k: E.tensor_add(lrsel[k][:, :],
                                                           lrsel[k][:, :], lpl[:, :]))
        v_lrsel = p.mark("vector", "V")
        p.wait("tensor", "V", v_lrsel)
        for k in range(2):
            t_tr = p.op_inc("tensor", "T", lambda E, k=k:
                            E.transpose(ps_a[:, 0:OTILE], lrsel[k][:, :],
                                        c_idf[0:16, 0:16]))
            p.wait("vector", "T", t_tr)
            p.op("vector", lambda E, k=k: E.tensor_copy(lrtt[k][:, :],
                                                        ps_a[:, 0:OTILE]))
            v_m = p.mark("vector", "V")
            p.wait("tensor", "V", v_m)
        v_lrtt = p.mark("vector", "V")

        dropf = A.take("dropf", [128, OTILE], F32)
        oflf = A.take("oflf", [128, OTILE], F32)
        basei = A.take("basei", [128, OTILE], F32)
        sidxf = A.take("sidxf", [128, OTILE], F32)
        tinv = A.take("tinv", [128, OTILE], F32)
        gtmp = A.take("gtmp", [128, OTILE], F32)

        p.op("vector", lambda E: E.tensor_scalar(
            tinv[:, :], c_tok[:, :], -1.0, float(T), op0=ALU.mult, op1=ALU.add))
        p.op("vector", lambda E: E.memset(tini[:, :], float(T)))
        for k in range(2):
            own = (idx1 if k == 0 else idx2)[:, 0:OTILE]
            p.op("vector", lambda E, k=k: E.tensor_single_scalar(
                dropf[:, :], rtt[k][:, 0:OTILE], float(CAP), op=ALU.is_ge))
            p.op("vector", lambda E, k=k: E.tensor_single_scalar(
                oflf[:, :], lrtt[k][:, :], float(CLOC), op=ALU.is_ge))
            p.op("vector", lambda E, k=k, own=own: E.scalar_tensor_tensor(
                basei[:, :], own, float(CLOC), lrtt[k][:, :],
                op0=ALU.mult, op1=ALU.add))
            p.op("vector", lambda E: E.scalar_tensor_tensor(
                sidxf[:, :], oflf[:, :], BIG, basei[:, :],
                op0=ALU.mult, op1=ALU.add))
            p.op("vector", lambda E, k=k: E.tensor_copy(sidxi[k][:, :], sidxf[:, :]))
            p.op("vector", lambda E: E.tensor_mul(gtmp[:, :], dropf[:, :], tinv[:, :]))
            p.op("vector", lambda E, k=k: E.tensor_add(svals[k][:, :],
                                                       gtmp[:, :], c_tok[:, :]))
            p.op("vector", lambda E: E.tensor_scalar(
                gtmp[:, :], basei[:, :], -1.0, float(NSLOT), op0=ALU.mult, op1=ALU.add))
            p.op("vector", lambda E: E.tensor_mul(gtmp[:, :], gtmp[:, :], oflf[:, :]))
            p.op("vector", lambda E: E.tensor_add(gtmp[:, :], gtmp[:, :], basei[:, :]))
            gi_sb = g0i if k == 0 else g1i
            p.op("vector", lambda E, gi_sb=gi_sb: E.tensor_copy(gi_sb[:, :], gtmp[:, :]))
        v_scati = p.mark("vector", "V")

        if stop_after == "scatidx":
            finish_early()
            return nc, dbg_out

        p.wait("gpsimd", "V", v_scati)
        lg_tini = p.dma("gpsimd", "LG",
                        slot_tab.rearrange("(p f) one -> p (f one)", p=128),
                        tini[:, :])
        p.wait("gpsimd", "LG", lg_tini)
        for k in range(2):
            for t in range(OTILE):
                p.iscatter("gpsimd", "LG", slot_tab[:, :],
                           svals[k][:, t:t + 1], sidxi[k][:, t:t + 1], NSLOT - 1)
        lg_scat = S["LG"].n
        p.wait("gpsimd", "LG", lg_scat)
        lg_tab = p.dma("gpsimd", "LG", tabf[:, :],
                       slot_tab.rearrange("(p f) one -> p (f one)", p=128))
        p.wait("vector", "LG", lg_tab)
        for i4 in range(4):
            p.op("vector", lambda E, i4=i4: E.tensor_copy(
                tabi4[i4][:, :], tabf[:, i4 * 14:(i4 + 1) * 14]))
        v_tabi = p.mark("vector", "V")

        if "tab" in dbg:
            od = dbgt("tab", [128, SCOLS])
            p.wait("scalar", "V", v_tabi)
            p.dma("scalar", "ST", od, tabf[:, :])
            od2 = dbgt("lrtt0", [128, OTILE])
            p.dma("scalar", "ST", od2, lrtt[0][:, :])
            od3 = dbgt("g0", [128, OTILE], I32)
            p.dma("scalar", "ST", od3, g0i[:, :])
            od4 = dbgt("g1", [128, OTILE], I32)
            p.dma("scalar", "ST", od4, g1i[:, :])

        if stop_after == "route":
            finish_early()
            return nc, dbg_out
        A.pop()
        A.pop()

        disp_st = [0] * SCOLS
        p.wait("gpsimd", "V", v_tabi)
        p.wait("gpsimd", "LG", st_own_last)
        for f in range(SCOLS):
            b = f % 2
            if f >= 2:
                p.wait("gpsimd", "LG", disp_st[f - 2])
            lg_g = p.igather("gpsimd", "LG", gxb[b][:, :], xn_pad[:, :],
                             tabi4[f // 14][:, f % 14:f % 14 + 1], T)
            p.wait("gpsimd", "LG", lg_g)
            disp_st[f] = p.dma(
                "gpsimd", "LG",
                exp_x.rearrange("(pp f) d -> pp f d", f=SCOLS)[:, f, :],
                gxb[b][:, :])
        lg_disp = S["LG"].n


        A.push()
        sq2 = A.take("sq2", [128, 2048], F32)
        pbuf = A.take("pbuf", [128, 2048], BF16)
        ptb = A.take("ptb", [128, 2048], BF16)
        ybuf = [A.take(f"ybuf{i}", [128, D], BF16) for i in range(2)]
        lcol = A.take("lcol", [128, 10], F32)

        p.op("vector", lambda E: E.memset(nrm[:, :], 0.0))
        v_nz = p.mark("vector", "V")
        a_norm_sq = []
        for which, tiles_, slot in (("q", qt_sb, 0), ("k", kt_sb, 1)):
            for it in range(8):
                p.wait("scalar", "V", v_qkv_done if which == "k" else v_q)
                p.wait("scalar", "T", t_g_done)
                p.wait("scalar", "V", v_scati)
                p.op("scalar", lambda E, tiles_=tiles_, it=it:
                     E.activation(sq2[:, :], tiles_[it][:, :], AF.Square))
                a_s = p.mark("scalar", "A")
                for tc in range(4):
                    p.wait("tensor", "A", a_s)
                    p.wait("tensor", "V", v_nz)
                    t_n = p.op_inc("tensor", "T", lambda E, tc=tc:
                                   E.matmul(ps_b[0:1, 0:512], c_oc[:, :],
                                            sq2[:, ts(tc, 512)], start=True, stop=True))
                    p.wait("vector", "T", t_n)
                    p.op("vector", lambda E: E.reduce_max(nrm[0:1, 2:3],
                                                          ps_b[0:1, 0:512], axis=AX.X))
                    p.op("vector", lambda E, slot=slot: E.tensor_tensor(
                        nrm[0:1, slot:slot + 1], nrm[0:1, slot:slot + 1],
                        nrm[0:1, 2:3], op=ALU.max))
                    v_nz = p.mark("vector", "V")
        p.op("vector", lambda E: E.tensor_mul(nrm[0:1, 3:4], nrm[0:1, 0:1],
                                              nrm[0:1, 1:2]))
        v_nn = p.mark("vector", "V")
        p.wait("scalar", "V", v_nn)
        p.op("scalar", lambda E: E.activation(nrm[0:1, 4:5], nrm[0:1, 3:4], AF.Sqrt))
        a_nn = p.mark("scalar", "A")
        p.wait("vector", "A", a_nn)
        p.op("vector", lambda E: E.tensor_scalar_mul(nrm[0:1, 5:6],
                                                     nrm[0:1, 4:5], -0.125))
        v_mh = p.mark("vector", "V")
        p.wait("tensor", "V", v_mh)
        t_mh = p.op_inc("tensor", "T", lambda E:
                        E.matmul(ps_b[:, 0:1], c_orf[0:1, :], nrm[0:1, 5:6],
                                 start=True, stop=True))
        p.wait("vector", "T", t_mh)
        p.op("vector", lambda E: E.tensor_copy(mhat[:, :], ps_b[:, 0:1]))
        v_mhat = p.mark("vector", "V")
        if "attn" in dbg:
            p.wait("scalar", "V", v_mhat)
            p.dma("scalar", "ST", dbgt("mhat", [128, 1]), mhat[:, :])
            p.dma("scalar", "ST", dbgt("nrm", [1, 8]), nrm[:, :])
        p.wait("scalar", "V", v_mhat)

        ymid_st = [0] * OTILE
        a_exp_prev = 0
        a_o_prev = 0
        t_pv_prev = 0
        v_pm_hist = []
        for qt in range(OTILE):
            L = (qt + 1) * 128
            nkb = L // 128
            yb = ybuf[qt % 2]
            if qt >= 2:
                p.wait("scalar", "LD", ymid_st[qt - 2])
            for h in range(HEADS):
                til, base = h // 2, (h % 2) * 64
                qsl = qt_sb[til][base:base + 64, ts(qt, 128)]
                lc = lcol[:, h % 4:h % 4 + 1]

                p.wait("tensor", "A", a_exp_prev)

                def mm_s(E, qsl=qsl, til=til, base=base, L=L):
                    for kb4 in range((L + 511) // 512):
                        w = min(512, L - kb4 * 512)
                        last = E.matmul(ps_s[:, kb4 * 512:kb4 * 512 + w],
                                        qsl, kt_sb[til][base:base + 64,
                                                        kb4 * 512:kb4 * 512 + w],
                                        start=True, stop=True)
                    return last
                t_s = p.op_inc("tensor", "T", mm_s)
                p.wait("scalar", "T", max(t_s, t_pv_prev))
                if len(v_pm_hist) >= 2:
                    p.wait("scalar", "V", v_pm_hist[-2])
                if L > 128:
                    p.op("scalar", lambda E, L=L, lc=lc: E.activation(
                        pbuf[:, 0:L - 128], ps_s[:, 0:L - 128], AF.Exp,
                        bias=mhat[:, :], scale=0.125, accum_out=lc))
                p.op("scalar", lambda E, L=L: E.activation(
                    pbuf[:, L - 128:L], ps_s[:, L - 128:L], AF.Exp,
                    bias=mhat[:, :], scale=0.125))
                a_exp = p.mark("scalar", "A")
                p.wait("vector", "A", a_exp)
                p.op("vector", lambda E, L=L: E.tensor_mul(
                    pbuf[:, L - 128:L], pbuf[:, L - 128:L], c_caus[:, :]))
                p.op("vector", lambda E, L=L: E.reduce_sum(
                    lcol[:, 8:9], pbuf[:, L - 128:L], axis=AX.X))
                if L > 128:
                    p.op("vector", lambda E, lc=lc: E.tensor_add(lc, lc, lcol[:, 8:9]))
                else:
                    p.op("vector", lambda E, lc=lc: E.tensor_copy(lc, lcol[:, 8:9]))
                rl = lcol[:, 4 + h % 4:5 + h % 4]
                p.op("vector", lambda E, lc=lc, rl=rl: E.reciprocal(rl, lc))
                v_pm = p.mark("vector", "V")
                v_pm_hist.append(v_pm)
                a_exp_prev = a_exp

                for kb in range(nkb):
                    p.wait("tensor", "V", v_pm)
                    if kb == 0:
                        p.wait("tensor", "A", a_o_prev)

                    def mm_t(E, kb=kb):
                        return E.transpose(ps_t[:, 0:128], pbuf[:, ts(kb, 128)],
                                           c_idb[:, :])
                    t_tr = p.op_inc("tensor", "T", mm_t)
                    p.wait("vector", "T", t_tr)
                    p.op("vector", lambda E, kb=kb:
                         E.tensor_copy(ptb[:, ts(kb, 128)], ps_t[:, 0:128]))
                    v_pt = p.mark("vector", "V")
                    p.wait("tensor", "V", v_pt)

                    def mm_pv(E, kb=kb, h=h, nkb=nkb):
                        return E.matmul(ps_o[:, 0:64], ptb[:, ts(kb, 128)],
                                        v_sb[kb][:, ts(h, 64)],
                                        start=(kb == 0), stop=(kb == nkb - 1))
                    t_pv = p.op_inc("tensor", "T", mm_pv)
                t_pv_prev = t_pv
                p.wait("scalar", "T", t_pv)
                p.wait("scalar", "V", v_pm)
                p.op("scalar", lambda E, yb=yb, h=h, rl=rl: E.activation(
                    yb[:, ts(h, 64)], ps_o[:, 0:64], AF.Copy,
                    scale=rl))
                a_o = p.mark("scalar", "A")
                a_o_prev = a_o
            p.wait("scalar", "A", a_o)
            ymid_st[qt] = p.dma("scalar", "LD", y_mid[ts(qt, 128), :], yb[:, :])
        ld_ymid = S["LD"].n
        t_attn_done = S["T"].n
        a_attn_done = S["A"].n
        v_attn_done = S["V"].n
        A.pop()
        A.pop()

        if "attn" in dbg:
            od = dbgt("attnY", [T, D])
            rb2 = A.take("at_rb", [128, D], BF16)
            rd2 = A.take("at_rd", [128, D], F32)
            for i in range(OTILE):
                p.wait("gpsimd", "LD", ld_ymid)
                lg = p.dma("gpsimd", "LG", rb2[:, :], y_mid[ts(i, 128), :])
                p.wait("vector", "LG", lg)
                p.op("vector", lambda E: E.tensor_copy(rd2[:, :], rb2[:, :]))
                vv = p.mark("vector", "V")
                p.wait("gpsimd", "V", vv)
                p.dma("gpsimd", "ST", od[ts(i, 128), :], rd2[:, :])

        if stop_after == "attn":
            finish_early()
            return nc, dbg_out

        attn_sb = A.take("attn_sb", [128, OTILE * D], BF16)

        A.push()
        ytb = A.take("ytb", [128, 8 * 2048], BF16)
        wo_sb = A.take("wo_sb", [128, 8 * 1024], BF16)
        p.wait("gpsimd", "T", t_attn_done)
        p.wait("gpsimd", "A", a_attn_done)
        p.wait("gpsimd", "V", v_attn_done)
        lg_wo = p.dma("gpsimd", "LG",
                      wo_sb[:, :].rearrange("p (dc i) -> p dc i", i=D),
                      wT["o"].rearrange("(dc p) i -> p dc i", p=128))
        p.wait("sync", "LD", ld_ymid)
        p.wait("sync", "T", t_attn_done)
        p.wait("sync", "LG", lg_disp)
        ldyT = 0
        for ic in range(8):
            ldyT = p.dmat("sync", "LD", ytb[:, ts(ic, 2048)],
                          y_mid[0:2048, ts(ic, 128)])
        p.wait("tensor", "LD", ldyT)
        p.wait("tensor", "LG", lg_wo)
        ev2 = []
        for t in range(OTILE):
            for jc in range(2):
                bank = ps_a if (t * 2 + jc) % 2 == 0 else ps_b
                if len(ev2) >= 2:
                    p.wait("tensor", "V", ev2[-2])

                def mm_yo(E, t=t, jc=jc, bank=bank):
                    E.matmul(bank[:, :], c_or[0:1, ts(t, 128)],
                             b_sb["o"][0:1, ts(jc, 512)], start=True, stop=False)
                    for ic in range(8):
                        last = E.matmul(
                            bank[:, :],
                            ytb[:, ic * 2048 + t * 128: ic * 2048 + (t + 1) * 128],
                            wo_sb[:, ic * 1024 + jc * 512: ic * 1024 + jc * 512 + 512],
                            start=False, stop=(ic == 7))
                    return last
                t_mm = p.op_inc("tensor", "T", mm_yo)
                p.wait("vector", "T", t_mm)
                p.op("vector", lambda E, t=t, jc=jc, bank=bank: E.tensor_copy(
                    attn_sb[:, t * D + jc * 512: t * D + jc * 512 + 512],
                    bank[:, :]))
                ev2.append(p.mark("vector", "V"))
        v_yproj = ev2[-1]
        t_yproj = S["T"].n
        if "yproj" in dbg:
            od = dbgt("attnout", [128, OTILE * D], BF16)
            p.wait("scalar", "V", v_yproj)
            p.dma("scalar", "ST", od, attn_sb[:, :])
            oy = dbgt("ymid", [T, D], BF16)
            for i in range(OTILE):
                p.dma("scalar", "ST", oy[ts(i, 128), :], y_mid[ts(i, 128), :])
        A.pop()
        if stop_after == "yproj":
            finish_early()
            return nc, dbg_out

        A.push()
        ht = A.take("ht", [128, 32 * CLOC], BF16)
        xtb = [A.take(f"xtb{i}", [128, 8 * CLOC], BF16) for i in range(2)]
        NSL = 4
        slab = [A.take(f"slab{i}", [128, D], BF16) for i in range(NSL)]
        NPC = 4
        pchk = [A.take(f"pchk{i}", [128, 512], BF16) for i in range(NPC)]
        eo = [A.take(f"eo{i}", [128, 512], BF16) for i in range(2)]

        slab_done = [None] * 32
        slab_consumed_T = [0] * NSL
        pchk_done = [None] * 64
        pchk_consumed_T = [0] * NPC
        exp_st = []
        t_h_done = [0] * NE
        t_out_done_prev = 0
        p.wait("tensor", "V", v_yproj)
        p.wait("sync", "T", t_yproj)
        p.wait("scalar", "T", t_yproj)
        p.wait("gpsimd", "T", t_yproj)
        p.wait("gpsimd", "LG", lg_disp)
        p.wait("sync", "LG", lg_disp)
        ldxT = [0] * NE
        a_h_prev = [0, 0]

        def issue_slab(ee_, fc):
            sbw = slab[fc % NSL]
            if slab_consumed_T[fc % NSL]:
                p.wait("gpsimd", "T", slab_consumed_T[fc % NSL])
            slab_done[fc] = ("LG", p.dma("gpsimd", "LG", sbw[:, :], cfc[ee_, fc]))

        def issue_pchk(ee_, step):
            dcp_, fc_ = divmod(step, 32)
            cb = pchk[step % NPC]
            if pchk_consumed_T[step % NPC]:
                p.wait("gpsimd", "T", pchk_consumed_T[step % NPC])
            pchk_done[step] = ("LG", p.dma("gpsimd", "LG", cb[:, :],
                                           cpj[ee_][ts(fc_, 128), ts(dcp_, 512)]))

        for fc in range(NSL):
            issue_slab(0, fc)
        for ee in range(NE):
            p.wait("sync", "T", t_h_done[max(ee - 1, 0)])
            for dc in range(8):
                ldxT[ee] = p.dmat("sync", "LD", xtb[ee % 2][:, ts(dc, CLOC)],
                                  exp_x[ts(ee, CLOC), ts(dc, 128)])
            for fc in range(32):
                sb = slab[fc % NSL]
                sem, val = slab_done[fc]
                p.wait("tensor", sem, val)
                if fc == 0:
                    p.wait("tensor", "LD", ldxT[ee])
                for sc in range(2):
                    w = 512 if sc == 0 else CLOC - 512
                    bank = ps_a if sc == 0 else ps_b
                    if a_h_prev[sc]:
                        p.wait("tensor", "A", a_h_prev[sc])

                    def mm_h(E, sb=sb, sc=sc, w=w, bank=bank, xt=xtb[ee % 2]):
                        for dc in range(8):
                            last = E.matmul(
                                bank[:, 0:w], sb[:, ts(dc, 128)],
                                xt[:, dc * CLOC + sc * 512: dc * CLOC + sc * 512 + w],
                                start=(dc == 0), stop=(dc == 7))
                        return last
                    t_h = p.op_inc("tensor", "T", mm_h)
                    p.wait("scalar", "T", t_h)
                    if fc == 0 and sc == 0:
                        p.wait("scalar", "T", t_out_done_prev)
                    p.op("scalar", lambda E, fc=fc, sc=sc, w=w, bank=bank:
                         E.activation(ht[:, fc * CLOC + sc * 512:
                                         fc * CLOC + sc * 512 + w],
                                      bank[:, 0:w], AF.Gelu))
                    a_h = p.mark("scalar", "A")
                    a_h_prev[sc] = a_h
                slab_consumed_T[fc % NSL] = S["T"].n
                if fc + NSL < 32:
                    issue_slab(ee, fc + NSL)
            t_h_done[ee] = S["T"].n
            a_ht_done = S["A"].n
            for st in range(NPC):
                issue_pchk(ee, st)
            if ee + 1 < NE:
                for fc in range(NSL):
                    issue_slab(ee + 1, fc)
            banks7 = [ps_s[:, ts(i, 512)] for i in range(4)] + \
                     [ps_a[:, :], ps_b[:, :], ps_o[:, :]]
            for dcp in range(2):
                for fc in range(32):
                    step = dcp * 32 + fc
                    cb = pchk[step % NPC]
                    sem, val = pchk_done[step]
                    p.wait("tensor", sem, val)
                    if fc == 0:
                        p.wait("tensor", "A", a_ht_done)
                        p.wait("tensor", "V")
                        if dcp == 0:
                            p.wait("tensor", "A", a_h_prev[0])
                            p.wait("tensor", "A", a_h_prev[1])

                    def mm_o(E, cb=cb, fc=fc):
                        for st7 in range(7):
                            last = E.matmul(
                                banks7[st7],
                                ht[:, fc * CLOC + st7 * 128: fc * CLOC + (st7 + 1) * 128],
                                cb[:, :], start=(fc == 0), stop=(fc == 31))
                        return last
                    t_o = p.op_inc("tensor", "T", mm_o)
                    pchk_consumed_T[step % NPC] = S["T"].n
                    if step + NPC < 64:
                        issue_pchk(ee, step + NPC)
                for st7 in range(7):
                    b2 = eo[st7 % 2]
                    if st7 >= 2:
                        p.wait("vector", "LD", exp_st[-2])
                    p.wait("vector", "T", t_o)
                    p.op("vector", lambda E, st7=st7, b2=b2:
                         E.tensor_copy(b2[:, :], banks7[st7]))
                    v_eo = p.mark("vector", "V")
                    p.wait("sync", "V", v_eo)
                    exp_st.append(p.dma(
                        "sync", "LD",
                        exp_out[ee * CLOC + st7 * 128: ee * CLOC + (st7 + 1) * 128,
                                ts(dcp, 512)],
                        b2[:, :]))
            t_out_done_prev = S["T"].n
        ld_expout = S["LD"].n
        if stop_after == "ffn":
            finish_early()
            return nc, dbg_out
        if "ffn" in dbg:
            od = dbgt("expout", [NSLOT + 1, D], BF16)
            ox = dbgt("expx", [NSLOT, D], BF16)
            p.wait("gpsimd", "LD", ld_expout)
            rb3 = A.take("ffn_rb", [128, D], BF16)
            for i in range(NSLOT // 128):
                lg = p.dma("gpsimd", "LG", rb3[:, :], exp_x[ts(i, 128), :])
                p.wait("gpsimd", "LG", lg)
                p.dma("gpsimd", "ST", ox[ts(i, 128), :], rb3[:, :])
                lg = p.dma("gpsimd", "LG", rb3[:, :], exp_out[ts(i, 128), :])
                p.wait("gpsimd", "LG", lg)
                p.dma("gpsimd", "ST", od[ts(i, 128), :], rb3[:, :])
        A.pop()

        A.push()
        cx = [A.take(f"cx{i}", [128, D], F32) for i in range(2)]
        mg0 = [A.take(f"mg0{i}", [128, D], BF16) for i in range(2)]
        mg1 = [A.take(f"mg1{i}", [128, D], BF16) for i in range(2)]
        ac1 = A.take("ac1", [128, D], F32)
        ac2 = A.take("ac2", [128, D], F32)
        ac3 = [A.take(f"ac3{i}", [128, D], F32) for i in range(2)]

        p.wait("gpsimd", "LD", ld_expout)
        p.wait("vector", "T")
        st_y = []
        v_comb = [0, 0]
        for t in range(OTILE):
            b = t % 2
            if t >= 2:
                p.wait("sync", "ST", st_y[t - 2])
                p.wait("sync", "V", v_comb[b])
                p.wait("gpsimd", "V", v_comb[b])
            ld_cx = p.dma("sync", "LD", cx[b][:, :], x_hbm[ts(t, 128), :])
            lg0 = p.igather("gpsimd", "LG", mg0[b][:, :], exp_out[:, :],
                            g0i[:, t:t + 1], NSLOT)
            lg1 = p.igather("gpsimd", "LG", mg1[b][:, :], exp_out[:, :],
                            g1i[:, t:t + 1], NSLOT)
            p.wait("vector", "LD", ld_cx)
            p.wait("vector", "LG", lg1)
            p.wait("vector", "V", v_yproj)
            p.op("vector", lambda E, t=t, b=b: E.tensor_add(
                ac1[:, :], cx[b][:, :], attn_sb[:, ts(t, D)]))
            p.op("vector", lambda E, t=t, b=b: E.scalar_tensor_tensor(
                ac2[:, :], mg0[b][:, :], p1[:, t:t + 1], ac1[:, :],
                op0=ALU.mult, op1=ALU.add))
            p.op("vector", lambda E, t=t, b=b: E.scalar_tensor_tensor(
                ac3[b][:, :], mg1[b][:, :], p2[:, t:t + 1], ac2[:, :],
                op0=ALU.mult, op1=ALU.add))
            v_comb[b] = p.mark("vector", "V")
            p.wait("sync", "V", v_comb[b])
            st_y.append(p.dma("sync", "ST", y_out[ts(t, 128), :], ac3[b][:, :]))
        p.wait("sync", "ST", S["ST"].n)
        A.pop()

        for eng, meth in (("sync", block.sync), ("gpsimd", block.gpsimd),
                          ("tensor", block.tensor), ("vector", block.vector),
                          ("scalar", block.scalar)):
            def runner(E, ops=p.q[eng]):
                for f in ops:
                    f(E)
            meth(runner)

    return nc, dbg_out



def make_consts(core):
    b = np.arange(128)
    true_b = (b + core * OTILE) % 128
    tri_rot = (true_b[:, None] < true_b[None, :]).astype(np.float32)
    return dict(
        tri_rot=tri_rot.astype(ml_dtypes.bfloat16),
        ones_bf=np.ones((128, 128), ml_dtypes.bfloat16),
        id_bf=np.eye(128, dtype=np.float32).astype(ml_dtypes.bfloat16),
        id_f=np.eye(128, dtype=np.float32),
        caus_bf=np.tril(np.ones((128, 128), np.float32)).astype(ml_dtypes.bfloat16),
        onesrow_bf=np.ones((1, 2048), ml_dtypes.bfloat16),
        onesrow_f=np.ones((1, 128), np.float32),
        onescol_f=np.ones((128, 1), np.float32),
        tokid_f=(np.arange(OTILE)[None, :] * 128
                 + np.arange(128)[:, None]).astype(np.float32),
    )


def make_inputs(L0_attn_x, L0_attn_Wq, L0_attn_bq, L0_attn_Wk, L0_attn_bk,
                L0_attn_Wv, L0_attn_bv, L0_attn_Wo, L0_attn_bo,
                L0_ffn_w_g, L0_ffn_c_fc, L0_ffn_c_proj):
    x = np.ascontiguousarray(np.asarray(L0_attn_x, np.float32).reshape(NT, D))
    shared = dict(
        wqT=np.ascontiguousarray(np.asarray(L0_attn_Wq, np.float32).T),
        wkT=np.ascontiguousarray(np.asarray(L0_attn_Wk, np.float32).T),
        wvT=np.ascontiguousarray(np.asarray(L0_attn_Wv, np.float32).T),
        woT=np.ascontiguousarray(np.asarray(L0_attn_Wo, np.float32).T),
        bq=np.asarray(L0_attn_bq, np.float32).reshape(1, D),
        bk=np.asarray(L0_attn_bk, np.float32).reshape(1, D),
        bv=np.asarray(L0_attn_bv, np.float32).reshape(1, D),
        bo=np.asarray(L0_attn_bo, np.float32).reshape(1, D),
        wg=np.asarray(L0_ffn_w_g, np.float32),
        cfc=np.ascontiguousarray(
            np.asarray(L0_ffn_c_fc, np.float32)
            .reshape(NE, 8, 128, 32, 128).transpose(0, 3, 2, 1, 4)
        ).reshape(NE, 32, 128, D),
        cpj=np.asarray(L0_ffn_c_proj, np.float32),
    )
    in_maps = []
    for c in range(8):
        m = dict(shared)
        m["x"] = np.ascontiguousarray(np.roll(x, -c * T, axis=0))
        m.update(make_consts(c))
        in_maps.append(m)
    return in_maps


def kernel(**inputs):
    in_maps = make_inputs(**inputs)
    nc, _ = build()
    res = run_bass_kernel_spmd(nc, in_maps, core_ids=list(range(8)))
    y = np.concatenate([res.results[c]["y"] for c in range(8)], axis=0)
    return y.reshape(8, T, D)


# version 10
# speedup vs baseline: 1.9687x; 1.5067x over previous
from contextlib import ExitStack

import numpy as np
import ml_dtypes

import concourse.bass as bass
import concourse.mybir as mybir
from concourse.bass_utils import run_bass_kernel_spmd

F32 = mybir.dt.float32
BF16 = mybir.dt.bfloat16
I32 = mybir.dt.int32
AF = mybir.ActivationFunctionType
ALU = mybir.AluOpType
AX = mybir.AxisListType
IOffs = bass.IndirectOffsetOnAxis

D = 1024
T = 2048
NT = 16384
NTILE = 128
OTILE = 16
HEADS = 16
DH = 64
NE = 8
DFF = 4096
CAP = 5120
CLOC = 896
NSLOT = NE * CLOC
SCOLS = NSLOT // 128
BIG = 1.0e9
SBIG = 1.0e4

ENGINES = ("sync", "gpsimd", "tensor", "vector", "scalar")


class Sem:
    def __init__(self, stack, nc, name):
        self.h = stack.enter_context(nc.semaphore(name))
        self.n = 0

    def bump(self, k):
        self.n += k
        return self.n


class Prog:

    def __init__(self, nc, stack):
        self.nc = nc
        self.q = {e: [] for e in ENGINES}
        self.sems = {name: Sem(stack, nc, name) for name in
                     ("LD", "LG", "T", "V", "A", "ST")}

    def wait(self, eng, sem, val=None):
        s = self.sems[sem]
        v = s.n if val is None else val
        assert v <= s.n, f"wait {sem}>={v} but only {s.n} emitted"
        if v > 0:
            self.q[eng].append(lambda E, h=s.h, v=v: E.wait_ge(h, v))

    def dma(self, eng, sem, out, in_, **kw):
        s = self.sems[sem]
        s.bump(16)
        self.q[eng].append(
            lambda E, o=out, i=in_, h=s.h, kw=kw:
            E.dma_start(out=o, in_=i, **kw).then_inc(h, 16))
        return s.n

    def dmat(self, eng, sem, out, in_):
        s = self.sems[sem]
        s.bump(16)
        self.q[eng].append(
            lambda E, o=out, i=in_, h=s.h:
            E.dma_start_transpose(o, i).then_inc(h, 16))
        return s.n

    def _bound_reg(self, E, bounds):
        if not hasattr(self, "_bregs"):
            self._bregs = {}
        if bounds not in self._bregs:
            r = E.alloc_register(f"bnd{bounds}")
            E.reg_mov(r, bounds)
            self._bregs[bounds] = r
        return self._bregs[bounds]

    def igather(self, eng, sem, out, in_, idx, bounds):
        s = self.sems[sem]
        s.bump(16)

        def f(E, o=out, i=in_, x=idx, b=bounds, h=s.h):
            return E.indirect_dma_start(
                out=o, out_offset=None, in_=i,
                in_offset=IOffs(ap=x, axis=0),
                bounds_check=self._bound_reg(E, b), oob_is_err=False
            ).then_inc(h, 16)
        self.q[eng].append(f)
        return s.n

    def iscatter(self, eng, sem, out, in_, idx, bounds):
        s = self.sems[sem]
        s.bump(16)

        def f(E, o=out, i=in_, x=idx, b=bounds, h=s.h):
            return E.indirect_dma_start(
                out=o, out_offset=IOffs(ap=x, axis=0), in_=i, in_offset=None,
                bounds_check=self._bound_reg(E, b), oob_is_err=False
            ).then_inc(h, 16)
        self.q[eng].append(f)
        return s.n

    def op(self, eng, fn):
        self.q[eng].append(fn)
        if eng == "vector":
            self.q[eng].append(lambda E: E.drain())

    def mark(self, eng, sem, k=1):
        s = self.sems[sem]
        v = s.bump(k)
        if eng in ("vector", "scalar"):
            self.q[eng].append(lambda E: E.drain())
        self.q[eng].append(lambda E, h=s.h, k=k: E.sem_inc(h, k))
        return v

    def op_inc(self, eng, sem, fn, k=1):
        s = self.sems[sem]
        v = s.bump(k)
        self.q[eng].append(lambda E, f=fn, h=s.h, k=k: f(E).then_inc(h, k))
        return v


def ts(i, n):
    return slice(i * n, (i + 1) * n)


_DT_SIZE = {F32: 4, BF16: 2, I32: 4}


class Arena:

    def __init__(self, nc):
        self.nc = nc
        self.stack = [nc.sbuf_base]
        self.peak = nc.sbuf_base
        self.n = 0

    def push(self):
        self.stack.append(self.stack[-1])

    def pop(self):
        self.stack.pop()

    def take(self, name, shape, dtype):
        off = (self.stack[-1] + 63) // 64 * 64
        pp = 1
        for d in shape[1:]:
            pp *= d
        pp *= _DT_SIZE[dtype]
        self.n += 1
        t = self.nc.alloc_sbuf_tensor_at(f"{name}_{self.n}", shape, dtype,
                                         offset=off)
        self.stack[-1] = off + pp
        self.peak = max(self.peak, self.stack[-1])
        assert self.stack[-1] <= self.nc.sbuf_top, \
            f"SBUF overflow at {name}: {self.stack[-1]} > {self.nc.sbuf_top}"
        return t


def build(dbg=(), stop_after=None):
    nc = bass.Bass("TRN2", target_bir_lowering=False, debug=False)
    dbg = set(dbg)

    x_hbm = nc.dram_tensor("x", [NT, D], F32, kind="ExternalInput").ap()
    wT = {k: nc.dram_tensor(f"w{k}T", [D, D], F32, kind="ExternalInput").ap()
          for k in "qkvo"}
    bias = {k: nc.dram_tensor(f"b{k}", [1, D], F32, kind="ExternalInput").ap()
            for k in "qkvo"}
    wg = nc.dram_tensor("wg", [D, NE], F32, kind="ExternalInput").ap()
    cfc = nc.dram_tensor("cfc", [NE, 32, 128, D], F32, kind="ExternalInput").ap()
    cpj = nc.dram_tensor("cpj", [NE, DFF, D], F32, kind="ExternalInput").ap()
    tri_hbm = nc.dram_tensor("tri_rot", [128, 128], BF16, kind="ExternalInput").ap()
    one_hbm = nc.dram_tensor("ones_bf", [128, 128], BF16, kind="ExternalInput").ap()
    idb_hbm = nc.dram_tensor("id_bf", [128, 128], BF16, kind="ExternalInput").ap()
    idf_hbm = nc.dram_tensor("id_f", [128, 128], F32, kind="ExternalInput").ap()
    caus_hbm = nc.dram_tensor("caus_bf", [128, 128], BF16, kind="ExternalInput").ap()
    orow_hbm = nc.dram_tensor("onesrow_bf", [1, 2048], BF16, kind="ExternalInput").ap()
    orowf_hbm = nc.dram_tensor("onesrow_f", [1, 128], F32, kind="ExternalInput").ap()
    ocol_hbm = nc.dram_tensor("onescol_f", [128, 1], F32, kind="ExternalInput").ap()
    tok_hbm = nc.dram_tensor("tokid_f", [128, OTILE], F32, kind="ExternalInput").ap()

    y_out = nc.dram_tensor("y", [T, D], F32, kind="ExternalOutput").ap()

    xn_pad = nc.dram_tensor("xn_pad", [T + 1, D], BF16).ap()
    exp_x = nc.dram_tensor("exp_x", [NSLOT, D], BF16).ap()
    exp_out = nc.dram_tensor("exp_out", [NSLOT + 1, D], BF16).ap()
    slot_tab = nc.dram_tensor("slot_tab", [NSLOT, 1], F32).ap()
    y_mid = nc.dram_tensor("y_mid", [T, D], BF16).ap()

    dbg_out = {}

    def dbgt(name, shape, dtype=F32):
        dbg_out[name] = nc.dram_tensor("dbg_" + name, shape, dtype,
                                       kind="ExternalOutput").ap()
        return dbg_out[name]

    with ExitStack() as top:
        e = top.enter_context
        p = Prog(nc, top)
        S = p.sems
        A = Arena(nc)

        c_tri = A.take("c_tri", [128, 128], BF16)
        c_one = A.take("c_one", [128, 128], BF16)
        c_idb = A.take("c_idb", [128, 128], BF16)
        c_idf = A.take("c_idf", [128, 128], F32)
        c_caus = A.take("c_caus", [128, 128], BF16)
        c_or = A.take("c_or", [1, 2048], BF16)
        c_orf = A.take("c_orf", [1, 128], F32)
        c_oc = A.take("c_oc", [128, 1], F32)
        c_tok = A.take("c_tok", [128, OTILE], F32)
        c_eps = A.take("c_eps", [128, 1], F32)
        zrow = A.take("zrow", [1, D], BF16)
        b_sb = {k: A.take(f"b_{k}", [1, D], BF16) for k in "qkvo"}
        wg_f = A.take("wg_f", [128, NE * 8], F32)
        wg_b = A.take("wg_b", [128, NE * 8], BF16)
        g_all = A.take("g_all", [128, NTILE * NE], F32)
        idx1 = A.take("idx1", [128, NTILE], F32)
        idx2 = A.take("idx2", [128, NTILE], F32)
        p1 = A.take("p1", [128, NTILE], F32)
        p2 = A.take("p2", [128, NTILE], F32)
        g0i = A.take("g0i", [128, OTILE], I32)
        g1i = A.take("g1i", [128, OTILE], I32)
        tabi4 = [A.take(f"tabi{i}", [128, 14], I32) for i in range(4)]
        mhat = A.take("mhat", [128, 1], F32)
        nrm = A.take("nrm", [1, 8], F32)

        ps_a = e(nc.psum_tensor("ps_a", [128, 512], F32))
        ps_b = e(nc.psum_tensor("ps_b", [128, 512], F32))
        ps_s = e(nc.psum_tensor("ps_s", [128, 2048], F32))
        ps_t = e(nc.psum_tensor("ps_t", [128, 512], BF16))
        ps_o = e(nc.psum_tensor("ps_o", [128, 512], F32))

        block = e(nc.Block())

        def finish_early():
            yz = g_all
            p.op("vector", lambda E: E.memset(yz[:, :], 0.0))
            vz = p.mark("vector", "V")
            p.wait("sync", "V", vz)
            p.wait("sync", "ST", S["ST"].n)
            for t in range(OTILE):
                p.dma("sync", "ST", y_out[ts(t, 128), :], yz[:, :])
            p.wait("sync", "ST", S["ST"].n)
            for eng, meth in (("sync", block.sync), ("gpsimd", block.gpsimd),
                              ("tensor", block.tensor), ("vector", block.vector),
                              ("scalar", block.scalar)):
                def runner(E, ops=p.q[eng]):
                    for f in ops:
                        f(E)
                meth(runner)

        for dst, src in ((c_tri, tri_hbm), (c_one, one_hbm), (c_idb, idb_hbm),
                         (c_idf, idf_hbm), (c_caus, caus_hbm), (c_or, orow_hbm),
                         (c_orf, orowf_hbm), (c_oc, ocol_hbm), (c_tok, tok_hbm)):
            p.dma("sync", "LD", dst[:, :], src)
        p.dma("sync", "LD", wg_f[:, :].rearrange("p (dc e) -> p dc e", e=NE),
              wg.rearrange("(dc p) e -> p dc e", p=128))
        for k in "qkvo":
            p.dma("gpsimd", "LG", b_sb[k][:, :], bias[k])
        ld_const = S["LD"].n
        lg_bias = S["LG"].n
        p.op("vector", lambda E: E.memset(c_eps[:, :], 1e-5))
        p.op("vector", lambda E: E.memset(zrow[:, :], 0.0))
        p.wait("vector", "LD", ld_const)
        p.op("vector", lambda E: E.tensor_copy(wg_b[:, :], wg_f[:, :]))
        v_const = p.mark("vector", "V")
        p.wait("sync", "V", v_const)
        p.dma("sync", "LD", xn_pad[T:T + 1, :], zrow[:, :])
        p.dma("sync", "LD", exp_out[NSLOT:NSLOT + 1, :], zrow[:, :])

        A.push()
        qt_sb = [A.take(f"qt{i}", [128, T], BF16) for i in range(8)]
        kt_sb = [A.take(f"kt{i}", [128, T], BF16) for i in range(8)]
        vp = [A.take(f"vp{i}", [128, HEADS * 65], BF16) for i in range(OTILE)]
        xnt = A.take("xnt", [128, 8 * 2048], BF16)
        wbuf = A.take("wbuf", [128, 8 * 1024], BF16)
        gxb = [A.take(f"gxb{i}", [128, D], BF16) for i in range(2)]
        sidxi = [A.take(f"sidxi{k}", [128, OTILE], I32) for k in range(2)]
        svals = [A.take(f"svals{k}", [128, OTILE], F32) for k in range(2)]
        tabf = A.take("tabf", [128, SCOLS], F32)
        tini = A.take("tini", [128, SCOLS], F32)

        lg_w = {}
        lg_w["q"] = p.dma("gpsimd", "LG",
                          wbuf[:, :].rearrange("p (dc i) -> p dc i", i=D),
                          wT["q"].rearrange("(dc p) i -> p dc i", p=128))
        A.push()
        NXB = 4
        xb = [A.take(f"xb{i}", [128, D], F32) for i in range(NXB)]
        xnb = [A.take(f"xnb{i}", [128, D], BF16) for i in range(NXB)]
        sq1 = A.take("sq1", [128, D], F32)
        colr = A.take("colr", [128, 8 * NXB], F32)
        xTb = [A.take(f"xTb{i}", [128, D], BF16) for i in range(2)]

        ps_t3 = ps_t[:, :].rearrange("p (dc t) -> p dc t", t=128)
        ps_o3 = ps_o[:, :].rearrange("p (dc t) -> p dc t", t=128)
        st_own = [0] * OTILE
        v_lastread = [0] * NXB
        a_lastread = [0] * NXB
        t_xnb_read = [0] * NXB
        g_xnb_read = [0] * NXB
        a_evA_prev = 0
        v_evB_prev = 0
        v_g_prev = 0
        v_chunk0 = 0
        for i in range(NTILE):
            b = i % NXB
            own = i < OTILE
            if v_lastread[b]:
                p.wait("sync", "V", v_lastread[b])
                p.wait("sync", "A", a_lastread[b])
            ldx = p.dma("sync", "LD", xb[b][:, :], x_hbm[ts(i, 128), :])
            cS = colr[:, b * 8 + 0:b * 8 + 1]
            cM = colr[:, b * 8 + 1:b * 8 + 2]
            cMS = colr[:, b * 8 + 2:b * 8 + 3]
            cV = colr[:, b * 8 + 3:b * 8 + 4]
            cR = colr[:, b * 8 + 4:b * 8 + 5]
            cSum = colr[:, b * 8 + 5:b * 8 + 6]
            p.wait("scalar", "LD", ldx)
            p.op("scalar", lambda E, b=b, cS=cS:
                 E.activation(sq1[:, :], xb[b][:, :], AF.Square, accum_out=cS))
            a_sq = p.mark("scalar", "A")
            p.wait("vector", "LD", ldx)
            p.op("vector", lambda E, b=b, cSum=cSum:
                 E.reduce_sum(cSum, xb[b][:, :], axis=AX.X))
            p.op("vector", lambda E, cSum=cSum, cM=cM:
                 E.tensor_scalar_mul(cM, cSum, 1.0 / D))
            p.op("vector", lambda E, cM=cM, cMS=cMS: E.tensor_mul(cMS, cM, cM))
            p.wait("vector", "A", a_sq)
            p.op("vector", lambda E, cS=cS, cMS=cMS, cV=cV:
                 E.scalar_tensor_tensor(cV, cS, 1.0 / D, cMS,
                                        op0=ALU.mult, op1=ALU.subtract))
            v_var = p.mark("vector", "V")
            p.wait("scalar", "V", v_var)
            p.op("scalar", lambda E, cV=cV, cMS=cMS:
                 E.activation(cMS, cV, AF.Sqrt, bias=c_eps[:, :]))
            a_sd = p.mark("scalar", "A")
            a_lastread[b] = a_sd
            p.wait("vector", "A", a_sd)
            p.op("vector", lambda E, cMS=cMS, cR=cR: E.reciprocal(cR, cMS))
            if t_xnb_read[b]:
                p.wait("vector", "T", t_xnb_read[b])
                p.wait("vector", "LG", g_xnb_read[b])
            p.op("vector", lambda E, b=b, cM=cM, cR=cR:
                 E.tensor_scalar(xnb[b][:, :], xb[b][:, :], cM, cR,
                                 op0=ALU.subtract, op1=ALU.mult))
            v_xnb = p.mark("vector", "V")
            v_lastread[b] = v_xnb
            if own:
                p.wait("gpsimd", "V", v_xnb)
                st_own[i] = p.dma("gpsimd", "LG", xn_pad[ts(i, 128), :],
                                  xnb[b][:, :])
                g_xnb_read[b] = st_own[i]
            if own:
                dst = xnt[:, :].rearrange("p (dc t) -> p dc t", t=2048)
                dstA = dst[:, 0:4, i * 128:(i + 1) * 128]
                dstB = dst[:, 4:8, i * 128:(i + 1) * 128]
            else:
                xv = xTb[i % 2]
                dstA = xv[:, 0:512].rearrange("p (dc t) -> p dc t", t=128)
                dstB = xv[:, 512:1024].rearrange("p (dc t) -> p dc t", t=128)
            p.wait("tensor", "V", v_xnb)
            if i == 0:
                p.wait("tensor", "LD", ld_const)
            if a_evA_prev:
                p.wait("tensor", "A", a_evA_prev)

            def trg(E, b, lo, dstp):
                for k2 in range(4):
                    if lo == 0:
                        last = E.transpose(
                            dstp[:, k2 * 128:(k2 + 1) * 128],
                            xnb[b][:, (lo + k2) * 128:(lo + k2 + 1) * 128],
                            c_idb[:, :])
                    else:
                        last = E.matmul(
                            dstp[:, k2 * 128:(k2 + 1) * 128],
                            xnb[b][:, (lo + k2) * 128:(lo + k2 + 1) * 128],
                            c_idb[:, :], start=True, stop=True)
                return last
            t_trA = p.op_inc("tensor", "T", lambda E, b=b: trg(E, b, 0, ps_t))
            if v_evB_prev:
                p.wait("tensor", "V", v_evB_prev)
            t_trB = p.op_inc("tensor", "T", lambda E, b=b: trg(E, b, 4, ps_o))
            p.wait("scalar", "T", t_trA)
            p.op("scalar", lambda E, dstA=dstA:
                 E.activation(dstA, ps_t3[:, 0:4, :], AF.Copy))
            a_evA = p.mark("scalar", "A")
            p.wait("vector", "T", t_trB)
            p.op("vector", lambda E, dstB=dstB:
                 E.tensor_copy(dstB, ps_o3[:, 0:4, :]))
            v_evB = p.mark("vector", "V")
            a_evA_prev, v_evB_prev = a_evA, v_evB
            if own and i == OTILE - 1:
                v_chunk0 = v_evB
            p.wait("tensor", "A", a_evA)
            p.wait("tensor", "V", v_evB)
            if i == 0:
                p.wait("tensor", "V", v_const)
            if v_g_prev:
                p.wait("tensor", "V", v_g_prev)
            if own:
                src, soff, sstr = xnt[:, :], i * 128, 2048
            else:
                src, soff, sstr = xTb[i % 2][:, :], 0, 128

            def mmg(E, src=src, soff=soff, sstr=sstr):
                for dc in range(8):
                    last = E.matmul(
                        ps_a[:, 0:8],
                        src[:, dc * sstr + soff: dc * sstr + soff + 128],
                        wg_b[:, dc * 8:(dc + 1) * 8],
                        start=(dc == 0), stop=(dc == 7))
                return last
            t_g = p.op_inc("tensor", "T", mmg)
            t_xnb_read[b] = t_g
            p.wait("vector", "T", t_g)
            p.op("vector", lambda E, i=i:
                 E.tensor_copy(g_all[:, i * 8:(i + 1) * 8], ps_a[:, 0:8]))
            v_g_prev = p.mark("vector", "V")
        st_own_last = st_own[OTILE - 1]
        v_g_done = v_g_prev
        t_g_done = S["T"].n
        v_p1 = S["V"].n
        a_p1 = S["A"].n

        if stop_after == "p1":
            finish_early()
            return nc, dbg_out
        A.pop()

        A.push()

        def proj_qk(dst_tiles, wkey, bias_row, t_sem_name):
            p.wait("tensor", "LG", lg_w[wkey])
            p.wait("tensor", "V", v_g_done)
            p.wait("tensor", "A", a_evA_prev)
            p.wait("tensor", "LG", lg_bias)
            last = 0
            evs = []
            for ic in range(8):
                for tc in range(4):
                    bank = ps_a if (ic * 4 + tc) % 2 == 0 else ps_b
                    if len(evs) >= 2:
                        p.wait("tensor", "V", evs[-2])

                    def mm(E, ic=ic, tc=tc, bank=bank, w=wkey, br=bias_row):
                        E.matmul(bank[:, :], br[0:1, ts(ic, 128)],
                                 c_or[0:1, ts(tc, 512)], start=True, stop=False)
                        for dc in range(8):
                            last = E.matmul(
                                bank[:, :],
                                wbuf[:, dc * 1024 + ic * 128: dc * 1024 + (ic + 1) * 128],
                                xnt[:, dc * 2048 + tc * 512: dc * 2048 + tc * 512 + 512],
                                start=False, stop=(dc == 7))
                        return last
                    t_mm = p.op_inc("tensor", "T", mm)
                    p.wait("vector", "T", t_mm)
                    p.op("vector", lambda E, ic=ic, tc=tc, bank=bank, dt=dst_tiles:
                         E.tensor_copy(dt[ic][:, ts(tc, 512)], bank[:, :]))
                    evs.append(p.mark("vector", "V"))
            return evs[-1]

        v_q = proj_qk(qt_sb, "q", b_sb["q"], "T")
        t_q_done = S["T"].n
        p.wait("gpsimd", "T", t_q_done)
        lg_w["k"] = p.dma("gpsimd", "LG",
                          wbuf[:, :].rearrange("p (dc i) -> p dc i", i=D),
                          wT["k"].rearrange("(dc p) i -> p dc i", p=128))
        v_k = proj_qk(kt_sb, "k", b_sb["k"], "T")
        t_k_done = S["T"].n
        p.wait("gpsimd", "T", t_k_done)
        lg_w["v"] = p.dma("gpsimd", "LG",
                          wbuf[:, :].rearrange("p (dc i) -> p dc i", i=D),
                          wT["v"].rearrange("(dc p) i -> p dc i", p=128))

        p.wait("tensor", "LG", lg_w["v"])
        for tt in range(OTILE):
            p.op("vector", lambda E, tt=tt: E.memset(
                vp[tt][:, :].rearrange("p (h c) -> p h c", c=65)[:, :, 64:65],
                1.0))
        v_vevs = []
        for tt in range(OTILE):
            for ic2 in range(2):
                bank = ps_a if (tt * 2 + ic2) % 2 == 0 else ps_b
                if len(v_vevs) >= 2:
                    p.wait("tensor", "V", v_vevs[-2])

                def mmv(E, tt=tt, ic2=ic2, bank=bank):
                    E.matmul(bank[:, :], c_or[0:1, ts(tt, 128)],
                             b_sb["v"][0:1, ts(ic2, 512)], start=True, stop=False)
                    for dc in range(8):
                        last = E.matmul(
                            bank[:, :],
                            xnt[:, dc * 2048 + tt * 128: dc * 2048 + (tt + 1) * 128],
                            wbuf[:, dc * 1024 + ic2 * 512: dc * 1024 + ic2 * 512 + 512],
                            start=False, stop=(dc == 7))
                    return last
                t_mm = p.op_inc("tensor", "T", mmv)
                p.wait("vector", "T", t_mm)
                p.op("vector", lambda E, tt=tt, ic2=ic2, bank=bank:
                     E.tensor_copy(
                         vp[tt][:, :].rearrange("p (h c) -> p h c", c=65)
                         [:, ic2 * 8:(ic2 + 1) * 8, 0:64],
                         bank[:, :].rearrange("p (h c) -> p h c", c=64)))
                v_vevs.append(p.mark("vector", "V"))
        v_qkv_done = v_vevs[-1]
        t_qkv_done = S["T"].n
        if "qkv" in dbg:
            oq = dbgt("qt", [1024, 2048], BF16)
            ok_ = dbgt("kt", [1024, 2048], BF16)
            ov = dbgt("v", [2048, HEADS * 65], BF16)
            p.wait("scalar", "V", v_qkv_done)
            for i in range(8):
                p.dma("scalar", "ST", oq[ts(i, 128), :], qt_sb[i][:, :])
                p.dma("scalar", "ST", ok_[ts(i, 128), :], kt_sb[i][:, :])
            for i in range(OTILE):
                p.dma("scalar", "ST", ov[ts(i, 128), :], vp[i][:, :])
        if stop_after == "qkv":
            finish_early()
            return nc, dbg_out

        if "g" in dbg:
            od = dbgt("g", [128, NTILE * NE])
            p.wait("scalar", "V", v_g_done)
            p.dma("scalar", "ST", od, g_all[:, :])

        A.push()
        r_v1 = A.take("r_v1", [128, NTILE], F32)
        r_v2 = A.take("r_v2", [128, NTILE], F32)
        r_eq = A.take("r_eq", [128, NTILE], F32)
        r_t = A.take("r_t", [128, NTILE], F32)
        r_big = A.take("r_big", [128, NTILE], F32)
        g2 = A.take("g2", [128, NTILE * NE], F32)
        zf = A.take("zf", [128, NTILE], F32)
        s12 = A.take("s12", [128, NTILE], F32)

        g3 = g_all[:, :].rearrange("p (t e) -> p t e", e=NE)
        g23 = g2[:, :].rearrange("p (t e) -> p t e", e=NE)
        p.op("vector", lambda E: E.memset(r_big[:, :], SBIG))
        p.op("vector", lambda E: E.memset(zf[:, :], 0.0))
        p.op("vector", lambda E: E.reduce_max(r_v1[:, :], g3, axis=AX.X))
        p.op("vector", lambda E: E.memset(idx1[:, :], SBIG))
        for ee in range(NE):
            p.op("vector", lambda E, ee=ee:
                 E.tensor_tensor(r_eq[:, :], g3[:, :, ee], r_v1[:, :], op=ALU.is_equal))
            p.op("vector", lambda E, ee=ee: E.scalar_tensor_tensor(
                r_t[:, :], r_eq[:, :], -(SBIG - ee), r_big[:, :],
                op0=ALU.mult, op1=ALU.add))
            p.op("vector", lambda E: E.tensor_tensor(idx1[:, :], idx1[:, :],
                                                     r_t[:, :], op=ALU.min))
        for ee in range(NE):
            p.op("vector", lambda E, ee=ee: E.tensor_single_scalar(
                r_eq[:, :], idx1[:, :], float(ee), op=ALU.is_equal))
            p.op("vector", lambda E, ee=ee: E.scalar_tensor_tensor(
                g23[:, :, ee], r_eq[:, :], -1e30, g3[:, :, ee],
                op0=ALU.mult, op1=ALU.add))
        p.op("vector", lambda E: E.reduce_max(r_v2[:, :], g23, axis=AX.X))
        p.op("vector", lambda E: E.memset(idx2[:, :], SBIG))
        for ee in range(NE):
            p.op("vector", lambda E, ee=ee:
                 E.tensor_tensor(r_eq[:, :], g23[:, :, ee], r_v2[:, :], op=ALU.is_equal))
            p.op("vector", lambda E, ee=ee: E.scalar_tensor_tensor(
                r_t[:, :], r_eq[:, :], -(SBIG - ee), r_big[:, :],
                op0=ALU.mult, op1=ALU.add))
            p.op("vector", lambda E: E.tensor_tensor(idx2[:, :], idx2[:, :],
                                                     r_t[:, :], op=ALU.min))
        p.op("vector", lambda E: E.tensor_sub(s12[:, :], r_v1[:, :], r_v2[:, :]))
        v_rt = p.mark("vector", "V")
        p.wait("scalar", "V", v_rt)
        p.op("scalar", lambda E: E.activation(p1[:, :], s12[:, :], AF.Sigmoid))
        p.op("scalar", lambda E: E.activation(p2[:, :], s12[:, :], AF.Sigmoid,
                                              scale=-1.0))
        a_probs = p.mark("scalar", "A")

        if "route" in dbg:
            for nm, buf in (("idx1", idx1), ("idx2", idx2), ("p1", p1), ("p2", p2)):
                od = dbgt(nm, [128, NTILE])
                p.wait("scalar", "A", a_probs)
                p.wait("scalar", "V", v_rt)
                p.dma("scalar", "ST", od, buf[:, :])

        idxT = [A.take(f"idxT{k}", [128, 128], F32) for k in range(2)]
        m_ke = A.take("m_ke", [128, 128], F32)
        incl = A.take("incl", [128, 128], F32)
        plane = A.take("plane", [128, 128], F32)
        rsel = [A.take(f"rsel{k}", [128, 128], F32) for k in range(2)]
        rtt = [A.take(f"rtt{k}", [128, 128], F32) for k in range(2)]
        totb = A.take("totb", [128, 2 * NE], BF16)
        carr = A.take("carr", [128, 2 * NE], F32)

        p.wait("tensor", "V", v_rt)
        for k, src in enumerate((idx1, idx2)):
            t_tr = p.op_inc("tensor", "T", lambda E, s=src:
                            E.transpose(ps_a[:, 0:128], s[:, :], c_idf[:, :]))
            p.wait("vector", "T", t_tr)
            p.op("vector", lambda E, k=k: E.tensor_copy(idxT[k][:, :], ps_a[:, 0:128]))
            v_m = p.mark("vector", "V")
            p.wait("tensor", "V", v_m)
        v_idxT = p.mark("vector", "V")

        for k in range(2):
            for ee in range(NE):
                p.op("vector", lambda E, k=k, ee=ee: E.tensor_single_scalar(
                    m_ke[:, :], idxT[k][:, :], float(ee), op=ALU.is_equal))
                p.op("vector", lambda E: E.tensor_tensor_scan(
                    incl[:, :], m_ke[:, :], zf[:, 0:128], 0.0,
                    op0=ALU.add, op1=ALU.add))
                p.op("vector", lambda E, k=k, ee=ee: E.tensor_copy(
                    totb[:, k * NE + ee:k * NE + ee + 1], incl[:, 127:128]))
        v_tot = p.mark("vector", "V")
        p.wait("tensor", "V", v_tot)

        def mm_car(E):
            E.matmul(ps_o[:, 0:8], c_tri[:, :], totb[:, 0:NE], start=True, stop=True)
            return E.matmul(ps_o[:, 8:16], c_tri[:, :], totb[:, NE:2 * NE],
                            start=True, stop=False)

        def mm_car2(E):
            return E.matmul(ps_o[:, 8:16], c_one[:, :], totb[:, 0:NE],
                            start=False, stop=True)
        t_car = p.op_inc("tensor", "T", lambda E: (mm_car(E), mm_car2(E))[1])
        p.wait("vector", "T", t_car)
        p.op("vector", lambda E: E.tensor_copy(carr[:, 0:NE], ps_o[:, 0:8]))
        p.op("vector", lambda E: E.tensor_copy(carr[:, NE:2 * NE], ps_o[:, 8:16]))
        for k in range(2):
            p.op("vector", lambda E, k=k: E.memset(rsel[k][:, :], 0.0))
            for ee in range(NE):
                p.op("vector", lambda E, k=k, ee=ee: E.tensor_single_scalar(
                    m_ke[:, :], idxT[k][:, :], float(ee), op=ALU.is_equal))
                p.op("vector", lambda E: E.tensor_tensor_scan(
                    incl[:, :], m_ke[:, :], zf[:, 0:128], 0.0,
                    op0=ALU.add, op1=ALU.add))
                p.op("vector", lambda E, k=k, ee=ee: E.scalar_tensor_tensor(
                    plane[:, :], incl[:, :], carr[:, k * NE + ee:k * NE + ee + 1],
                    m_ke[:, :], op0=ALU.add, op1=ALU.subtract))
                p.op("vector", lambda E: E.tensor_mul(plane[:, :], plane[:, :],
                                                      m_ke[:, :]))
                p.op("vector", lambda E, k=k: E.tensor_add(rsel[k][:, :],
                                                           rsel[k][:, :], plane[:, :]))
        v_rsel = p.mark("vector", "V")
        p.wait("tensor", "V", v_rsel)
        for k in range(2):
            t_tr = p.op_inc("tensor", "T", lambda E, k=k:
                            E.transpose(ps_a[:, 0:128], rsel[k][:, :], c_idf[:, :]))
            p.wait("vector", "T", t_tr)
            p.op("vector", lambda E, k=k: E.tensor_copy(rtt[k][:, :], ps_a[:, 0:128]))
            v_m = p.mark("vector", "V")
            p.wait("tensor", "V", v_m)
        v_rtt = p.mark("vector", "V")

        if "rank" in dbg:
            for k in range(2):
                od = dbgt(f"rtt{k}", [128, 128])
                p.wait("scalar", "V", v_rtt)
                p.dma("scalar", "ST", od, rtt[k][:, :])
            p.dma("scalar", "ST", dbgt("rsel0", [128, 128]), rsel[0][:, :])
            p.dma("scalar", "ST", dbgt("idxT0", [128, 128]), idxT[0][:, :])
            p.dma("scalar", "ST", dbgt("carr", [128, 16]), carr[:, :])

        if stop_after == "rank":
            finish_early()
            return nc, dbg_out

        oloc = [A.take(f"oloc{k}", [16, 128], F32) for k in range(2)]
        lincl = A.take("lincl", [16, 128], F32)
        lm = A.take("lm", [16, 128], F32)
        lpl = A.take("lpl", [16, 128], F32)
        lrsel = [A.take(f"lrsel{k}", [16, 128], F32) for k in range(2)]
        lrtt = [A.take(f"lrtt{k}", [128, OTILE], F32) for k in range(2)]
        ltotb = A.take("ltotb", [16, 2 * NE], BF16)
        lcarr = A.take("lcarr", [16, 2 * NE], F32)

        for k, src in enumerate((idx1, idx2)):
            t_tr = p.op_inc("tensor", "T", lambda E, s=src:
                            E.transpose(ps_a[0:16, 0:128], s[:, 0:OTILE],
                                        c_idf[:, :]))
            p.wait("vector", "T", t_tr)
            p.op("vector", lambda E, k=k: E.tensor_copy(oloc[k][:, :], ps_a[0:16, 0:128]))
            v_m = p.mark("vector", "V")
            p.wait("tensor", "V", v_m)
        v_oloc = p.mark("vector", "V")
        for k in range(2):
            for ee in range(NE):
                p.op("vector", lambda E, k=k, ee=ee: E.tensor_single_scalar(
                    lm[:, :], oloc[k][:, :], float(ee), op=ALU.is_equal))
                p.op("vector", lambda E: E.tensor_tensor_scan(
                    lincl[:, :], lm[:, :], zf[0:16, 0:128], 0.0,
                    op0=ALU.add, op1=ALU.add))
                p.op("vector", lambda E, k=k, ee=ee: E.tensor_copy(
                    ltotb[:, k * NE + ee:k * NE + ee + 1], lincl[:, 127:128]))
        v_ltot = p.mark("vector", "V")
        p.wait("tensor", "V", v_ltot)

        def mm_lcar(E):
            E.matmul(ps_o[0:16, 0:8], c_tri[0:16, 0:16], ltotb[:, 0:NE],
                     start=True, stop=True)
            E.matmul(ps_o[0:16, 8:16], c_tri[0:16, 0:16], ltotb[:, NE:2 * NE],
                     start=True, stop=False)
            return E.matmul(ps_o[0:16, 8:16], c_one[0:16, 0:16], ltotb[:, 0:NE],
                            start=False, stop=True)
        t_lcar = p.op_inc("tensor", "T", mm_lcar)
        p.wait("vector", "T", t_lcar)
        p.op("vector", lambda E: E.tensor_copy(lcarr[:, 0:NE], ps_o[0:16, 0:8]))
        p.op("vector", lambda E: E.tensor_copy(lcarr[:, NE:2 * NE], ps_o[0:16, 8:16]))
        for k in range(2):
            p.op("vector", lambda E, k=k: E.memset(lrsel[k][:, :], 0.0))
            for ee in range(NE):
                p.op("vector", lambda E, k=k, ee=ee: E.tensor_single_scalar(
                    lm[:, :], oloc[k][:, :], float(ee), op=ALU.is_equal))
                p.op("vector", lambda E: E.tensor_tensor_scan(
                    lincl[:, :], lm[:, :], zf[0:16, 0:128], 0.0,
                    op0=ALU.add, op1=ALU.add))
                p.op("vector", lambda E, k=k, ee=ee: E.scalar_tensor_tensor(
                    lpl[:, :], lincl[:, :], lcarr[:, k * NE + ee:k * NE + ee + 1],
                    lm[:, :], op0=ALU.add, op1=ALU.subtract))
                p.op("vector", lambda E: E.tensor_mul(lpl[:, :], lpl[:, :], lm[:, :]))
                p.op("vector", lambda E, k=k: E.tensor_add(lrsel[k][:, :],
                                                           lrsel[k][:, :], lpl[:, :]))
        v_lrsel = p.mark("vector", "V")
        p.wait("tensor", "V", v_lrsel)
        for k in range(2):
            t_tr = p.op_inc("tensor", "T", lambda E, k=k:
                            E.transpose(ps_a[:, 0:OTILE], lrsel[k][:, :],
                                        c_idf[0:16, 0:16]))
            p.wait("vector", "T", t_tr)
            p.op("vector", lambda E, k=k: E.tensor_copy(lrtt[k][:, :],
                                                        ps_a[:, 0:OTILE]))
            v_m = p.mark("vector", "V")
            p.wait("tensor", "V", v_m)
        v_lrtt = p.mark("vector", "V")

        dropf = A.take("dropf", [128, OTILE], F32)
        oflf = A.take("oflf", [128, OTILE], F32)
        basei = A.take("basei", [128, OTILE], F32)
        sidxf = A.take("sidxf", [128, OTILE], F32)
        tinv = A.take("tinv", [128, OTILE], F32)
        gtmp = A.take("gtmp", [128, OTILE], F32)

        p.op("vector", lambda E: E.tensor_scalar(
            tinv[:, :], c_tok[:, :], -1.0, float(T), op0=ALU.mult, op1=ALU.add))
        p.op("vector", lambda E: E.memset(tini[:, :], float(T)))
        for k in range(2):
            own = (idx1 if k == 0 else idx2)[:, 0:OTILE]
            p.op("vector", lambda E, k=k: E.tensor_single_scalar(
                dropf[:, :], rtt[k][:, 0:OTILE], float(CAP), op=ALU.is_ge))
            p.op("vector", lambda E, k=k: E.tensor_single_scalar(
                oflf[:, :], lrtt[k][:, :], float(CLOC), op=ALU.is_ge))
            p.op("vector", lambda E, k=k, own=own: E.scalar_tensor_tensor(
                basei[:, :], own, float(CLOC), lrtt[k][:, :],
                op0=ALU.mult, op1=ALU.add))
            p.op("vector", lambda E: E.scalar_tensor_tensor(
                sidxf[:, :], oflf[:, :], BIG, basei[:, :],
                op0=ALU.mult, op1=ALU.add))
            p.op("vector", lambda E, k=k: E.tensor_copy(sidxi[k][:, :], sidxf[:, :]))
            p.op("vector", lambda E: E.tensor_mul(gtmp[:, :], dropf[:, :], tinv[:, :]))
            p.op("vector", lambda E, k=k: E.tensor_add(svals[k][:, :],
                                                       gtmp[:, :], c_tok[:, :]))
            p.op("vector", lambda E: E.tensor_scalar(
                gtmp[:, :], basei[:, :], -1.0, float(NSLOT), op0=ALU.mult, op1=ALU.add))
            p.op("vector", lambda E: E.tensor_mul(gtmp[:, :], gtmp[:, :], oflf[:, :]))
            p.op("vector", lambda E: E.tensor_add(gtmp[:, :], gtmp[:, :], basei[:, :]))
            gi_sb = g0i if k == 0 else g1i
            p.op("vector", lambda E, gi_sb=gi_sb: E.tensor_copy(gi_sb[:, :], gtmp[:, :]))
        v_scati = p.mark("vector", "V")

        if stop_after == "scatidx":
            finish_early()
            return nc, dbg_out

        p.wait("gpsimd", "V", v_scati)
        lg_tini = p.dma("gpsimd", "LG",
                        slot_tab.rearrange("(p f) one -> p (f one)", p=128),
                        tini[:, :])
        p.wait("gpsimd", "LG", lg_tini)
        for k in range(2):
            for t in range(OTILE):
                p.iscatter("gpsimd", "LG", slot_tab[:, :],
                           svals[k][:, t:t + 1], sidxi[k][:, t:t + 1], NSLOT - 1)
        lg_scat = S["LG"].n
        p.wait("gpsimd", "LG", lg_scat)
        lg_tab = p.dma("gpsimd", "LG", tabf[:, :],
                       slot_tab.rearrange("(p f) one -> p (f one)", p=128))
        p.wait("vector", "LG", lg_tab)
        for i4 in range(4):
            p.op("vector", lambda E, i4=i4: E.tensor_copy(
                tabi4[i4][:, :], tabf[:, i4 * 14:(i4 + 1) * 14]))
        v_tabi = p.mark("vector", "V")

        if "tab" in dbg:
            od = dbgt("tab", [128, SCOLS])
            p.wait("scalar", "V", v_tabi)
            p.dma("scalar", "ST", od, tabf[:, :])
            od2 = dbgt("lrtt0", [128, OTILE])
            p.dma("scalar", "ST", od2, lrtt[0][:, :])
            od3 = dbgt("g0", [128, OTILE], I32)
            p.dma("scalar", "ST", od3, g0i[:, :])
            od4 = dbgt("g1", [128, OTILE], I32)
            p.dma("scalar", "ST", od4, g1i[:, :])

        if stop_after == "route":
            finish_early()
            return nc, dbg_out
        A.pop()
        A.pop()

        disp_st = [0] * SCOLS
        p.wait("gpsimd", "V", v_tabi)
        p.wait("gpsimd", "LG", st_own_last)
        for f in range(SCOLS):
            b = f % 2
            if f >= 2:
                p.wait("gpsimd", "LG", disp_st[f - 2])
            lg_g = p.igather("gpsimd", "LG", gxb[b][:, :], xn_pad[:, :],
                             tabi4[f // 14][:, f % 14:f % 14 + 1], T)
            p.wait("gpsimd", "LG", lg_g)
            disp_st[f] = p.dma(
                "gpsimd", "LG",
                exp_x.rearrange("(pp f) d -> pp f d", f=SCOLS)[:, f, :],
                gxb[b][:, :])
        lg_disp = S["LG"].n


        A.push()
        sq2 = A.take("sq2", [128, 2048], F32)
        pbuf = A.take("pbuf", [128, 2048], BF16)
        ybuf = [A.take(f"ybuf{i}", [128, D], BF16) for i in range(2)]
        lcol = A.take("lcol", [128, 10], F32)

        p.op("vector", lambda E: E.memset(nrm[:, :], 0.0))
        v_nz = p.mark("vector", "V")
        a_norm_sq = []
        for which, tiles_, slot in (("q", qt_sb, 0), ("k", kt_sb, 1)):
            for it in range(8):
                p.wait("scalar", "V", v_qkv_done if which == "k" else v_q)
                p.wait("scalar", "T", t_g_done)
                p.wait("scalar", "V", v_scati)
                p.op("scalar", lambda E, tiles_=tiles_, it=it:
                     E.activation(sq2[:, :], tiles_[it][:, :], AF.Square))
                a_s = p.mark("scalar", "A")
                for tc in range(4):
                    p.wait("tensor", "A", a_s)
                    p.wait("tensor", "V", v_nz)
                    t_n = p.op_inc("tensor", "T", lambda E, tc=tc:
                                   E.matmul(ps_b[0:1, 0:512], c_oc[:, :],
                                            sq2[:, ts(tc, 512)], start=True, stop=True))
                    p.wait("vector", "T", t_n)
                    p.op("vector", lambda E: E.reduce_max(nrm[0:1, 2:3],
                                                          ps_b[0:1, 0:512], axis=AX.X))
                    p.op("vector", lambda E, slot=slot: E.tensor_tensor(
                        nrm[0:1, slot:slot + 1], nrm[0:1, slot:slot + 1],
                        nrm[0:1, 2:3], op=ALU.max))
                    v_nz = p.mark("vector", "V")
        p.op("vector", lambda E: E.tensor_mul(nrm[0:1, 3:4], nrm[0:1, 0:1],
                                              nrm[0:1, 1:2]))
        v_nn = p.mark("vector", "V")
        p.wait("scalar", "V", v_nn)
        p.op("scalar", lambda E: E.activation(nrm[0:1, 4:5], nrm[0:1, 3:4], AF.Sqrt))
        a_nn = p.mark("scalar", "A")
        p.wait("vector", "A", a_nn)
        p.op("vector", lambda E: E.tensor_scalar_mul(nrm[0:1, 5:6],
                                                     nrm[0:1, 4:5], -0.125))
        v_mh = p.mark("vector", "V")
        p.wait("tensor", "V", v_mh)
        t_mh = p.op_inc("tensor", "T", lambda E:
                        E.matmul(ps_b[:, 0:1], c_orf[0:1, :], nrm[0:1, 5:6],
                                 start=True, stop=True))
        p.wait("vector", "T", t_mh)
        p.op("vector", lambda E: E.tensor_copy(mhat[:, :], ps_b[:, 0:1]))
        v_mhat = p.mark("vector", "V")
        if "attn" in dbg:
            p.wait("scalar", "V", v_mhat)
            p.dma("scalar", "ST", dbgt("mhat", [128, 1]), mhat[:, :])
            p.dma("scalar", "ST", dbgt("nrm", [1, 8]), nrm[:, :])
        p.wait("scalar", "V", v_mhat)

        ymid_st = [0] * OTILE
        a_exp_prev = 0
        a_o_prev = 0
        t_pv_prev = 0
        v_pm_hist = []
        for qt in range(OTILE):
            L = (qt + 1) * 128
            nkb = qt + 1
            yb = ybuf[qt % 2]
            if qt >= 2:
                p.wait("scalar", "LD", ymid_st[qt - 2])
            for h in range(HEADS):
                til, base = h // 2, (h % 2) * 64
                qsl = qt_sb[til][base:base + 64, ts(qt, 128)]
                rl = lcol[:, 4 + h % 4:5 + h % 4]

                p.wait("tensor", "A", a_exp_prev)

                def mm_st(E, qsl=qsl, til=til, base=base, nkb=nkb):
                    for kb in range(nkb):
                        last = E.matmul(
                            ps_s[:, kb * 128:(kb + 1) * 128],
                            kt_sb[til][base:base + 64, kb * 128:(kb + 1) * 128],
                            qsl, start=True, stop=True)
                    return last
                t_s = p.op_inc("tensor", "T", mm_st)
                p.wait("scalar", "T", max(t_s, t_pv_prev))
                if len(v_pm_hist) >= 2:
                    p.wait("scalar", "V", v_pm_hist[-2])
                p.op("scalar", lambda E, L=L: E.activation(
                    pbuf[:, 0:L], ps_s[:, 0:L], AF.Exp,
                    bias=mhat[:, :], scale=0.125))
                a_exp = p.mark("scalar", "A")
                a_exp_prev = a_exp
                p.wait("vector", "A", a_exp)
                p.op("vector", lambda E, L=L: E.tensor_mul(
                    pbuf[:, L - 128:L], pbuf[:, L - 128:L], c_caus[:, :]))
                v_pm = p.mark("vector", "V")
                v_pm_hist.append(v_pm)

                p.wait("tensor", "V", v_pm)
                p.wait("tensor", "A", a_o_prev)

                def mm_pv(E, h=h, nkb=nkb):
                    for kb in range(nkb):
                        last = E.matmul(ps_o[:, 0:65], pbuf[:, ts(kb, 128)],
                                        vp[kb][:, h * 65:(h + 1) * 65],
                                        start=(kb == 0), stop=(kb == nkb - 1))
                    return last
                t_pv = p.op_inc("tensor", "T", mm_pv)
                t_pv_prev = t_pv
                p.wait("vector", "T", t_pv)
                p.op("vector", lambda E, rl=rl: E.reciprocal(rl, ps_o[:, 64:65]))
                v_rl = p.mark("vector", "V")
                p.wait("scalar", "T", t_pv)
                p.wait("scalar", "V", v_rl)
                p.op("scalar", lambda E, yb=yb, h=h, rl=rl: E.activation(
                    yb[:, ts(h, 64)], ps_o[:, 0:64], AF.Copy,
                    scale=rl))
                a_o = p.mark("scalar", "A")
                a_o_prev = a_o
            p.wait("scalar", "A", a_o)
            ymid_st[qt] = p.dma("scalar", "LD", y_mid[ts(qt, 128), :], yb[:, :])
        ld_ymid = S["LD"].n
        t_attn_done = S["T"].n
        a_attn_done = S["A"].n
        v_attn_done = S["V"].n
        A.pop()
        A.pop()

        if "attn" in dbg:
            od = dbgt("attnY", [T, D])
            rb2 = A.take("at_rb", [128, D], BF16)
            rd2 = A.take("at_rd", [128, D], F32)
            for i in range(OTILE):
                p.wait("gpsimd", "LD", ld_ymid)
                lg = p.dma("gpsimd", "LG", rb2[:, :], y_mid[ts(i, 128), :])
                p.wait("vector", "LG", lg)
                p.op("vector", lambda E: E.tensor_copy(rd2[:, :], rb2[:, :]))
                vv = p.mark("vector", "V")
                p.wait("gpsimd", "V", vv)
                p.dma("gpsimd", "ST", od[ts(i, 128), :], rd2[:, :])

        if stop_after == "attn":
            finish_early()
            return nc, dbg_out

        attn_sb = A.take("attn_sb", [128, OTILE * D], BF16)

        A.push()
        ytb = A.take("ytb", [128, 8 * 2048], BF16)
        wo_sb = A.take("wo_sb", [128, 8 * 1024], BF16)
        p.wait("gpsimd", "T", t_attn_done)
        p.wait("gpsimd", "A", a_attn_done)
        p.wait("gpsimd", "V", v_attn_done)
        lg_wo = p.dma("gpsimd", "LG",
                      wo_sb[:, :].rearrange("p (dc i) -> p dc i", i=D),
                      wT["o"].rearrange("(dc p) i -> p dc i", p=128))
        p.wait("sync", "LD", ld_ymid)
        p.wait("sync", "T", t_attn_done)
        p.wait("sync", "LG", lg_disp)
        ldyT = 0
        for ic in range(8):
            ldyT = p.dmat("sync", "LD", ytb[:, ts(ic, 2048)],
                          y_mid[0:2048, ts(ic, 128)])
        p.wait("tensor", "LD", ldyT)
        p.wait("tensor", "LG", lg_wo)
        ev2 = []
        for t in range(OTILE):
            for jc in range(2):
                bank = ps_a if (t * 2 + jc) % 2 == 0 else ps_b
                if len(ev2) >= 2:
                    p.wait("tensor", "V", ev2[-2])

                def mm_yo(E, t=t, jc=jc, bank=bank):
                    E.matmul(bank[:, :], c_or[0:1, ts(t, 128)],
                             b_sb["o"][0:1, ts(jc, 512)], start=True, stop=False)
                    for ic in range(8):
                        last = E.matmul(
                            bank[:, :],
                            ytb[:, ic * 2048 + t * 128: ic * 2048 + (t + 1) * 128],
                            wo_sb[:, ic * 1024 + jc * 512: ic * 1024 + jc * 512 + 512],
                            start=False, stop=(ic == 7))
                    return last
                t_mm = p.op_inc("tensor", "T", mm_yo)
                p.wait("vector", "T", t_mm)
                p.op("vector", lambda E, t=t, jc=jc, bank=bank: E.tensor_copy(
                    attn_sb[:, t * D + jc * 512: t * D + jc * 512 + 512],
                    bank[:, :]))
                ev2.append(p.mark("vector", "V"))
        v_yproj = ev2[-1]
        t_yproj = S["T"].n
        if "yproj" in dbg:
            od = dbgt("attnout", [128, OTILE * D], BF16)
            p.wait("scalar", "V", v_yproj)
            p.dma("scalar", "ST", od, attn_sb[:, :])
            oy = dbgt("ymid", [T, D], BF16)
            for i in range(OTILE):
                p.dma("scalar", "ST", oy[ts(i, 128), :], y_mid[ts(i, 128), :])
        A.pop()
        if stop_after == "yproj":
            finish_early()
            return nc, dbg_out

        A.push()
        ht = A.take("ht", [128, 32 * CLOC], BF16)
        xtb = [A.take(f"xtb{i}", [128, 8 * CLOC], BF16) for i in range(2)]
        NSL = 4
        slab = [A.take(f"slab{i}", [128, D], BF16) for i in range(NSL)]
        NPC = 4
        pchk = [A.take(f"pchk{i}", [128, 512], BF16) for i in range(NPC)]
        eo = [A.take(f"eo{i}", [128, 512], BF16) for i in range(2)]

        slab_done = [None] * 32
        slab_consumed_T = [0] * NSL
        pchk_done = [None] * 64
        pchk_consumed_T = [0] * NPC
        exp_st = []
        t_h_done = [0] * NE
        t_out_done_prev = 0
        p.wait("tensor", "V", v_yproj)
        p.wait("sync", "T", t_yproj)
        p.wait("scalar", "T", t_yproj)
        p.wait("gpsimd", "T", t_yproj)
        p.wait("gpsimd", "LG", lg_disp)
        p.wait("sync", "LG", lg_disp)
        ldxT = [0] * NE
        a_h_prev = [0, 0]

        def issue_slab(ee_, fc):
            sbw = slab[fc % NSL]
            if slab_consumed_T[fc % NSL]:
                p.wait("gpsimd", "T", slab_consumed_T[fc % NSL])
            slab_done[fc] = ("LG", p.dma("gpsimd", "LG", sbw[:, :], cfc[ee_, fc]))

        def issue_pchk(ee_, step):
            dcp_, fc_ = divmod(step, 32)
            cb = pchk[step % NPC]
            if pchk_consumed_T[step % NPC]:
                p.wait("gpsimd", "T", pchk_consumed_T[step % NPC])
            pchk_done[step] = ("LG", p.dma("gpsimd", "LG", cb[:, :],
                                           cpj[ee_][ts(fc_, 128), ts(dcp_, 512)]))

        for fc in range(NSL):
            issue_slab(0, fc)
        for ee in range(NE):
            p.wait("sync", "T", t_h_done[max(ee - 1, 0)])
            for dc in range(8):
                ldxT[ee] = p.dmat("sync", "LD", xtb[ee % 2][:, ts(dc, CLOC)],
                                  exp_x[ts(ee, CLOC), ts(dc, 128)])
            for fc in range(32):
                sb = slab[fc % NSL]
                sem, val = slab_done[fc]
                p.wait("tensor", sem, val)
                if fc == 0:
                    p.wait("tensor", "LD", ldxT[ee])
                for sc in range(2):
                    w = 512 if sc == 0 else CLOC - 512
                    bank = ps_a if sc == 0 else ps_b
                    if a_h_prev[sc]:
                        p.wait("tensor", "A", a_h_prev[sc])

                    def mm_h(E, sb=sb, sc=sc, w=w, bank=bank, xt=xtb[ee % 2]):
                        for dc in range(8):
                            last = E.matmul(
                                bank[:, 0:w], sb[:, ts(dc, 128)],
                                xt[:, dc * CLOC + sc * 512: dc * CLOC + sc * 512 + w],
                                start=(dc == 0), stop=(dc == 7))
                        return last
                    t_h = p.op_inc("tensor", "T", mm_h)
                    p.wait("scalar", "T", t_h)
                    if fc == 0 and sc == 0:
                        p.wait("scalar", "T", t_out_done_prev)
                    p.op("scalar", lambda E, fc=fc, sc=sc, w=w, bank=bank:
                         E.activation(ht[:, fc * CLOC + sc * 512:
                                         fc * CLOC + sc * 512 + w],
                                      bank[:, 0:w], AF.Gelu))
                    a_h = p.mark("scalar", "A")
                    a_h_prev[sc] = a_h
                slab_consumed_T[fc % NSL] = S["T"].n
                if fc + NSL < 32:
                    issue_slab(ee, fc + NSL)
            t_h_done[ee] = S["T"].n
            a_ht_done = S["A"].n
            for st in range(NPC):
                issue_pchk(ee, st)
            if ee + 1 < NE:
                for fc in range(NSL):
                    issue_slab(ee + 1, fc)
            banks7 = [ps_s[:, ts(i, 512)] for i in range(4)] + \
                     [ps_a[:, :], ps_b[:, :], ps_o[:, :]]
            for dcp in range(2):
                for fc in range(32):
                    step = dcp * 32 + fc
                    cb = pchk[step % NPC]
                    sem, val = pchk_done[step]
                    p.wait("tensor", sem, val)
                    if fc == 0:
                        p.wait("tensor", "A", a_ht_done)
                        p.wait("tensor", "V")
                        if dcp == 0:
                            p.wait("tensor", "A", a_h_prev[0])
                            p.wait("tensor", "A", a_h_prev[1])

                    def mm_o(E, cb=cb, fc=fc):
                        for st7 in range(7):
                            last = E.matmul(
                                banks7[st7],
                                ht[:, fc * CLOC + st7 * 128: fc * CLOC + (st7 + 1) * 128],
                                cb[:, :], start=(fc == 0), stop=(fc == 31))
                        return last
                    t_o = p.op_inc("tensor", "T", mm_o)
                    pchk_consumed_T[step % NPC] = S["T"].n
                    if step + NPC < 64:
                        issue_pchk(ee, step + NPC)
                for st7 in range(7):
                    b2 = eo[st7 % 2]
                    if st7 >= 2:
                        p.wait("vector", "LD", exp_st[-2])
                    p.wait("vector", "T", t_o)
                    p.op("vector", lambda E, st7=st7, b2=b2:
                         E.tensor_copy(b2[:, :], banks7[st7]))
                    v_eo = p.mark("vector", "V")
                    p.wait("sync", "V", v_eo)
                    exp_st.append(p.dma(
                        "sync", "LD",
                        exp_out[ee * CLOC + st7 * 128: ee * CLOC + (st7 + 1) * 128,
                                ts(dcp, 512)],
                        b2[:, :]))
            t_out_done_prev = S["T"].n
        ld_expout = S["LD"].n
        if stop_after == "ffn":
            finish_early()
            return nc, dbg_out
        if "ffn" in dbg:
            od = dbgt("expout", [NSLOT + 1, D], BF16)
            ox = dbgt("expx", [NSLOT, D], BF16)
            p.wait("gpsimd", "LD", ld_expout)
            rb3 = A.take("ffn_rb", [128, D], BF16)
            for i in range(NSLOT // 128):
                lg = p.dma("gpsimd", "LG", rb3[:, :], exp_x[ts(i, 128), :])
                p.wait("gpsimd", "LG", lg)
                p.dma("gpsimd", "ST", ox[ts(i, 128), :], rb3[:, :])
                lg = p.dma("gpsimd", "LG", rb3[:, :], exp_out[ts(i, 128), :])
                p.wait("gpsimd", "LG", lg)
                p.dma("gpsimd", "ST", od[ts(i, 128), :], rb3[:, :])
        A.pop()

        A.push()
        cx = [A.take(f"cx{i}", [128, D], F32) for i in range(2)]
        mg0 = [A.take(f"mg0{i}", [128, D], BF16) for i in range(2)]
        mg1 = [A.take(f"mg1{i}", [128, D], BF16) for i in range(2)]
        ac1 = A.take("ac1", [128, D], F32)
        ac2 = A.take("ac2", [128, D], F32)
        ac3 = [A.take(f"ac3{i}", [128, D], F32) for i in range(2)]

        p.wait("gpsimd", "LD", ld_expout)
        p.wait("vector", "T")
        st_y = []
        v_comb = [0, 0]
        for t in range(OTILE):
            b = t % 2
            if t >= 2:
                p.wait("sync", "ST", st_y[t - 2])
                p.wait("sync", "V", v_comb[b])
                p.wait("gpsimd", "V", v_comb[b])
            ld_cx = p.dma("sync", "LD", cx[b][:, :], x_hbm[ts(t, 128), :])
            lg0 = p.igather("gpsimd", "LG", mg0[b][:, :], exp_out[:, :],
                            g0i[:, t:t + 1], NSLOT)
            lg1 = p.igather("gpsimd", "LG", mg1[b][:, :], exp_out[:, :],
                            g1i[:, t:t + 1], NSLOT)
            p.wait("vector", "LD", ld_cx)
            p.wait("vector", "LG", lg1)
            p.wait("vector", "V", v_yproj)
            p.op("vector", lambda E, t=t, b=b: E.tensor_add(
                ac1[:, :], cx[b][:, :], attn_sb[:, ts(t, D)]))
            p.op("vector", lambda E, t=t, b=b: E.scalar_tensor_tensor(
                ac2[:, :], mg0[b][:, :], p1[:, t:t + 1], ac1[:, :],
                op0=ALU.mult, op1=ALU.add))
            p.op("vector", lambda E, t=t, b=b: E.scalar_tensor_tensor(
                ac3[b][:, :], mg1[b][:, :], p2[:, t:t + 1], ac2[:, :],
                op0=ALU.mult, op1=ALU.add))
            v_comb[b] = p.mark("vector", "V")
            p.wait("sync", "V", v_comb[b])
            st_y.append(p.dma("sync", "ST", y_out[ts(t, 128), :], ac3[b][:, :]))
        p.wait("sync", "ST", S["ST"].n)
        A.pop()

        for eng, meth in (("sync", block.sync), ("gpsimd", block.gpsimd),
                          ("tensor", block.tensor), ("vector", block.vector),
                          ("scalar", block.scalar)):
            def runner(E, ops=p.q[eng]):
                for f in ops:
                    f(E)
            meth(runner)

    return nc, dbg_out



def make_consts(core):
    b = np.arange(128)
    true_b = (b + core * OTILE) % 128
    tri_rot = (true_b[:, None] < true_b[None, :]).astype(np.float32)
    return dict(
        tri_rot=tri_rot.astype(ml_dtypes.bfloat16),
        ones_bf=np.ones((128, 128), ml_dtypes.bfloat16),
        id_bf=np.eye(128, dtype=np.float32).astype(ml_dtypes.bfloat16),
        id_f=np.eye(128, dtype=np.float32),
        caus_bf=np.triu(np.ones((128, 128), np.float32)).astype(ml_dtypes.bfloat16),
        onesrow_bf=np.ones((1, 2048), ml_dtypes.bfloat16),
        onesrow_f=np.ones((1, 128), np.float32),
        onescol_f=np.ones((128, 1), np.float32),
        tokid_f=(np.arange(OTILE)[None, :] * 128
                 + np.arange(128)[:, None]).astype(np.float32),
    )


def make_inputs(L0_attn_x, L0_attn_Wq, L0_attn_bq, L0_attn_Wk, L0_attn_bk,
                L0_attn_Wv, L0_attn_bv, L0_attn_Wo, L0_attn_bo,
                L0_ffn_w_g, L0_ffn_c_fc, L0_ffn_c_proj):
    x = np.ascontiguousarray(np.asarray(L0_attn_x, np.float32).reshape(NT, D))
    shared = dict(
        wqT=np.ascontiguousarray(np.asarray(L0_attn_Wq, np.float32).T),
        wkT=np.ascontiguousarray(np.asarray(L0_attn_Wk, np.float32).T),
        wvT=np.ascontiguousarray(np.asarray(L0_attn_Wv, np.float32).T),
        woT=np.ascontiguousarray(np.asarray(L0_attn_Wo, np.float32).T),
        bq=np.asarray(L0_attn_bq, np.float32).reshape(1, D),
        bk=np.asarray(L0_attn_bk, np.float32).reshape(1, D),
        bv=np.asarray(L0_attn_bv, np.float32).reshape(1, D),
        bo=np.asarray(L0_attn_bo, np.float32).reshape(1, D),
        wg=np.asarray(L0_ffn_w_g, np.float32),
        cfc=np.ascontiguousarray(
            np.asarray(L0_ffn_c_fc, np.float32)
            .reshape(NE, 8, 128, 32, 128).transpose(0, 3, 2, 1, 4)
        ).reshape(NE, 32, 128, D),
        cpj=np.asarray(L0_ffn_c_proj, np.float32),
    )
    in_maps = []
    for c in range(8):
        m = dict(shared)
        m["x"] = np.ascontiguousarray(np.roll(x, -c * T, axis=0))
        m.update(make_consts(c))
        in_maps.append(m)
    return in_maps


def kernel(**inputs):
    in_maps = make_inputs(**inputs)
    nc, _ = build()
    res = run_bass_kernel_spmd(nc, in_maps, core_ids=list(range(8)))
    y = np.concatenate([res.results[c]["y"] for c in range(8)], axis=0)
    return y.reshape(8, T, D)


# version 11
# speedup vs baseline: 1.9692x; 1.0002x over previous
from contextlib import ExitStack

import numpy as np
import ml_dtypes

import concourse.bass as bass
import concourse.mybir as mybir
from concourse.bass_utils import run_bass_kernel_spmd

F32 = mybir.dt.float32
BF16 = mybir.dt.bfloat16
I32 = mybir.dt.int32
AF = mybir.ActivationFunctionType
ALU = mybir.AluOpType
AX = mybir.AxisListType
IOffs = bass.IndirectOffsetOnAxis

D = 1024
T = 2048
NT = 16384
NTILE = 128
OTILE = 16
HEADS = 16
DH = 64
NE = 8
DFF = 4096
CAP = 5120
CLOC = 896
NSLOT = NE * CLOC
SCOLS = NSLOT // 128
BIG = 1.0e9
SBIG = 1.0e4

ENGINES = ("sync", "gpsimd", "tensor", "vector", "scalar")


class Sem:
    def __init__(self, stack, nc, name):
        self.h = stack.enter_context(nc.semaphore(name))
        self.n = 0

    def bump(self, k):
        self.n += k
        return self.n


class Prog:

    def __init__(self, nc, stack):
        self.nc = nc
        self.q = {e: [] for e in ENGINES}
        self.sems = {name: Sem(stack, nc, name) for name in
                     ("LD", "LG", "T", "V", "A", "ST")}

    def wait(self, eng, sem, val=None):
        s = self.sems[sem]
        v = s.n if val is None else val
        assert v <= s.n, f"wait {sem}>={v} but only {s.n} emitted"
        if v > 0:
            self.q[eng].append(lambda E, h=s.h, v=v: E.wait_ge(h, v))

    def dma(self, eng, sem, out, in_, **kw):
        s = self.sems[sem]
        s.bump(16)
        self.q[eng].append(
            lambda E, o=out, i=in_, h=s.h, kw=kw:
            E.dma_start(out=o, in_=i, **kw).then_inc(h, 16))
        return s.n

    def dmat(self, eng, sem, out, in_):
        s = self.sems[sem]
        s.bump(16)
        self.q[eng].append(
            lambda E, o=out, i=in_, h=s.h:
            E.dma_start_transpose(o, i).then_inc(h, 16))
        return s.n

    def _bound_reg(self, E, bounds):
        if not hasattr(self, "_bregs"):
            self._bregs = {}
        if bounds not in self._bregs:
            r = E.alloc_register(f"bnd{bounds}")
            E.reg_mov(r, bounds)
            self._bregs[bounds] = r
        return self._bregs[bounds]

    def igather(self, eng, sem, out, in_, idx, bounds):
        s = self.sems[sem]
        s.bump(16)

        def f(E, o=out, i=in_, x=idx, b=bounds, h=s.h):
            return E.indirect_dma_start(
                out=o, out_offset=None, in_=i,
                in_offset=IOffs(ap=x, axis=0),
                bounds_check=self._bound_reg(E, b), oob_is_err=False
            ).then_inc(h, 16)
        self.q[eng].append(f)
        return s.n

    def iscatter(self, eng, sem, out, in_, idx, bounds):
        s = self.sems[sem]
        s.bump(16)

        def f(E, o=out, i=in_, x=idx, b=bounds, h=s.h):
            return E.indirect_dma_start(
                out=o, out_offset=IOffs(ap=x, axis=0), in_=i, in_offset=None,
                bounds_check=self._bound_reg(E, b), oob_is_err=False
            ).then_inc(h, 16)
        self.q[eng].append(f)
        return s.n

    def op(self, eng, fn):
        self.q[eng].append(fn)
        if eng == "vector":
            self.q[eng].append(lambda E: E.drain())

    def mark(self, eng, sem, k=1):
        s = self.sems[sem]
        v = s.bump(k)
        if eng in ("vector", "scalar"):
            self.q[eng].append(lambda E: E.drain())
        self.q[eng].append(lambda E, h=s.h, k=k: E.sem_inc(h, k))
        return v

    def op_inc(self, eng, sem, fn, k=1):
        s = self.sems[sem]
        v = s.bump(k)
        self.q[eng].append(lambda E, f=fn, h=s.h, k=k: f(E).then_inc(h, k))
        return v


def ts(i, n):
    return slice(i * n, (i + 1) * n)


_DT_SIZE = {F32: 4, BF16: 2, I32: 4}


class Arena:

    def __init__(self, nc):
        self.nc = nc
        self.stack = [nc.sbuf_base]
        self.peak = nc.sbuf_base
        self.n = 0

    def push(self):
        self.stack.append(self.stack[-1])

    def pop(self):
        self.stack.pop()

    def take(self, name, shape, dtype):
        off = (self.stack[-1] + 63) // 64 * 64
        pp = 1
        for d in shape[1:]:
            pp *= d
        pp *= _DT_SIZE[dtype]
        self.n += 1
        t = self.nc.alloc_sbuf_tensor_at(f"{name}_{self.n}", shape, dtype,
                                         offset=off)
        self.stack[-1] = off + pp
        self.peak = max(self.peak, self.stack[-1])
        assert self.stack[-1] <= self.nc.sbuf_top, \
            f"SBUF overflow at {name}: {self.stack[-1]} > {self.nc.sbuf_top}"
        return t


def build(dbg=(), stop_after=None):
    nc = bass.Bass("TRN2", target_bir_lowering=False, debug=False)
    dbg = set(dbg)

    x_hbm = nc.dram_tensor("x", [NT, D], F32, kind="ExternalInput").ap()
    wT = {k: nc.dram_tensor(f"w{k}T", [D, D], F32, kind="ExternalInput").ap()
          for k in "qkvo"}
    bias = {k: nc.dram_tensor(f"b{k}", [1, D], F32, kind="ExternalInput").ap()
            for k in "qkvo"}
    wg = nc.dram_tensor("wg", [D, NE], F32, kind="ExternalInput").ap()
    cfc = nc.dram_tensor("cfc", [NE, 32, 128, D], F32, kind="ExternalInput").ap()
    cpj = nc.dram_tensor("cpj", [NE, DFF, D], F32, kind="ExternalInput").ap()
    tri_hbm = nc.dram_tensor("tri_rot", [128, 128], BF16, kind="ExternalInput").ap()
    one_hbm = nc.dram_tensor("ones_bf", [128, 128], BF16, kind="ExternalInput").ap()
    idb_hbm = nc.dram_tensor("id_bf", [128, 128], BF16, kind="ExternalInput").ap()
    idf_hbm = nc.dram_tensor("id_f", [128, 128], F32, kind="ExternalInput").ap()
    caus_hbm = nc.dram_tensor("caus_bf", [128, 128], BF16, kind="ExternalInput").ap()
    orow_hbm = nc.dram_tensor("onesrow_bf", [1, 2048], BF16, kind="ExternalInput").ap()
    orowf_hbm = nc.dram_tensor("onesrow_f", [1, 128], F32, kind="ExternalInput").ap()
    ocol_hbm = nc.dram_tensor("onescol_f", [128, 1], F32, kind="ExternalInput").ap()
    tok_hbm = nc.dram_tensor("tokid_f", [128, OTILE], F32, kind="ExternalInput").ap()

    y_out = nc.dram_tensor("y", [T, D], F32, kind="ExternalOutput").ap()

    xn_pad = nc.dram_tensor("xn_pad", [T + 1, D], BF16).ap()
    exp_x = nc.dram_tensor("exp_x", [NSLOT, D], BF16).ap()
    exp_out = nc.dram_tensor("exp_out", [NSLOT + 1, D], BF16).ap()
    slot_tab = nc.dram_tensor("slot_tab", [NSLOT, 1], F32).ap()
    y_mid = nc.dram_tensor("y_mid", [T, D], BF16).ap()

    dbg_out = {}

    def dbgt(name, shape, dtype=F32):
        dbg_out[name] = nc.dram_tensor("dbg_" + name, shape, dtype,
                                       kind="ExternalOutput").ap()
        return dbg_out[name]

    with ExitStack() as top:
        e = top.enter_context
        p = Prog(nc, top)
        S = p.sems
        A = Arena(nc)

        c_tri = A.take("c_tri", [128, 128], BF16)
        c_one = A.take("c_one", [128, 128], BF16)
        c_idb = A.take("c_idb", [128, 128], BF16)
        c_idf = A.take("c_idf", [128, 128], F32)
        c_caus = A.take("c_caus", [128, 128], BF16)
        c_or = A.take("c_or", [1, 2048], BF16)
        c_orf = A.take("c_orf", [1, 128], F32)
        c_oc = A.take("c_oc", [128, 1], F32)
        c_tok = A.take("c_tok", [128, OTILE], F32)
        c_eps = A.take("c_eps", [128, 1], F32)
        zrow = A.take("zrow", [1, D], BF16)
        b_sb = {k: A.take(f"b_{k}", [1, D], BF16) for k in "qkvo"}
        wg_f = A.take("wg_f", [128, NE * 8], F32)
        wg_b = A.take("wg_b", [128, NE * 8], BF16)
        g_all = A.take("g_all", [128, NTILE * NE], F32)
        idx1 = A.take("idx1", [128, NTILE], F32)
        idx2 = A.take("idx2", [128, NTILE], F32)
        p1 = A.take("p1", [128, NTILE], F32)
        p2 = A.take("p2", [128, NTILE], F32)
        g0i = A.take("g0i", [128, OTILE], I32)
        g1i = A.take("g1i", [128, OTILE], I32)
        tabi4 = [A.take(f"tabi{i}", [128, 14], I32) for i in range(4)]
        mhat = A.take("mhat", [128, 1], F32)
        nrm = A.take("nrm", [1, 8], F32)

        ps_a = e(nc.psum_tensor("ps_a", [128, 512], F32))
        ps_b = e(nc.psum_tensor("ps_b", [128, 512], F32))
        ps_s = e(nc.psum_tensor("ps_s", [128, 2048], F32))
        ps_t = e(nc.psum_tensor("ps_t", [128, 512], BF16))
        ps_o = e(nc.psum_tensor("ps_o", [128, 512], F32))

        block = e(nc.Block())

        def finish_early():
            yz = g_all
            p.op("vector", lambda E: E.memset(yz[:, :], 0.0))
            vz = p.mark("vector", "V")
            p.wait("sync", "V", vz)
            p.wait("sync", "ST", S["ST"].n)
            for t in range(OTILE):
                p.dma("sync", "ST", y_out[ts(t, 128), :], yz[:, :])
            p.wait("sync", "ST", S["ST"].n)
            for eng, meth in (("sync", block.sync), ("gpsimd", block.gpsimd),
                              ("tensor", block.tensor), ("vector", block.vector),
                              ("scalar", block.scalar)):
                def runner(E, ops=p.q[eng]):
                    for f in ops:
                        f(E)
                meth(runner)

        for dst, src in ((c_tri, tri_hbm), (c_one, one_hbm), (c_idb, idb_hbm),
                         (c_idf, idf_hbm), (c_caus, caus_hbm), (c_or, orow_hbm),
                         (c_orf, orowf_hbm), (c_oc, ocol_hbm), (c_tok, tok_hbm)):
            p.dma("sync", "LD", dst[:, :], src)
        p.dma("sync", "LD", wg_f[:, :].rearrange("p (dc e) -> p dc e", e=NE),
              wg.rearrange("(dc p) e -> p dc e", p=128))
        for k in "qkvo":
            p.dma("gpsimd", "LG", b_sb[k][:, :], bias[k])
        ld_const = S["LD"].n
        lg_bias = S["LG"].n
        p.op("vector", lambda E: E.memset(c_eps[:, :], 1e-5))
        p.op("vector", lambda E: E.memset(zrow[:, :], 0.0))
        p.wait("vector", "LD", ld_const)
        p.op("vector", lambda E: E.tensor_copy(wg_b[:, :], wg_f[:, :]))
        v_const = p.mark("vector", "V")
        p.wait("sync", "V", v_const)
        p.dma("sync", "LD", xn_pad[T:T + 1, :], zrow[:, :])
        p.dma("sync", "LD", exp_out[NSLOT:NSLOT + 1, :], zrow[:, :])

        A.push()
        qt_sb = [A.take(f"qt{i}", [128, T], BF16) for i in range(8)]
        kt_sb = [A.take(f"kt{i}", [128, T], BF16) for i in range(8)]
        vp = [A.take(f"vp{i}", [128, HEADS * 65], BF16) for i in range(OTILE)]
        xnt = A.take("xnt", [128, 8 * 2048], BF16)
        wbuf = A.take("wbuf", [128, 8 * 1024], BF16)
        gxb = [A.take(f"gxb{i}", [128, D], BF16) for i in range(2)]
        sidxi = [A.take(f"sidxi{k}", [128, OTILE], I32) for k in range(2)]
        svals = [A.take(f"svals{k}", [128, OTILE], F32) for k in range(2)]
        tabf = A.take("tabf", [128, SCOLS], F32)
        tini = A.take("tini", [128, SCOLS], F32)

        lg_w = {}
        lg_w["q"] = p.dma("gpsimd", "LG",
                          wbuf[:, :].rearrange("p (dc i) -> p dc i", i=D),
                          wT["q"].rearrange("(dc p) i -> p dc i", p=128))
        A.push()
        NXB = 4
        xb = [A.take(f"xb{i}", [128, D], F32) for i in range(NXB)]
        xnb = [A.take(f"xnb{i}", [128, D], BF16) for i in range(NXB)]
        sq1 = A.take("sq1", [128, D], F32)
        colr = A.take("colr", [128, 8 * NXB], F32)
        xTb = [A.take(f"xTb{i}", [128, D], BF16) for i in range(2)]

        ps_t3 = ps_t[:, :].rearrange("p (dc t) -> p dc t", t=128)
        ps_o3 = ps_o[:, :].rearrange("p (dc t) -> p dc t", t=128)
        st_own = [0] * OTILE
        v_lastread = [0] * NXB
        a_lastread = [0] * NXB
        t_xnb_read = [0] * NXB
        g_xnb_read = [0] * NXB
        a_evA_prev = 0
        v_evB_prev = 0
        v_g_prev = 0
        v_chunk0 = 0
        for i in range(NTILE):
            b = i % NXB
            own = i < OTILE
            if v_lastread[b]:
                p.wait("sync", "V", v_lastread[b])
                p.wait("sync", "A", a_lastread[b])
            ldx = p.dma("sync", "LD", xb[b][:, :], x_hbm[ts(i, 128), :])
            cS = colr[:, b * 8 + 0:b * 8 + 1]
            cM = colr[:, b * 8 + 1:b * 8 + 2]
            cMS = colr[:, b * 8 + 2:b * 8 + 3]
            cV = colr[:, b * 8 + 3:b * 8 + 4]
            cR = colr[:, b * 8 + 4:b * 8 + 5]
            cSum = colr[:, b * 8 + 5:b * 8 + 6]
            p.wait("scalar", "LD", ldx)
            p.op("scalar", lambda E, b=b, cS=cS:
                 E.activation(sq1[:, :], xb[b][:, :], AF.Square, accum_out=cS))
            a_sq = p.mark("scalar", "A")
            p.wait("vector", "LD", ldx)
            p.op("vector", lambda E, b=b, cSum=cSum:
                 E.reduce_sum(cSum, xb[b][:, :], axis=AX.X))
            p.op("vector", lambda E, cSum=cSum, cM=cM:
                 E.tensor_scalar_mul(cM, cSum, 1.0 / D))
            p.op("vector", lambda E, cM=cM, cMS=cMS: E.tensor_mul(cMS, cM, cM))
            p.wait("vector", "A", a_sq)
            p.op("vector", lambda E, cS=cS, cMS=cMS, cV=cV:
                 E.scalar_tensor_tensor(cV, cS, 1.0 / D, cMS,
                                        op0=ALU.mult, op1=ALU.subtract))
            v_var = p.mark("vector", "V")
            p.wait("scalar", "V", v_var)
            p.op("scalar", lambda E, cV=cV, cMS=cMS:
                 E.activation(cMS, cV, AF.Sqrt, bias=c_eps[:, :]))
            a_sd = p.mark("scalar", "A")
            a_lastread[b] = a_sd
            p.wait("vector", "A", a_sd)
            p.op("vector", lambda E, cMS=cMS, cR=cR: E.reciprocal(cR, cMS))
            if t_xnb_read[b]:
                p.wait("vector", "T", t_xnb_read[b])
                p.wait("vector", "LG", g_xnb_read[b])
            p.op("vector", lambda E, b=b, cM=cM, cR=cR:
                 E.tensor_scalar(xnb[b][:, :], xb[b][:, :], cM, cR,
                                 op0=ALU.subtract, op1=ALU.mult))
            v_xnb = p.mark("vector", "V")
            v_lastread[b] = v_xnb
            if own:
                p.wait("gpsimd", "V", v_xnb)
                st_own[i] = p.dma("gpsimd", "LG", xn_pad[ts(i, 128), :],
                                  xnb[b][:, :])
                g_xnb_read[b] = st_own[i]
            if own:
                dst = xnt[:, :].rearrange("p (dc t) -> p dc t", t=2048)
                dstA = dst[:, 0:4, i * 128:(i + 1) * 128]
                dstB = dst[:, 4:8, i * 128:(i + 1) * 128]
            else:
                xv = xTb[i % 2]
                dstA = xv[:, 0:512].rearrange("p (dc t) -> p dc t", t=128)
                dstB = xv[:, 512:1024].rearrange("p (dc t) -> p dc t", t=128)
            p.wait("tensor", "V", v_xnb)
            if i == 0:
                p.wait("tensor", "LD", ld_const)
            if a_evA_prev:
                p.wait("tensor", "A", a_evA_prev)

            def trg(E, b, lo, dstp):
                for k2 in range(4):
                    if lo == 0:
                        last = E.transpose(
                            dstp[:, k2 * 128:(k2 + 1) * 128],
                            xnb[b][:, (lo + k2) * 128:(lo + k2 + 1) * 128],
                            c_idb[:, :])
                    else:
                        last = E.matmul(
                            dstp[:, k2 * 128:(k2 + 1) * 128],
                            xnb[b][:, (lo + k2) * 128:(lo + k2 + 1) * 128],
                            c_idb[:, :], start=True, stop=True)
                return last
            t_trA = p.op_inc("tensor", "T", lambda E, b=b: trg(E, b, 0, ps_t))
            if v_evB_prev:
                p.wait("tensor", "V", v_evB_prev)
            t_trB = p.op_inc("tensor", "T", lambda E, b=b: trg(E, b, 4, ps_o))
            p.wait("scalar", "T", t_trA)
            p.op("scalar", lambda E, dstA=dstA:
                 E.activation(dstA, ps_t3[:, 0:4, :], AF.Copy))
            a_evA = p.mark("scalar", "A")
            p.wait("vector", "T", t_trB)
            p.op("vector", lambda E, dstB=dstB:
                 E.tensor_copy(dstB, ps_o3[:, 0:4, :]))
            v_evB = p.mark("vector", "V")
            a_evA_prev, v_evB_prev = a_evA, v_evB
            if own and i == OTILE - 1:
                v_chunk0 = v_evB
            p.wait("tensor", "A", a_evA)
            p.wait("tensor", "V", v_evB)
            if i == 0:
                p.wait("tensor", "V", v_const)
            if v_g_prev:
                p.wait("tensor", "V", v_g_prev)
            if own:
                src, soff, sstr = xnt[:, :], i * 128, 2048
            else:
                src, soff, sstr = xTb[i % 2][:, :], 0, 128

            def mmg(E, src=src, soff=soff, sstr=sstr):
                for dc in range(8):
                    last = E.matmul(
                        ps_a[:, 0:8],
                        src[:, dc * sstr + soff: dc * sstr + soff + 128],
                        wg_b[:, dc * 8:(dc + 1) * 8],
                        start=(dc == 0), stop=(dc == 7))
                return last
            t_g = p.op_inc("tensor", "T", mmg)
            t_xnb_read[b] = t_g
            p.wait("vector", "T", t_g)
            p.op("vector", lambda E, i=i:
                 E.tensor_copy(g_all[:, i * 8:(i + 1) * 8], ps_a[:, 0:8]))
            v_g_prev = p.mark("vector", "V")
        st_own_last = st_own[OTILE - 1]
        v_g_done = v_g_prev
        t_g_done = S["T"].n
        v_p1 = S["V"].n
        a_p1 = S["A"].n

        if stop_after == "p1":
            finish_early()
            return nc, dbg_out
        A.pop()

        A.push()

        def proj_qk(dst_tiles, wkey, bias_row, t_sem_name):
            p.wait("tensor", "LG", lg_w[wkey])
            p.wait("tensor", "V", v_g_done)
            p.wait("tensor", "A", a_evA_prev)
            p.wait("tensor", "LG", lg_bias)
            last = 0
            evs = []
            for ic in range(8):
                for tc in range(4):
                    bank = ps_a if (ic * 4 + tc) % 2 == 0 else ps_b
                    if len(evs) >= 2:
                        p.wait("tensor", "V", evs[-2])

                    def mm(E, ic=ic, tc=tc, bank=bank, w=wkey, br=bias_row):
                        E.matmul(bank[:, :], br[0:1, ts(ic, 128)],
                                 c_or[0:1, ts(tc, 512)], start=True, stop=False)
                        for dc in range(8):
                            last = E.matmul(
                                bank[:, :],
                                wbuf[:, dc * 1024 + ic * 128: dc * 1024 + (ic + 1) * 128],
                                xnt[:, dc * 2048 + tc * 512: dc * 2048 + tc * 512 + 512],
                                start=False, stop=(dc == 7))
                        return last
                    t_mm = p.op_inc("tensor", "T", mm)
                    p.wait("vector", "T", t_mm)
                    p.op("vector", lambda E, ic=ic, tc=tc, bank=bank, dt=dst_tiles:
                         E.tensor_copy(dt[ic][:, ts(tc, 512)], bank[:, :]))
                    evs.append(p.mark("vector", "V"))
            return evs[-1]

        v_q = proj_qk(qt_sb, "q", b_sb["q"], "T")
        t_q_done = S["T"].n
        p.wait("gpsimd", "T", t_q_done)
        lg_w["k"] = p.dma("gpsimd", "LG",
                          wbuf[:, :].rearrange("p (dc i) -> p dc i", i=D),
                          wT["k"].rearrange("(dc p) i -> p dc i", p=128))
        v_k = proj_qk(kt_sb, "k", b_sb["k"], "T")
        t_k_done = S["T"].n
        p.wait("gpsimd", "T", t_k_done)
        lg_w["v"] = p.dma("gpsimd", "LG",
                          wbuf[:, :].rearrange("p (dc i) -> p dc i", i=D),
                          wT["v"].rearrange("(dc p) i -> p dc i", p=128))

        p.wait("tensor", "LG", lg_w["v"])
        for tt in range(OTILE):
            p.op("vector", lambda E, tt=tt: E.memset(
                vp[tt][:, :].rearrange("p (h c) -> p h c", c=65)[:, :, 64:65],
                1.0))
        v_vevs = []
        for tt in range(OTILE):
            for ic2 in range(2):
                bank = ps_a if (tt * 2 + ic2) % 2 == 0 else ps_b
                if len(v_vevs) >= 2:
                    p.wait("tensor", "V", v_vevs[-2])

                def mmv(E, tt=tt, ic2=ic2, bank=bank):
                    E.matmul(bank[:, :], c_or[0:1, ts(tt, 128)],
                             b_sb["v"][0:1, ts(ic2, 512)], start=True, stop=False)
                    for dc in range(8):
                        last = E.matmul(
                            bank[:, :],
                            xnt[:, dc * 2048 + tt * 128: dc * 2048 + (tt + 1) * 128],
                            wbuf[:, dc * 1024 + ic2 * 512: dc * 1024 + ic2 * 512 + 512],
                            start=False, stop=(dc == 7))
                    return last
                t_mm = p.op_inc("tensor", "T", mmv)
                p.wait("vector", "T", t_mm)
                p.op("vector", lambda E, tt=tt, ic2=ic2, bank=bank:
                     E.tensor_copy(
                         vp[tt][:, :].rearrange("p (h c) -> p h c", c=65)
                         [:, ic2 * 8:(ic2 + 1) * 8, 0:64],
                         bank[:, :].rearrange("p (h c) -> p h c", c=64)))
                v_vevs.append(p.mark("vector", "V"))
        v_qkv_done = v_vevs[-1]
        t_qkv_done = S["T"].n
        if "qkv" in dbg:
            oq = dbgt("qt", [1024, 2048], BF16)
            ok_ = dbgt("kt", [1024, 2048], BF16)
            ov = dbgt("v", [2048, HEADS * 65], BF16)
            p.wait("scalar", "V", v_qkv_done)
            for i in range(8):
                p.dma("scalar", "ST", oq[ts(i, 128), :], qt_sb[i][:, :])
                p.dma("scalar", "ST", ok_[ts(i, 128), :], kt_sb[i][:, :])
            for i in range(OTILE):
                p.dma("scalar", "ST", ov[ts(i, 128), :], vp[i][:, :])
        if stop_after == "qkv":
            finish_early()
            return nc, dbg_out

        if "g" in dbg:
            od = dbgt("g", [128, NTILE * NE])
            p.wait("scalar", "V", v_g_done)
            p.dma("scalar", "ST", od, g_all[:, :])

        A.push()
        r_v1 = A.take("r_v1", [128, NTILE], F32)
        r_v2 = A.take("r_v2", [128, NTILE], F32)
        r_eq = A.take("r_eq", [128, NTILE], F32)
        r_t = A.take("r_t", [128, NTILE], F32)
        r_big = A.take("r_big", [128, NTILE], F32)
        g2 = A.take("g2", [128, NTILE * NE], F32)
        zf = A.take("zf", [128, NTILE], F32)
        s12 = A.take("s12", [128, NTILE], F32)

        g3 = g_all[:, :].rearrange("p (t e) -> p t e", e=NE)
        g23 = g2[:, :].rearrange("p (t e) -> p t e", e=NE)
        p.op("vector", lambda E: E.memset(r_big[:, :], SBIG))
        p.op("vector", lambda E: E.memset(zf[:, :], 0.0))
        p.op("vector", lambda E: E.reduce_max(r_v1[:, :], g3, axis=AX.X))
        p.op("vector", lambda E: E.memset(idx1[:, :], SBIG))
        for ee in range(NE):
            p.op("vector", lambda E, ee=ee:
                 E.tensor_tensor(r_eq[:, :], g3[:, :, ee], r_v1[:, :], op=ALU.is_equal))
            p.op("vector", lambda E, ee=ee: E.scalar_tensor_tensor(
                r_t[:, :], r_eq[:, :], -(SBIG - ee), r_big[:, :],
                op0=ALU.mult, op1=ALU.add))
            p.op("vector", lambda E: E.tensor_tensor(idx1[:, :], idx1[:, :],
                                                     r_t[:, :], op=ALU.min))
        for ee in range(NE):
            p.op("vector", lambda E, ee=ee: E.tensor_single_scalar(
                r_eq[:, :], idx1[:, :], float(ee), op=ALU.is_equal))
            p.op("vector", lambda E, ee=ee: E.scalar_tensor_tensor(
                g23[:, :, ee], r_eq[:, :], -1e30, g3[:, :, ee],
                op0=ALU.mult, op1=ALU.add))
        p.op("vector", lambda E: E.reduce_max(r_v2[:, :], g23, axis=AX.X))
        p.op("vector", lambda E: E.memset(idx2[:, :], SBIG))
        for ee in range(NE):
            p.op("vector", lambda E, ee=ee:
                 E.tensor_tensor(r_eq[:, :], g23[:, :, ee], r_v2[:, :], op=ALU.is_equal))
            p.op("vector", lambda E, ee=ee: E.scalar_tensor_tensor(
                r_t[:, :], r_eq[:, :], -(SBIG - ee), r_big[:, :],
                op0=ALU.mult, op1=ALU.add))
            p.op("vector", lambda E: E.tensor_tensor(idx2[:, :], idx2[:, :],
                                                     r_t[:, :], op=ALU.min))
        p.op("vector", lambda E: E.tensor_sub(s12[:, :], r_v1[:, :], r_v2[:, :]))
        v_rt = p.mark("vector", "V")
        p.wait("scalar", "V", v_rt)
        p.op("scalar", lambda E: E.activation(p1[:, :], s12[:, :], AF.Sigmoid))
        p.op("scalar", lambda E: E.activation(p2[:, :], s12[:, :], AF.Sigmoid,
                                              scale=-1.0))
        a_probs = p.mark("scalar", "A")

        if "route" in dbg:
            for nm, buf in (("idx1", idx1), ("idx2", idx2), ("p1", p1), ("p2", p2)):
                od = dbgt(nm, [128, NTILE])
                p.wait("scalar", "A", a_probs)
                p.wait("scalar", "V", v_rt)
                p.dma("scalar", "ST", od, buf[:, :])

        idxT = [A.take(f"idxT{k}", [128, 128], F32) for k in range(2)]
        m_ke = A.take("m_ke", [128, 128], F32)
        incl = A.take("incl", [128, 128], F32)
        plane = A.take("plane", [128, 128], F32)
        rsel = [A.take(f"rsel{k}", [128, 128], F32) for k in range(2)]
        rtt = [A.take(f"rtt{k}", [128, 128], F32) for k in range(2)]
        totb = A.take("totb", [128, 2 * NE], BF16)
        carr = A.take("carr", [128, 2 * NE], F32)

        p.wait("tensor", "V", v_rt)
        for k, src in enumerate((idx1, idx2)):
            t_tr = p.op_inc("tensor", "T", lambda E, s=src:
                            E.transpose(ps_a[:, 0:128], s[:, :], c_idf[:, :]))
            p.wait("vector", "T", t_tr)
            p.op("vector", lambda E, k=k: E.tensor_copy(idxT[k][:, :], ps_a[:, 0:128]))
            v_m = p.mark("vector", "V")
            p.wait("tensor", "V", v_m)
        v_idxT = p.mark("vector", "V")

        for k in range(2):
            for ee in range(NE):
                p.op("vector", lambda E, k=k, ee=ee: E.tensor_single_scalar(
                    m_ke[:, :], idxT[k][:, :], float(ee), op=ALU.is_equal))
                p.op("vector", lambda E: E.tensor_tensor_scan(
                    incl[:, :], m_ke[:, :], zf[:, 0:128], 0.0,
                    op0=ALU.add, op1=ALU.add))
                p.op("vector", lambda E, k=k, ee=ee: E.tensor_copy(
                    totb[:, k * NE + ee:k * NE + ee + 1], incl[:, 127:128]))
        v_tot = p.mark("vector", "V")
        p.wait("tensor", "V", v_tot)

        def mm_car(E):
            E.matmul(ps_o[:, 0:8], c_tri[:, :], totb[:, 0:NE], start=True, stop=True)
            return E.matmul(ps_o[:, 8:16], c_tri[:, :], totb[:, NE:2 * NE],
                            start=True, stop=False)

        def mm_car2(E):
            return E.matmul(ps_o[:, 8:16], c_one[:, :], totb[:, 0:NE],
                            start=False, stop=True)
        t_car = p.op_inc("tensor", "T", lambda E: (mm_car(E), mm_car2(E))[1])
        p.wait("vector", "T", t_car)
        p.op("vector", lambda E: E.tensor_copy(carr[:, 0:NE], ps_o[:, 0:8]))
        p.op("vector", lambda E: E.tensor_copy(carr[:, NE:2 * NE], ps_o[:, 8:16]))
        for k in range(2):
            p.op("vector", lambda E, k=k: E.memset(rsel[k][:, :], 0.0))
            for ee in range(NE):
                p.op("vector", lambda E, k=k, ee=ee: E.tensor_single_scalar(
                    m_ke[:, :], idxT[k][:, :], float(ee), op=ALU.is_equal))
                p.op("vector", lambda E: E.tensor_tensor_scan(
                    incl[:, :], m_ke[:, :], zf[:, 0:128], 0.0,
                    op0=ALU.add, op1=ALU.add))
                p.op("vector", lambda E, k=k, ee=ee: E.scalar_tensor_tensor(
                    plane[:, :], incl[:, :], carr[:, k * NE + ee:k * NE + ee + 1],
                    m_ke[:, :], op0=ALU.add, op1=ALU.subtract))
                p.op("vector", lambda E: E.tensor_mul(plane[:, :], plane[:, :],
                                                      m_ke[:, :]))
                p.op("vector", lambda E, k=k: E.tensor_add(rsel[k][:, :],
                                                           rsel[k][:, :], plane[:, :]))
        v_rsel = p.mark("vector", "V")
        p.wait("tensor", "V", v_rsel)
        for k in range(2):
            t_tr = p.op_inc("tensor", "T", lambda E, k=k:
                            E.transpose(ps_a[:, 0:128], rsel[k][:, :], c_idf[:, :]))
            p.wait("vector", "T", t_tr)
            p.op("vector", lambda E, k=k: E.tensor_copy(rtt[k][:, :], ps_a[:, 0:128]))
            v_m = p.mark("vector", "V")
            p.wait("tensor", "V", v_m)
        v_rtt = p.mark("vector", "V")

        if "rank" in dbg:
            for k in range(2):
                od = dbgt(f"rtt{k}", [128, 128])
                p.wait("scalar", "V", v_rtt)
                p.dma("scalar", "ST", od, rtt[k][:, :])
            p.dma("scalar", "ST", dbgt("rsel0", [128, 128]), rsel[0][:, :])
            p.dma("scalar", "ST", dbgt("idxT0", [128, 128]), idxT[0][:, :])
            p.dma("scalar", "ST", dbgt("carr", [128, 16]), carr[:, :])

        if stop_after == "rank":
            finish_early()
            return nc, dbg_out

        oloc = [A.take(f"oloc{k}", [16, 128], F32) for k in range(2)]
        lincl = A.take("lincl", [16, 128], F32)
        lm = A.take("lm", [16, 128], F32)
        lpl = A.take("lpl", [16, 128], F32)
        lrsel = [A.take(f"lrsel{k}", [16, 128], F32) for k in range(2)]
        lrtt = [A.take(f"lrtt{k}", [128, OTILE], F32) for k in range(2)]
        ltotb = A.take("ltotb", [16, 2 * NE], BF16)
        lcarr = A.take("lcarr", [16, 2 * NE], F32)

        for k, src in enumerate((idx1, idx2)):
            t_tr = p.op_inc("tensor", "T", lambda E, s=src:
                            E.transpose(ps_a[0:16, 0:128], s[:, 0:OTILE],
                                        c_idf[:, :]))
            p.wait("vector", "T", t_tr)
            p.op("vector", lambda E, k=k: E.tensor_copy(oloc[k][:, :], ps_a[0:16, 0:128]))
            v_m = p.mark("vector", "V")
            p.wait("tensor", "V", v_m)
        v_oloc = p.mark("vector", "V")
        for k in range(2):
            for ee in range(NE):
                p.op("vector", lambda E, k=k, ee=ee: E.tensor_single_scalar(
                    lm[:, :], oloc[k][:, :], float(ee), op=ALU.is_equal))
                p.op("vector", lambda E: E.tensor_tensor_scan(
                    lincl[:, :], lm[:, :], zf[0:16, 0:128], 0.0,
                    op0=ALU.add, op1=ALU.add))
                p.op("vector", lambda E, k=k, ee=ee: E.tensor_copy(
                    ltotb[:, k * NE + ee:k * NE + ee + 1], lincl[:, 127:128]))
        v_ltot = p.mark("vector", "V")
        p.wait("tensor", "V", v_ltot)

        def mm_lcar(E):
            E.matmul(ps_o[0:16, 0:8], c_tri[0:16, 0:16], ltotb[:, 0:NE],
                     start=True, stop=True)
            E.matmul(ps_o[0:16, 8:16], c_tri[0:16, 0:16], ltotb[:, NE:2 * NE],
                     start=True, stop=False)
            return E.matmul(ps_o[0:16, 8:16], c_one[0:16, 0:16], ltotb[:, 0:NE],
                            start=False, stop=True)
        t_lcar = p.op_inc("tensor", "T", mm_lcar)
        p.wait("vector", "T", t_lcar)
        p.op("vector", lambda E: E.tensor_copy(lcarr[:, 0:NE], ps_o[0:16, 0:8]))
        p.op("vector", lambda E: E.tensor_copy(lcarr[:, NE:2 * NE], ps_o[0:16, 8:16]))
        for k in range(2):
            p.op("vector", lambda E, k=k: E.memset(lrsel[k][:, :], 0.0))
            for ee in range(NE):
                p.op("vector", lambda E, k=k, ee=ee: E.tensor_single_scalar(
                    lm[:, :], oloc[k][:, :], float(ee), op=ALU.is_equal))
                p.op("vector", lambda E: E.tensor_tensor_scan(
                    lincl[:, :], lm[:, :], zf[0:16, 0:128], 0.0,
                    op0=ALU.add, op1=ALU.add))
                p.op("vector", lambda E, k=k, ee=ee: E.scalar_tensor_tensor(
                    lpl[:, :], lincl[:, :], lcarr[:, k * NE + ee:k * NE + ee + 1],
                    lm[:, :], op0=ALU.add, op1=ALU.subtract))
                p.op("vector", lambda E: E.tensor_mul(lpl[:, :], lpl[:, :], lm[:, :]))
                p.op("vector", lambda E, k=k: E.tensor_add(lrsel[k][:, :],
                                                           lrsel[k][:, :], lpl[:, :]))
        v_lrsel = p.mark("vector", "V")
        p.wait("tensor", "V", v_lrsel)
        for k in range(2):
            t_tr = p.op_inc("tensor", "T", lambda E, k=k:
                            E.transpose(ps_a[:, 0:OTILE], lrsel[k][:, :],
                                        c_idf[0:16, 0:16]))
            p.wait("vector", "T", t_tr)
            p.op("vector", lambda E, k=k: E.tensor_copy(lrtt[k][:, :],
                                                        ps_a[:, 0:OTILE]))
            v_m = p.mark("vector", "V")
            p.wait("tensor", "V", v_m)
        v_lrtt = p.mark("vector", "V")

        dropf = A.take("dropf", [128, OTILE], F32)
        oflf = A.take("oflf", [128, OTILE], F32)
        basei = A.take("basei", [128, OTILE], F32)
        sidxf = A.take("sidxf", [128, OTILE], F32)
        tinv = A.take("tinv", [128, OTILE], F32)
        gtmp = A.take("gtmp", [128, OTILE], F32)

        p.op("vector", lambda E: E.tensor_scalar(
            tinv[:, :], c_tok[:, :], -1.0, float(T), op0=ALU.mult, op1=ALU.add))
        p.op("vector", lambda E: E.memset(tini[:, :], float(T)))
        for k in range(2):
            own = (idx1 if k == 0 else idx2)[:, 0:OTILE]
            p.op("vector", lambda E, k=k: E.tensor_single_scalar(
                dropf[:, :], rtt[k][:, 0:OTILE], float(CAP), op=ALU.is_ge))
            p.op("vector", lambda E, k=k: E.tensor_single_scalar(
                oflf[:, :], lrtt[k][:, :], float(CLOC), op=ALU.is_ge))
            p.op("vector", lambda E, k=k, own=own: E.scalar_tensor_tensor(
                basei[:, :], own, float(CLOC), lrtt[k][:, :],
                op0=ALU.mult, op1=ALU.add))
            p.op("vector", lambda E: E.scalar_tensor_tensor(
                sidxf[:, :], oflf[:, :], BIG, basei[:, :],
                op0=ALU.mult, op1=ALU.add))
            p.op("vector", lambda E, k=k: E.tensor_copy(sidxi[k][:, :], sidxf[:, :]))
            p.op("vector", lambda E: E.tensor_mul(gtmp[:, :], dropf[:, :], tinv[:, :]))
            p.op("vector", lambda E, k=k: E.tensor_add(svals[k][:, :],
                                                       gtmp[:, :], c_tok[:, :]))
            p.op("vector", lambda E: E.tensor_scalar(
                gtmp[:, :], basei[:, :], -1.0, float(NSLOT), op0=ALU.mult, op1=ALU.add))
            p.op("vector", lambda E: E.tensor_mul(gtmp[:, :], gtmp[:, :], oflf[:, :]))
            p.op("vector", lambda E: E.tensor_add(gtmp[:, :], gtmp[:, :], basei[:, :]))
            gi_sb = g0i if k == 0 else g1i
            p.op("vector", lambda E, gi_sb=gi_sb: E.tensor_copy(gi_sb[:, :], gtmp[:, :]))
        v_scati = p.mark("vector", "V")

        if stop_after == "scatidx":
            finish_early()
            return nc, dbg_out

        p.wait("gpsimd", "V", v_scati)
        lg_tini = p.dma("gpsimd", "LG",
                        slot_tab.rearrange("(p f) one -> p (f one)", p=128),
                        tini[:, :])
        p.wait("gpsimd", "LG", lg_tini)
        for k in range(2):
            for t in range(OTILE):
                p.iscatter("gpsimd", "LG", slot_tab[:, :],
                           svals[k][:, t:t + 1], sidxi[k][:, t:t + 1], NSLOT - 1)
        lg_scat = S["LG"].n
        p.wait("gpsimd", "LG", lg_scat)
        lg_tab = p.dma("gpsimd", "LG", tabf[:, :],
                       slot_tab.rearrange("(p f) one -> p (f one)", p=128))
        p.wait("vector", "LG", lg_tab)
        for i4 in range(4):
            p.op("vector", lambda E, i4=i4: E.tensor_copy(
                tabi4[i4][:, :], tabf[:, i4 * 14:(i4 + 1) * 14]))
        v_tabi = p.mark("vector", "V")

        if "tab" in dbg:
            od = dbgt("tab", [128, SCOLS])
            p.wait("scalar", "V", v_tabi)
            p.dma("scalar", "ST", od, tabf[:, :])
            od2 = dbgt("lrtt0", [128, OTILE])
            p.dma("scalar", "ST", od2, lrtt[0][:, :])
            od3 = dbgt("g0", [128, OTILE], I32)
            p.dma("scalar", "ST", od3, g0i[:, :])
            od4 = dbgt("g1", [128, OTILE], I32)
            p.dma("scalar", "ST", od4, g1i[:, :])

        if stop_after == "route":
            finish_early()
            return nc, dbg_out
        A.pop()
        A.pop()

        disp_st = [0] * SCOLS
        p.wait("gpsimd", "V", v_tabi)
        p.wait("gpsimd", "LG", st_own_last)
        for f in range(SCOLS):
            b = f % 2
            if f >= 2:
                p.wait("gpsimd", "LG", disp_st[f - 2])
            lg_g = p.igather("gpsimd", "LG", gxb[b][:, :], xn_pad[:, :],
                             tabi4[f // 14][:, f % 14:f % 14 + 1], T)
            p.wait("gpsimd", "LG", lg_g)
            disp_st[f] = p.dma(
                "gpsimd", "LG",
                exp_x.rearrange("(pp f) d -> pp f d", f=SCOLS)[:, f, :],
                gxb[b][:, :])
        lg_disp = S["LG"].n


        A.push()
        sq2 = A.take("sq2", [128, 2048], F32)
        pbuf = A.take("pbuf", [128, 2048], BF16)
        ybuf = [A.take(f"ybuf{i}", [128, D], BF16) for i in range(2)]
        lcol = A.take("lcol", [128, 10], F32)

        p.op("vector", lambda E: E.memset(nrm[:, :], 0.0))
        v_nz = p.mark("vector", "V")
        a_norm_sq = []
        for which, tiles_, slot in (("q", qt_sb, 0), ("k", kt_sb, 1)):
            for it in range(8):
                p.wait("scalar", "V", v_qkv_done if which == "k" else v_q)
                p.wait("scalar", "T", t_g_done)
                p.wait("scalar", "V", v_scati)
                p.op("scalar", lambda E, tiles_=tiles_, it=it:
                     E.activation(sq2[:, :], tiles_[it][:, :], AF.Square))
                a_s = p.mark("scalar", "A")
                for tc in range(4):
                    p.wait("tensor", "A", a_s)
                    p.wait("tensor", "V", v_nz)
                    t_n = p.op_inc("tensor", "T", lambda E, tc=tc:
                                   E.matmul(ps_b[0:1, 0:512], c_oc[:, :],
                                            sq2[:, ts(tc, 512)], start=True, stop=True))
                    p.wait("vector", "T", t_n)
                    p.op("vector", lambda E: E.reduce_max(nrm[0:1, 2:3],
                                                          ps_b[0:1, 0:512], axis=AX.X))
                    p.op("vector", lambda E, slot=slot: E.tensor_tensor(
                        nrm[0:1, slot:slot + 1], nrm[0:1, slot:slot + 1],
                        nrm[0:1, 2:3], op=ALU.max))
                    v_nz = p.mark("vector", "V")
        p.op("vector", lambda E: E.tensor_mul(nrm[0:1, 3:4], nrm[0:1, 0:1],
                                              nrm[0:1, 1:2]))
        v_nn = p.mark("vector", "V")
        p.wait("scalar", "V", v_nn)
        p.op("scalar", lambda E: E.activation(nrm[0:1, 4:5], nrm[0:1, 3:4], AF.Sqrt))
        a_nn = p.mark("scalar", "A")
        p.wait("vector", "A", a_nn)
        p.op("vector", lambda E: E.tensor_scalar_mul(nrm[0:1, 5:6],
                                                     nrm[0:1, 4:5], -0.125))
        v_mh = p.mark("vector", "V")
        p.wait("tensor", "V", v_mh)
        t_mh = p.op_inc("tensor", "T", lambda E:
                        E.matmul(ps_b[:, 0:1], c_orf[0:1, :], nrm[0:1, 5:6],
                                 start=True, stop=True))
        p.wait("vector", "T", t_mh)
        p.op("vector", lambda E: E.tensor_copy(mhat[:, :], ps_b[:, 0:1]))
        v_mhat = p.mark("vector", "V")
        if "attn" in dbg:
            p.wait("scalar", "V", v_mhat)
            p.dma("scalar", "ST", dbgt("mhat", [128, 1]), mhat[:, :])
            p.dma("scalar", "ST", dbgt("nrm", [1, 8]), nrm[:, :])
        p.wait("scalar", "V", v_mhat)

        ymid_st = [0] * OTILE
        a_exp_half = [0, 0]
        t_pv_half = [0, 0]
        a_o_prev = 0
        v_pm_hist = []
        unit = 0
        for qt in range(OTILE):
            nkb = qt + 1
            yb = ybuf[qt % 2]
            if qt >= 2:
                p.wait("scalar", "LD", ymid_st[qt - 2])
            for h in range(HEADS):
                til, base = h // 2, (h % 2) * 64
                qsl = qt_sb[til][base:base + 64, ts(qt, 128)]
                rl = lcol[:, 4 + h % 4:5 + h % 4]
                if nkb <= 8:
                    off = 1024 * (unit % 2)
                    chunks = [(unit % 2, 0, nkb)]
                else:
                    off = 0
                    chunks = [(0, 0, 8), (1, 8, nkb)]
                exp_marks = []
                for hf, k0, k1 in chunks:
                    p.wait("tensor", "A", a_exp_half[hf])

                    def mm_st(E, qsl=qsl, til=til, base=base, k0=k0, k1=k1,
                              off=off):
                        for kb in range(k0, k1):
                            last = E.matmul(
                                ps_s[:, off + kb * 128: off + (kb + 1) * 128],
                                kt_sb[til][base:base + 64,
                                           kb * 128:(kb + 1) * 128],
                                qsl, start=True, stop=True)
                        return last
                    t_s = p.op_inc("tensor", "T", mm_st)
                    p.wait("scalar", "T", max(t_s, t_pv_half[hf]))
                    if len(v_pm_hist) >= 2:
                        p.wait("scalar", "V", v_pm_hist[-2])
                    p.op("scalar", lambda E, k0=k0, k1=k1, off=off:
                         E.activation(pbuf[:, off + k0 * 128: off + k1 * 128],
                                      ps_s[:, off + k0 * 128: off + k1 * 128],
                                      AF.Exp, bias=mhat[:, :], scale=0.125))
                    a_exp = p.mark("scalar", "A")
                    a_exp_half[hf] = a_exp
                    exp_marks.append(a_exp)
                p.wait("vector", "A", exp_marks[-1])
                p.op("vector", lambda E, nkb=nkb, off=off: E.tensor_mul(
                    pbuf[:, off + (nkb - 1) * 128: off + nkb * 128],
                    pbuf[:, off + (nkb - 1) * 128: off + nkb * 128],
                    c_caus[:, :]))
                v_pm = p.mark("vector", "V")
                v_pm_hist.append(v_pm)
                for ci, (hf, k0, k1) in enumerate(chunks):
                    p.wait("tensor", "A", exp_marks[ci])
                    if ci == len(chunks) - 1:
                        p.wait("tensor", "V", v_pm)
                    if ci == 0:
                        p.wait("tensor", "A", a_o_prev)

                    def mm_pv(E, h=h, k0=k0, k1=k1, off=off, nkb=nkb):
                        for kb in range(k0, k1):
                            last = E.matmul(
                                ps_o[:, 0:65],
                                pbuf[:, off + kb * 128: off + (kb + 1) * 128],
                                vp[kb][:, h * 65:(h + 1) * 65],
                                start=(kb == 0), stop=(kb == nkb - 1))
                        return last
                    t_pv = p.op_inc("tensor", "T", mm_pv)
                    t_pv_half[hf] = t_pv
                p.wait("vector", "T", t_pv)
                p.op("vector", lambda E, rl=rl: E.reciprocal(rl, ps_o[:, 64:65]))
                v_rl = p.mark("vector", "V")
                p.wait("scalar", "T", t_pv)
                p.wait("scalar", "V", v_rl)
                p.op("scalar", lambda E, yb=yb, h=h, rl=rl: E.activation(
                    yb[:, ts(h, 64)], ps_o[:, 0:64], AF.Copy,
                    scale=rl))
                a_o = p.mark("scalar", "A")
                a_o_prev = a_o
                unit += 1
            p.wait("scalar", "A", a_o)
            ymid_st[qt] = p.dma("scalar", "LD", y_mid[ts(qt, 128), :], yb[:, :])
        ld_ymid = S["LD"].n
        t_attn_done = S["T"].n
        a_attn_done = S["A"].n
        v_attn_done = S["V"].n
        A.pop()
        A.pop()

        if "attn" in dbg:
            od = dbgt("attnY", [T, D])
            rb2 = A.take("at_rb", [128, D], BF16)
            rd2 = A.take("at_rd", [128, D], F32)
            for i in range(OTILE):
                p.wait("gpsimd", "LD", ld_ymid)
                lg = p.dma("gpsimd", "LG", rb2[:, :], y_mid[ts(i, 128), :])
                p.wait("vector", "LG", lg)
                p.op("vector", lambda E: E.tensor_copy(rd2[:, :], rb2[:, :]))
                vv = p.mark("vector", "V")
                p.wait("gpsimd", "V", vv)
                p.dma("gpsimd", "ST", od[ts(i, 128), :], rd2[:, :])

        if stop_after == "attn":
            finish_early()
            return nc, dbg_out

        attn_sb = A.take("attn_sb", [128, OTILE * D], BF16)

        A.push()
        ytb = A.take("ytb", [128, 8 * 2048], BF16)
        wo_sb = A.take("wo_sb", [128, 8 * 1024], BF16)
        p.wait("gpsimd", "T", t_attn_done)
        p.wait("gpsimd", "A", a_attn_done)
        p.wait("gpsimd", "V", v_attn_done)
        lg_wo = p.dma("gpsimd", "LG",
                      wo_sb[:, :].rearrange("p (dc i) -> p dc i", i=D),
                      wT["o"].rearrange("(dc p) i -> p dc i", p=128))
        p.wait("sync", "LD", ld_ymid)
        p.wait("sync", "T", t_attn_done)
        p.wait("sync", "LG", lg_disp)
        ldyT = 0
        for ic in range(8):
            ldyT = p.dmat("sync", "LD", ytb[:, ts(ic, 2048)],
                          y_mid[0:2048, ts(ic, 128)])
        p.wait("tensor", "LD", ldyT)
        p.wait("tensor", "LG", lg_wo)
        ev2 = []
        for t in range(OTILE):
            for jc in range(2):
                bank = ps_a if (t * 2 + jc) % 2 == 0 else ps_b
                if len(ev2) >= 2:
                    p.wait("tensor", "V", ev2[-2])

                def mm_yo(E, t=t, jc=jc, bank=bank):
                    E.matmul(bank[:, :], c_or[0:1, ts(t, 128)],
                             b_sb["o"][0:1, ts(jc, 512)], start=True, stop=False)
                    for ic in range(8):
                        last = E.matmul(
                            bank[:, :],
                            ytb[:, ic * 2048 + t * 128: ic * 2048 + (t + 1) * 128],
                            wo_sb[:, ic * 1024 + jc * 512: ic * 1024 + jc * 512 + 512],
                            start=False, stop=(ic == 7))
                    return last
                t_mm = p.op_inc("tensor", "T", mm_yo)
                p.wait("vector", "T", t_mm)
                p.op("vector", lambda E, t=t, jc=jc, bank=bank: E.tensor_copy(
                    attn_sb[:, t * D + jc * 512: t * D + jc * 512 + 512],
                    bank[:, :]))
                ev2.append(p.mark("vector", "V"))
        v_yproj = ev2[-1]
        t_yproj = S["T"].n
        if "yproj" in dbg:
            od = dbgt("attnout", [128, OTILE * D], BF16)
            p.wait("scalar", "V", v_yproj)
            p.dma("scalar", "ST", od, attn_sb[:, :])
            oy = dbgt("ymid", [T, D], BF16)
            for i in range(OTILE):
                p.dma("scalar", "ST", oy[ts(i, 128), :], y_mid[ts(i, 128), :])
        A.pop()
        if stop_after == "yproj":
            finish_early()
            return nc, dbg_out

        A.push()
        ht = A.take("ht", [128, 32 * CLOC], BF16)
        xtb = [A.take(f"xtb{i}", [128, 8 * CLOC], BF16) for i in range(2)]
        NSL = 4
        slab = [A.take(f"slab{i}", [128, D], BF16) for i in range(NSL)]
        NPC = 4
        pchk = [A.take(f"pchk{i}", [128, 512], BF16) for i in range(NPC)]
        eo = [A.take(f"eo{i}", [128, 512], BF16) for i in range(2)]

        slab_done = [None] * 32
        slab_consumed_T = [0] * NSL
        pchk_done = [None] * 64
        pchk_consumed_T = [0] * NPC
        exp_st = []
        t_h_done = [0] * NE
        t_out_done_prev = 0
        p.wait("tensor", "V", v_yproj)
        p.wait("sync", "T", t_yproj)
        p.wait("scalar", "T", t_yproj)
        p.wait("gpsimd", "T", t_yproj)
        p.wait("gpsimd", "LG", lg_disp)
        p.wait("sync", "LG", lg_disp)
        ldxT = [0] * NE
        a_h_prev = [0, 0]

        def issue_slab(ee_, fc):
            sbw = slab[fc % NSL]
            if slab_consumed_T[fc % NSL]:
                p.wait("gpsimd", "T", slab_consumed_T[fc % NSL])
            slab_done[fc] = ("LG", p.dma("gpsimd", "LG", sbw[:, :], cfc[ee_, fc]))

        def issue_pchk(ee_, step):
            dcp_, fc_ = divmod(step, 32)
            cb = pchk[step % NPC]
            if pchk_consumed_T[step % NPC]:
                p.wait("gpsimd", "T", pchk_consumed_T[step % NPC])
            pchk_done[step] = ("LG", p.dma("gpsimd", "LG", cb[:, :],
                                           cpj[ee_][ts(fc_, 128), ts(dcp_, 512)]))

        for fc in range(NSL):
            issue_slab(0, fc)
        for ee in range(NE):
            p.wait("sync", "T", t_h_done[max(ee - 1, 0)])
            for dc in range(8):
                ldxT[ee] = p.dmat("sync", "LD", xtb[ee % 2][:, ts(dc, CLOC)],
                                  exp_x[ts(ee, CLOC), ts(dc, 128)])
            for fc in range(32):
                sb = slab[fc % NSL]
                sem, val = slab_done[fc]
                p.wait("tensor", sem, val)
                if fc == 0:
                    p.wait("tensor", "LD", ldxT[ee])
                for sc in range(2):
                    w = 512 if sc == 0 else CLOC - 512
                    bank = ps_a if sc == 0 else ps_b
                    if a_h_prev[sc]:
                        p.wait("tensor", "A", a_h_prev[sc])

                    def mm_h(E, sb=sb, sc=sc, w=w, bank=bank, xt=xtb[ee % 2]):
                        for dc in range(8):
                            last = E.matmul(
                                bank[:, 0:w], sb[:, ts(dc, 128)],
                                xt[:, dc * CLOC + sc * 512: dc * CLOC + sc * 512 + w],
                                start=(dc == 0), stop=(dc == 7))
                        return last
                    t_h = p.op_inc("tensor", "T", mm_h)
                    p.wait("scalar", "T", t_h)
                    if fc == 0 and sc == 0:
                        p.wait("scalar", "T", t_out_done_prev)
                    p.op("scalar", lambda E, fc=fc, sc=sc, w=w, bank=bank:
                         E.activation(ht[:, fc * CLOC + sc * 512:
                                         fc * CLOC + sc * 512 + w],
                                      bank[:, 0:w], AF.Gelu))
                    a_h = p.mark("scalar", "A")
                    a_h_prev[sc] = a_h
                slab_consumed_T[fc % NSL] = S["T"].n
                if fc + NSL < 32:
                    issue_slab(ee, fc + NSL)
            t_h_done[ee] = S["T"].n
            a_ht_done = S["A"].n
            for st in range(NPC):
                issue_pchk(ee, st)
            if ee + 1 < NE:
                for fc in range(NSL):
                    issue_slab(ee + 1, fc)
            banks7 = [ps_s[:, ts(i, 512)] for i in range(4)] + \
                     [ps_a[:, :], ps_b[:, :], ps_o[:, :]]
            for dcp in range(2):
                for fc in range(32):
                    step = dcp * 32 + fc
                    cb = pchk[step % NPC]
                    sem, val = pchk_done[step]
                    p.wait("tensor", sem, val)
                    if fc == 0:
                        p.wait("tensor", "A", a_ht_done)
                        p.wait("tensor", "V")
                        if dcp == 0:
                            p.wait("tensor", "A", a_h_prev[0])
                            p.wait("tensor", "A", a_h_prev[1])

                    def mm_o(E, cb=cb, fc=fc):
                        for st7 in range(7):
                            last = E.matmul(
                                banks7[st7],
                                ht[:, fc * CLOC + st7 * 128: fc * CLOC + (st7 + 1) * 128],
                                cb[:, :], start=(fc == 0), stop=(fc == 31))
                        return last
                    t_o = p.op_inc("tensor", "T", mm_o)
                    pchk_consumed_T[step % NPC] = S["T"].n
                    if step + NPC < 64:
                        issue_pchk(ee, step + NPC)
                for st7 in range(7):
                    b2 = eo[st7 % 2]
                    if st7 >= 2:
                        p.wait("vector", "LD", exp_st[-2])
                    p.wait("vector", "T", t_o)
                    p.op("vector", lambda E, st7=st7, b2=b2:
                         E.tensor_copy(b2[:, :], banks7[st7]))
                    v_eo = p.mark("vector", "V")
                    p.wait("sync", "V", v_eo)
                    exp_st.append(p.dma(
                        "sync", "LD",
                        exp_out[ee * CLOC + st7 * 128: ee * CLOC + (st7 + 1) * 128,
                                ts(dcp, 512)],
                        b2[:, :]))
            t_out_done_prev = S["T"].n
        ld_expout = S["LD"].n
        if stop_after == "ffn":
            finish_early()
            return nc, dbg_out
        if "ffn" in dbg:
            od = dbgt("expout", [NSLOT + 1, D], BF16)
            ox = dbgt("expx", [NSLOT, D], BF16)
            p.wait("gpsimd", "LD", ld_expout)
            rb3 = A.take("ffn_rb", [128, D], BF16)
            for i in range(NSLOT // 128):
                lg = p.dma("gpsimd", "LG", rb3[:, :], exp_x[ts(i, 128), :])
                p.wait("gpsimd", "LG", lg)
                p.dma("gpsimd", "ST", ox[ts(i, 128), :], rb3[:, :])
                lg = p.dma("gpsimd", "LG", rb3[:, :], exp_out[ts(i, 128), :])
                p.wait("gpsimd", "LG", lg)
                p.dma("gpsimd", "ST", od[ts(i, 128), :], rb3[:, :])
        A.pop()

        A.push()
        cx = [A.take(f"cx{i}", [128, D], F32) for i in range(2)]
        mg0 = [A.take(f"mg0{i}", [128, D], BF16) for i in range(2)]
        mg1 = [A.take(f"mg1{i}", [128, D], BF16) for i in range(2)]
        ac1 = A.take("ac1", [128, D], F32)
        ac2 = A.take("ac2", [128, D], F32)
        ac3 = [A.take(f"ac3{i}", [128, D], F32) for i in range(2)]

        p.wait("gpsimd", "LD", ld_expout)
        p.wait("vector", "T")
        st_y = []
        v_comb = [0, 0]
        for t in range(OTILE):
            b = t % 2
            if t >= 2:
                p.wait("sync", "ST", st_y[t - 2])
                p.wait("sync", "V", v_comb[b])
                p.wait("gpsimd", "V", v_comb[b])
            ld_cx = p.dma("sync", "LD", cx[b][:, :], x_hbm[ts(t, 128), :])
            lg0 = p.igather("gpsimd", "LG", mg0[b][:, :], exp_out[:, :],
                            g0i[:, t:t + 1], NSLOT)
            lg1 = p.igather("gpsimd", "LG", mg1[b][:, :], exp_out[:, :],
                            g1i[:, t:t + 1], NSLOT)
            p.wait("vector", "LD", ld_cx)
            p.wait("vector", "LG", lg1)
            p.wait("vector", "V", v_yproj)
            p.op("vector", lambda E, t=t, b=b: E.tensor_add(
                ac1[:, :], cx[b][:, :], attn_sb[:, ts(t, D)]))
            p.op("vector", lambda E, t=t, b=b: E.scalar_tensor_tensor(
                ac2[:, :], mg0[b][:, :], p1[:, t:t + 1], ac1[:, :],
                op0=ALU.mult, op1=ALU.add))
            p.op("vector", lambda E, t=t, b=b: E.scalar_tensor_tensor(
                ac3[b][:, :], mg1[b][:, :], p2[:, t:t + 1], ac2[:, :],
                op0=ALU.mult, op1=ALU.add))
            v_comb[b] = p.mark("vector", "V")
            p.wait("sync", "V", v_comb[b])
            st_y.append(p.dma("sync", "ST", y_out[ts(t, 128), :], ac3[b][:, :]))
        p.wait("sync", "ST", S["ST"].n)
        A.pop()

        for eng, meth in (("sync", block.sync), ("gpsimd", block.gpsimd),
                          ("tensor", block.tensor), ("vector", block.vector),
                          ("scalar", block.scalar)):
            def runner(E, ops=p.q[eng]):
                for f in ops:
                    f(E)
            meth(runner)

    return nc, dbg_out



def make_consts(core):
    b = np.arange(128)
    true_b = (b + core * OTILE) % 128
    tri_rot = (true_b[:, None] < true_b[None, :]).astype(np.float32)
    return dict(
        tri_rot=tri_rot.astype(ml_dtypes.bfloat16),
        ones_bf=np.ones((128, 128), ml_dtypes.bfloat16),
        id_bf=np.eye(128, dtype=np.float32).astype(ml_dtypes.bfloat16),
        id_f=np.eye(128, dtype=np.float32),
        caus_bf=np.triu(np.ones((128, 128), np.float32)).astype(ml_dtypes.bfloat16),
        onesrow_bf=np.ones((1, 2048), ml_dtypes.bfloat16),
        onesrow_f=np.ones((1, 128), np.float32),
        onescol_f=np.ones((128, 1), np.float32),
        tokid_f=(np.arange(OTILE)[None, :] * 128
                 + np.arange(128)[:, None]).astype(np.float32),
    )


def make_inputs(L0_attn_x, L0_attn_Wq, L0_attn_bq, L0_attn_Wk, L0_attn_bk,
                L0_attn_Wv, L0_attn_bv, L0_attn_Wo, L0_attn_bo,
                L0_ffn_w_g, L0_ffn_c_fc, L0_ffn_c_proj):
    x = np.ascontiguousarray(np.asarray(L0_attn_x, np.float32).reshape(NT, D))
    shared = dict(
        wqT=np.ascontiguousarray(np.asarray(L0_attn_Wq, np.float32).T),
        wkT=np.ascontiguousarray(np.asarray(L0_attn_Wk, np.float32).T),
        wvT=np.ascontiguousarray(np.asarray(L0_attn_Wv, np.float32).T),
        woT=np.ascontiguousarray(np.asarray(L0_attn_Wo, np.float32).T),
        bq=np.asarray(L0_attn_bq, np.float32).reshape(1, D),
        bk=np.asarray(L0_attn_bk, np.float32).reshape(1, D),
        bv=np.asarray(L0_attn_bv, np.float32).reshape(1, D),
        bo=np.asarray(L0_attn_bo, np.float32).reshape(1, D),
        wg=np.asarray(L0_ffn_w_g, np.float32),
        cfc=np.ascontiguousarray(
            np.asarray(L0_ffn_c_fc, np.float32)
            .reshape(NE, 8, 128, 32, 128).transpose(0, 3, 2, 1, 4)
        ).reshape(NE, 32, 128, D),
        cpj=np.asarray(L0_ffn_c_proj, np.float32),
    )
    in_maps = []
    for c in range(8):
        m = dict(shared)
        m["x"] = np.ascontiguousarray(np.roll(x, -c * T, axis=0))
        m.update(make_consts(c))
        in_maps.append(m)
    return in_maps


def kernel(**inputs):
    in_maps = make_inputs(**inputs)
    nc, _ = build()
    res = run_bass_kernel_spmd(nc, in_maps, core_ids=list(range(8)))
    y = np.concatenate([res.results[c]["y"] for c in range(8)], axis=0)
    return y.reshape(8, T, D)


# version 12 (final)
# speedup vs baseline: 2.0625x; 1.0474x over previous
from contextlib import ExitStack

import numpy as np
import ml_dtypes

import concourse.bass as bass
import concourse.mybir as mybir
from concourse.bass_utils import run_bass_kernel_spmd

F32 = mybir.dt.float32
BF16 = mybir.dt.bfloat16
I32 = mybir.dt.int32
AF = mybir.ActivationFunctionType
ALU = mybir.AluOpType
AX = mybir.AxisListType
IOffs = bass.IndirectOffsetOnAxis

D = 1024
T = 2048
NT = 16384
NTILE = 128
OTILE = 16
HEADS = 16
DH = 64
NE = 8
DFF = 4096
CAP = 5120
CLOC = 896
NSLOT = NE * CLOC
SCOLS = NSLOT // 128
BIG = 1.0e9
SBIG = 1.0e4

ENGINES = ("sync", "gpsimd", "tensor", "vector", "scalar")


class Sem:
    def __init__(self, stack, nc, name):
        self.h = stack.enter_context(nc.semaphore(name))
        self.n = 0

    def bump(self, k):
        self.n += k
        return self.n


class Prog:

    def __init__(self, nc, stack):
        self.nc = nc
        self.q = {e: [] for e in ENGINES}
        self.sems = {name: Sem(stack, nc, name) for name in
                     ("LD", "LG", "T", "V", "A", "ST")}

    def wait(self, eng, sem, val=None):
        s = self.sems[sem]
        v = s.n if val is None else val
        assert v <= s.n, f"wait {sem}>={v} but only {s.n} emitted"
        if v > 0:
            self.q[eng].append(lambda E, h=s.h, v=v: E.wait_ge(h, v))

    def dma(self, eng, sem, out, in_, **kw):
        s = self.sems[sem]
        s.bump(16)
        self.q[eng].append(
            lambda E, o=out, i=in_, h=s.h, kw=kw:
            E.dma_start(out=o, in_=i, **kw).then_inc(h, 16))
        return s.n

    def dmat(self, eng, sem, out, in_):
        s = self.sems[sem]
        s.bump(16)
        self.q[eng].append(
            lambda E, o=out, i=in_, h=s.h:
            E.dma_start_transpose(o, i).then_inc(h, 16))
        return s.n

    def _bound_reg(self, E, bounds):
        if not hasattr(self, "_bregs"):
            self._bregs = {}
        if bounds not in self._bregs:
            r = E.alloc_register(f"bnd{bounds}")
            E.reg_mov(r, bounds)
            self._bregs[bounds] = r
        return self._bregs[bounds]

    def igather(self, eng, sem, out, in_, idx, bounds):
        s = self.sems[sem]
        s.bump(16)

        def f(E, o=out, i=in_, x=idx, b=bounds, h=s.h):
            return E.indirect_dma_start(
                out=o, out_offset=None, in_=i,
                in_offset=IOffs(ap=x, axis=0),
                bounds_check=self._bound_reg(E, b), oob_is_err=False
            ).then_inc(h, 16)
        self.q[eng].append(f)
        return s.n

    def iscatter(self, eng, sem, out, in_, idx, bounds):
        s = self.sems[sem]
        s.bump(16)

        def f(E, o=out, i=in_, x=idx, b=bounds, h=s.h):
            return E.indirect_dma_start(
                out=o, out_offset=IOffs(ap=x, axis=0), in_=i, in_offset=None,
                bounds_check=self._bound_reg(E, b), oob_is_err=False
            ).then_inc(h, 16)
        self.q[eng].append(f)
        return s.n

    def op(self, eng, fn, nd=False):
        self.q[eng].append(fn)
        if eng == "vector" and not nd:
            self.q[eng].append(lambda E: E.drain())

    def dr(self, eng):
        self.q[eng].append(lambda E: E.drain())

    def mark(self, eng, sem, k=1):
        s = self.sems[sem]
        v = s.bump(k)
        if eng in ("vector", "scalar"):
            self.q[eng].append(lambda E: E.drain())
        self.q[eng].append(lambda E, h=s.h, k=k: E.sem_inc(h, k))
        return v

    def op_inc(self, eng, sem, fn, k=1):
        s = self.sems[sem]
        v = s.bump(k)
        self.q[eng].append(lambda E, f=fn, h=s.h, k=k: f(E).then_inc(h, k))
        return v


def ts(i, n):
    return slice(i * n, (i + 1) * n)


_DT_SIZE = {F32: 4, BF16: 2, I32: 4}


class Arena:

    def __init__(self, nc):
        self.nc = nc
        self.stack = [nc.sbuf_base]
        self.peak = nc.sbuf_base
        self.n = 0

    def push(self):
        self.stack.append(self.stack[-1])

    def pop(self):
        self.stack.pop()

    def take(self, name, shape, dtype):
        off = (self.stack[-1] + 63) // 64 * 64
        pp = 1
        for d in shape[1:]:
            pp *= d
        pp *= _DT_SIZE[dtype]
        self.n += 1
        t = self.nc.alloc_sbuf_tensor_at(f"{name}_{self.n}", shape, dtype,
                                         offset=off)
        self.stack[-1] = off + pp
        self.peak = max(self.peak, self.stack[-1])
        assert self.stack[-1] <= self.nc.sbuf_top, \
            f"SBUF overflow at {name}: {self.stack[-1]} > {self.nc.sbuf_top}"
        return t


def build(dbg=(), stop_after=None):
    nc = bass.Bass("TRN2", target_bir_lowering=False, debug=False)
    dbg = set(dbg)

    x_hbm = nc.dram_tensor("x", [NT, D], F32, kind="ExternalInput").ap()
    wT = {k: nc.dram_tensor(f"w{k}T", [D, D], F32, kind="ExternalInput").ap()
          for k in "qkvo"}
    bias = {k: nc.dram_tensor(f"b{k}", [1, D], F32, kind="ExternalInput").ap()
            for k in "qkvo"}
    wg = nc.dram_tensor("wg", [D, NE], F32, kind="ExternalInput").ap()
    cfc = nc.dram_tensor("cfc", [NE, 32, 128, D], F32, kind="ExternalInput").ap()
    cpj = nc.dram_tensor("cpj", [NE, DFF, D], F32, kind="ExternalInput").ap()
    tri_hbm = nc.dram_tensor("tri_rot", [128, 128], BF16, kind="ExternalInput").ap()
    one_hbm = nc.dram_tensor("ones_bf", [128, 128], BF16, kind="ExternalInput").ap()
    idb_hbm = nc.dram_tensor("id_bf", [128, 128], BF16, kind="ExternalInput").ap()
    idf_hbm = nc.dram_tensor("id_f", [128, 128], F32, kind="ExternalInput").ap()
    caus_hbm = nc.dram_tensor("caus_bf", [128, 128], BF16, kind="ExternalInput").ap()
    orow_hbm = nc.dram_tensor("onesrow_bf", [1, 2048], BF16, kind="ExternalInput").ap()
    orowf_hbm = nc.dram_tensor("onesrow_f", [1, 128], F32, kind="ExternalInput").ap()
    ocol_hbm = nc.dram_tensor("onescol_f", [128, 1], F32, kind="ExternalInput").ap()
    tok_hbm = nc.dram_tensor("tokid_f", [128, OTILE], F32, kind="ExternalInput").ap()

    y_out = nc.dram_tensor("y", [T, D], F32, kind="ExternalOutput").ap()

    xn_pad = nc.dram_tensor("xn_pad", [T + 1, D], BF16).ap()
    exp_x = nc.dram_tensor("exp_x", [NSLOT, D], BF16).ap()
    exp_out = nc.dram_tensor("exp_out", [NSLOT + 1, D], BF16).ap()
    slot_tab = nc.dram_tensor("slot_tab", [NSLOT, 1], F32).ap()
    y_mid = nc.dram_tensor("y_mid", [T, D], BF16).ap()

    dbg_out = {}

    def dbgt(name, shape, dtype=F32):
        dbg_out[name] = nc.dram_tensor("dbg_" + name, shape, dtype,
                                       kind="ExternalOutput").ap()
        return dbg_out[name]

    with ExitStack() as top:
        e = top.enter_context
        p = Prog(nc, top)
        S = p.sems
        A = Arena(nc)

        c_tri = A.take("c_tri", [128, 128], BF16)
        c_one = A.take("c_one", [128, 128], BF16)
        c_idb = A.take("c_idb", [128, 128], BF16)
        c_idf = A.take("c_idf", [128, 128], F32)
        c_caus = A.take("c_caus", [128, 128], BF16)
        c_or = A.take("c_or", [1, 2048], BF16)
        c_orf = A.take("c_orf", [1, 128], F32)
        c_oc = A.take("c_oc", [128, 1], F32)
        c_tok = A.take("c_tok", [128, OTILE], F32)
        c_eps = A.take("c_eps", [128, 1], F32)
        zrow = A.take("zrow", [1, D], BF16)
        b_sb = {k: A.take(f"b_{k}", [1, D], BF16) for k in "qkvo"}
        wg_f = A.take("wg_f", [128, NE * 8], F32)
        wg_b = A.take("wg_b", [128, NE * 8], BF16)
        g_all = A.take("g_all", [128, NTILE * NE], F32)
        idx1 = A.take("idx1", [128, NTILE], F32)
        idx2 = A.take("idx2", [128, NTILE], F32)
        p1 = A.take("p1", [128, NTILE], F32)
        p2 = A.take("p2", [128, NTILE], F32)
        g0i = A.take("g0i", [128, OTILE], I32)
        g1i = A.take("g1i", [128, OTILE], I32)
        tabi4 = [A.take(f"tabi{i}", [128, 14], I32) for i in range(4)]
        mhat = A.take("mhat", [128, 1], F32)
        nrm = A.take("nrm", [1, 8], F32)

        ps_a = e(nc.psum_tensor("ps_a", [128, 512], F32))
        ps_b = e(nc.psum_tensor("ps_b", [128, 512], F32))
        ps_s = e(nc.psum_tensor("ps_s", [128, 2048], F32))
        ps_t = e(nc.psum_tensor("ps_t", [128, 512], BF16))
        ps_o = e(nc.psum_tensor("ps_o", [128, 512], F32))

        block = e(nc.Block())

        def finish_early():
            yz = g_all
            p.op("vector", lambda E: E.memset(yz[:, :], 0.0))
            vz = p.mark("vector", "V")
            p.wait("sync", "V", vz)
            p.wait("sync", "ST", S["ST"].n)
            for t in range(OTILE):
                p.dma("sync", "ST", y_out[ts(t, 128), :], yz[:, :])
            p.wait("sync", "ST", S["ST"].n)
            for eng, meth in (("sync", block.sync), ("gpsimd", block.gpsimd),
                              ("tensor", block.tensor), ("vector", block.vector),
                              ("scalar", block.scalar)):
                def runner(E, ops=p.q[eng]):
                    for f in ops:
                        f(E)
                meth(runner)

        for dst, src in ((c_tri, tri_hbm), (c_one, one_hbm), (c_idb, idb_hbm),
                         (c_idf, idf_hbm), (c_caus, caus_hbm), (c_or, orow_hbm),
                         (c_orf, orowf_hbm), (c_oc, ocol_hbm), (c_tok, tok_hbm)):
            p.dma("sync", "LD", dst[:, :], src)
        p.dma("sync", "LD", wg_f[:, :].rearrange("p (dc e) -> p dc e", e=NE),
              wg.rearrange("(dc p) e -> p dc e", p=128))
        for k in "qkvo":
            p.dma("gpsimd", "LG", b_sb[k][:, :], bias[k])
        ld_const = S["LD"].n
        lg_bias = S["LG"].n
        p.op("vector", lambda E: E.memset(c_eps[:, :], 1e-5))
        p.op("vector", lambda E: E.memset(zrow[:, :], 0.0))
        p.wait("vector", "LD", ld_const)
        p.op("vector", lambda E: E.tensor_copy(wg_b[:, :], wg_f[:, :]))
        v_const = p.mark("vector", "V")
        p.wait("sync", "V", v_const)
        p.dma("sync", "LD", xn_pad[T:T + 1, :], zrow[:, :])
        p.dma("sync", "LD", exp_out[NSLOT:NSLOT + 1, :], zrow[:, :])

        A.push()
        qt_sb = [A.take(f"qt{i}", [128, T], BF16) for i in range(8)]
        kt_sb = [A.take(f"kt{i}", [128, T], BF16) for i in range(8)]
        vp = [A.take(f"vp{i}", [128, HEADS * 65], BF16) for i in range(OTILE)]
        xnt = A.take("xnt", [128, 8 * 2048], BF16)
        wbuf = A.take("wbuf", [128, 8 * 1024], BF16)
        gxb = [A.take(f"gxb{i}", [128, D], BF16) for i in range(2)]
        sidxi = [A.take(f"sidxi{k}", [128, OTILE], I32) for k in range(2)]
        svals = [A.take(f"svals{k}", [128, OTILE], F32) for k in range(2)]
        tabf = A.take("tabf", [128, SCOLS], F32)
        tini = A.take("tini", [128, SCOLS], F32)

        lg_w = {}
        lg_w["q"] = p.dma("gpsimd", "LG",
                          wbuf[:, :].rearrange("p (dc i) -> p dc i", i=D),
                          wT["q"].rearrange("(dc p) i -> p dc i", p=128))
        A.push()
        NXB = 4
        xb = [A.take(f"xb{i}", [128, D], F32) for i in range(NXB)]
        xnb = [A.take(f"xnb{i}", [128, D], BF16) for i in range(NXB)]
        sq1 = A.take("sq1", [128, D], F32)
        colr = A.take("colr", [128, 8 * NXB], F32)
        xTb = [A.take(f"xTb{i}", [128, D], BF16) for i in range(2)]

        ps_t3 = ps_t[:, :].rearrange("p (dc t) -> p dc t", t=128)
        ps_o3 = ps_o[:, :].rearrange("p (dc t) -> p dc t", t=128)
        st_own = [0] * OTILE
        v_lastread = [0] * NXB
        a_lastread = [0] * NXB
        t_xnb_read = [0] * NXB
        g_xnb_read = [0] * NXB
        a_evA_prev = 0
        v_evB_prev = 0
        v_g_prev = 0
        v_chunk0 = 0
        def pcol(b, k):
            return colr[:, b * 8 + k:b * 8 + k + 1]

        def emit_tile_tail(i, v_xnb):
            nonlocal a_evA_prev, v_evB_prev, v_g_prev, v_chunk0
            b = i % NXB
            own = i < OTILE
            if own:
                p.wait("gpsimd", "V", v_xnb)
                st_own[i] = p.dma("gpsimd", "LG", xn_pad[ts(i, 128), :],
                                  xnb[b][:, :])
                g_xnb_read[b] = st_own[i]
            if own:
                dst = xnt[:, :].rearrange("p (dc t) -> p dc t", t=2048)
                dstA = dst[:, 0:4, i * 128:(i + 1) * 128]
                dstB = dst[:, 4:8, i * 128:(i + 1) * 128]
            else:
                xv = xTb[i % 2]
                dstA = xv[:, 0:512].rearrange("p (dc t) -> p dc t", t=128)
                dstB = xv[:, 512:1024].rearrange("p (dc t) -> p dc t", t=128)
            p.wait("tensor", "V", v_xnb)
            if i == 0:
                p.wait("tensor", "LD", ld_const)
            if a_evA_prev:
                p.wait("tensor", "A", a_evA_prev)

            def trg(E, b, lo, dstp):
                for k2 in range(4):
                    if lo == 0:
                        last = E.transpose(
                            dstp[:, k2 * 128:(k2 + 1) * 128],
                            xnb[b][:, (lo + k2) * 128:(lo + k2 + 1) * 128],
                            c_idb[:, :])
                    else:
                        last = E.matmul(
                            dstp[:, k2 * 128:(k2 + 1) * 128],
                            xnb[b][:, (lo + k2) * 128:(lo + k2 + 1) * 128],
                            c_idb[:, :], start=True, stop=True)
                return last
            t_trA = p.op_inc("tensor", "T", lambda E, b=b: trg(E, b, 0, ps_t))
            if v_evB_prev:
                p.wait("tensor", "V", v_evB_prev)
            t_trB = p.op_inc("tensor", "T", lambda E, b=b: trg(E, b, 4, ps_o))
            p.wait("scalar", "T", t_trA)
            p.op("scalar", lambda E, dstA=dstA:
                 E.activation(dstA, ps_t3[:, 0:4, :], AF.Copy))
            a_evA = p.mark("scalar", "A")
            p.wait("vector", "T", t_trB)
            p.op("vector", lambda E, dstB=dstB:
                 E.tensor_copy(dstB, ps_o3[:, 0:4, :]))
            v_evB = p.mark("vector", "V")
            a_evA_prev, v_evB_prev = a_evA, v_evB
            if own and i == OTILE - 1:
                v_chunk0 = v_evB
            p.wait("tensor", "A", a_evA)
            p.wait("tensor", "V", v_evB)
            if i == 0:
                p.wait("tensor", "V", v_const)
            if v_g_prev:
                p.wait("tensor", "V", v_g_prev)
            if own:
                src, soff, sstr = xnt[:, :], i * 128, 2048
            else:
                src, soff, sstr = xTb[i % 2][:, :], 0, 128

            def mmg(E, src=src, soff=soff, sstr=sstr):
                for dc in range(8):
                    last = E.matmul(
                        ps_a[:, 0:8],
                        src[:, dc * sstr + soff: dc * sstr + soff + 128],
                        wg_b[:, dc * 8:(dc + 1) * 8],
                        start=(dc == 0), stop=(dc == 7))
                return last
            t_g = p.op_inc("tensor", "T", mmg)
            t_xnb_read[b] = t_g
            p.wait("vector", "T", t_g)
            p.op("vector", lambda E, i=i:
                 E.tensor_copy(g_all[:, i * 8:(i + 1) * 8], ps_a[:, 0:8]))
            v_g_prev = p.mark("vector", "V")

        for i0 in range(0, NTILE, 2):
            tiles = (i0, i0 + 1)
            lds = {}
            for i in tiles:
                b = i % NXB
                if v_lastread[b]:
                    p.wait("sync", "V", v_lastread[b])
                    p.wait("sync", "A", a_lastread[b])
                lds[i] = p.dma("sync", "LD", xb[b][:, :], x_hbm[ts(i, 128), :])
            for i in tiles:
                b = i % NXB
                p.wait("scalar", "LD", lds[i])
                p.op("scalar", lambda E, b=b: E.activation(
                    sq1[:, :], xb[b][:, :], AF.Square, accum_out=pcol(b, 0)))
            a_sq = p.mark("scalar", "A")
            for i in tiles:
                b = i % NXB
                p.wait("vector", "LD", lds[i])
                p.op("vector", lambda E, b=b:
                     E.reduce_sum(pcol(b, 5), xb[b][:, :], axis=AX.X), nd=True)
            p.dr("vector")
            for i in tiles:
                p.op("vector", lambda E, b=i % NXB:
                     E.tensor_scalar_mul(pcol(b, 1), pcol(b, 5), 1.0 / D),
                     nd=True)
            p.dr("vector")
            for i in tiles:
                p.op("vector", lambda E, b=i % NXB:
                     E.tensor_mul(pcol(b, 2), pcol(b, 1), pcol(b, 1)), nd=True)
            p.wait("vector", "A", a_sq)
            p.dr("vector")
            for i in tiles:
                p.op("vector", lambda E, b=i % NXB: E.scalar_tensor_tensor(
                    pcol(b, 3), pcol(b, 0), 1.0 / D, pcol(b, 2),
                    op0=ALU.mult, op1=ALU.subtract), nd=True)
            v_var = p.mark("vector", "V")
            p.wait("scalar", "V", v_var)
            for i in tiles:
                p.op("scalar", lambda E, b=i % NXB:
                     E.activation(pcol(b, 2), pcol(b, 3), AF.Sqrt,
                                  bias=c_eps[:, :]))
            a_sd = p.mark("scalar", "A")
            p.wait("vector", "A", a_sd)
            for i in tiles:
                b = i % NXB
                a_lastread[b] = a_sd
                p.op("vector", lambda E, b=b:
                     E.reciprocal(pcol(b, 4), pcol(b, 2)), nd=True)
            for i in tiles:
                b = i % NXB
                if t_xnb_read[b]:
                    p.wait("vector", "T", t_xnb_read[b])
                    p.wait("vector", "LG", g_xnb_read[b])
            p.dr("vector")
            for i in tiles:
                p.op("vector", lambda E, b=i % NXB:
                     E.tensor_scalar(xnb[b][:, :], xb[b][:, :],
                                     pcol(b, 1), pcol(b, 4),
                                     op0=ALU.subtract, op1=ALU.mult), nd=True)
            v_xnb = p.mark("vector", "V")
            for i in tiles:
                v_lastread[i % NXB] = v_xnb
            for i in tiles:
                emit_tile_tail(i, v_xnb)

        st_own_last = st_own[OTILE - 1]
        v_g_done = v_g_prev
        t_g_done = S["T"].n
        v_p1 = S["V"].n
        a_p1 = S["A"].n

        if stop_after == "p1":
            finish_early()
            return nc, dbg_out
        A.pop()

        A.push()

        def proj_qk(dst_tiles, wkey, bias_row, t_sem_name):
            p.wait("tensor", "LG", lg_w[wkey])
            p.wait("tensor", "V", v_g_done)
            p.wait("tensor", "A", a_evA_prev)
            p.wait("tensor", "LG", lg_bias)
            last = 0
            evs = []
            for ic in range(8):
                for tc in range(4):
                    bank = ps_a if (ic * 4 + tc) % 2 == 0 else ps_b
                    if len(evs) >= 2:
                        p.wait("tensor", "V", evs[-2])

                    def mm(E, ic=ic, tc=tc, bank=bank, w=wkey, br=bias_row):
                        E.matmul(bank[:, :], br[0:1, ts(ic, 128)],
                                 c_or[0:1, ts(tc, 512)], start=True, stop=False)
                        for dc in range(8):
                            last = E.matmul(
                                bank[:, :],
                                wbuf[:, dc * 1024 + ic * 128: dc * 1024 + (ic + 1) * 128],
                                xnt[:, dc * 2048 + tc * 512: dc * 2048 + tc * 512 + 512],
                                start=False, stop=(dc == 7))
                        return last
                    t_mm = p.op_inc("tensor", "T", mm)
                    p.wait("vector", "T", t_mm)
                    p.op("vector", lambda E, ic=ic, tc=tc, bank=bank, dt=dst_tiles:
                         E.tensor_copy(dt[ic][:, ts(tc, 512)], bank[:, :]))
                    evs.append(p.mark("vector", "V"))
            return evs[-1]

        v_q = proj_qk(qt_sb, "q", b_sb["q"], "T")
        t_q_done = S["T"].n
        p.wait("gpsimd", "T", t_q_done)
        lg_w["k"] = p.dma("gpsimd", "LG",
                          wbuf[:, :].rearrange("p (dc i) -> p dc i", i=D),
                          wT["k"].rearrange("(dc p) i -> p dc i", p=128))
        v_k = proj_qk(kt_sb, "k", b_sb["k"], "T")
        t_k_done = S["T"].n
        p.wait("gpsimd", "T", t_k_done)
        lg_w["v"] = p.dma("gpsimd", "LG",
                          wbuf[:, :].rearrange("p (dc i) -> p dc i", i=D),
                          wT["v"].rearrange("(dc p) i -> p dc i", p=128))

        p.wait("tensor", "LG", lg_w["v"])
        for tt in range(OTILE):
            p.op("vector", lambda E, tt=tt: E.memset(
                vp[tt][:, :].rearrange("p (h c) -> p h c", c=65)[:, :, 64:65],
                1.0))
        v_vevs = []
        for tt in range(OTILE):
            for ic2 in range(2):
                bank = ps_a if (tt * 2 + ic2) % 2 == 0 else ps_b
                if len(v_vevs) >= 2:
                    p.wait("tensor", "V", v_vevs[-2])

                def mmv(E, tt=tt, ic2=ic2, bank=bank):
                    E.matmul(bank[:, :], c_or[0:1, ts(tt, 128)],
                             b_sb["v"][0:1, ts(ic2, 512)], start=True, stop=False)
                    for dc in range(8):
                        last = E.matmul(
                            bank[:, :],
                            xnt[:, dc * 2048 + tt * 128: dc * 2048 + (tt + 1) * 128],
                            wbuf[:, dc * 1024 + ic2 * 512: dc * 1024 + ic2 * 512 + 512],
                            start=False, stop=(dc == 7))
                    return last
                t_mm = p.op_inc("tensor", "T", mmv)
                p.wait("vector", "T", t_mm)
                p.op("vector", lambda E, tt=tt, ic2=ic2, bank=bank:
                     E.tensor_copy(
                         vp[tt][:, :].rearrange("p (h c) -> p h c", c=65)
                         [:, ic2 * 8:(ic2 + 1) * 8, 0:64],
                         bank[:, :].rearrange("p (h c) -> p h c", c=64)))
                v_vevs.append(p.mark("vector", "V"))
        v_qkv_done = v_vevs[-1]
        t_qkv_done = S["T"].n
        if "qkv" in dbg:
            oq = dbgt("qt", [1024, 2048], BF16)
            ok_ = dbgt("kt", [1024, 2048], BF16)
            ov = dbgt("v", [2048, HEADS * 65], BF16)
            p.wait("scalar", "V", v_qkv_done)
            for i in range(8):
                p.dma("scalar", "ST", oq[ts(i, 128), :], qt_sb[i][:, :])
                p.dma("scalar", "ST", ok_[ts(i, 128), :], kt_sb[i][:, :])
            for i in range(OTILE):
                p.dma("scalar", "ST", ov[ts(i, 128), :], vp[i][:, :])
        if stop_after == "qkv":
            finish_early()
            return nc, dbg_out

        if "g" in dbg:
            od = dbgt("g", [128, NTILE * NE])
            p.wait("scalar", "V", v_g_done)
            p.dma("scalar", "ST", od, g_all[:, :])

        A.push()
        r_v1 = A.take("r_v1", [128, NTILE], F32)
        r_v2 = A.take("r_v2", [128, NTILE], F32)
        r_eq = A.take("r_eq", [128, NTILE], F32)
        r_t = A.take("r_t", [128, NTILE], F32)
        r_big = A.take("r_big", [128, NTILE], F32)
        g2 = A.take("g2", [128, NTILE * NE], F32)
        zf = A.take("zf", [128, NTILE], F32)
        s12 = A.take("s12", [128, NTILE], F32)

        g3 = g_all[:, :].rearrange("p (t e) -> p t e", e=NE)
        g23 = g2[:, :].rearrange("p (t e) -> p t e", e=NE)
        p.op("vector", lambda E: E.memset(r_big[:, :], SBIG))
        p.op("vector", lambda E: E.memset(zf[:, :], 0.0))
        p.op("vector", lambda E: E.reduce_max(r_v1[:, :], g3, axis=AX.X))
        p.op("vector", lambda E: E.memset(idx1[:, :], SBIG))
        for ee in range(NE):
            p.op("vector", lambda E, ee=ee:
                 E.tensor_tensor(r_eq[:, :], g3[:, :, ee], r_v1[:, :], op=ALU.is_equal))
            p.op("vector", lambda E, ee=ee: E.scalar_tensor_tensor(
                r_t[:, :], r_eq[:, :], -(SBIG - ee), r_big[:, :],
                op0=ALU.mult, op1=ALU.add))
            p.op("vector", lambda E: E.tensor_tensor(idx1[:, :], idx1[:, :],
                                                     r_t[:, :], op=ALU.min))
        for ee in range(NE):
            p.op("vector", lambda E, ee=ee: E.tensor_single_scalar(
                r_eq[:, :], idx1[:, :], float(ee), op=ALU.is_equal))
            p.op("vector", lambda E, ee=ee: E.scalar_tensor_tensor(
                g23[:, :, ee], r_eq[:, :], -1e30, g3[:, :, ee],
                op0=ALU.mult, op1=ALU.add))
        p.op("vector", lambda E: E.reduce_max(r_v2[:, :], g23, axis=AX.X))
        p.op("vector", lambda E: E.memset(idx2[:, :], SBIG))
        for ee in range(NE):
            p.op("vector", lambda E, ee=ee:
                 E.tensor_tensor(r_eq[:, :], g23[:, :, ee], r_v2[:, :], op=ALU.is_equal))
            p.op("vector", lambda E, ee=ee: E.scalar_tensor_tensor(
                r_t[:, :], r_eq[:, :], -(SBIG - ee), r_big[:, :],
                op0=ALU.mult, op1=ALU.add))
            p.op("vector", lambda E: E.tensor_tensor(idx2[:, :], idx2[:, :],
                                                     r_t[:, :], op=ALU.min))
        p.op("vector", lambda E: E.tensor_sub(s12[:, :], r_v1[:, :], r_v2[:, :]))
        v_rt = p.mark("vector", "V")
        p.wait("scalar", "V", v_rt)
        p.op("scalar", lambda E: E.activation(p1[:, :], s12[:, :], AF.Sigmoid))
        p.op("scalar", lambda E: E.activation(p2[:, :], s12[:, :], AF.Sigmoid,
                                              scale=-1.0))
        a_probs = p.mark("scalar", "A")

        if "route" in dbg:
            for nm, buf in (("idx1", idx1), ("idx2", idx2), ("p1", p1), ("p2", p2)):
                od = dbgt(nm, [128, NTILE])
                p.wait("scalar", "A", a_probs)
                p.wait("scalar", "V", v_rt)
                p.dma("scalar", "ST", od, buf[:, :])

        idxT = [A.take(f"idxT{k}", [128, 128], F32) for k in range(2)]
        m_ke = A.take("m_ke", [128, 128], F32)
        incl = A.take("incl", [128, 128], F32)
        plane = A.take("plane", [128, 128], F32)
        rsel = [A.take(f"rsel{k}", [128, 128], F32) for k in range(2)]
        rtt = [A.take(f"rtt{k}", [128, 128], F32) for k in range(2)]
        totb = A.take("totb", [128, 2 * NE], BF16)
        carr = A.take("carr", [128, 2 * NE], F32)

        p.wait("tensor", "V", v_rt)
        for k, src in enumerate((idx1, idx2)):
            t_tr = p.op_inc("tensor", "T", lambda E, s=src:
                            E.transpose(ps_a[:, 0:128], s[:, :], c_idf[:, :]))
            p.wait("vector", "T", t_tr)
            p.op("vector", lambda E, k=k: E.tensor_copy(idxT[k][:, :], ps_a[:, 0:128]))
            v_m = p.mark("vector", "V")
            p.wait("tensor", "V", v_m)
        v_idxT = p.mark("vector", "V")

        for k in range(2):
            for ee in range(NE):
                p.op("vector", lambda E, k=k, ee=ee: E.tensor_single_scalar(
                    m_ke[:, :], idxT[k][:, :], float(ee), op=ALU.is_equal))
                p.op("vector", lambda E: E.tensor_tensor_scan(
                    incl[:, :], m_ke[:, :], zf[:, 0:128], 0.0,
                    op0=ALU.add, op1=ALU.add))
                p.op("vector", lambda E, k=k, ee=ee: E.tensor_copy(
                    totb[:, k * NE + ee:k * NE + ee + 1], incl[:, 127:128]))
        v_tot = p.mark("vector", "V")
        p.wait("tensor", "V", v_tot)

        def mm_car(E):
            E.matmul(ps_o[:, 0:8], c_tri[:, :], totb[:, 0:NE], start=True, stop=True)
            return E.matmul(ps_o[:, 8:16], c_tri[:, :], totb[:, NE:2 * NE],
                            start=True, stop=False)

        def mm_car2(E):
            return E.matmul(ps_o[:, 8:16], c_one[:, :], totb[:, 0:NE],
                            start=False, stop=True)
        t_car = p.op_inc("tensor", "T", lambda E: (mm_car(E), mm_car2(E))[1])
        p.wait("vector", "T", t_car)
        p.op("vector", lambda E: E.tensor_copy(carr[:, 0:NE], ps_o[:, 0:8]))
        p.op("vector", lambda E: E.tensor_copy(carr[:, NE:2 * NE], ps_o[:, 8:16]))
        for k in range(2):
            p.op("vector", lambda E, k=k: E.memset(rsel[k][:, :], 0.0))
            for ee in range(NE):
                p.op("vector", lambda E, k=k, ee=ee: E.tensor_single_scalar(
                    m_ke[:, :], idxT[k][:, :], float(ee), op=ALU.is_equal))
                p.op("vector", lambda E: E.tensor_tensor_scan(
                    incl[:, :], m_ke[:, :], zf[:, 0:128], 0.0,
                    op0=ALU.add, op1=ALU.add))
                p.op("vector", lambda E, k=k, ee=ee: E.scalar_tensor_tensor(
                    plane[:, :], incl[:, :], carr[:, k * NE + ee:k * NE + ee + 1],
                    m_ke[:, :], op0=ALU.add, op1=ALU.subtract))
                p.op("vector", lambda E: E.tensor_mul(plane[:, :], plane[:, :],
                                                      m_ke[:, :]))
                p.op("vector", lambda E, k=k: E.tensor_add(rsel[k][:, :],
                                                           rsel[k][:, :], plane[:, :]))
        v_rsel = p.mark("vector", "V")
        p.wait("tensor", "V", v_rsel)
        for k in range(2):
            t_tr = p.op_inc("tensor", "T", lambda E, k=k:
                            E.transpose(ps_a[:, 0:128], rsel[k][:, :], c_idf[:, :]))
            p.wait("vector", "T", t_tr)
            p.op("vector", lambda E, k=k: E.tensor_copy(rtt[k][:, :], ps_a[:, 0:128]))
            v_m = p.mark("vector", "V")
            p.wait("tensor", "V", v_m)
        v_rtt = p.mark("vector", "V")

        if "rank" in dbg:
            for k in range(2):
                od = dbgt(f"rtt{k}", [128, 128])
                p.wait("scalar", "V", v_rtt)
                p.dma("scalar", "ST", od, rtt[k][:, :])
            p.dma("scalar", "ST", dbgt("rsel0", [128, 128]), rsel[0][:, :])
            p.dma("scalar", "ST", dbgt("idxT0", [128, 128]), idxT[0][:, :])
            p.dma("scalar", "ST", dbgt("carr", [128, 16]), carr[:, :])

        if stop_after == "rank":
            finish_early()
            return nc, dbg_out

        oloc = [A.take(f"oloc{k}", [16, 128], F32) for k in range(2)]
        lincl = A.take("lincl", [16, 128], F32)
        lm = A.take("lm", [16, 128], F32)
        lpl = A.take("lpl", [16, 128], F32)
        lrsel = [A.take(f"lrsel{k}", [16, 128], F32) for k in range(2)]
        lrtt = [A.take(f"lrtt{k}", [128, OTILE], F32) for k in range(2)]
        ltotb = A.take("ltotb", [16, 2 * NE], BF16)
        lcarr = A.take("lcarr", [16, 2 * NE], F32)

        for k, src in enumerate((idx1, idx2)):
            t_tr = p.op_inc("tensor", "T", lambda E, s=src:
                            E.transpose(ps_a[0:16, 0:128], s[:, 0:OTILE],
                                        c_idf[:, :]))
            p.wait("vector", "T", t_tr)
            p.op("vector", lambda E, k=k: E.tensor_copy(oloc[k][:, :], ps_a[0:16, 0:128]))
            v_m = p.mark("vector", "V")
            p.wait("tensor", "V", v_m)
        v_oloc = p.mark("vector", "V")
        for k in range(2):
            for ee in range(NE):
                p.op("vector", lambda E, k=k, ee=ee: E.tensor_single_scalar(
                    lm[:, :], oloc[k][:, :], float(ee), op=ALU.is_equal))
                p.op("vector", lambda E: E.tensor_tensor_scan(
                    lincl[:, :], lm[:, :], zf[0:16, 0:128], 0.0,
                    op0=ALU.add, op1=ALU.add))
                p.op("vector", lambda E, k=k, ee=ee: E.tensor_copy(
                    ltotb[:, k * NE + ee:k * NE + ee + 1], lincl[:, 127:128]))
        v_ltot = p.mark("vector", "V")
        p.wait("tensor", "V", v_ltot)

        def mm_lcar(E):
            E.matmul(ps_o[0:16, 0:8], c_tri[0:16, 0:16], ltotb[:, 0:NE],
                     start=True, stop=True)
            E.matmul(ps_o[0:16, 8:16], c_tri[0:16, 0:16], ltotb[:, NE:2 * NE],
                     start=True, stop=False)
            return E.matmul(ps_o[0:16, 8:16], c_one[0:16, 0:16], ltotb[:, 0:NE],
                            start=False, stop=True)
        t_lcar = p.op_inc("tensor", "T", mm_lcar)
        p.wait("vector", "T", t_lcar)
        p.op("vector", lambda E: E.tensor_copy(lcarr[:, 0:NE], ps_o[0:16, 0:8]))
        p.op("vector", lambda E: E.tensor_copy(lcarr[:, NE:2 * NE], ps_o[0:16, 8:16]))
        for k in range(2):
            p.op("vector", lambda E, k=k: E.memset(lrsel[k][:, :], 0.0))
            for ee in range(NE):
                p.op("vector", lambda E, k=k, ee=ee: E.tensor_single_scalar(
                    lm[:, :], oloc[k][:, :], float(ee), op=ALU.is_equal))
                p.op("vector", lambda E: E.tensor_tensor_scan(
                    lincl[:, :], lm[:, :], zf[0:16, 0:128], 0.0,
                    op0=ALU.add, op1=ALU.add))
                p.op("vector", lambda E, k=k, ee=ee: E.scalar_tensor_tensor(
                    lpl[:, :], lincl[:, :], lcarr[:, k * NE + ee:k * NE + ee + 1],
                    lm[:, :], op0=ALU.add, op1=ALU.subtract))
                p.op("vector", lambda E: E.tensor_mul(lpl[:, :], lpl[:, :], lm[:, :]))
                p.op("vector", lambda E, k=k: E.tensor_add(lrsel[k][:, :],
                                                           lrsel[k][:, :], lpl[:, :]))
        v_lrsel = p.mark("vector", "V")
        p.wait("tensor", "V", v_lrsel)
        for k in range(2):
            t_tr = p.op_inc("tensor", "T", lambda E, k=k:
                            E.transpose(ps_a[:, 0:OTILE], lrsel[k][:, :],
                                        c_idf[0:16, 0:16]))
            p.wait("vector", "T", t_tr)
            p.op("vector", lambda E, k=k: E.tensor_copy(lrtt[k][:, :],
                                                        ps_a[:, 0:OTILE]))
            v_m = p.mark("vector", "V")
            p.wait("tensor", "V", v_m)
        v_lrtt = p.mark("vector", "V")

        dropf = A.take("dropf", [128, OTILE], F32)
        oflf = A.take("oflf", [128, OTILE], F32)
        basei = A.take("basei", [128, OTILE], F32)
        sidxf = A.take("sidxf", [128, OTILE], F32)
        tinv = A.take("tinv", [128, OTILE], F32)
        gtmp = A.take("gtmp", [128, OTILE], F32)

        p.op("vector", lambda E: E.tensor_scalar(
            tinv[:, :], c_tok[:, :], -1.0, float(T), op0=ALU.mult, op1=ALU.add))
        p.op("vector", lambda E: E.memset(tini[:, :], float(T)))
        for k in range(2):
            own = (idx1 if k == 0 else idx2)[:, 0:OTILE]
            p.op("vector", lambda E, k=k: E.tensor_single_scalar(
                dropf[:, :], rtt[k][:, 0:OTILE], float(CAP), op=ALU.is_ge))
            p.op("vector", lambda E, k=k: E.tensor_single_scalar(
                oflf[:, :], lrtt[k][:, :], float(CLOC), op=ALU.is_ge))
            p.op("vector", lambda E, k=k, own=own: E.scalar_tensor_tensor(
                basei[:, :], own, float(CLOC), lrtt[k][:, :],
                op0=ALU.mult, op1=ALU.add))
            p.op("vector", lambda E: E.scalar_tensor_tensor(
                sidxf[:, :], oflf[:, :], BIG, basei[:, :],
                op0=ALU.mult, op1=ALU.add))
            p.op("vector", lambda E, k=k: E.tensor_copy(sidxi[k][:, :], sidxf[:, :]))
            p.op("vector", lambda E: E.tensor_mul(gtmp[:, :], dropf[:, :], tinv[:, :]))
            p.op("vector", lambda E, k=k: E.tensor_add(svals[k][:, :],
                                                       gtmp[:, :], c_tok[:, :]))
            p.op("vector", lambda E: E.tensor_scalar(
                gtmp[:, :], basei[:, :], -1.0, float(NSLOT), op0=ALU.mult, op1=ALU.add))
            p.op("vector", lambda E: E.tensor_mul(gtmp[:, :], gtmp[:, :], oflf[:, :]))
            p.op("vector", lambda E: E.tensor_add(gtmp[:, :], gtmp[:, :], basei[:, :]))
            gi_sb = g0i if k == 0 else g1i
            p.op("vector", lambda E, gi_sb=gi_sb: E.tensor_copy(gi_sb[:, :], gtmp[:, :]))
        v_scati = p.mark("vector", "V")

        if stop_after == "scatidx":
            finish_early()
            return nc, dbg_out

        p.wait("gpsimd", "V", v_scati)
        lg_tini = p.dma("gpsimd", "LG",
                        slot_tab.rearrange("(p f) one -> p (f one)", p=128),
                        tini[:, :])
        p.wait("gpsimd", "LG", lg_tini)
        for k in range(2):
            for t in range(OTILE):
                p.iscatter("gpsimd", "LG", slot_tab[:, :],
                           svals[k][:, t:t + 1], sidxi[k][:, t:t + 1], NSLOT - 1)
        lg_scat = S["LG"].n
        p.wait("gpsimd", "LG", lg_scat)
        lg_tab = p.dma("gpsimd", "LG", tabf[:, :],
                       slot_tab.rearrange("(p f) one -> p (f one)", p=128))
        p.wait("vector", "LG", lg_tab)
        for i4 in range(4):
            p.op("vector", lambda E, i4=i4: E.tensor_copy(
                tabi4[i4][:, :], tabf[:, i4 * 14:(i4 + 1) * 14]))
        v_tabi = p.mark("vector", "V")

        if "tab" in dbg:
            od = dbgt("tab", [128, SCOLS])
            p.wait("scalar", "V", v_tabi)
            p.dma("scalar", "ST", od, tabf[:, :])
            od2 = dbgt("lrtt0", [128, OTILE])
            p.dma("scalar", "ST", od2, lrtt[0][:, :])
            od3 = dbgt("g0", [128, OTILE], I32)
            p.dma("scalar", "ST", od3, g0i[:, :])
            od4 = dbgt("g1", [128, OTILE], I32)
            p.dma("scalar", "ST", od4, g1i[:, :])

        if stop_after == "route":
            finish_early()
            return nc, dbg_out
        A.pop()
        A.pop()

        disp_st = [0] * SCOLS
        p.wait("gpsimd", "V", v_tabi)
        p.wait("gpsimd", "LG", st_own_last)
        for f in range(SCOLS):
            b = f % 2
            if f >= 2:
                p.wait("gpsimd", "LG", disp_st[f - 2])
            lg_g = p.igather("gpsimd", "LG", gxb[b][:, :], xn_pad[:, :],
                             tabi4[f // 14][:, f % 14:f % 14 + 1], T)
            p.wait("gpsimd", "LG", lg_g)
            disp_st[f] = p.dma(
                "gpsimd", "LG",
                exp_x.rearrange("(pp f) d -> pp f d", f=SCOLS)[:, f, :],
                gxb[b][:, :])
        lg_disp = S["LG"].n


        A.push()
        sq2 = A.take("sq2", [128, 2048], F32)
        pbuf = A.take("pbuf", [128, 2048], BF16)
        ybuf = [A.take(f"ybuf{i}", [128, D], BF16) for i in range(2)]
        lcol = A.take("lcol", [128, 10], F32)

        p.op("vector", lambda E: E.memset(nrm[:, :], 0.0))
        v_nz = p.mark("vector", "V")
        a_norm_sq = []
        for which, tiles_, slot in (("q", qt_sb, 0), ("k", kt_sb, 1)):
            for it in range(8):
                p.wait("scalar", "V", v_qkv_done if which == "k" else v_q)
                p.wait("scalar", "T", t_g_done)
                p.wait("scalar", "V", v_scati)
                p.op("scalar", lambda E, tiles_=tiles_, it=it:
                     E.activation(sq2[:, :], tiles_[it][:, :], AF.Square))
                a_s = p.mark("scalar", "A")
                for tc in range(4):
                    p.wait("tensor", "A", a_s)
                    p.wait("tensor", "V", v_nz)
                    t_n = p.op_inc("tensor", "T", lambda E, tc=tc:
                                   E.matmul(ps_b[0:1, 0:512], c_oc[:, :],
                                            sq2[:, ts(tc, 512)], start=True, stop=True))
                    p.wait("vector", "T", t_n)
                    p.op("vector", lambda E: E.reduce_max(nrm[0:1, 2:3],
                                                          ps_b[0:1, 0:512], axis=AX.X))
                    p.op("vector", lambda E, slot=slot: E.tensor_tensor(
                        nrm[0:1, slot:slot + 1], nrm[0:1, slot:slot + 1],
                        nrm[0:1, 2:3], op=ALU.max))
                    v_nz = p.mark("vector", "V")
        p.op("vector", lambda E: E.tensor_mul(nrm[0:1, 3:4], nrm[0:1, 0:1],
                                              nrm[0:1, 1:2]))
        v_nn = p.mark("vector", "V")
        p.wait("scalar", "V", v_nn)
        p.op("scalar", lambda E: E.activation(nrm[0:1, 4:5], nrm[0:1, 3:4], AF.Sqrt))
        a_nn = p.mark("scalar", "A")
        p.wait("vector", "A", a_nn)
        p.op("vector", lambda E: E.tensor_scalar_mul(nrm[0:1, 5:6],
                                                     nrm[0:1, 4:5], -0.125))
        v_mh = p.mark("vector", "V")
        p.wait("tensor", "V", v_mh)
        t_mh = p.op_inc("tensor", "T", lambda E:
                        E.matmul(ps_b[:, 0:1], c_orf[0:1, :], nrm[0:1, 5:6],
                                 start=True, stop=True))
        p.wait("vector", "T", t_mh)
        p.op("vector", lambda E: E.tensor_copy(mhat[:, :], ps_b[:, 0:1]))
        v_mhat = p.mark("vector", "V")
        if "attn" in dbg:
            p.wait("scalar", "V", v_mhat)
            p.dma("scalar", "ST", dbgt("mhat", [128, 1]), mhat[:, :])
            p.dma("scalar", "ST", dbgt("nrm", [1, 8]), nrm[:, :])
        p.wait("scalar", "V", v_mhat)

        ymid_st = [0] * OTILE
        a_exp_half = [0, 0]
        t_pv_half = [0, 0]
        a_o_prev = 0
        v_pm_hist = []
        unit = 0
        for qt in range(OTILE):
            nkb = qt + 1
            yb = ybuf[qt % 2]
            if qt >= 2:
                p.wait("scalar", "LD", ymid_st[qt - 2])
            for h in range(HEADS):
                til, base = h // 2, (h % 2) * 64
                qsl = qt_sb[til][base:base + 64, ts(qt, 128)]
                rl = lcol[:, 4 + h % 4:5 + h % 4]
                if nkb <= 8:
                    off = 1024 * (unit % 2)
                    chunks = [(unit % 2, 0, nkb)]
                else:
                    off = 0
                    chunks = [(0, 0, 8), (1, 8, nkb)]
                exp_marks = []
                for hf, k0, k1 in chunks:
                    p.wait("tensor", "A", a_exp_half[hf])

                    def mm_st(E, qsl=qsl, til=til, base=base, k0=k0, k1=k1,
                              off=off):
                        for kb in range(k0, k1):
                            last = E.matmul(
                                ps_s[:, off + kb * 128: off + (kb + 1) * 128],
                                kt_sb[til][base:base + 64,
                                           kb * 128:(kb + 1) * 128],
                                qsl, start=True, stop=True)
                        return last
                    t_s = p.op_inc("tensor", "T", mm_st)
                    p.wait("scalar", "T", max(t_s, t_pv_half[hf]))
                    if len(v_pm_hist) >= 2:
                        p.wait("scalar", "V", v_pm_hist[-2])
                    p.op("scalar", lambda E, k0=k0, k1=k1, off=off:
                         E.activation(pbuf[:, off + k0 * 128: off + k1 * 128],
                                      ps_s[:, off + k0 * 128: off + k1 * 128],
                                      AF.Exp, bias=mhat[:, :], scale=0.125))
                    a_exp = p.mark("scalar", "A")
                    a_exp_half[hf] = a_exp
                    exp_marks.append(a_exp)
                p.wait("vector", "A", exp_marks[-1])
                p.op("vector", lambda E, nkb=nkb, off=off: E.tensor_mul(
                    pbuf[:, off + (nkb - 1) * 128: off + nkb * 128],
                    pbuf[:, off + (nkb - 1) * 128: off + nkb * 128],
                    c_caus[:, :]))
                v_pm = p.mark("vector", "V")
                v_pm_hist.append(v_pm)
                for ci, (hf, k0, k1) in enumerate(chunks):
                    p.wait("tensor", "A", exp_marks[ci])
                    if ci == len(chunks) - 1:
                        p.wait("tensor", "V", v_pm)
                    if ci == 0:
                        p.wait("tensor", "A", a_o_prev)

                    def mm_pv(E, h=h, k0=k0, k1=k1, off=off, nkb=nkb):
                        for kb in range(k0, k1):
                            last = E.matmul(
                                ps_o[:, 0:65],
                                pbuf[:, off + kb * 128: off + (kb + 1) * 128],
                                vp[kb][:, h * 65:(h + 1) * 65],
                                start=(kb == 0), stop=(kb == nkb - 1))
                        return last
                    t_pv = p.op_inc("tensor", "T", mm_pv)
                    t_pv_half[hf] = t_pv
                p.wait("vector", "T", t_pv)
                p.op("vector", lambda E, rl=rl: E.reciprocal(rl, ps_o[:, 64:65]))
                v_rl = p.mark("vector", "V")
                p.wait("scalar", "T", t_pv)
                p.wait("scalar", "V", v_rl)
                p.op("scalar", lambda E, yb=yb, h=h, rl=rl: E.activation(
                    yb[:, ts(h, 64)], ps_o[:, 0:64], AF.Copy,
                    scale=rl))
                a_o = p.mark("scalar", "A")
                a_o_prev = a_o
                unit += 1
            p.wait("scalar", "A", a_o)
            ymid_st[qt] = p.dma("scalar", "LD", y_mid[ts(qt, 128), :], yb[:, :])
        ld_ymid = S["LD"].n
        t_attn_done = S["T"].n
        a_attn_done = S["A"].n
        v_attn_done = S["V"].n
        A.pop()
        A.pop()

        if "attn" in dbg:
            od = dbgt("attnY", [T, D])
            rb2 = A.take("at_rb", [128, D], BF16)
            rd2 = A.take("at_rd", [128, D], F32)
            for i in range(OTILE):
                p.wait("gpsimd", "LD", ld_ymid)
                lg = p.dma("gpsimd", "LG", rb2[:, :], y_mid[ts(i, 128), :])
                p.wait("vector", "LG", lg)
                p.op("vector", lambda E: E.tensor_copy(rd2[:, :], rb2[:, :]))
                vv = p.mark("vector", "V")
                p.wait("gpsimd", "V", vv)
                p.dma("gpsimd", "ST", od[ts(i, 128), :], rd2[:, :])

        if stop_after == "attn":
            finish_early()
            return nc, dbg_out

        attn_sb = A.take("attn_sb", [128, OTILE * D], BF16)

        A.push()
        ytb = A.take("ytb", [128, 8 * 2048], BF16)
        wo_sb = A.take("wo_sb", [128, 8 * 1024], BF16)
        p.wait("gpsimd", "T", t_attn_done)
        p.wait("gpsimd", "A", a_attn_done)
        p.wait("gpsimd", "V", v_attn_done)
        lg_wo = p.dma("gpsimd", "LG",
                      wo_sb[:, :].rearrange("p (dc i) -> p dc i", i=D),
                      wT["o"].rearrange("(dc p) i -> p dc i", p=128))
        p.wait("sync", "LD", ld_ymid)
        p.wait("sync", "T", t_attn_done)
        p.wait("sync", "LG", lg_disp)
        ldyT = 0
        for ic in range(8):
            ldyT = p.dmat("sync", "LD", ytb[:, ts(ic, 2048)],
                          y_mid[0:2048, ts(ic, 128)])
        p.wait("tensor", "LD", ldyT)
        p.wait("tensor", "LG", lg_wo)
        ev2 = []
        for t in range(OTILE):
            for jc in range(2):
                bank = ps_a if (t * 2 + jc) % 2 == 0 else ps_b
                if len(ev2) >= 2:
                    p.wait("tensor", "V", ev2[-2])

                def mm_yo(E, t=t, jc=jc, bank=bank):
                    E.matmul(bank[:, :], c_or[0:1, ts(t, 128)],
                             b_sb["o"][0:1, ts(jc, 512)], start=True, stop=False)
                    for ic in range(8):
                        last = E.matmul(
                            bank[:, :],
                            ytb[:, ic * 2048 + t * 128: ic * 2048 + (t + 1) * 128],
                            wo_sb[:, ic * 1024 + jc * 512: ic * 1024 + jc * 512 + 512],
                            start=False, stop=(ic == 7))
                    return last
                t_mm = p.op_inc("tensor", "T", mm_yo)
                p.wait("vector", "T", t_mm)
                p.op("vector", lambda E, t=t, jc=jc, bank=bank: E.tensor_copy(
                    attn_sb[:, t * D + jc * 512: t * D + jc * 512 + 512],
                    bank[:, :]))
                ev2.append(p.mark("vector", "V"))
        v_yproj = ev2[-1]
        t_yproj = S["T"].n
        if "yproj" in dbg:
            od = dbgt("attnout", [128, OTILE * D], BF16)
            p.wait("scalar", "V", v_yproj)
            p.dma("scalar", "ST", od, attn_sb[:, :])
            oy = dbgt("ymid", [T, D], BF16)
            for i in range(OTILE):
                p.dma("scalar", "ST", oy[ts(i, 128), :], y_mid[ts(i, 128), :])
        A.pop()
        if stop_after == "yproj":
            finish_early()
            return nc, dbg_out

        A.push()
        ht = A.take("ht", [128, 32 * CLOC], BF16)
        xtb = [A.take(f"xtb{i}", [128, 8 * CLOC], BF16) for i in range(2)]
        NSL = 4
        slab = [A.take(f"slab{i}", [128, D], BF16) for i in range(NSL)]
        NPC = 4
        pchk = [A.take(f"pchk{i}", [128, 512], BF16) for i in range(NPC)]
        eo = [A.take(f"eo{i}", [128, 512], BF16) for i in range(2)]

        slab_done = [None] * 32
        slab_consumed_T = [0] * NSL
        pchk_done = [None] * 64
        pchk_consumed_T = [0] * NPC
        exp_st = []
        t_h_done = [0] * NE
        t_out_done_prev = 0
        p.wait("tensor", "V", v_yproj)
        p.wait("sync", "T", t_yproj)
        p.wait("scalar", "T", t_yproj)
        p.wait("gpsimd", "T", t_yproj)
        p.wait("gpsimd", "LG", lg_disp)
        p.wait("sync", "LG", lg_disp)
        ldxT = [0] * NE
        a_h_prev = [0, 0]

        def issue_slab(ee_, fc):
            sbw = slab[fc % NSL]
            if slab_consumed_T[fc % NSL]:
                p.wait("gpsimd", "T", slab_consumed_T[fc % NSL])
            slab_done[fc] = ("LG", p.dma("gpsimd", "LG", sbw[:, :], cfc[ee_, fc]))

        def issue_pchk(ee_, step):
            dcp_, fc_ = divmod(step, 32)
            cb = pchk[step % NPC]
            if pchk_consumed_T[step % NPC]:
                p.wait("gpsimd", "T", pchk_consumed_T[step % NPC])
            pchk_done[step] = ("LG", p.dma("gpsimd", "LG", cb[:, :],
                                           cpj[ee_][ts(fc_, 128), ts(dcp_, 512)]))

        for fc in range(NSL):
            issue_slab(0, fc)
        for ee in range(NE):
            p.wait("sync", "T", t_h_done[max(ee - 1, 0)])
            for dc in range(8):
                ldxT[ee] = p.dmat("sync", "LD", xtb[ee % 2][:, ts(dc, CLOC)],
                                  exp_x[ts(ee, CLOC), ts(dc, 128)])
            for fc in range(32):
                sb = slab[fc % NSL]
                sem, val = slab_done[fc]
                p.wait("tensor", sem, val)
                if fc == 0:
                    p.wait("tensor", "LD", ldxT[ee])
                for sc in range(2):
                    w = 512 if sc == 0 else CLOC - 512
                    bank = ps_a if sc == 0 else ps_b
                    if a_h_prev[sc]:
                        p.wait("tensor", "A", a_h_prev[sc])

                    def mm_h(E, sb=sb, sc=sc, w=w, bank=bank, xt=xtb[ee % 2]):
                        for dc in range(8):
                            last = E.matmul(
                                bank[:, 0:w], sb[:, ts(dc, 128)],
                                xt[:, dc * CLOC + sc * 512: dc * CLOC + sc * 512 + w],
                                start=(dc == 0), stop=(dc == 7))
                        return last
                    t_h = p.op_inc("tensor", "T", mm_h)
                    p.wait("scalar", "T", t_h)
                    if fc == 0 and sc == 0:
                        p.wait("scalar", "T", t_out_done_prev)
                    p.op("scalar", lambda E, fc=fc, sc=sc, w=w, bank=bank:
                         E.activation(ht[:, fc * CLOC + sc * 512:
                                         fc * CLOC + sc * 512 + w],
                                      bank[:, 0:w], AF.Gelu))
                    a_h = p.mark("scalar", "A")
                    a_h_prev[sc] = a_h
                slab_consumed_T[fc % NSL] = S["T"].n
                if fc + NSL < 32:
                    issue_slab(ee, fc + NSL)
            t_h_done[ee] = S["T"].n
            a_ht_done = S["A"].n
            for st in range(NPC):
                issue_pchk(ee, st)
            if ee + 1 < NE:
                for fc in range(NSL):
                    issue_slab(ee + 1, fc)
            banks7 = [ps_s[:, ts(i, 512)] for i in range(4)] + \
                     [ps_a[:, :], ps_b[:, :], ps_o[:, :]]
            for dcp in range(2):
                for fc in range(32):
                    step = dcp * 32 + fc
                    cb = pchk[step % NPC]
                    sem, val = pchk_done[step]
                    p.wait("tensor", sem, val)
                    if fc == 0:
                        p.wait("tensor", "A", a_ht_done)
                        p.wait("tensor", "V")
                        if dcp == 0:
                            p.wait("tensor", "A", a_h_prev[0])
                            p.wait("tensor", "A", a_h_prev[1])

                    def mm_o(E, cb=cb, fc=fc):
                        for st7 in range(7):
                            last = E.matmul(
                                banks7[st7],
                                ht[:, fc * CLOC + st7 * 128: fc * CLOC + (st7 + 1) * 128],
                                cb[:, :], start=(fc == 0), stop=(fc == 31))
                        return last
                    t_o = p.op_inc("tensor", "T", mm_o)
                    pchk_consumed_T[step % NPC] = S["T"].n
                    if step + NPC < 64:
                        issue_pchk(ee, step + NPC)
                for st7 in range(7):
                    b2 = eo[st7 % 2]
                    if st7 >= 2:
                        p.wait("vector", "LD", exp_st[-2])
                    p.wait("vector", "T", t_o)
                    p.op("vector", lambda E, st7=st7, b2=b2:
                         E.tensor_copy(b2[:, :], banks7[st7]))
                    v_eo = p.mark("vector", "V")
                    p.wait("sync", "V", v_eo)
                    exp_st.append(p.dma(
                        "sync", "LD",
                        exp_out[ee * CLOC + st7 * 128: ee * CLOC + (st7 + 1) * 128,
                                ts(dcp, 512)],
                        b2[:, :]))
            t_out_done_prev = S["T"].n
        ld_expout = S["LD"].n
        if stop_after == "ffn":
            finish_early()
            return nc, dbg_out
        if "ffn" in dbg:
            od = dbgt("expout", [NSLOT + 1, D], BF16)
            ox = dbgt("expx", [NSLOT, D], BF16)
            p.wait("gpsimd", "LD", ld_expout)
            rb3 = A.take("ffn_rb", [128, D], BF16)
            for i in range(NSLOT // 128):
                lg = p.dma("gpsimd", "LG", rb3[:, :], exp_x[ts(i, 128), :])
                p.wait("gpsimd", "LG", lg)
                p.dma("gpsimd", "ST", ox[ts(i, 128), :], rb3[:, :])
                lg = p.dma("gpsimd", "LG", rb3[:, :], exp_out[ts(i, 128), :])
                p.wait("gpsimd", "LG", lg)
                p.dma("gpsimd", "ST", od[ts(i, 128), :], rb3[:, :])
        A.pop()

        A.push()
        cx = [A.take(f"cx{i}", [128, D], F32) for i in range(2)]
        mg0 = [A.take(f"mg0{i}", [128, D], BF16) for i in range(2)]
        mg1 = [A.take(f"mg1{i}", [128, D], BF16) for i in range(2)]
        ac1 = A.take("ac1", [128, D], F32)
        ac2 = A.take("ac2", [128, D], F32)
        ac3 = [A.take(f"ac3{i}", [128, D], F32) for i in range(2)]

        p.wait("gpsimd", "LD", ld_expout)
        p.wait("vector", "T")
        st_y = []
        v_comb = [0, 0]
        for t in range(OTILE):
            b = t % 2
            if t >= 2:
                p.wait("sync", "ST", st_y[t - 2])
                p.wait("sync", "V", v_comb[b])
                p.wait("gpsimd", "V", v_comb[b])
            ld_cx = p.dma("sync", "LD", cx[b][:, :], x_hbm[ts(t, 128), :])
            lg0 = p.igather("gpsimd", "LG", mg0[b][:, :], exp_out[:, :],
                            g0i[:, t:t + 1], NSLOT)
            lg1 = p.igather("gpsimd", "LG", mg1[b][:, :], exp_out[:, :],
                            g1i[:, t:t + 1], NSLOT)
            p.wait("vector", "LD", ld_cx)
            p.wait("vector", "LG", lg1)
            p.wait("vector", "V", v_yproj)
            p.op("vector", lambda E, t=t, b=b: E.tensor_add(
                ac1[:, :], cx[b][:, :], attn_sb[:, ts(t, D)]))
            p.op("vector", lambda E, t=t, b=b: E.scalar_tensor_tensor(
                ac2[:, :], mg0[b][:, :], p1[:, t:t + 1], ac1[:, :],
                op0=ALU.mult, op1=ALU.add))
            p.op("vector", lambda E, t=t, b=b: E.scalar_tensor_tensor(
                ac3[b][:, :], mg1[b][:, :], p2[:, t:t + 1], ac2[:, :],
                op0=ALU.mult, op1=ALU.add))
            v_comb[b] = p.mark("vector", "V")
            p.wait("sync", "V", v_comb[b])
            st_y.append(p.dma("sync", "ST", y_out[ts(t, 128), :], ac3[b][:, :]))
        p.wait("sync", "ST", S["ST"].n)
        A.pop()

        for eng, meth in (("sync", block.sync), ("gpsimd", block.gpsimd),
                          ("tensor", block.tensor), ("vector", block.vector),
                          ("scalar", block.scalar)):
            def runner(E, ops=p.q[eng]):
                for f in ops:
                    f(E)
            meth(runner)

    return nc, dbg_out



def make_consts(core):
    b = np.arange(128)
    true_b = (b + core * OTILE) % 128
    tri_rot = (true_b[:, None] < true_b[None, :]).astype(np.float32)
    return dict(
        tri_rot=tri_rot.astype(ml_dtypes.bfloat16),
        ones_bf=np.ones((128, 128), ml_dtypes.bfloat16),
        id_bf=np.eye(128, dtype=np.float32).astype(ml_dtypes.bfloat16),
        id_f=np.eye(128, dtype=np.float32),
        caus_bf=np.triu(np.ones((128, 128), np.float32)).astype(ml_dtypes.bfloat16),
        onesrow_bf=np.ones((1, 2048), ml_dtypes.bfloat16),
        onesrow_f=np.ones((1, 128), np.float32),
        onescol_f=np.ones((128, 1), np.float32),
        tokid_f=(np.arange(OTILE)[None, :] * 128
                 + np.arange(128)[:, None]).astype(np.float32),
    )


def make_inputs(L0_attn_x, L0_attn_Wq, L0_attn_bq, L0_attn_Wk, L0_attn_bk,
                L0_attn_Wv, L0_attn_bv, L0_attn_Wo, L0_attn_bo,
                L0_ffn_w_g, L0_ffn_c_fc, L0_ffn_c_proj):
    x = np.ascontiguousarray(np.asarray(L0_attn_x, np.float32).reshape(NT, D))
    shared = dict(
        wqT=np.ascontiguousarray(np.asarray(L0_attn_Wq, np.float32).T),
        wkT=np.ascontiguousarray(np.asarray(L0_attn_Wk, np.float32).T),
        wvT=np.ascontiguousarray(np.asarray(L0_attn_Wv, np.float32).T),
        woT=np.ascontiguousarray(np.asarray(L0_attn_Wo, np.float32).T),
        bq=np.asarray(L0_attn_bq, np.float32).reshape(1, D),
        bk=np.asarray(L0_attn_bk, np.float32).reshape(1, D),
        bv=np.asarray(L0_attn_bv, np.float32).reshape(1, D),
        bo=np.asarray(L0_attn_bo, np.float32).reshape(1, D),
        wg=np.asarray(L0_ffn_w_g, np.float32),
        cfc=np.ascontiguousarray(
            np.asarray(L0_ffn_c_fc, np.float32)
            .reshape(NE, 8, 128, 32, 128).transpose(0, 3, 2, 1, 4)
        ).reshape(NE, 32, 128, D),
        cpj=np.asarray(L0_ffn_c_proj, np.float32),
    )
    in_maps = []
    for c in range(8):
        m = dict(shared)
        m["x"] = np.ascontiguousarray(np.roll(x, -c * T, axis=0))
        m.update(make_consts(c))
        in_maps.append(m)
    return in_maps


def kernel(**inputs):
    in_maps = make_inputs(**inputs)
    nc, _ = build()
    res = run_bass_kernel_spmd(nc, in_maps, core_ids=list(range(8)))
    y = np.concatenate([res.results[c]["y"] for c in range(8)], axis=0)
    return y.reshape(8, T, D)


# version 14
# speedup vs baseline: 2.0681x; 1.0027x over previous
from contextlib import ExitStack

import numpy as np
import ml_dtypes

import concourse.bass as bass
import concourse.mybir as mybir
from concourse.bass_utils import run_bass_kernel_spmd

F32 = mybir.dt.float32
BF16 = mybir.dt.bfloat16
I32 = mybir.dt.int32
AF = mybir.ActivationFunctionType
ALU = mybir.AluOpType
AX = mybir.AxisListType
IOffs = bass.IndirectOffsetOnAxis

D = 1024
T = 2048
NT = 16384
NTILE = 128
OTILE = 16
HEADS = 16
DH = 64
NE = 8
DFF = 4096
CAP = 5120
CLOC = 896
NSLOT = NE * CLOC
SCOLS = NSLOT // 128
BIG = 1.0e9
SBIG = 1.0e4

ENGINES = ("sync", "gpsimd", "tensor", "vector", "scalar")


class Sem:
    def __init__(self, stack, nc, name):
        self.h = stack.enter_context(nc.semaphore(name))
        self.n = 0

    def bump(self, k):
        self.n += k
        return self.n


class Prog:

    def __init__(self, nc, stack):
        self.nc = nc
        self.q = {e: [] for e in ENGINES}
        self.sems = {name: Sem(stack, nc, name) for name in
                     ("LD", "LG", "T", "V", "A", "ST")}

    def wait(self, eng, sem, val=None):
        s = self.sems[sem]
        v = s.n if val is None else val
        assert v <= s.n, f"wait {sem}>={v} but only {s.n} emitted"
        if v > 0:
            self.q[eng].append(lambda E, h=s.h, v=v: E.wait_ge(h, v))

    def dma(self, eng, sem, out, in_, **kw):
        s = self.sems[sem]
        s.bump(16)
        self.q[eng].append(
            lambda E, o=out, i=in_, h=s.h, kw=kw:
            E.dma_start(out=o, in_=i, **kw).then_inc(h, 16))
        return s.n

    def dmat(self, eng, sem, out, in_):
        s = self.sems[sem]
        s.bump(16)
        self.q[eng].append(
            lambda E, o=out, i=in_, h=s.h:
            E.dma_start_transpose(o, i).then_inc(h, 16))
        return s.n

    def _bound_reg(self, E, bounds):
        if not hasattr(self, "_bregs"):
            self._bregs = {}
        if bounds not in self._bregs:
            r = E.alloc_register(f"bnd{bounds}")
            E.reg_mov(r, bounds)
            self._bregs[bounds] = r
        return self._bregs[bounds]

    def igather(self, eng, sem, out, in_, idx, bounds):
        s = self.sems[sem]
        s.bump(16)

        def f(E, o=out, i=in_, x=idx, b=bounds, h=s.h):
            return E.indirect_dma_start(
                out=o, out_offset=None, in_=i,
                in_offset=IOffs(ap=x, axis=0),
                bounds_check=self._bound_reg(E, b), oob_is_err=False
            ).then_inc(h, 16)
        self.q[eng].append(f)
        return s.n

    def iscatter(self, eng, sem, out, in_, idx, bounds):
        s = self.sems[sem]
        s.bump(16)

        def f(E, o=out, i=in_, x=idx, b=bounds, h=s.h):
            return E.indirect_dma_start(
                out=o, out_offset=IOffs(ap=x, axis=0), in_=i, in_offset=None,
                bounds_check=self._bound_reg(E, b), oob_is_err=False
            ).then_inc(h, 16)
        self.q[eng].append(f)
        return s.n

    def op(self, eng, fn, nd=False):
        self.q[eng].append(fn)
        if eng == "vector" and not nd:
            self.q[eng].append(lambda E: E.drain())

    def dr(self, eng):
        self.q[eng].append(lambda E: E.drain())

    def mark(self, eng, sem, k=1):
        s = self.sems[sem]
        v = s.bump(k)
        if eng in ("vector", "scalar"):
            self.q[eng].append(lambda E: E.drain())
        self.q[eng].append(lambda E, h=s.h, k=k: E.sem_inc(h, k))
        return v

    def op_inc(self, eng, sem, fn, k=1):
        s = self.sems[sem]
        v = s.bump(k)
        self.q[eng].append(lambda E, f=fn, h=s.h, k=k: f(E).then_inc(h, k))
        return v


def ts(i, n):
    return slice(i * n, (i + 1) * n)


_DT_SIZE = {F32: 4, BF16: 2, I32: 4}


class Arena:

    def __init__(self, nc):
        self.nc = nc
        self.stack = [nc.sbuf_base]
        self.peak = nc.sbuf_base
        self.n = 0

    def push(self):
        self.stack.append(self.stack[-1])

    def pop(self):
        self.stack.pop()

    def take(self, name, shape, dtype):
        off = (self.stack[-1] + 63) // 64 * 64
        pp = 1
        for d in shape[1:]:
            pp *= d
        pp *= _DT_SIZE[dtype]
        self.n += 1
        t = self.nc.alloc_sbuf_tensor_at(f"{name}_{self.n}", shape, dtype,
                                         offset=off)
        self.stack[-1] = off + pp
        self.peak = max(self.peak, self.stack[-1])
        assert self.stack[-1] <= self.nc.sbuf_top, \
            f"SBUF overflow at {name}: {self.stack[-1]} > {self.nc.sbuf_top}"
        return t


def build(dbg=(), stop_after=None):
    nc = bass.Bass("TRN2", target_bir_lowering=False, debug=False)
    dbg = set(dbg)

    x_hbm = nc.dram_tensor("x", [NT, D], F32, kind="ExternalInput").ap()
    wT = {k: nc.dram_tensor(f"w{k}T", [D, D], F32, kind="ExternalInput").ap()
          for k in "qkvo"}
    bias = {k: nc.dram_tensor(f"b{k}", [1, D], F32, kind="ExternalInput").ap()
            for k in "qkvo"}
    wg = nc.dram_tensor("wg", [D, NE], F32, kind="ExternalInput").ap()
    cfc = nc.dram_tensor("cfc", [NE, 32, 128, D], F32, kind="ExternalInput").ap()
    cpj = nc.dram_tensor("cpj", [NE, DFF, D], F32, kind="ExternalInput").ap()
    tri_hbm = nc.dram_tensor("tri_rot", [128, 128], BF16, kind="ExternalInput").ap()
    one_hbm = nc.dram_tensor("ones_bf", [128, 128], BF16, kind="ExternalInput").ap()
    idb_hbm = nc.dram_tensor("id_bf", [128, 128], BF16, kind="ExternalInput").ap()
    idf_hbm = nc.dram_tensor("id_f", [128, 128], F32, kind="ExternalInput").ap()
    caus_hbm = nc.dram_tensor("caus_bf", [128, 128], BF16, kind="ExternalInput").ap()
    orow_hbm = nc.dram_tensor("onesrow_bf", [1, 2048], BF16, kind="ExternalInput").ap()
    orowf_hbm = nc.dram_tensor("onesrow_f", [1, 128], F32, kind="ExternalInput").ap()
    ocol_hbm = nc.dram_tensor("onescol_f", [128, 1], F32, kind="ExternalInput").ap()
    tok_hbm = nc.dram_tensor("tokid_f", [128, OTILE], F32, kind="ExternalInput").ap()

    y_out = nc.dram_tensor("y", [T, D], F32, kind="ExternalOutput").ap()

    xn_pad = nc.dram_tensor("xn_pad", [T + 1, D], BF16).ap()
    exp_x = nc.dram_tensor("exp_x", [NSLOT, D], BF16).ap()
    exp_out = nc.dram_tensor("exp_out", [NSLOT + 1, D], BF16).ap()
    slot_tab = nc.dram_tensor("slot_tab", [NSLOT, 1], F32).ap()
    y_mid = nc.dram_tensor("y_mid", [T, D], BF16).ap()

    dbg_out = {}

    def dbgt(name, shape, dtype=F32):
        dbg_out[name] = nc.dram_tensor("dbg_" + name, shape, dtype,
                                       kind="ExternalOutput").ap()
        return dbg_out[name]

    with ExitStack() as top:
        e = top.enter_context
        p = Prog(nc, top)
        S = p.sems
        A = Arena(nc)

        c_tri = A.take("c_tri", [128, 128], BF16)
        c_one = A.take("c_one", [128, 128], BF16)
        c_idb = A.take("c_idb", [128, 128], BF16)
        c_idf = A.take("c_idf", [128, 128], F32)
        c_caus = A.take("c_caus", [128, 128], BF16)
        c_or = A.take("c_or", [1, 2048], BF16)
        c_orf = A.take("c_orf", [1, 128], F32)
        c_oc = A.take("c_oc", [128, 1], F32)
        c_tok = A.take("c_tok", [128, OTILE], F32)
        c_eps = A.take("c_eps", [128, 1], F32)
        zrow = A.take("zrow", [1, D], BF16)
        b_sb = {k: A.take(f"b_{k}", [1, D], BF16) for k in "qkvo"}
        wg_f = A.take("wg_f", [128, NE * 8], F32)
        wg_b = A.take("wg_b", [128, NE * 8], BF16)
        g_all = A.take("g_all", [128, NTILE * NE], F32)
        idx1 = A.take("idx1", [128, NTILE], F32)
        idx2 = A.take("idx2", [128, NTILE], F32)
        p1 = A.take("p1", [128, NTILE], F32)
        p2 = A.take("p2", [128, NTILE], F32)
        g0i = A.take("g0i", [128, OTILE], I32)
        g1i = A.take("g1i", [128, OTILE], I32)
        tabi4 = [A.take(f"tabi{i}", [128, 14], I32) for i in range(4)]
        mhat = A.take("mhat", [128, 1], F32)
        nrm = A.take("nrm", [1, 8], F32)

        ps_a = e(nc.psum_tensor("ps_a", [128, 512], F32))
        ps_b = e(nc.psum_tensor("ps_b", [128, 512], F32))
        ps_s = e(nc.psum_tensor("ps_s", [128, 2048], F32))
        ps_t = e(nc.psum_tensor("ps_t", [128, 512], BF16))
        ps_o = e(nc.psum_tensor("ps_o", [128, 512], F32))

        block = e(nc.Block())

        def finish_early():
            yz = g_all
            p.op("vector", lambda E: E.memset(yz[:, :], 0.0))
            vz = p.mark("vector", "V")
            p.wait("sync", "V", vz)
            p.wait("sync", "ST", S["ST"].n)
            for t in range(OTILE):
                p.dma("sync", "ST", y_out[ts(t, 128), :], yz[:, :])
            p.wait("sync", "ST", S["ST"].n)
            for eng, meth in (("sync", block.sync), ("gpsimd", block.gpsimd),
                              ("tensor", block.tensor), ("vector", block.vector),
                              ("scalar", block.scalar)):
                def runner(E, ops=p.q[eng]):
                    for f in ops:
                        f(E)
                meth(runner)

        for dst, src in ((c_tri, tri_hbm), (c_one, one_hbm), (c_idb, idb_hbm),
                         (c_idf, idf_hbm), (c_caus, caus_hbm), (c_or, orow_hbm),
                         (c_orf, orowf_hbm), (c_oc, ocol_hbm), (c_tok, tok_hbm)):
            p.dma("sync", "LD", dst[:, :], src)
        p.dma("sync", "LD", wg_f[:, :].rearrange("p (dc e) -> p dc e", e=NE),
              wg.rearrange("(dc p) e -> p dc e", p=128))
        for k in "qkvo":
            p.dma("gpsimd", "LG", b_sb[k][:, :], bias[k])
        ld_const = S["LD"].n
        lg_bias = S["LG"].n
        p.op("vector", lambda E: E.memset(c_eps[:, :], 1e-5))
        p.op("vector", lambda E: E.memset(zrow[:, :], 0.0))
        p.wait("vector", "LD", ld_const)
        p.op("vector", lambda E: E.tensor_copy(wg_b[:, :], wg_f[:, :]))
        v_const = p.mark("vector", "V")
        p.wait("sync", "V", v_const)
        p.dma("sync", "LD", xn_pad[T:T + 1, :], zrow[:, :])
        p.dma("sync", "LD", exp_out[NSLOT:NSLOT + 1, :], zrow[:, :])

        A.push()
        qt_sb = [A.take(f"qt{i}", [128, T], BF16) for i in range(8)]
        kt_sb = [A.take(f"kt{i}", [128, T], BF16) for i in range(8)]
        vp = [A.take(f"vp{i}", [128, HEADS * 65], BF16) for i in range(OTILE)]
        xnt = A.take("xnt", [128, 8 * 2048], BF16)
        wbuf = A.take("wbuf", [128, 8 * 1024], BF16)
        gxb = [A.take(f"gxb{i}", [128, D], BF16) for i in range(2)]
        sidxi = [A.take(f"sidxi{k}", [128, OTILE], I32) for k in range(2)]
        svals = [A.take(f"svals{k}", [128, OTILE], F32) for k in range(2)]
        tabf = A.take("tabf", [128, SCOLS], F32)
        tini = A.take("tini", [128, SCOLS], F32)

        lg_w = {}
        lg_w["q"] = p.dma("gpsimd", "LG",
                          wbuf[:, :].rearrange("p (dc i) -> p dc i", i=D),
                          wT["q"].rearrange("(dc p) i -> p dc i", p=128))
        A.push()
        NXB = 4
        xb = [A.take(f"xb{i}", [128, D], F32) for i in range(NXB)]
        xnb = [A.take(f"xnb{i}", [128, D], BF16) for i in range(NXB)]
        sq1 = A.take("sq1", [128, D], F32)
        colr = A.take("colr", [128, 8 * NXB], F32)
        xTb = [A.take(f"xTb{i}", [128, D], BF16) for i in range(2)]

        ps_t3 = ps_t[:, :].rearrange("p (dc t) -> p dc t", t=128)
        ps_o3 = ps_o[:, :].rearrange("p (dc t) -> p dc t", t=128)
        st_own = [0] * OTILE
        v_lastread = [0] * NXB
        a_lastread = [0] * NXB
        t_xnb_read = [0] * NXB
        g_xnb_read = [0] * NXB
        a_evA_prev = 0
        v_evB_prev = 0
        v_g_prev = 0
        v_chunk0 = 0
        def pcol(b, k):
            return colr[:, b * 8 + k:b * 8 + k + 1]

        def emit_tile_tail(i, v_xnb):
            nonlocal a_evA_prev, v_evB_prev, v_g_prev, v_chunk0
            b = i % NXB
            own = i < OTILE
            if own:
                p.wait("gpsimd", "V", v_xnb)
                st_own[i] = p.dma("gpsimd", "LG", xn_pad[ts(i, 128), :],
                                  xnb[b][:, :])
                g_xnb_read[b] = st_own[i]
            if own:
                dst = xnt[:, :].rearrange("p (dc t) -> p dc t", t=2048)
                dstA = dst[:, 0:4, i * 128:(i + 1) * 128]
                dstB = dst[:, 4:8, i * 128:(i + 1) * 128]
            else:
                xv = xTb[i % 2]
                dstA = xv[:, 0:512].rearrange("p (dc t) -> p dc t", t=128)
                dstB = xv[:, 512:1024].rearrange("p (dc t) -> p dc t", t=128)
            p.wait("tensor", "V", v_xnb)
            if i == 0:
                p.wait("tensor", "LD", ld_const)
            if a_evA_prev:
                p.wait("tensor", "A", a_evA_prev)

            def trg(E, b, lo, dstp):
                for k2 in range(4):
                    if lo == 0:
                        last = E.transpose(
                            dstp[:, k2 * 128:(k2 + 1) * 128],
                            xnb[b][:, (lo + k2) * 128:(lo + k2 + 1) * 128],
                            c_idb[:, :])
                    else:
                        last = E.matmul(
                            dstp[:, k2 * 128:(k2 + 1) * 128],
                            xnb[b][:, (lo + k2) * 128:(lo + k2 + 1) * 128],
                            c_idb[:, :], start=True, stop=True)
                return last
            t_trA = p.op_inc("tensor", "T", lambda E, b=b: trg(E, b, 0, ps_t))
            t_trB = p.op_inc("tensor", "T", lambda E, b=b: trg(E, b, 4, ps_o))
            p.wait("scalar", "T", t_trA)
            p.op("scalar", lambda E, dstA=dstA:
                 E.activation(dstA, ps_t3[:, 0:4, :], AF.Copy))
            p.wait("scalar", "T", t_trB)
            p.op("scalar", lambda E, dstB=dstB:
                 E.activation(dstB, ps_o3[:, 0:4, :], AF.Copy))
            a_evA = p.mark("scalar", "A")
            a_evA_prev = a_evA
            if own and i == OTILE - 1:
                v_chunk0 = S["V"].n
            p.wait("tensor", "A", a_evA)
            if i == 0:
                p.wait("tensor", "V", v_const)
            if v_g_prev:
                p.wait("tensor", "V", v_g_prev)
            if own:
                src, soff, sstr = xnt[:, :], i * 128, 2048
            else:
                src, soff, sstr = xTb[i % 2][:, :], 0, 128

            def mmg(E, src=src, soff=soff, sstr=sstr):
                for dc in range(8):
                    last = E.matmul(
                        ps_a[:, 0:8],
                        src[:, dc * sstr + soff: dc * sstr + soff + 128],
                        wg_b[:, dc * 8:(dc + 1) * 8],
                        start=(dc == 0), stop=(dc == 7))
                return last
            t_g = p.op_inc("tensor", "T", mmg)
            t_xnb_read[b] = t_g
            p.wait("vector", "T", t_g)
            p.op("vector", lambda E, i=i:
                 E.tensor_copy(g_all[:, i * 8:(i + 1) * 8], ps_a[:, 0:8]))
            v_g_prev = p.mark("vector", "V")

        for i0 in range(0, NTILE, 2):
            tiles = (i0, i0 + 1)
            lds = {}
            for i in tiles:
                b = i % NXB
                if v_lastread[b]:
                    p.wait("sync", "V", v_lastread[b])
                    p.wait("sync", "A", a_lastread[b])
                lds[i] = p.dma("sync", "LD", xb[b][:, :], x_hbm[ts(i, 128), :])
            for i in tiles:
                b = i % NXB
                p.wait("scalar", "LD", lds[i])
                p.op("scalar", lambda E, b=b: E.activation(
                    sq1[:, :], xb[b][:, :], AF.Square, accum_out=pcol(b, 0)))
            a_sq = p.mark("scalar", "A")
            for i in tiles:
                b = i % NXB
                p.wait("vector", "LD", lds[i])
                p.op("vector", lambda E, b=b:
                     E.reduce_sum(pcol(b, 5), xb[b][:, :], axis=AX.X), nd=True)
            p.dr("vector")
            for i in tiles:
                p.op("vector", lambda E, b=i % NXB:
                     E.tensor_scalar_mul(pcol(b, 1), pcol(b, 5), 1.0 / D),
                     nd=True)
            p.dr("vector")
            for i in tiles:
                p.op("vector", lambda E, b=i % NXB:
                     E.tensor_mul(pcol(b, 2), pcol(b, 1), pcol(b, 1)), nd=True)
            p.wait("vector", "A", a_sq)
            p.dr("vector")
            for i in tiles:
                p.op("vector", lambda E, b=i % NXB: E.scalar_tensor_tensor(
                    pcol(b, 3), pcol(b, 0), 1.0 / D, pcol(b, 2),
                    op0=ALU.mult, op1=ALU.subtract), nd=True)
            v_var = p.mark("vector", "V")
            p.wait("scalar", "V", v_var)
            for i in tiles:
                p.op("scalar", lambda E, b=i % NXB:
                     E.activation(pcol(b, 2), pcol(b, 3), AF.Sqrt,
                                  bias=c_eps[:, :]))
            a_sd = p.mark("scalar", "A")
            p.wait("vector", "A", a_sd)
            for i in tiles:
                b = i % NXB
                a_lastread[b] = a_sd
                p.op("vector", lambda E, b=b:
                     E.reciprocal(pcol(b, 4), pcol(b, 2)), nd=True)
            for i in tiles:
                b = i % NXB
                if t_xnb_read[b]:
                    p.wait("vector", "T", t_xnb_read[b])
                    p.wait("vector", "LG", g_xnb_read[b])
            p.dr("vector")
            for i in tiles:
                p.op("vector", lambda E, b=i % NXB:
                     E.tensor_scalar(xnb[b][:, :], xb[b][:, :],
                                     pcol(b, 1), pcol(b, 4),
                                     op0=ALU.subtract, op1=ALU.mult), nd=True)
            v_xnb = p.mark("vector", "V")
            for i in tiles:
                v_lastread[i % NXB] = v_xnb
            for i in tiles:
                emit_tile_tail(i, v_xnb)

        st_own_last = st_own[OTILE - 1]
        v_g_done = v_g_prev
        t_g_done = S["T"].n
        v_p1 = S["V"].n
        a_p1 = S["A"].n

        if stop_after == "p1":
            finish_early()
            return nc, dbg_out
        A.pop()

        A.push()

        def proj_qk(dst_tiles, wkey, bias_row, t_sem_name):
            p.wait("tensor", "LG", lg_w[wkey])
            p.wait("tensor", "V", v_g_done)
            p.wait("tensor", "A", a_evA_prev)
            p.wait("tensor", "LG", lg_bias)
            last = 0
            evs = []
            for ic in range(8):
                for tc in range(4):
                    bank = ps_a if (ic * 4 + tc) % 2 == 0 else ps_b
                    if len(evs) >= 2:
                        p.wait("tensor", "V", evs[-2])

                    def mm(E, ic=ic, tc=tc, bank=bank, w=wkey, br=bias_row):
                        E.matmul(bank[:, :], br[0:1, ts(ic, 128)],
                                 c_or[0:1, ts(tc, 512)], start=True, stop=False)
                        for dc in range(8):
                            last = E.matmul(
                                bank[:, :],
                                wbuf[:, dc * 1024 + ic * 128: dc * 1024 + (ic + 1) * 128],
                                xnt[:, dc * 2048 + tc * 512: dc * 2048 + tc * 512 + 512],
                                start=False, stop=(dc == 7))
                        return last
                    t_mm = p.op_inc("tensor", "T", mm)
                    p.wait("vector", "T", t_mm)
                    p.op("vector", lambda E, ic=ic, tc=tc, bank=bank, dt=dst_tiles:
                         E.tensor_copy(dt[ic][:, ts(tc, 512)], bank[:, :]))
                    evs.append(p.mark("vector", "V"))
            return evs[-1]

        v_q = proj_qk(qt_sb, "q", b_sb["q"], "T")
        t_q_done = S["T"].n
        p.wait("gpsimd", "T", t_q_done)
        lg_w["k"] = p.dma("gpsimd", "LG",
                          wbuf[:, :].rearrange("p (dc i) -> p dc i", i=D),
                          wT["k"].rearrange("(dc p) i -> p dc i", p=128))
        v_k = proj_qk(kt_sb, "k", b_sb["k"], "T")
        t_k_done = S["T"].n
        p.wait("gpsimd", "T", t_k_done)
        lg_w["v"] = p.dma("gpsimd", "LG",
                          wbuf[:, :].rearrange("p (dc i) -> p dc i", i=D),
                          wT["v"].rearrange("(dc p) i -> p dc i", p=128))

        p.wait("tensor", "LG", lg_w["v"])
        for tt in range(OTILE):
            p.op("vector", lambda E, tt=tt: E.memset(
                vp[tt][:, :].rearrange("p (h c) -> p h c", c=65)[:, :, 64:65],
                1.0))
        v_vevs = []
        for tt in range(OTILE):
            for ic2 in range(2):
                bank = ps_a if (tt * 2 + ic2) % 2 == 0 else ps_b
                if len(v_vevs) >= 2:
                    p.wait("tensor", "V", v_vevs[-2])

                def mmv(E, tt=tt, ic2=ic2, bank=bank):
                    E.matmul(bank[:, :], c_or[0:1, ts(tt, 128)],
                             b_sb["v"][0:1, ts(ic2, 512)], start=True, stop=False)
                    for dc in range(8):
                        last = E.matmul(
                            bank[:, :],
                            xnt[:, dc * 2048 + tt * 128: dc * 2048 + (tt + 1) * 128],
                            wbuf[:, dc * 1024 + ic2 * 512: dc * 1024 + ic2 * 512 + 512],
                            start=False, stop=(dc == 7))
                    return last
                t_mm = p.op_inc("tensor", "T", mmv)
                p.wait("vector", "T", t_mm)
                p.op("vector", lambda E, tt=tt, ic2=ic2, bank=bank:
                     E.tensor_copy(
                         vp[tt][:, :].rearrange("p (h c) -> p h c", c=65)
                         [:, ic2 * 8:(ic2 + 1) * 8, 0:64],
                         bank[:, :].rearrange("p (h c) -> p h c", c=64)))
                v_vevs.append(p.mark("vector", "V"))
        v_qkv_done = v_vevs[-1]
        t_qkv_done = S["T"].n
        if "qkv" in dbg:
            oq = dbgt("qt", [1024, 2048], BF16)
            ok_ = dbgt("kt", [1024, 2048], BF16)
            ov = dbgt("v", [2048, HEADS * 65], BF16)
            p.wait("scalar", "V", v_qkv_done)
            for i in range(8):
                p.dma("scalar", "ST", oq[ts(i, 128), :], qt_sb[i][:, :])
                p.dma("scalar", "ST", ok_[ts(i, 128), :], kt_sb[i][:, :])
            for i in range(OTILE):
                p.dma("scalar", "ST", ov[ts(i, 128), :], vp[i][:, :])
        if stop_after == "qkv":
            finish_early()
            return nc, dbg_out

        if "g" in dbg:
            od = dbgt("g", [128, NTILE * NE])
            p.wait("scalar", "V", v_g_done)
            p.dma("scalar", "ST", od, g_all[:, :])

        A.push()
        r_v1 = A.take("r_v1", [128, NTILE], F32)
        r_v2 = A.take("r_v2", [128, NTILE], F32)
        r_eq = A.take("r_eq", [128, NTILE], F32)
        r_t = A.take("r_t", [128, NTILE], F32)
        r_big = A.take("r_big", [128, NTILE], F32)
        g2 = A.take("g2", [128, NTILE * NE], F32)
        zf = A.take("zf", [128, NTILE], F32)
        s12 = A.take("s12", [128, NTILE], F32)

        g3 = g_all[:, :].rearrange("p (t e) -> p t e", e=NE)
        g23 = g2[:, :].rearrange("p (t e) -> p t e", e=NE)
        p.op("vector", lambda E: E.memset(r_big[:, :], SBIG))
        p.op("vector", lambda E: E.memset(zf[:, :], 0.0))
        p.op("vector", lambda E: E.reduce_max(r_v1[:, :], g3, axis=AX.X))
        p.op("vector", lambda E: E.memset(idx1[:, :], SBIG))
        for ee in range(NE):
            p.op("vector", lambda E, ee=ee:
                 E.tensor_tensor(r_eq[:, :], g3[:, :, ee], r_v1[:, :], op=ALU.is_equal))
            p.op("vector", lambda E, ee=ee: E.scalar_tensor_tensor(
                r_t[:, :], r_eq[:, :], -(SBIG - ee), r_big[:, :],
                op0=ALU.mult, op1=ALU.add))
            p.op("vector", lambda E: E.tensor_tensor(idx1[:, :], idx1[:, :],
                                                     r_t[:, :], op=ALU.min))
        for ee in range(NE):
            p.op("vector", lambda E, ee=ee: E.tensor_single_scalar(
                r_eq[:, :], idx1[:, :], float(ee), op=ALU.is_equal))
            p.op("vector", lambda E, ee=ee: E.scalar_tensor_tensor(
                g23[:, :, ee], r_eq[:, :], -1e30, g3[:, :, ee],
                op0=ALU.mult, op1=ALU.add))
        p.op("vector", lambda E: E.reduce_max(r_v2[:, :], g23, axis=AX.X))
        p.op("vector", lambda E: E.memset(idx2[:, :], SBIG))
        for ee in range(NE):
            p.op("vector", lambda E, ee=ee:
                 E.tensor_tensor(r_eq[:, :], g23[:, :, ee], r_v2[:, :], op=ALU.is_equal))
            p.op("vector", lambda E, ee=ee: E.scalar_tensor_tensor(
                r_t[:, :], r_eq[:, :], -(SBIG - ee), r_big[:, :],
                op0=ALU.mult, op1=ALU.add))
            p.op("vector", lambda E: E.tensor_tensor(idx2[:, :], idx2[:, :],
                                                     r_t[:, :], op=ALU.min))
        p.op("vector", lambda E: E.tensor_sub(s12[:, :], r_v1[:, :], r_v2[:, :]))
        v_rt = p.mark("vector", "V")
        p.wait("scalar", "V", v_rt)
        p.op("scalar", lambda E: E.activation(p1[:, :], s12[:, :], AF.Sigmoid))
        p.op("scalar", lambda E: E.activation(p2[:, :], s12[:, :], AF.Sigmoid,
                                              scale=-1.0))
        a_probs = p.mark("scalar", "A")

        if "route" in dbg:
            for nm, buf in (("idx1", idx1), ("idx2", idx2), ("p1", p1), ("p2", p2)):
                od = dbgt(nm, [128, NTILE])
                p.wait("scalar", "A", a_probs)
                p.wait("scalar", "V", v_rt)
                p.dma("scalar", "ST", od, buf[:, :])

        idxT = [A.take(f"idxT{k}", [128, 128], F32) for k in range(2)]
        m_ke = A.take("m_ke", [128, 128], F32)
        incl = A.take("incl", [128, 128], F32)
        plane = A.take("plane", [128, 128], F32)
        rsel = [A.take(f"rsel{k}", [128, 128], F32) for k in range(2)]
        rtt = [A.take(f"rtt{k}", [128, 128], F32) for k in range(2)]
        totb = A.take("totb", [128, 2 * NE], BF16)
        carr = A.take("carr", [128, 2 * NE], F32)

        p.wait("tensor", "V", v_rt)
        for k, src in enumerate((idx1, idx2)):
            t_tr = p.op_inc("tensor", "T", lambda E, s=src:
                            E.transpose(ps_a[:, 0:128], s[:, :], c_idf[:, :]))
            p.wait("vector", "T", t_tr)
            p.op("vector", lambda E, k=k: E.tensor_copy(idxT[k][:, :], ps_a[:, 0:128]))
            v_m = p.mark("vector", "V")
            p.wait("tensor", "V", v_m)
        v_idxT = p.mark("vector", "V")

        for k in range(2):
            for ee in range(NE):
                p.op("vector", lambda E, k=k, ee=ee: E.tensor_single_scalar(
                    m_ke[:, :], idxT[k][:, :], float(ee), op=ALU.is_equal))
                p.op("vector", lambda E: E.tensor_tensor_scan(
                    incl[:, :], m_ke[:, :], zf[:, 0:128], 0.0,
                    op0=ALU.add, op1=ALU.add))
                p.op("vector", lambda E, k=k, ee=ee: E.tensor_copy(
                    totb[:, k * NE + ee:k * NE + ee + 1], incl[:, 127:128]))
        v_tot = p.mark("vector", "V")
        p.wait("tensor", "V", v_tot)

        def mm_car(E):
            E.matmul(ps_o[:, 0:8], c_tri[:, :], totb[:, 0:NE], start=True, stop=True)
            return E.matmul(ps_o[:, 8:16], c_tri[:, :], totb[:, NE:2 * NE],
                            start=True, stop=False)

        def mm_car2(E):
            return E.matmul(ps_o[:, 8:16], c_one[:, :], totb[:, 0:NE],
                            start=False, stop=True)
        t_car = p.op_inc("tensor", "T", lambda E: (mm_car(E), mm_car2(E))[1])
        p.wait("vector", "T", t_car)
        p.op("vector", lambda E: E.tensor_copy(carr[:, 0:NE], ps_o[:, 0:8]))
        p.op("vector", lambda E: E.tensor_copy(carr[:, NE:2 * NE], ps_o[:, 8:16]))
        for k in range(2):
            p.op("vector", lambda E, k=k: E.memset(rsel[k][:, :], 0.0))
            for ee in range(NE):
                p.op("vector", lambda E, k=k, ee=ee: E.tensor_single_scalar(
                    m_ke[:, :], idxT[k][:, :], float(ee), op=ALU.is_equal))
                p.op("vector", lambda E: E.tensor_tensor_scan(
                    incl[:, :], m_ke[:, :], zf[:, 0:128], 0.0,
                    op0=ALU.add, op1=ALU.add))
                p.op("vector", lambda E, k=k, ee=ee: E.scalar_tensor_tensor(
                    plane[:, :], incl[:, :], carr[:, k * NE + ee:k * NE + ee + 1],
                    m_ke[:, :], op0=ALU.add, op1=ALU.subtract))
                p.op("vector", lambda E: E.tensor_mul(plane[:, :], plane[:, :],
                                                      m_ke[:, :]))
                p.op("vector", lambda E, k=k: E.tensor_add(rsel[k][:, :],
                                                           rsel[k][:, :], plane[:, :]))
        v_rsel = p.mark("vector", "V")
        p.wait("tensor", "V", v_rsel)
        for k in range(2):
            t_tr = p.op_inc("tensor", "T", lambda E, k=k:
                            E.transpose(ps_a[:, 0:128], rsel[k][:, :], c_idf[:, :]))
            p.wait("vector", "T", t_tr)
            p.op("vector", lambda E, k=k: E.tensor_copy(rtt[k][:, :], ps_a[:, 0:128]))
            v_m = p.mark("vector", "V")
            p.wait("tensor", "V", v_m)
        v_rtt = p.mark("vector", "V")

        if "rank" in dbg:
            for k in range(2):
                od = dbgt(f"rtt{k}", [128, 128])
                p.wait("scalar", "V", v_rtt)
                p.dma("scalar", "ST", od, rtt[k][:, :])
            p.dma("scalar", "ST", dbgt("rsel0", [128, 128]), rsel[0][:, :])
            p.dma("scalar", "ST", dbgt("idxT0", [128, 128]), idxT[0][:, :])
            p.dma("scalar", "ST", dbgt("carr", [128, 16]), carr[:, :])

        if stop_after == "rank":
            finish_early()
            return nc, dbg_out

        oloc = [A.take(f"oloc{k}", [16, 128], F32) for k in range(2)]
        lincl = A.take("lincl", [16, 128], F32)
        lm = A.take("lm", [16, 128], F32)
        lpl = A.take("lpl", [16, 128], F32)
        lrsel = [A.take(f"lrsel{k}", [16, 128], F32) for k in range(2)]
        lrtt = [A.take(f"lrtt{k}", [128, OTILE], F32) for k in range(2)]
        ltotb = A.take("ltotb", [16, 2 * NE], BF16)
        lcarr = A.take("lcarr", [16, 2 * NE], F32)

        for k, src in enumerate((idx1, idx2)):
            t_tr = p.op_inc("tensor", "T", lambda E, s=src:
                            E.transpose(ps_a[0:16, 0:128], s[:, 0:OTILE],
                                        c_idf[:, :]))
            p.wait("vector", "T", t_tr)
            p.op("vector", lambda E, k=k: E.tensor_copy(oloc[k][:, :], ps_a[0:16, 0:128]))
            v_m = p.mark("vector", "V")
            p.wait("tensor", "V", v_m)
        v_oloc = p.mark("vector", "V")
        for k in range(2):
            for ee in range(NE):
                p.op("vector", lambda E, k=k, ee=ee: E.tensor_single_scalar(
                    lm[:, :], oloc[k][:, :], float(ee), op=ALU.is_equal))
                p.op("vector", lambda E: E.tensor_tensor_scan(
                    lincl[:, :], lm[:, :], zf[0:16, 0:128], 0.0,
                    op0=ALU.add, op1=ALU.add))
                p.op("vector", lambda E, k=k, ee=ee: E.tensor_copy(
                    ltotb[:, k * NE + ee:k * NE + ee + 1], lincl[:, 127:128]))
        v_ltot = p.mark("vector", "V")
        p.wait("tensor", "V", v_ltot)

        def mm_lcar(E):
            E.matmul(ps_o[0:16, 0:8], c_tri[0:16, 0:16], ltotb[:, 0:NE],
                     start=True, stop=True)
            E.matmul(ps_o[0:16, 8:16], c_tri[0:16, 0:16], ltotb[:, NE:2 * NE],
                     start=True, stop=False)
            return E.matmul(ps_o[0:16, 8:16], c_one[0:16, 0:16], ltotb[:, 0:NE],
                            start=False, stop=True)
        t_lcar = p.op_inc("tensor", "T", mm_lcar)
        p.wait("vector", "T", t_lcar)
        p.op("vector", lambda E: E.tensor_copy(lcarr[:, 0:NE], ps_o[0:16, 0:8]))
        p.op("vector", lambda E: E.tensor_copy(lcarr[:, NE:2 * NE], ps_o[0:16, 8:16]))
        for k in range(2):
            p.op("vector", lambda E, k=k: E.memset(lrsel[k][:, :], 0.0))
            for ee in range(NE):
                p.op("vector", lambda E, k=k, ee=ee: E.tensor_single_scalar(
                    lm[:, :], oloc[k][:, :], float(ee), op=ALU.is_equal))
                p.op("vector", lambda E: E.tensor_tensor_scan(
                    lincl[:, :], lm[:, :], zf[0:16, 0:128], 0.0,
                    op0=ALU.add, op1=ALU.add))
                p.op("vector", lambda E, k=k, ee=ee: E.scalar_tensor_tensor(
                    lpl[:, :], lincl[:, :], lcarr[:, k * NE + ee:k * NE + ee + 1],
                    lm[:, :], op0=ALU.add, op1=ALU.subtract))
                p.op("vector", lambda E: E.tensor_mul(lpl[:, :], lpl[:, :], lm[:, :]))
                p.op("vector", lambda E, k=k: E.tensor_add(lrsel[k][:, :],
                                                           lrsel[k][:, :], lpl[:, :]))
        v_lrsel = p.mark("vector", "V")
        p.wait("tensor", "V", v_lrsel)
        for k in range(2):
            t_tr = p.op_inc("tensor", "T", lambda E, k=k:
                            E.transpose(ps_a[:, 0:OTILE], lrsel[k][:, :],
                                        c_idf[0:16, 0:16]))
            p.wait("vector", "T", t_tr)
            p.op("vector", lambda E, k=k: E.tensor_copy(lrtt[k][:, :],
                                                        ps_a[:, 0:OTILE]))
            v_m = p.mark("vector", "V")
            p.wait("tensor", "V", v_m)
        v_lrtt = p.mark("vector", "V")

        dropf = A.take("dropf", [128, OTILE], F32)
        oflf = A.take("oflf", [128, OTILE], F32)
        basei = A.take("basei", [128, OTILE], F32)
        sidxf = A.take("sidxf", [128, OTILE], F32)
        tinv = A.take("tinv", [128, OTILE], F32)
        gtmp = A.take("gtmp", [128, OTILE], F32)

        p.op("vector", lambda E: E.tensor_scalar(
            tinv[:, :], c_tok[:, :], -1.0, float(T), op0=ALU.mult, op1=ALU.add))
        p.op("vector", lambda E: E.memset(tini[:, :], float(T)))
        for k in range(2):
            own = (idx1 if k == 0 else idx2)[:, 0:OTILE]
            p.op("vector", lambda E, k=k: E.tensor_single_scalar(
                dropf[:, :], rtt[k][:, 0:OTILE], float(CAP), op=ALU.is_ge))
            p.op("vector", lambda E, k=k: E.tensor_single_scalar(
                oflf[:, :], lrtt[k][:, :], float(CLOC), op=ALU.is_ge))
            p.op("vector", lambda E, k=k, own=own: E.scalar_tensor_tensor(
                basei[:, :], own, float(CLOC), lrtt[k][:, :],
                op0=ALU.mult, op1=ALU.add))
            p.op("vector", lambda E: E.scalar_tensor_tensor(
                sidxf[:, :], oflf[:, :], BIG, basei[:, :],
                op0=ALU.mult, op1=ALU.add))
            p.op("vector", lambda E, k=k: E.tensor_copy(sidxi[k][:, :], sidxf[:, :]))
            p.op("vector", lambda E: E.tensor_mul(gtmp[:, :], dropf[:, :], tinv[:, :]))
            p.op("vector", lambda E, k=k: E.tensor_add(svals[k][:, :],
                                                       gtmp[:, :], c_tok[:, :]))
            p.op("vector", lambda E: E.tensor_scalar(
                gtmp[:, :], basei[:, :], -1.0, float(NSLOT), op0=ALU.mult, op1=ALU.add))
            p.op("vector", lambda E: E.tensor_mul(gtmp[:, :], gtmp[:, :], oflf[:, :]))
            p.op("vector", lambda E: E.tensor_add(gtmp[:, :], gtmp[:, :], basei[:, :]))
            gi_sb = g0i if k == 0 else g1i
            p.op("vector", lambda E, gi_sb=gi_sb: E.tensor_copy(gi_sb[:, :], gtmp[:, :]))
        v_scati = p.mark("vector", "V")

        if stop_after == "scatidx":
            finish_early()
            return nc, dbg_out

        p.wait("gpsimd", "V", v_scati)
        lg_tini = p.dma("gpsimd", "LG",
                        slot_tab.rearrange("(p f) one -> p (f one)", p=128),
                        tini[:, :])
        p.wait("gpsimd", "LG", lg_tini)
        for k in range(2):
            for t in range(OTILE):
                p.iscatter("gpsimd", "LG", slot_tab[:, :],
                           svals[k][:, t:t + 1], sidxi[k][:, t:t + 1], NSLOT - 1)
        lg_scat = S["LG"].n
        p.wait("gpsimd", "LG", lg_scat)
        lg_tab = p.dma("gpsimd", "LG", tabf[:, :],
                       slot_tab.rearrange("(p f) one -> p (f one)", p=128))
        p.wait("vector", "LG", lg_tab)
        for i4 in range(4):
            p.op("vector", lambda E, i4=i4: E.tensor_copy(
                tabi4[i4][:, :], tabf[:, i4 * 14:(i4 + 1) * 14]))
        v_tabi = p.mark("vector", "V")

        if "tab" in dbg:
            od = dbgt("tab", [128, SCOLS])
            p.wait("scalar", "V", v_tabi)
            p.dma("scalar", "ST", od, tabf[:, :])
            od2 = dbgt("lrtt0", [128, OTILE])
            p.dma("scalar", "ST", od2, lrtt[0][:, :])
            od3 = dbgt("g0", [128, OTILE], I32)
            p.dma("scalar", "ST", od3, g0i[:, :])
            od4 = dbgt("g1", [128, OTILE], I32)
            p.dma("scalar", "ST", od4, g1i[:, :])

        if stop_after == "route":
            finish_early()
            return nc, dbg_out
        A.pop()
        A.pop()

        disp_st = [0] * SCOLS
        p.wait("gpsimd", "V", v_tabi)
        p.wait("gpsimd", "LG", st_own_last)
        for f in range(SCOLS):
            b = f % 2
            if f >= 2:
                p.wait("gpsimd", "LG", disp_st[f - 2])
            lg_g = p.igather("gpsimd", "LG", gxb[b][:, :], xn_pad[:, :],
                             tabi4[f // 14][:, f % 14:f % 14 + 1], T)
            p.wait("gpsimd", "LG", lg_g)
            disp_st[f] = p.dma(
                "gpsimd", "LG",
                exp_x.rearrange("(pp f) d -> pp f d", f=SCOLS)[:, f, :],
                gxb[b][:, :])
        lg_disp = S["LG"].n


        A.push()
        sq2 = A.take("sq2", [128, 2048], F32)
        pbuf = A.take("pbuf", [128, 2048], BF16)
        ybuf = [A.take(f"ybuf{i}", [128, D], BF16) for i in range(2)]
        lcol = A.take("lcol", [128, 10], F32)

        p.op("vector", lambda E: E.memset(nrm[:, :], 0.0))
        v_nz = p.mark("vector", "V")
        a_norm_sq = []
        for which, tiles_, slot in (("q", qt_sb, 0), ("k", kt_sb, 1)):
            for it in range(8):
                p.wait("scalar", "V", v_qkv_done if which == "k" else v_q)
                p.wait("scalar", "T", t_g_done)
                p.wait("scalar", "V", v_scati)
                p.op("scalar", lambda E, tiles_=tiles_, it=it:
                     E.activation(sq2[:, :], tiles_[it][:, :], AF.Square))
                a_s = p.mark("scalar", "A")
                for tc in range(4):
                    p.wait("tensor", "A", a_s)
                    p.wait("tensor", "V", v_nz)
                    t_n = p.op_inc("tensor", "T", lambda E, tc=tc:
                                   E.matmul(ps_b[0:1, 0:512], c_oc[:, :],
                                            sq2[:, ts(tc, 512)], start=True, stop=True))
                    p.wait("vector", "T", t_n)
                    p.op("vector", lambda E: E.reduce_max(nrm[0:1, 2:3],
                                                          ps_b[0:1, 0:512], axis=AX.X))
                    p.op("vector", lambda E, slot=slot: E.tensor_tensor(
                        nrm[0:1, slot:slot + 1], nrm[0:1, slot:slot + 1],
                        nrm[0:1, 2:3], op=ALU.max))
                    v_nz = p.mark("vector", "V")
        p.op("vector", lambda E: E.tensor_mul(nrm[0:1, 3:4], nrm[0:1, 0:1],
                                              nrm[0:1, 1:2]))
        v_nn = p.mark("vector", "V")
        p.wait("scalar", "V", v_nn)
        p.op("scalar", lambda E: E.activation(nrm[0:1, 4:5], nrm[0:1, 3:4], AF.Sqrt))
        a_nn = p.mark("scalar", "A")
        p.wait("vector", "A", a_nn)
        p.op("vector", lambda E: E.tensor_scalar_mul(nrm[0:1, 5:6],
                                                     nrm[0:1, 4:5], -0.125))
        v_mh = p.mark("vector", "V")
        p.wait("tensor", "V", v_mh)
        t_mh = p.op_inc("tensor", "T", lambda E:
                        E.matmul(ps_b[:, 0:1], c_orf[0:1, :], nrm[0:1, 5:6],
                                 start=True, stop=True))
        p.wait("vector", "T", t_mh)
        p.op("vector", lambda E: E.tensor_copy(mhat[:, :], ps_b[:, 0:1]))
        v_mhat = p.mark("vector", "V")
        if "attn" in dbg:
            p.wait("scalar", "V", v_mhat)
            p.dma("scalar", "ST", dbgt("mhat", [128, 1]), mhat[:, :])
            p.dma("scalar", "ST", dbgt("nrm", [1, 8]), nrm[:, :])
        p.wait("scalar", "V", v_mhat)

        ymid_st = [0] * OTILE
        a_exp_half = [0, 0]
        t_pv_half = [0, 0]
        a_o_prev = 0
        v_pm_hist = []
        unit = 0
        for qt in range(OTILE):
            nkb = qt + 1
            yb = ybuf[qt % 2]
            if qt >= 2:
                p.wait("scalar", "LD", ymid_st[qt - 2])
            for h in range(HEADS):
                til, base = h // 2, (h % 2) * 64
                qsl = qt_sb[til][base:base + 64, ts(qt, 128)]
                rl = lcol[:, 4 + h % 4:5 + h % 4]
                if nkb <= 8:
                    off = 1024 * (unit % 2)
                    chunks = [(unit % 2, 0, nkb)]
                else:
                    off = 0
                    chunks = [(0, 0, 8), (1, 8, nkb)]
                exp_marks = []
                for hf, k0, k1 in chunks:
                    p.wait("tensor", "A", a_exp_half[hf])

                    def mm_st(E, qsl=qsl, til=til, base=base, k0=k0, k1=k1,
                              off=off):
                        for kb in range(k0, k1):
                            last = E.matmul(
                                ps_s[:, off + kb * 128: off + (kb + 1) * 128],
                                kt_sb[til][base:base + 64,
                                           kb * 128:(kb + 1) * 128],
                                qsl, start=True, stop=True)
                        return last
                    t_s = p.op_inc("tensor", "T", mm_st)
                    p.wait("scalar", "T", max(t_s, t_pv_half[hf]))
                    if len(v_pm_hist) >= 2:
                        p.wait("scalar", "V", v_pm_hist[-2])
                    p.op("scalar", lambda E, k0=k0, k1=k1, off=off:
                         E.activation(pbuf[:, off + k0 * 128: off + k1 * 128],
                                      ps_s[:, off + k0 * 128: off + k1 * 128],
                                      AF.Exp, bias=mhat[:, :], scale=0.125))
                    a_exp = p.mark("scalar", "A")
                    a_exp_half[hf] = a_exp
                    exp_marks.append(a_exp)
                p.wait("vector", "A", exp_marks[-1])
                p.op("vector", lambda E, nkb=nkb, off=off: E.tensor_mul(
                    pbuf[:, off + (nkb - 1) * 128: off + nkb * 128],
                    pbuf[:, off + (nkb - 1) * 128: off + nkb * 128],
                    c_caus[:, :]))
                v_pm = p.mark("vector", "V")
                v_pm_hist.append(v_pm)
                for ci, (hf, k0, k1) in enumerate(chunks):
                    p.wait("tensor", "A", exp_marks[ci])
                    if ci == len(chunks) - 1:
                        p.wait("tensor", "V", v_pm)
                    if ci == 0:
                        p.wait("tensor", "A", a_o_prev)

                    def mm_pv(E, h=h, k0=k0, k1=k1, off=off, nkb=nkb):
                        for kb in range(k0, k1):
                            last = E.matmul(
                                ps_o[:, 0:65],
                                pbuf[:, off + kb * 128: off + (kb + 1) * 128],
                                vp[kb][:, h * 65:(h + 1) * 65],
                                start=(kb == 0), stop=(kb == nkb - 1))
                        return last
                    t_pv = p.op_inc("tensor", "T", mm_pv)
                    t_pv_half[hf] = t_pv
                p.wait("vector", "T", t_pv)
                p.op("vector", lambda E, rl=rl: E.reciprocal(rl, ps_o[:, 64:65]))
                v_rl = p.mark("vector", "V")
                p.wait("scalar", "T", t_pv)
                p.wait("scalar", "V", v_rl)
                p.op("scalar", lambda E, yb=yb, h=h, rl=rl: E.activation(
                    yb[:, ts(h, 64)], ps_o[:, 0:64], AF.Copy,
                    scale=rl))
                a_o = p.mark("scalar", "A")
                a_o_prev = a_o
                unit += 1
            p.wait("scalar", "A", a_o)
            ymid_st[qt] = p.dma("scalar", "LD", y_mid[ts(qt, 128), :], yb[:, :])
        ld_ymid = S["LD"].n
        t_attn_done = S["T"].n
        a_attn_done = S["A"].n
        v_attn_done = S["V"].n
        A.pop()
        A.pop()

        if "attn" in dbg:
            od = dbgt("attnY", [T, D])
            rb2 = A.take("at_rb", [128, D], BF16)
            rd2 = A.take("at_rd", [128, D], F32)
            for i in range(OTILE):
                p.wait("gpsimd", "LD", ld_ymid)
                lg = p.dma("gpsimd", "LG", rb2[:, :], y_mid[ts(i, 128), :])
                p.wait("vector", "LG", lg)
                p.op("vector", lambda E: E.tensor_copy(rd2[:, :], rb2[:, :]))
                vv = p.mark("vector", "V")
                p.wait("gpsimd", "V", vv)
                p.dma("gpsimd", "ST", od[ts(i, 128), :], rd2[:, :])

        if stop_after == "attn":
            finish_early()
            return nc, dbg_out

        attn_sb = A.take("attn_sb", [128, OTILE * D], BF16)

        A.push()
        ytb = A.take("ytb", [128, 8 * 2048], BF16)
        wo_sb = A.take("wo_sb", [128, 8 * 1024], BF16)
        p.wait("gpsimd", "T", t_attn_done)
        p.wait("gpsimd", "A", a_attn_done)
        p.wait("gpsimd", "V", v_attn_done)
        lg_wo = p.dma("gpsimd", "LG",
                      wo_sb[:, :].rearrange("p (dc i) -> p dc i", i=D),
                      wT["o"].rearrange("(dc p) i -> p dc i", p=128))
        p.wait("sync", "LD", ld_ymid)
        p.wait("sync", "T", t_attn_done)
        p.wait("sync", "LG", lg_disp)
        ldyT = 0
        for ic in range(8):
            ldyT = p.dmat("sync", "LD", ytb[:, ts(ic, 2048)],
                          y_mid[0:2048, ts(ic, 128)])
        p.wait("tensor", "LD", ldyT)
        p.wait("tensor", "LG", lg_wo)
        ev2 = []
        for t in range(OTILE):
            for jc in range(2):
                bank = ps_a if (t * 2 + jc) % 2 == 0 else ps_b
                if len(ev2) >= 2:
                    p.wait("tensor", "V", ev2[-2])

                def mm_yo(E, t=t, jc=jc, bank=bank):
                    E.matmul(bank[:, :], c_or[0:1, ts(t, 128)],
                             b_sb["o"][0:1, ts(jc, 512)], start=True, stop=False)
                    for ic in range(8):
                        last = E.matmul(
                            bank[:, :],
                            ytb[:, ic * 2048 + t * 128: ic * 2048 + (t + 1) * 128],
                            wo_sb[:, ic * 1024 + jc * 512: ic * 1024 + jc * 512 + 512],
                            start=False, stop=(ic == 7))
                    return last
                t_mm = p.op_inc("tensor", "T", mm_yo)
                p.wait("vector", "T", t_mm)
                p.op("vector", lambda E, t=t, jc=jc, bank=bank: E.tensor_copy(
                    attn_sb[:, t * D + jc * 512: t * D + jc * 512 + 512],
                    bank[:, :]))
                ev2.append(p.mark("vector", "V"))
        v_yproj = ev2[-1]
        t_yproj = S["T"].n
        if "yproj" in dbg:
            od = dbgt("attnout", [128, OTILE * D], BF16)
            p.wait("scalar", "V", v_yproj)
            p.dma("scalar", "ST", od, attn_sb[:, :])
            oy = dbgt("ymid", [T, D], BF16)
            for i in range(OTILE):
                p.dma("scalar", "ST", oy[ts(i, 128), :], y_mid[ts(i, 128), :])
        A.pop()
        if stop_after == "yproj":
            finish_early()
            return nc, dbg_out

        A.push()
        ht = A.take("ht", [128, 32 * CLOC], BF16)
        xtb = [A.take(f"xtb{i}", [128, 8 * CLOC], BF16) for i in range(2)]
        NSL = 4
        slab = [A.take(f"slab{i}", [128, D], BF16) for i in range(NSL)]
        NPC = 4
        pchk = [A.take(f"pchk{i}", [128, 512], BF16) for i in range(NPC)]
        eo = [A.take(f"eo{i}", [128, 512], BF16) for i in range(2)]

        slab_done = [None] * 32
        slab_consumed_T = [0] * NSL
        pchk_done = [None] * 64
        pchk_consumed_T = [0] * NPC
        exp_st = []
        t_h_done = [0] * NE
        t_out_done_prev = 0
        p.wait("tensor", "V", v_yproj)
        p.wait("sync", "T", t_yproj)
        p.wait("scalar", "T", t_yproj)
        p.wait("gpsimd", "T", t_yproj)
        p.wait("gpsimd", "LG", lg_disp)
        p.wait("sync", "LG", lg_disp)
        ldxT = [0] * NE
        a_h_prev = [0, 0]

        def issue_slab(ee_, fc):
            sbw = slab[fc % NSL]
            if slab_consumed_T[fc % NSL]:
                p.wait("gpsimd", "T", slab_consumed_T[fc % NSL])
            slab_done[fc] = ("LG", p.dma("gpsimd", "LG", sbw[:, :], cfc[ee_, fc]))

        def issue_pchk(ee_, step):
            dcp_, fc_ = divmod(step, 32)
            cb = pchk[step % NPC]
            if pchk_consumed_T[step % NPC]:
                p.wait("gpsimd", "T", pchk_consumed_T[step % NPC])
            pchk_done[step] = ("LG", p.dma("gpsimd", "LG", cb[:, :],
                                           cpj[ee_][ts(fc_, 128), ts(dcp_, 512)]))

        for fc in range(NSL):
            issue_slab(0, fc)
        for ee in range(NE):
            p.wait("sync", "T", t_h_done[max(ee - 1, 0)])
            for dc in range(8):
                ldxT[ee] = p.dmat("sync", "LD", xtb[ee % 2][:, ts(dc, CLOC)],
                                  exp_x[ts(ee, CLOC), ts(dc, 128)])
            for fc in range(32):
                sb = slab[fc % NSL]
                sem, val = slab_done[fc]
                p.wait("tensor", sem, val)
                if fc == 0:
                    p.wait("tensor", "LD", ldxT[ee])
                for sc in range(2):
                    w = 512 if sc == 0 else CLOC - 512
                    bank = ps_a if sc == 0 else ps_b
                    if a_h_prev[sc]:
                        p.wait("tensor", "A", a_h_prev[sc])

                    def mm_h(E, sb=sb, sc=sc, w=w, bank=bank, xt=xtb[ee % 2]):
                        for dc in range(8):
                            last = E.matmul(
                                bank[:, 0:w], sb[:, ts(dc, 128)],
                                xt[:, dc * CLOC + sc * 512: dc * CLOC + sc * 512 + w],
                                start=(dc == 0), stop=(dc == 7))
                        return last
                    t_h = p.op_inc("tensor", "T", mm_h)
                    p.wait("scalar", "T", t_h)
                    if fc == 0 and sc == 0:
                        p.wait("scalar", "T", t_out_done_prev)
                    p.op("scalar", lambda E, fc=fc, sc=sc, w=w, bank=bank:
                         E.activation(ht[:, fc * CLOC + sc * 512:
                                         fc * CLOC + sc * 512 + w],
                                      bank[:, 0:w], AF.Gelu))
                    a_h = p.mark("scalar", "A")
                    a_h_prev[sc] = a_h
                slab_consumed_T[fc % NSL] = S["T"].n
                if fc + NSL < 32:
                    issue_slab(ee, fc + NSL)
            t_h_done[ee] = S["T"].n
            a_ht_done = S["A"].n
            for st in range(NPC):
                issue_pchk(ee, st)
            if ee + 1 < NE:
                for fc in range(NSL):
                    issue_slab(ee + 1, fc)
            banks7 = [ps_s[:, ts(i, 512)] for i in range(4)] + \
                     [ps_a[:, :], ps_b[:, :], ps_o[:, :]]
            for dcp in range(2):
                for fc in range(32):
                    step = dcp * 32 + fc
                    cb = pchk[step % NPC]
                    sem, val = pchk_done[step]
                    p.wait("tensor", sem, val)
                    if fc == 0:
                        p.wait("tensor", "A", a_ht_done)
                        p.wait("tensor", "V")
                        if dcp == 0:
                            p.wait("tensor", "A", a_h_prev[0])
                            p.wait("tensor", "A", a_h_prev[1])

                    def mm_o(E, cb=cb, fc=fc):
                        for st7 in range(7):
                            last = E.matmul(
                                banks7[st7],
                                ht[:, fc * CLOC + st7 * 128: fc * CLOC + (st7 + 1) * 128],
                                cb[:, :], start=(fc == 0), stop=(fc == 31))
                        return last
                    t_o = p.op_inc("tensor", "T", mm_o)
                    pchk_consumed_T[step % NPC] = S["T"].n
                    if step + NPC < 64:
                        issue_pchk(ee, step + NPC)
                for st7 in range(7):
                    b2 = eo[st7 % 2]
                    if st7 >= 2:
                        p.wait("vector", "LD", exp_st[-2])
                    p.wait("vector", "T", t_o)
                    p.op("vector", lambda E, st7=st7, b2=b2:
                         E.tensor_copy(b2[:, :], banks7[st7]))
                    v_eo = p.mark("vector", "V")
                    p.wait("sync", "V", v_eo)
                    exp_st.append(p.dma(
                        "sync", "LD",
                        exp_out[ee * CLOC + st7 * 128: ee * CLOC + (st7 + 1) * 128,
                                ts(dcp, 512)],
                        b2[:, :]))
            t_out_done_prev = S["T"].n
        ld_expout = S["LD"].n
        if stop_after == "ffn":
            finish_early()
            return nc, dbg_out
        if "ffn" in dbg:
            od = dbgt("expout", [NSLOT + 1, D], BF16)
            ox = dbgt("expx", [NSLOT, D], BF16)
            p.wait("gpsimd", "LD", ld_expout)
            rb3 = A.take("ffn_rb", [128, D], BF16)
            for i in range(NSLOT // 128):
                lg = p.dma("gpsimd", "LG", rb3[:, :], exp_x[ts(i, 128), :])
                p.wait("gpsimd", "LG", lg)
                p.dma("gpsimd", "ST", ox[ts(i, 128), :], rb3[:, :])
                lg = p.dma("gpsimd", "LG", rb3[:, :], exp_out[ts(i, 128), :])
                p.wait("gpsimd", "LG", lg)
                p.dma("gpsimd", "ST", od[ts(i, 128), :], rb3[:, :])
        A.pop()

        A.push()
        cx = [A.take(f"cx{i}", [128, D], F32) for i in range(2)]
        mg0 = [A.take(f"mg0{i}", [128, D], BF16) for i in range(2)]
        mg1 = [A.take(f"mg1{i}", [128, D], BF16) for i in range(2)]
        ac1 = A.take("ac1", [128, D], F32)
        ac2 = A.take("ac2", [128, D], F32)
        ac3 = [A.take(f"ac3{i}", [128, D], F32) for i in range(2)]

        p.wait("gpsimd", "LD", ld_expout)
        p.wait("vector", "T")
        st_y = []
        v_comb = [0, 0]
        for t in range(OTILE):
            b = t % 2
            if t >= 2:
                p.wait("sync", "ST", st_y[t - 2])
                p.wait("sync", "V", v_comb[b])
                p.wait("gpsimd", "V", v_comb[b])
            ld_cx = p.dma("sync", "LD", cx[b][:, :], x_hbm[ts(t, 128), :])
            lg0 = p.igather("gpsimd", "LG", mg0[b][:, :], exp_out[:, :],
                            g0i[:, t:t + 1], NSLOT)
            lg1 = p.igather("gpsimd", "LG", mg1[b][:, :], exp_out[:, :],
                            g1i[:, t:t + 1], NSLOT)
            p.wait("vector", "LD", ld_cx)
            p.wait("vector", "LG", lg1)
            p.wait("vector", "V", v_yproj)
            p.op("vector", lambda E, t=t, b=b: E.tensor_add(
                ac1[:, :], cx[b][:, :], attn_sb[:, ts(t, D)]))
            p.op("vector", lambda E, t=t, b=b: E.scalar_tensor_tensor(
                ac2[:, :], mg0[b][:, :], p1[:, t:t + 1], ac1[:, :],
                op0=ALU.mult, op1=ALU.add))
            p.op("vector", lambda E, t=t, b=b: E.scalar_tensor_tensor(
                ac3[b][:, :], mg1[b][:, :], p2[:, t:t + 1], ac2[:, :],
                op0=ALU.mult, op1=ALU.add))
            v_comb[b] = p.mark("vector", "V")
            p.wait("sync", "V", v_comb[b])
            st_y.append(p.dma("sync", "ST", y_out[ts(t, 128), :], ac3[b][:, :]))
        p.wait("sync", "ST", S["ST"].n)
        A.pop()

        for eng, meth in (("sync", block.sync), ("gpsimd", block.gpsimd),
                          ("tensor", block.tensor), ("vector", block.vector),
                          ("scalar", block.scalar)):
            def runner(E, ops=p.q[eng]):
                for f in ops:
                    f(E)
            meth(runner)

    return nc, dbg_out



def make_consts(core):
    b = np.arange(128)
    true_b = (b + core * OTILE) % 128
    tri_rot = (true_b[:, None] < true_b[None, :]).astype(np.float32)
    return dict(
        tri_rot=tri_rot.astype(ml_dtypes.bfloat16),
        ones_bf=np.ones((128, 128), ml_dtypes.bfloat16),
        id_bf=np.eye(128, dtype=np.float32).astype(ml_dtypes.bfloat16),
        id_f=np.eye(128, dtype=np.float32),
        caus_bf=np.triu(np.ones((128, 128), np.float32)).astype(ml_dtypes.bfloat16),
        onesrow_bf=np.ones((1, 2048), ml_dtypes.bfloat16),
        onesrow_f=np.ones((1, 128), np.float32),
        onescol_f=np.ones((128, 1), np.float32),
        tokid_f=(np.arange(OTILE)[None, :] * 128
                 + np.arange(128)[:, None]).astype(np.float32),
    )


def make_inputs(L0_attn_x, L0_attn_Wq, L0_attn_bq, L0_attn_Wk, L0_attn_bk,
                L0_attn_Wv, L0_attn_bv, L0_attn_Wo, L0_attn_bo,
                L0_ffn_w_g, L0_ffn_c_fc, L0_ffn_c_proj):
    x = np.ascontiguousarray(np.asarray(L0_attn_x, np.float32).reshape(NT, D))
    shared = dict(
        wqT=np.ascontiguousarray(np.asarray(L0_attn_Wq, np.float32).T),
        wkT=np.ascontiguousarray(np.asarray(L0_attn_Wk, np.float32).T),
        wvT=np.ascontiguousarray(np.asarray(L0_attn_Wv, np.float32).T),
        woT=np.ascontiguousarray(np.asarray(L0_attn_Wo, np.float32).T),
        bq=np.asarray(L0_attn_bq, np.float32).reshape(1, D),
        bk=np.asarray(L0_attn_bk, np.float32).reshape(1, D),
        bv=np.asarray(L0_attn_bv, np.float32).reshape(1, D),
        bo=np.asarray(L0_attn_bo, np.float32).reshape(1, D),
        wg=np.asarray(L0_ffn_w_g, np.float32),
        cfc=np.ascontiguousarray(
            np.asarray(L0_ffn_c_fc, np.float32)
            .reshape(NE, 8, 128, 32, 128).transpose(0, 3, 2, 1, 4)
        ).reshape(NE, 32, 128, D),
        cpj=np.asarray(L0_ffn_c_proj, np.float32),
    )
    in_maps = []
    for c in range(8):
        m = dict(shared)
        m["x"] = np.ascontiguousarray(np.roll(x, -c * T, axis=0))
        m.update(make_consts(c))
        in_maps.append(m)
    return in_maps


def kernel(**inputs):
    in_maps = make_inputs(**inputs)
    nc, _ = build()
    res = run_bass_kernel_spmd(nc, in_maps, core_ids=list(range(8)))
    y = np.concatenate([res.results[c]["y"] for c in range(8)], axis=0)
    return y.reshape(8, T, D)
